# version 06
# baseline (speedup 1.0000x reference)
import numpy as np
import ml_dtypes
from contextlib import ExitStack
import concourse.bass as bass
import concourse.mybir as mybir
from concourse.bass_utils import run_bass_kernel_spmd

F32 = mybir.dt.float32
BF16 = mybir.dt.bfloat16
I32 = mybir.dt.int32
ALU = mybir.AluOpType
AF = mybir.ActivationFunctionType

NCORES = 8
D = 1024
SEQLEN = 2048
TOK = 4096
STRIPE = 512
NSTRIPE = TOK // STRIPE
TPS = 4
NCH = D // 128
H = 16
DH = 64
BS = 256
HID = 3072
EPS = 1e-5

P1STAGE = 99

ENGINES = ("pe", "act", "dve", "pool", "sp")
ENG_ATTR = {"pe": "tensor", "act": "scalar", "dve": "vector", "pool": "gpsimd",
            "sp": "sync"}


class Em:

    def __init__(self, nc, stack):
        self.nc = nc
        self.stack = stack
        self.sems = {}
        self.cnt = {}
        self.ops = {e: [] for e in ENGINES}
        self.marks = {}

    def sem(self, name):
        if name not in self.sems:
            self.sems[name] = self.stack.enter_context(self.nc.semaphore(name))
            self.cnt[name] = 0
        return self.sems[name]

    def op(self, eng, fn, waits=(), inc=None, inc_amt=1, mark=None):
        wl = []
        for w in waits:
            if w is None:
                continue
            if isinstance(w, str):
                if w not in self.marks:
                    raise KeyError(f"unknown mark {w!r}")
                wl.append(self.marks[w])
            else:
                self.sem(w[0])
                wl.append(w)
        if mark is not None and inc is None:
            inc = "s_" + eng
        rec = {"fn": fn, "waits": wl, "inc": None}
        if inc is not None:
            self.sem(inc)
            self.cnt[inc] += inc_amt
            rec["inc"] = (inc, inc_amt)
            if mark is not None:
                self.marks[mark] = (inc, self.cnt[inc])
        self.ops[eng].append(rec)

    def dma(self, eng, fn, inc, mark=None, waits=()):
        self.op(eng, fn, waits=waits, inc=inc, inc_amt=16, mark=mark)

    def barrier(self):
        for eng in ENGINES:
            if self.ops[eng]:
                rec = self.ops[eng][-1]
                if rec["inc"] is None:
                    s = "s_" + eng
                    self.sem(s)
                    self.cnt[s] += 1
                    rec["inc"] = (s, 1)
        targets = [(s, self.cnt[s]) for s in list(self.cnt) if self.cnt[s] > 0]
        for eng in ENGINES:
            if not self.ops[eng]:
                continue
            for (s, v) in targets:
                self.ops[eng].append({"fn": None, "waits": [(s, v)],
                                      "inc": None})

    def finalize(self, blk):
        for eng in ENGINES:
            if not self.ops[eng]:
                continue

            def mkbody(ops, sems, sep):
                def run(e):
                    waited = {}
                    prev_compute = False
                    for rec in ops:
                        for (s, v) in rec["waits"]:
                            if waited.get(s, -1) < v:
                                e.wait_ge(sems[s], v)
                                waited[s] = v
                        if rec["fn"] is not None:
                            if sep and prev_compute:
                                e.drain()
                            ins = rec["fn"](e)
                            prev_compute = True
                            if rec["inc"] is not None:
                                s, amt = rec["inc"]
                                ins.then_inc(sems[s], amt)
                return run

            getattr(blk, ENG_ATTR[eng])(mkbody(self.ops[eng], self.sems,
                                               eng in ('dve', 'act', 'pool')))


class BankPool:

    def __init__(self, banks):
        self.banks = banks
        self.uses = 0
        self.free_mark = [None] * len(banks)

    def acquire(self):
        i = self.uses % len(self.banks)
        self.uses += 1
        return self.banks[i], self.free_mark[i], i

    def release(self, i, mark):
        self.free_mark[i] = mark


def newton_rsqrt(em, v_ap, y_ap, tmp_ap, ci_ap, waits=(), mark=None):
    MAGIC = 0x5f3759df
    em.op("dve", lambda e: e.tensor_scalar(
        y_ap.bitcast(I32), v_ap.bitcast(I32), 1, None, ALU.arith_shift_right),
        waits=waits)
    em.op("dve", lambda e: e.memset(ci_ap, MAGIC))
    em.op("dve", lambda e: e.tensor_tensor(
        y_ap.bitcast(I32), ci_ap, y_ap.bitcast(I32), ALU.subtract))
    for it in range(3):
        em.op("dve", lambda e: e.tensor_tensor(tmp_ap, y_ap, y_ap, ALU.mult))
        em.op("dve", lambda e: e.tensor_tensor(tmp_ap, tmp_ap, v_ap, ALU.mult))
        em.op("dve", lambda e: e.tensor_scalar(
            tmp_ap, tmp_ap, -0.5, 1.5, ALU.mult, ALU.add))
        em.op("dve", lambda e: e.tensor_tensor(y_ap, y_ap, tmp_ap, ALU.mult),
              mark=(mark if it == 2 else None))


class LNBufs:
    def __init__(self, sb, tag):
        self.st = sb(f"{tag}_st", [128, TPS * 12], F32)
        self.mv = sb(f"{tag}_mv", [128, TPS * 2], F32)
        self.vs = sb(f"{tag}_vs", [128, TPS], F32)
        self.rs = sb(f"{tag}_rs", [128, TPS], F32)
        self.nb = sb(f"{tag}_nb", [128, TPS], F32)
        self.tmp = sb(f"{tag}_tmp", [128, TPS], F32)
        self.ci = sb(f"{tag}_ci", [128, TPS], I32)


def ln_stats(em, ln, src_tiles, tag, waits_first=()):
    for t in range(TPS):
        for hh in range(2):
            em.op("dve", lambda e, t=t, hh=hh: e.bn_stats(
                ln.st[:, t * 12 + hh * 6:t * 12 + hh * 6 + 6],
                src_tiles[t][:, hh * 512:hh * 512 + 512]),
                waits=(waits_first if (t == 0 and hh == 0) else ()))
    for t in range(TPS):
        em.op("dve", lambda e, t=t: e.bn_aggr(
            ln.mv[:, t * 2:t * 2 + 2], ln.st[:, t * 12:t * 12 + 12]))
    for t in range(TPS):
        em.op("dve", lambda e, t=t: e.tensor_scalar(
            ln.vs[:, t:t + 1], ln.mv[:, 2 * t + 1:2 * t + 2],
            EPS, None, ALU.add))
    newton_rsqrt(em, ln.vs[:, 0:TPS], ln.rs[:, 0:TPS], ln.tmp[:, 0:TPS], ln.ci[:, 0:TPS])
    for t in range(TPS):
        em.op("dve", lambda e, t=t: e.scalar_tensor_tensor(
            ln.nb[:, t:t + 1], ln.mv[:, 2 * t:2 * t + 1],
            -1.0, ln.rs[:, t:t + 1], ALU.mult, ALU.mult),
            mark=(f"{tag}_lnp" if t == TPS - 1 else None))


class StripeCtx:

    def __init__(self, em, sb, ps, tag, carry=False, nxs=2, nxnt=2,
                 ptr_aps=None):
        self.em = em
        self.tag = tag
        self.carry = carry
        self.nxs = nxs
        self.nxnt = nxnt
        self.ptr_aps = ptr_aps
        self.w = 520 if carry else 512
        self.off = 1 if carry else 0
        self.xs = [sb(f"{tag}_x{i}", [128, TPS * D], F32)
                   for i in range(getattr(self, "nxs", 2))]
        self.xnT = [sb(f"{tag}_xnT{i}", [128, NCH * self.w], BF16)
                    for i in range(getattr(self, "nxnt", 2))]
        self.xn = sb(f"{tag}_xn", [128, TPS * D], BF16)
        self.ln = [LNBufs(sb, f"{tag}_li{i}") for i in range(2)]
        if getattr(self, "ptr_aps", None) is not None:
            self.ptr = self.ptr_aps
        else:
            self.ptr = [ps(f"{tag}_ptr{i}", [128, NCH * 128], BF16)
                        for i in range(2)]

    def chunk(self, bi, c):
        o = c * self.w + self.off
        return self.xnT[bi][:, o:o + 512]

    def chunk_shift(self, bi, c):
        o = c * self.w + self.off
        return self.xnT[bi][:, o - 1:o + 511]

    def chunk3d(self, bi):
        return self.xnT[bi][:, 0:NCH * self.w].rearrange(
            "p (c w) -> p c w", c=NCH)


def stripe_prologue(em, sc, s, x_dram, ident, xnt_war=None, x_extra_wait=None,
                    ptr_war=None):
    bi = s % len(sc.xnT)
    xs = sc.xs[s % len(sc.xs)]
    T0 = sc.tag
    tag = f"{T0}s{s}"
    w = []
    sx = s - len(sc.xs)
    if sx >= 0 and f"{T0}s{sx}_store" in em.marks:
        w.append(f"{T0}s{sx}_store")
    if x_extra_wait:
        w.append(x_extra_wait)
    em.dma("sp", lambda e: e.dma_start(
        xs.rearrange("p (t d) -> p t d", t=TPS),
        x_dram[s * STRIPE:(s + 1) * STRIPE, :]
        .rearrange("(t p) d -> p t d", p=128)),
        inc=f"ld{s % 2}", mark=f"{tag}_xld", waits=w)
    tiles = [xs[:, t * D:(t + 1) * D] for t in range(TPS)]
    ln_stats(em, sc.ln[bi], tiles, tag, waits_first=[f"{tag}_xld"])
    for t in range(TPS):
        ww = [f"{tag}_lnp"] if t == 0 else []
        if t == 0 and s >= 1:
            ww.append(f"{T0}s{s - 1}_tr{TPS - 1}")
        em.op("act", lambda e, t=t, bi=bi: e.activation(
            sc.xn[:, t * D:(t + 1) * D], xs[:, t * D:(t + 1) * D],
            AF.Identity,
            bias=sc.ln[bi].nb[:, t:t + 1], scale=sc.ln[bi].rs[:, t:t + 1]),
            waits=ww, mark=f"{tag}_xn{t}")
    em.marks[f"{tag}_xn"] = em.marks[f"{tag}_xn{TPS - 1}"]
    for t in range(TPS):
        pb = sc.ptr[t % 2]
        bank_war = None
        if t >= 2:
            bank_war = f"{tag}_trev{t - 2}"
        elif s >= 1:
            bank_war = f"{T0}s{s - 1}_trev{t + 2}"
        bw = [bank_war] if bank_war else []
        if t < 2 and ptr_war:
            bw += [m for m in ptr_war if m in em.marks]
        for c in range(NCH):
            em.op("pe", lambda e, t=t, c=c, pb=pb: e.transpose(
                pb[:, c * 128:(c + 1) * 128],
                sc.xn[:, t * D + c * 128:t * D + c * 128 + 128], ident),
                waits=([f"{tag}_xn{t}", "c_ident"] + bw)
                if c == 0 else (),
                mark=(f"{tag}_tr{t}" if c == NCH - 1 else None))
        ev_w = [f"{tag}_tr{t}"]
        if t == 0 and xnt_war:
            ev_w += [m for m in xnt_war if m in em.marks]
        em.op("act", lambda e, t=t, pb=pb, bi=bi: e.activation(
            sc.chunk3d(bi)[:, :, sc.off + t * 128:sc.off + t * 128 + 128],
            pb[:, 0:NCH * 128].rearrange("p (c w) -> p c w", c=NCH), AF.Copy),
            waits=ev_w, mark=f"{tag}_trev{t}")
    em.marks[f"{tag}_xnT"] = em.marks[f"{tag}_trev{TPS - 1}"]
    return bi


def ln_post_add(em, lno, s, outb, xs, x_out, tag, waits_first):
    tiles = [outb[:, t * D:(t + 1) * D] for t in range(TPS)]
    ln = lno[s % 2]
    ln_stats(em, ln, tiles, tag + "o", waits_first=waits_first)
    for t in range(TPS):
        em.op("act", lambda e, t=t: e.activation(
            outb[:, t * D:(t + 1) * D], outb[:, t * D:(t + 1) * D],
            AF.Identity,
            bias=ln.nb[:, t:t + 1], scale=ln.rs[:, t:t + 1]),
            waits=[f"{tag}o_lnp"] if t == 0 else (),
            mark=(f"{tag}_lnapp" if t == TPS - 1 else None))
    em.op("dve", lambda e: e.tensor_tensor(xs[:], outb[:], xs[:], ALU.add),
          waits=[f"{tag}_lnapp"], mark=f"{tag}_add")
    em.dma("pool", lambda e: e.dma_start(
        x_out[s * STRIPE:(s + 1) * STRIPE, :]
        .rearrange("(t p) d -> p t d", p=128),
        xs.rearrange("p (t d) -> p t d", t=TPS)),
        inc=f"st{s % 2}", mark=f"{tag}_store", waits=[f"{tag}_add"])


def load_weights(em, wstage, dsts, srcs, tag, scale=None):
    n = 0
    for wi, (wsb, wd) in enumerate(zip(dsts, srcs)):
        ncw = wd.shape[0] // 128
        cols = wd.shape[1]
        for c in range(ncw):
            st = wstage[n % len(wstage)]
            nst = len(wstage)
            prev = f"{tag}_cast{n - nst}" if n >= nst else None
            em.dma("sp", lambda e, st=st, wd=wd, c=c, cols=cols: e.dma_start(
                st[:, 0:cols], wd[c * 128:(c + 1) * 128, :]),
                inc=f"{tag}_wp{n % len(wstage)}", waits=[prev] if prev else (),
                mark=f"{tag}_wl{n}")
            kw = {} if scale is None else {"scale": scale}
            em.op("act", lambda e, wsb=wsb, st=st, c=c, cols=cols, kw=kw:
                  e.activation(wsb[:, c * cols:(c + 1) * cols],
                               st[:, 0:cols], AF.Copy, **kw),
                  waits=[f"{tag}_wl{n}"], mark=f"{tag}_cast{n}")
            em.marks[f"{tag}_w{wi}c{c}"] = em.marks[f"{tag}_cast{n}"]
            n += 1
        em.marks[f"{tag}_w{wi}"] = em.marks[f"{tag}_cast{n - 1}"]



def build_pass_l0ffn(em, sb, ps, cd, x_in, x_out):
    ident = cd["ident_s"]
    P = "p2"
    wr = sb(P + "_wr", [128, NCH * D], BF16)
    wk = sb(P + "_wk", [128, NCH * D], BF16)
    wv = sb(P + "_wv", [128, NCH * D], BF16)
    wstage = [sb(P + f"_wst{i}", [128, D], F32) for i in range(2)]
    mur = sb(P + "_mur", [128, NCH], F32)
    muk = sb(P + "_muk", [128, NCH], F32)
    sc = StripeCtx(em, sb, ps, P, carry=True)
    xrT = sb(P + "_xr", [128, NCH * 512], BF16)
    xkT = sb(P + "_xk", [128, NCH * 512], BF16)
    dT = sb(P + "_d", [128, NCH * 512], BF16)
    rsig = sb(P + "_rsig", [128, TPS * D], F32)
    kact = sb(P + "_kact", [128, NCH * 512], BF16)
    krelu = [sb(P + f"_krelu{i}", [128, 512], F32) for i in range(2)]
    outb = sb(P + "_out", [128, TPS * D], F32)
    lno = [LNBufs(sb, P + f"_lo{i}") for i in range(2)]
    pj = BankPool([ps(P + f"_pj{i}", [128, 512], F32) for i in range(4)])

    load_weights(em, wstage, [wr, wk, wv],
                 [cd["l0f_wrT"], cd["l0f_wkT"], cd["l0f_wvT"]], P)
    em.dma("sp", lambda e: e.dma_start(mur[:], cd["l0f_mur"]),
           inc=P + "_mi", mark=P + "_mul")
    em.dma("sp", lambda e: e.dma_start(muk[:], cd["l0f_muk"]),
           inc=P + "_mi", waits=[P + "_mul"], mark=P + "_mu")

    for s in range(NSTRIPE):
        tag = f"{P}s{s}"
        war = []
        if s >= 2:
            war.append(f"{P}s{s - 2}_mix")
        if s >= 1:
            war.append(f"{P}s{s - 1}_carry")
        bi = stripe_prologue(em, sc, s, x_in, ident, xnt_war=war)
        if s % (NSTRIPE // 2) == 0:
            em.op("dve", lambda e, bi=bi: e.memset(
                sc.chunk3d(bi)[:, :, 0:1], 0.0),
                waits=[f"{tag}_xnT"], mark=f"{tag}_carry")
        else:
            pb = 1 - bi
            em.op("dve", lambda e, bi=bi, pb=pb: e.tensor_copy(
                sc.chunk3d(bi)[:, :, 0:1], sc.chunk3d(pb)[:, :, 512:513]),
                waits=[f"{tag}_xnT"], mark=f"{tag}_carry")
        mix_war = []
        if s >= 1:
            mix_war = [f"{P}s{s - 1}_rmm{TPS - 1}_1", f"{P}s{s - 1}_kmm{NCH - 1}"]
        for c in range(NCH):
            em.op("dve", lambda e, c=c, bi=bi: e.tensor_tensor(
                dT[:, c * 512:(c + 1) * 512], sc.chunk(bi, c),
                sc.chunk_shift(bi, c)[:, 0:512], ALU.subtract),
                waits=([f"{tag}_carry", P + "_mu"] + mix_war) if c == 0 else ())
            em.op("dve", lambda e, c=c, bi=bi: e.scalar_tensor_tensor(
                xrT[:, c * 512:(c + 1) * 512], dT[:, c * 512:(c + 1) * 512],
                mur[:, c:c + 1], sc.chunk_shift(bi, c)[:, 0:512],
                ALU.mult, ALU.add))
            em.op("dve", lambda e, c=c, bi=bi: e.scalar_tensor_tensor(
                xkT[:, c * 512:(c + 1) * 512], dT[:, c * 512:(c + 1) * 512],
                muk[:, c:c + 1], sc.chunk_shift(bi, c)[:, 0:512],
                ALU.mult, ALU.add),
                mark=(f"{tag}_mix" if c == NCH - 1 else None))
        for t in range(TPS):
            for eh in range(2):
                bank, fw, bidx = pj.acquire()
                for c in range(NCH):
                    em.op("pe", lambda e, t=t, eh=eh, c=c, bank=bank: e.matmul(
                        bank, xrT[:, c * 512 + t * 128:c * 512 + t * 128 + 128],
                        wr[:, c * D + eh * 512:c * D + eh * 512 + 512],
                        start=(c == 0), stop=(c == NCH - 1)),
                        waits=([f"{tag}_mix", P + "_w0"] +
                               ([fw] if fw else [])) if c == 0 else (),
                        mark=(f"{tag}_rmm{t}_{eh}" if c == NCH - 1 else None))
                rs_war = ([f"{P}s{s - 1}_kv{t}_{eh}"] if s >= 1 else [])
                em.op("act", lambda e, t=t, eh=eh, bank=bank: e.activation(
                    rsig[:, t * D + eh * 512:t * D + eh * 512 + 512], bank,
                    AF.Sigmoid),
                    waits=[f"{tag}_rmm{t}_{eh}"] + rs_war,
                    mark=f"{tag}_rs{t}_{eh}")
                pj.release(bidx, f"{tag}_rs{t}_{eh}")
        for co in range(NCH):
            bank, fw, bidx = pj.acquire()
            for c in range(NCH):
                em.op("pe", lambda e, co=co, c=c, bank=bank: e.matmul(
                    bank, wk[:, c * D + co * 128:c * D + co * 128 + 128],
                    xkT[:, c * 512:(c + 1) * 512],
                    start=(c == 0), stop=(c == NCH - 1)),
                    waits=([P + "_w1"] + ([fw] if fw else [])) if c == 0 else (),
                    mark=(f"{tag}_kmm{co}" if c == NCH - 1 else None))
            kr = krelu[co % 2]
            kr_war = [f"{tag}_ksq{co - 2}"] if co >= 2 else (
                [f"{P}s{s - 1}_ksq{NCH - 2 + (co % 2) * 0 + co}"]
                if False else [])
            em.op("dve", lambda e, bank=bank, kr=kr: e.tensor_scalar(
                kr[:], bank, 0.0, None, ALU.max),
                waits=[f"{tag}_kmm{co}"] + kr_war, mark=f"{tag}_krel{co}")
            pj.release(bidx, f"{tag}_krel{co}")
            ks_war = ([f"{P}s{s - 1}_vmm{TPS - 1}_1"]
                      if (s >= 1 and co == 0) else [])
            em.op("act", lambda e, co=co, kr=kr: e.activation(
                kact[:, co * 512:(co + 1) * 512], kr[:], AF.Square),
                waits=[f"{tag}_krel{co}"] + ks_war, mark=f"{tag}_ksq{co}")
        ob = outb
        for t in range(TPS):
            for eh in range(2):
                bank, fw, bidx = pj.acquire()
                for c in range(NCH):
                    em.op("pe", lambda e, t=t, eh=eh, c=c, bank=bank: e.matmul(
                        bank, kact[:, c * 512 + t * 128:c * 512 + t * 128 + 128],
                        wv[:, c * D + eh * 512:c * D + eh * 512 + 512],
                        start=(c == 0), stop=(c == NCH - 1)),
                        waits=[f"{tag}_ksq{c}", P + "_w2"] +
                              ([fw] if (c == 0 and fw) else []),
                        mark=(f"{tag}_vmm{t}_{eh}" if c == NCH - 1 else None))
                em.op("dve", lambda e, t=t, eh=eh, bank=bank, ob=ob:
                      e.tensor_tensor(
                          ob[:, t * D + eh * 512:t * D + eh * 512 + 512],
                          bank,
                          rsig[:, t * D + eh * 512:t * D + eh * 512 + 512],
                          ALU.mult),
                      waits=[f"{tag}_vmm{t}_{eh}", f"{tag}_rs{t}_{eh}"],
                      mark=f"{tag}_kv{t}_{eh}")
                pj.release(bidx, f"{tag}_kv{t}_{eh}")
        ln_post_add(em, lno, s, outb, sc.xs[s % len(sc.xs)], x_out, tag,
                    waits_first=[f"{tag}_kv{TPS - 1}_1"])


def _p1_passthrough(em, sc, bi, s, tag, x_out, wait_mark):
    em.dma("pool", lambda e: e.dma_start(
        x_out[s * STRIPE:(s + 1) * STRIPE, :]
        .rearrange("(t p) d -> p t d", p=128),
        sc.xs[bi].rearrange("p (t d) -> p t d", t=TPS)),
        inc=f"st{s % 2}", mark=f"{tag}_store", waits=[wait_mark])



def build_pass_l0attn(em, sb, ps, cd, x_in, x_out):
    ident = cd["ident_s"]
    P = "p1"
    wq = sb(P + "_wq", [128, NCH * D], BF16)
    wk = sb(P + "_wk", [128, NCH * D], BF16)
    wv = sb(P + "_wv", [128, NCH * D], BF16)
    wo = sb(P + "_wo", [128, NCH * D], BF16)
    wstage = [sb(P + f"_wst{i}", [128, D], F32) for i in range(2)]
    bq = sb(P + "_bq", [128, NCH], F32)
    bk = sb(P + "_bk", [128, NCH], F32)
    bvb = sb(P + "_bvb", [128, D], BF16)
    bob = sb(P + "_bob", [128, D], BF16)
    bstage = wstage[1]
    mask4 = sb(P + "_mask4", [128, 512], BF16)
    sc = StripeCtx(em, sb, ps, P, carry=False)
    qf = sb(P + "_qf", [128, NCH * 512], BF16)
    kze = sb(P + "_kze", [128, NCH * 512], BF16)
    kzo = sb(P + "_kzo", [128, NCH * 512], BF16)
    vst = sb(P + "_vst", [128, TPS * D], BF16)
    ktm = sb(P + "_ktm", [128, TPS * D], BF16)
    sT = sb(P + "_sT", [128, 2 * 512], BF16)
    stf = sb(P + "_stf", [128, H * DH], F32)
    stb = sb(P + "_stb", [128, H * DH], BF16)
    ofm = sb(P + "_ofm", [128, NCH * 512], BF16)
    outb = sb(P + "_out", [128, TPS * D], F32)
    lno = [LNBufs(sb, P + f"_lo{i}") for i in range(2)]
    pj = BankPool([ps(P + f"_pj{i}", [128, 512], F32) for i in range(4)])
    po = [ps(P + f"_po{i}", [128, 512], F32) for i in range(2)]

    load_weights(em, wstage, [wq, wk, wv, wo],
                 [cd["l0a_wqT"], cd["l0a_wkT"], cd["l0a_wvT"], cd["l0a_woT"]],
                 P)
    em.dma("sp", lambda e: e.dma_start(bq[:], cd["l0a_bq"]), inc=P + "_mi",
           mark=P + "_bql")
    em.dma("sp", lambda e: e.dma_start(bk[:], cd["l0a_bk"]), inc=P + "_mi",
           waits=[P + "_bql"], mark=P + "_bqk")
    em.dma("sp", lambda e: e.dma_start(bstage[:], cd["l0a_bvb"]),
           inc=P + "_mi", waits=[P + "_bqk", P + "_w3"], mark=P + "_bvl")
    em.op("act", lambda e: e.activation(bvb[:], bstage[:], AF.Copy),
          waits=[P + "_bvl"], mark=P + "_bv")
    em.dma("sp", lambda e: e.dma_start(bstage[:], cd["l0a_bob"]),
           inc=P + "_mi", waits=[P + "_bv"], mark=P + "_bol")
    em.op("act", lambda e: e.activation(bob[:], bstage[:], AF.Copy),
          waits=[P + "_bol"], mark=P + "_bo")
    em.dma("sp", lambda e: e.dma_start(mask4[:], cd["l0mask4"]),
           inc=P + "_mi", waits=[P + "_bol"], mark=P + "_msk")
    em.op("dve", lambda e: e.memset(kze[64:128, :], 0.0))
    em.op("dve", lambda e: e.memset(kzo[0:64, :], 0.0), mark=P + "_kz0")

    for s in range(NSTRIPE):
        tag = f"{P}s{s}"
        bi = stripe_prologue(em, sc, s, x_in, ident)
        for co in range(NCH):
            bank, fw, bidx = pj.acquire()
            for c in range(NCH):
                em.op("pe", lambda e, co=co, c=c, bank=bank, bi=bi: e.matmul(
                    bank[:],
                    wq[:, c * D + co * 128:c * D + co * 128 + 128],
                    sc.chunk(bi, c),
                    start=(c == 0), stop=(c == NCH - 1)),
                    waits=([f"{tag}_xnT", P + "_w0", P + "_bqk"] +
                           ([fw] if fw else [])) if c == 0 else (),
                    mark=(f"{tag}_qmm{co}" if c == NCH - 1 else None))
            em.op("act", lambda e, co=co, bank=bank: e.activation(
                qf[:, co * 512:(co + 1) * 512], bank[:],
                AF.Identity, bias=bq[:, co:co + 1]),
                waits=[f"{tag}_qmm{co}"], mark=f"{tag}_qf{co}")
            pj.release(bidx, f"{tag}_qf{co}")
        for co in range(NCH):
            bank, fw, bidx = pj.acquire()
            for c in range(NCH):
                em.op("pe", lambda e, co=co, c=c, bank=bank, bi=bi: e.matmul(
                    bank[:],
                    wk[:, c * D + co * 128:c * D + co * 128 + 128],
                    sc.chunk(bi, c),
                    start=(c == 0), stop=(c == NCH - 1)),
                    waits=([P + "_w1", P + "_kz0"] +
                           ([fw] if fw else [])) if c == 0 else (),
                    mark=(f"{tag}_kmm{co}" if c == NCH - 1 else None))
            em.op("act", lambda e, co=co, bank=bank: e.activation(
                kze[0:64, co * 512:(co + 1) * 512], bank[0:64, :],
                AF.Identity, bias=bk[0:64, co:co + 1]),
                waits=[f"{tag}_kmm{co}"], mark=f"{tag}_kfe{co}")
            em.op("act", lambda e, co=co, bank=bank: e.activation(
                kzo[64:128, co * 512:(co + 1) * 512], bank[64:128, :],
                AF.Identity, bias=bk[64:128, co:co + 1]),
                waits=[f"{tag}_kmm{co}"], mark=f"{tag}_kfo{co}")
            pj.release(bidx, f"{tag}_kfo{co}")
        for t in range(TPS):
            for eh in range(2):
                bank, fw, bidx = pj.acquire()
                for c in range(NCH):
                    em.op("pe", lambda e, t=t, eh=eh, c=c, bank=bank,
                          bi=bi: e.matmul(
                        bank[:],
                        sc.chunk(bi, c)[:, t * 128:t * 128 + 128],
                        wv[:, c * D + eh * 512:c * D + eh * 512 + 512],
                        start=(c == 0), stop=(c == NCH - 1)),
                        waits=([f"{tag}_xnT", P + "_w2", P + "_bv"] +
                               ([fw] if fw else [])) if c == 0 else (),
                        mark=(f"{tag}_vmm{t}_{eh}" if c == NCH - 1 else None))
                em.op("dve", lambda e, t=t, eh=eh, bank=bank: e.tensor_tensor(
                    vst[:, t * D + eh * 512:t * D + eh * 512 + 512], bank[:],
                    bvb[:, eh * 512:eh * 512 + 512], ALU.add),
                    waits=[f"{tag}_vmm{t}_{eh}"], mark=f"{tag}_vf{t}_{eh}")
                pj.release(bidx, f"{tag}_vf{t}_{eh}")
        for t in range(TPS):
            pa, pb2 = sc.ptr[0], sc.ptr[1]
            war = ([f"{tag}_ktev{t - 1}", f"{tag}_kteA{t - 1}"] if t >= 1
                   else [f"{tag}_trev{TPS - 1}"])
            for p in range(NCH):
                em.op("pe", lambda e, t=t, p=p, pa=pa: e.matmul(
                    pa[:, p * 128:(p + 1) * 128],
                    kze[:, p * 512 + t * 128:p * 512 + t * 128 + 128],
                    ident, is_transpose=True, start=True, stop=True),
                    waits=([f"{tag}_kfe{NCH - 1}", f"{tag}_kfo{NCH - 1}"] + war)
                    if p == 0 else (),
                    mark=(f"{tag}_ktrA{t}" if p == NCH - 1 else None))
            for p in range(NCH):
                em.op("pe", lambda e, t=t, p=p, pb2=pb2: e.matmul(
                    pb2[:, p * 128:(p + 1) * 128],
                    kzo[:, p * 512 + t * 128:p * 512 + t * 128 + 128],
                    ident, is_transpose=True, start=True, stop=True),
                    mark=(f"{tag}_ktrB{t}" if p == NCH - 1 else None))
            em.op("act", lambda e, t=t, pa=pa: e.activation(
                ktm[:, t * D:(t + 1) * D], pa[:, 0:D], AF.Copy),
                waits=[f"{tag}_ktrA{t}"], mark=f"{tag}_kteA{t}")
            em.op("dve", lambda e, t=t, pb2=pb2: e.tensor_tensor(
                ktm[:, t * D:(t + 1) * D], ktm[:, t * D:(t + 1) * D],
                pb2[:, 0:D], ALU.add),
                waits=[f"{tag}_ktrB{t}", f"{tag}_kteA{t}"],
                mark=f"{tag}_ktev{t}")
        em.marks[f"{tag}_trev2"] = em.marks[f"{tag}_kteA{TPS - 1}"]
        em.marks[f"{tag}_trev3"] = em.marks[f"{tag}_ktev{TPS - 1}"]
        for t in range(TPS):
            j = (s % (NSTRIPE // 2)) * TPS + t
            ctag = f"{tag}c{t}"
            if j == 0:
                em.op("dve", lambda e: e.memset(stf[:], 0.0),
                      waits=([f"{P}s{s - 1}c{TPS - 1}_sadd"] if s > 0 else []),
                      mark=f"{ctag}_sadd")
            else:
                ptag = f"{tag}c{t - 1}" if t > 0 else f"{P}s{s - 1}c{TPS - 1}"
                em.op("act", lambda e: e.activation(stb[:], stf[:], AF.Copy),
                      waits=[f"{ptag}_sadd"], mark=f"{ctag}_scp")
            for g in range(4):
                sg = sT[:, (g % 2) * 512:(g % 2) * 512 + 512]
                bank, fw, bidx = pj.acquire()
                for hh in range(4):
                    h = 4 * g + hh
                    p = h // 2
                    kz = kze if h % 2 == 0 else kzo
                    em.op("pe", lambda e, p=p, hh=hh, kz=kz, bank=bank,
                          t=t: e.matmul(
                              bank[:, hh * 128:hh * 128 + 128],
                              kz[:, p * 512 + t * 128:p * 512 + t * 128 + 128],
                              qf[:, p * 512 + t * 128:p * 512 + t * 128 + 128],
                              start=True, stop=True),
                          waits=([f"{tag}_ktev{t}", P + "_msk"] +
                                 ([fw] if fw else [])) if hh == 0 else (),
                          mark=(f"{ctag}_smm{g}" if hh == 3 else None))
                em.op("dve", lambda e, sg=sg, bank=bank: e.tensor_tensor(
                    sg, bank[:], mask4[:], ALU.mult),
                    waits=[f"{ctag}_smm{g}"], mark=f"{ctag}_msk{g}")
                pj.release(bidx, f"{ctag}_msk{g}")
                for hh in range(4):
                    h = 4 * g + hh
                    p = h // 2
                    ob_bank = po[p // 4]
                    col = (p % 4) * 128
                    cb = (h % 2) * 64
                    em.op("pe", lambda e, h=h, col=col, cb=cb, sg=sg,
                          hh=hh, t=t, ob_bank=ob_bank, j=j: e.matmul(
                              ob_bank[cb:cb + 64, col:col + 128],
                              vst[:, t * D + h * DH:t * D + h * DH + 64],
                              sg[:, hh * 128:hh * 128 + 128],
                              start=True, stop=(j == 0),
                              tile_position=(0, cb)),
                          waits=([f"{ctag}_msk{g}", f"{tag}_vf{t}_0",
                                  f"{tag}_vf{t}_1"] +
                                 ([f"{tag}c{t - 1}_oev{p // 4}"]
                                  if t >= 1 else
                                  ([f"{P}s{s - 1}c{TPS - 1}_oev{p // 4}"]
                                   if s >= 1 else [])))
                          if (hh == 0 and g == 0) else
                          ([f"{ctag}_msk{g}"] if hh == 0 else ()),
                          mark=(f"{ctag}_omm{g}"
                                if (j == 0 and hh == 3) else None))
                    if j > 0:
                        em.op("pe", lambda e, h=h, col=col, cb=cb, t=t,
                              ob_bank=ob_bank: e.matmul(
                                  ob_bank[cb:cb + 64, col:col + 128],
                                  stb[:, h * DH:(h + 1) * DH],
                                  qf[:, (h // 2) * 512 + t * 128:
                                     (h // 2) * 512 + t * 128 + 128],
                                  start=False, stop=True,
                                  tile_position=(0, cb)),
                              waits=[f"{ctag}_scp"] if hh == 0 else (),
                              mark=(f"{ctag}_omm{g}" if hh == 3 else None))
            bank, fw, bidx = pj.acquire()
            for h in range(H):
                p = h // 2
                mb = (h % 2) * 64
                em.op("pe", lambda e, h=h, p=p, mb=mb, t=t, bank=bank:
                      e.matmul(
                          bank[mb:mb + 64, p * 64:(p + 1) * 64],
                          ktm[:, t * D + p * 128 + mb:t * D + p * 128 + mb + 64],
                          vst[:, t * D + h * DH:t * D + h * DH + 64],
                          start=True, stop=True, tile_position=(0, mb)),
                      waits=([f"{tag}_ktev{t}"] + ([fw] if fw else []))
                      if h == 0 else (),
                      mark=(f"{ctag}_dmm" if h == H - 1 else None))
            em.op("dve", lambda e, bank=bank: e.tensor_tensor(
                stf[0:64, :].rearrange("p (h c) -> p h c", c=DH)[:, 0::2, :],
                stf[0:64, :].rearrange("p (h c) -> p h c", c=DH)[:, 0::2, :],
                bank[0:64, :].rearrange("p (h c) -> p h c", c=DH),
                ALU.add),
                waits=[f"{ctag}_dmm"] + ([f"{ctag}_scp"] if j > 0 else []))
            em.op("dve", lambda e, bank=bank: e.tensor_tensor(
                stf[64:128, :].rearrange("p (h c) -> p h c", c=DH)[:, 1::2, :],
                stf[64:128, :].rearrange("p (h c) -> p h c", c=DH)[:, 1::2, :],
                bank[64:128, :].rearrange("p (h c) -> p h c", c=DH),
                ALU.add),
                mark=f"{ctag}_sadd")
            pj.release(bidx, f"{ctag}_sadd")
            for b in range(2):
                em.op("act", lambda e, b=b, t=t: e.activation(
                    ofm[:, 0:NCH * 512].rearrange("p (c w) -> p c w", c=NCH)
                    [:, 4 * b:4 * b + 4, t * 128:t * 128 + 128],
                    po[b][:, 0:512].rearrange("p (c w) -> p c w", c=4),
                    AF.Copy),
                    waits=[f"{ctag}_omm{2 * b}", f"{ctag}_omm{2 * b + 1}"],
                    mark=f"{ctag}_oev{b}")
            ob = outb
            for eh in range(2):
                bank, fw, bidx = pj.acquire()
                for co in range(NCH):
                    em.op("pe", lambda e, t=t, eh=eh, co=co, bank=bank:
                          e.matmul(
                              bank[:],
                              ofm[:, co * 512 + t * 128:co * 512 + t * 128 + 128],
                              wo[:, co * D + eh * 512:co * D + eh * 512 + 512],
                              start=(co == 0), stop=(co == NCH - 1)),
                          waits=([f"{ctag}_oev0", f"{ctag}_oev1",
                                  P + "_w3", P + "_bo"] +
                                 ([fw] if fw else [])) if co == 0 else (),
                          mark=(f"{ctag}_wmm{eh}" if co == NCH - 1 else None))
                em.op("dve", lambda e, t=t, eh=eh, bank=bank, ob=ob:
                      e.tensor_tensor(
                          ob[:, t * D + eh * 512:t * D + eh * 512 + 512],
                          bank[:], bob[:, eh * 512:eh * 512 + 512], ALU.add),
                      waits=[f"{ctag}_wmm{eh}"],
                      mark=f"{ctag}_wf{eh}")
                pj.release(bidx, f"{ctag}_wf{eh}")
        ln_post_add(em, lno, s, outb, sc.xs[s % len(sc.xs)], x_out, tag,
                    waits_first=[f"{tag}c{TPS - 1}_wf1"])



def build_pass_l1attn(em, sb, ps, cd, x_in, x_out):
    ident = cd["ident_s"]
    P = "p3"
    W3 = 3 * D
    wqkv = sb(P + "_wqkv", [128, NCH * W3], BF16)
    wout = sb(P + "_wout", [128, NCH * D], BF16)
    outb = sb(P + "_outb", [128, TPS * D], F32)
    wstage = [outb[:, 0:W3]]
    cosT = sb(P + "_cos", [128, SEQLEN], BF16)
    ssT = sb(P + "_ss", [128, SEQLEN], BF16)
    lmask = sb(P + "_lm", [128, 4 * BS], BF16)
    pj = BankPool([ps(P + f"_pj{i}", [128, 512], F32) for i in range(2)])
    sp2 = [ps(P + "_spa", [128, 1024], F32), ps(P + "_spb", [128, 1024], F32)]
    po = ps(P + "_po", [128, 1024], F32)
    ptr_aps = [po.bitcast(BF16)[:, 0:NCH * 128],
               sp2[1].bitcast(BF16)[:, 0:NCH * 128]]
    sc = StripeCtx(em, sb, ps, P, carry=False, nxs=1, nxnt=1,
                   ptr_aps=ptr_aps)
    qraw = sb(P + "_qraw", [128, NCH * 512], BF16)
    kraw = sb(P + "_kraw", [128, NCH * 512], BF16)
    qrot = sb(P + "_qrot", [128, NCH * 512], BF16)
    kze = sb(P + "_kze", [128, NCH * 512], BF16)
    kzo = sb(P + "_kzo", [128, NCH * 512], BF16)
    tmp1 = outb[:, 3072:3328].bitcast(BF16)
    tmp2 = outb[:, 3328:3584].bitcast(BF16)
    tmp3 = outb[:, 3584:3840].bitcast(BF16)
    vst = sb(P + "_vst", [128, TPS * H * 128], BF16)
    ptile = [sb(P + f"_pt{i}", [128, 1024], BF16) for i in range(2)]
    lns = sb(P + "_lns", [128, 1024], F32)
    rcp = sb(P + "_rcp", [128, 1024], F32)
    ofm = sb(P + "_ofm", [128, NCH * 512], BF16)
    lno = [LNBufs(sb, P + f"_lo{i}") for i in range(2)]

    load_weights(em, wstage, [wqkv, wout],
                 [cd["l1a_wqkvT"], cd["l1a_woutT"]], P)
    em.dma("sp", lambda e: e.dma_start(cosT[:], cd["l1cos"]), inc=P + "_mi",
           mark=P + "_c0")
    em.dma("sp", lambda e: e.dma_start(ssT[:], cd["l1sin"]), inc=P + "_mi",
           waits=[P + "_c0"], mark=P + "_c1")
    em.dma("sp", lambda e: e.dma_start(lmask[:], cd["l1mask"]), inc=P + "_mi",
           waits=[P + "_c1"], mark=P + "_c2")
    em.op("dve", lambda e: e.memset(kze[64:128, :], 0.0))
    em.op("dve", lambda e: e.memset(kzo[0:64, :], 0.0))
    em.op("dve", lambda e: e.memset(
        vst.rearrange("p (s c) -> p s c", c=128)[:, :, 64:128], 1.0),
        mark=P + "_init")

    chain = {"exp": [None, None], "omm": [None, None],
             "norm": [None, None], "ofmd": None}

    for s in range(NSTRIPE):
        tag = f"{P}s{s}"
        war = [f"{P}s{s - 1}_pedone"] if s >= 1 else []
        xw = None
        pw = []
        if s >= 1:
            pw = [c for c in (chain["norm"][0], chain["norm"][1],
                              chain["exp"][1]) if c]
        bi = stripe_prologue(em, sc, s, x_in, ident, xnt_war=war,
                             x_extra_wait=xw, ptr_war=pw)

        def proj_fm(nm, dst, woff):
            for co in range(NCH):
                bank, fw, bidx = pj.acquire()
                for c in range(NCH):
                    em.op("pe", lambda e, co=co, c=c, bank=bank, bi=bi,
                          woff=woff: e.matmul(
                              bank[:],
                              wqkv[:, c * W3 + woff + co * 128:
                                   c * W3 + woff + co * 128 + 128],
                              sc.chunk(bi, c),
                              start=(c == 0), stop=(c == NCH - 1)),
                          waits=([f"{tag}_xnT", P + "_w0", P + "_init"] +
                                 ([fw] if fw else [])) if c == 0 else (),
                          mark=(f"{tag}_{nm}mm{co}" if c == NCH - 1
                                else None))
                evw = [f"{tag}_{nm}mm{co}"]
                if s >= 1:
                    evw.append(f"{P}s{s - 1}_{nm}rot{co}")
                em.op("act", lambda e, co=co, bank=bank, dst=dst:
                      e.activation(dst[:, co * 512:(co + 1) * 512], bank[:],
                                   AF.Copy),
                      waits=evw, mark=f"{tag}_{nm}ev{co}")
                pj.release(bidx, f"{tag}_{nm}ev{co}")
        proj_fm("q", qraw, 0)
        proj_fm("k", kraw, D)
        for t in range(TPS):
            for eh in range(2):
                bank, fw, bidx = pj.acquire()
                for c in range(NCH):
                    em.op("pe", lambda e, t=t, eh=eh, c=c, bank=bank,
                          bi=bi: e.matmul(
                              bank[:],
                              sc.chunk(bi, c)[:, t * 128:t * 128 + 128],
                              wqkv[:, c * W3 + 2 * D + eh * 512:
                                   c * W3 + 2 * D + eh * 512 + 512],
                              start=(c == 0), stop=(c == NCH - 1)),
                          waits=([fw] if (c == 0 and fw) else ()),
                          mark=(f"{tag}_vmm{t}_{eh}" if c == NCH - 1
                                else None))
                vw = [f"{tag}_vmm{t}_{eh}"]
                if s >= 1 and t == 0 and eh == 0:
                    vw.append(f"{P}s{s - 1}_odone")
                em.op("act", lambda e, t=t, eh=eh, bank=bank: e.activation(
                    vst.rearrange("p (tt h c) -> p tt h c", tt=TPS, h=H)
                    [:, t, 8 * eh:8 * eh + 8, 0:64],
                    bank[:].rearrange("p (h c) -> p h c", c=64),
                    AF.Copy),
                    waits=vw, mark=f"{tag}_vev{t}_{eh}")
                pj.release(bidx, f"{tag}_vev{t}_{eh}")
        off = (s % (NSTRIPE // 2)) * 512
        for c in range(NCH):
            em.op("dve", lambda e, c=c, off=off: e.tensor_tensor(
                qrot[:, c * 512:(c + 1) * 512],
                qraw[:, c * 512:(c + 1) * 512],
                cosT[:, off:off + 512], ALU.mult),
                waits=[f"{tag}_qev{c}", P + "_c2"] if c == 0
                else [f"{tag}_qev{c}"])
            for sq4 in range(4):
                sq = sq4 * 32
                tw = []
                if sq4 == 0:
                    if c > 0:
                        tw = [f"{tag}_qc{c - 1}"]
                    elif s >= 1:
                        tw = [f"{P}s{s - 1}_kc{NCH - 1}"]
                em.op("dve", lambda e, c=c, sq=sq, off=off:
                      e.tensor_tensor(
                          tmp1[sq:sq + 32, :],
                          qraw[sq:sq + 32, c * 512:(c + 1) * 512],
                          ssT[sq:sq + 32, off:off + 512],
                          ALU.mult),
                      waits=tw,
                      mark=(f"{tag}_qq{c}" if sq4 == 3 else None))
            for qt in range(4):
                sq = (qt ^ 1) * 32
                em.op("act", lambda e, qt=qt, sq=sq: e.activation(
                    tmp2[qt * 32:qt * 32 + 32, :], tmp1[sq:sq + 32, :],
                    AF.Copy),
                    waits=[f"{tag}_qq{c}"] if qt == 0 else (),
                    mark=(f"{tag}_qc{c}" if qt == 3 else None))
            em.op("dve", lambda e, c=c: e.tensor_tensor(
                qrot[:, c * 512:(c + 1) * 512],
                qrot[:, c * 512:(c + 1) * 512], tmp2[:], ALU.add),
                waits=[f"{tag}_qc{c}"],
                mark=f"{tag}_qrot{c}")
        for c in range(NCH):
            em.op("dve", lambda e, c=c, off=off: e.tensor_tensor(
                tmp3[:], kraw[:, c * 512:(c + 1) * 512],
                cosT[:, off:off + 512], ALU.mult),
                waits=[f"{tag}_kev{c}"])
            for sq4 in range(4):
                sq = sq4 * 32
                tw = []
                if sq4 == 0:
                    tw = [f"{tag}_kc{c - 1}" if c > 0
                          else f"{tag}_qc{NCH - 1}"]
                em.op("dve", lambda e, c=c, sq=sq, off=off:
                      e.tensor_tensor(
                          tmp1[sq:sq + 32, :],
                          kraw[sq:sq + 32, c * 512:(c + 1) * 512],
                          ssT[sq:sq + 32, off:off + 512],
                          ALU.mult),
                      waits=tw,
                      mark=(f"{tag}_kq{c}" if sq4 == 3 else None))
            for qt in range(4):
                sq = (qt ^ 1) * 32
                em.op("act", lambda e, qt=qt, sq=sq: e.activation(
                    tmp2[qt * 32:qt * 32 + 32, :], tmp1[sq:sq + 32, :],
                    AF.Copy),
                    waits=[f"{tag}_kq{c}"] if qt == 0 else (),
                    mark=(f"{tag}_kc{c}" if qt == 3 else None))
            em.op("dve", lambda e, c=c: e.tensor_tensor(
                tmp3[:], tmp3[:], tmp2[:], ALU.add),
                waits=[f"{tag}_kc{c}"])
            em.op("dve", lambda e, c=c: e.tensor_copy(
                kze[0:64, c * 512:(c + 1) * 512], tmp3[0:64, :]))
            em.op("dve", lambda e, c=c: e.tensor_copy(
                kzo[64:128, c * 512:(c + 1) * 512], tmp3[64:128, :]),
                mark=f"{tag}_krot{c}")
        em.marks[f"{tag}_rope"] = em.marks[f"{tag}_krot{NCH - 1}"]
        for b in range(2):
            btag = f"{tag}b{b}"
            for gp in range(8):
                pt = ptile[gp % 2]
                sp = sp2[gp % 2]
                poh = po[:, (gp % 2) * 512:(gp % 2) * 512 + 512]
                for hi in range(2):
                    h = 2 * gp + hi
                    co = h // 2
                    kz = kze if h % 2 == 0 else kzo
                    for kt in range(2):
                        ws = []
                        if hi == 0 and kt == 0:
                            ws = [f"{tag}_rope"]
                            if chain["exp"][gp % 2] is not None:
                                ws.append(chain["exp"][gp % 2])
                        em.op("pe", lambda e, co=co, kz=kz, kt=kt, b=b,
                              hi=hi, sp=sp: e.matmul(
                                  sp[:, hi * 512 + kt * 256:
                                     hi * 512 + kt * 256 + 256],
                                  kz[:, co * 512 + b * 256 + kt * 128:
                                     co * 512 + b * 256 + kt * 128 + 128],
                                  qrot[:, co * 512 + b * 256:
                                       co * 512 + b * 256 + 256],
                                  start=True, stop=True),
                              waits=ws,
                              mark=(f"{btag}g{gp}_smm"
                                    if (hi == 1 and kt == 1) else None))
                ptw = [f"{btag}g{gp}_smm"]
                if chain["omm"][gp % 2] is not None:
                    ptw.append(chain["omm"][gp % 2])
                em.op("act", lambda e, pt=pt, sp=sp: e.activation(
                    pt[:], sp[:], AF.Exp, scale=DH ** (-0.5)),
                    waits=ptw, mark=f"{btag}g{gp}_exp")
                chain["exp"][gp % 2] = f"{btag}g{gp}_exp"
                em.op("dve", lambda e, pt=pt: e.tensor_tensor(
                    pt[:], pt[:], lmask[:], ALU.mult),
                    waits=[f"{btag}g{gp}_exp"], mark=f"{btag}g{gp}_msk")
                for hi in range(2):
                    h = 2 * gp + hi
                    for kt in range(2):
                        ws = [f"{btag}g{gp}_msk", f"{tag}_vev{TPS - 1}_1"]
                        if hi == 0 and kt == 0:
                            if chain["norm"][gp % 2] is not None:
                                ws.append(chain["norm"][gp % 2])
                        em.op("pe", lambda e, h=h, hi=hi, kt=kt, b=b, pt=pt,
                              poh=poh: e.matmul(
                                  poh[:, hi * 256:hi * 256 + 256],
                                  vst[:, (2 * b + kt) * H * 128 + h * 128:
                                      (2 * b + kt) * H * 128 + h * 128 + 128],
                                  pt[:, hi * 512 + kt * 256:
                                     hi * 512 + kt * 256 + 256],
                                  start=(kt == 0), stop=(kt == 1)),
                              waits=ws,
                              mark=(f"{btag}g{gp}_omm"
                                    if (hi == 1 and kt == 1) else None))
                chain["omm"][gp % 2] = f"{btag}g{gp}_omm"
                ftag = f"{btag}g{gp}"
                hb = (gp % 2) * 512
                lw = [f"{btag}g{gp}_omm"]
                if chain["ofmd"] is not None:
                    lw.append(chain["ofmd"])
                em.op("act", lambda e, hb=hb: e.activation(
                    lns[64:128, hb:hb + 512], po[64:128, hb:hb + 512],
                    AF.Ln),
                    waits=lw, mark=f"{ftag}_ln")
                em.op("act", lambda e, hb=hb: e.activation(
                    rcp[0:64, hb:hb + 512], lns[64:128, hb:hb + 512],
                    AF.Exp, scale=-1.0),
                    waits=[f"{ftag}_ln"], mark=f"{ftag}_rcp")
                em.op("dve", lambda e, hb=hb: e.tensor_tensor(
                    lns[0:64, hb:hb + 512].bitcast(BF16)[:, 0:1024]
                    [:, 0:512] if False else
                    lns[0:64, :].bitcast(BF16)[:, hb:hb + 512],
                    po[0:64, hb:hb + 512], rcp[0:64, hb:hb + 512],
                    ALU.mult),
                    waits=[f"{ftag}_rcp"], mark=f"{ftag}_on")
                for hh in range(2):
                    h = 2 * gp + hh
                    cb = (h % 2) * 64
                    co = h // 2
                    em.op("act", lambda e, hh=hh, cb=cb, co=co, b=b, hb=hb:
                          e.activation(
                              ofm[cb:cb + 64,
                                  co * 512 + b * 256:co * 512 + b * 256 + 256],
                              lns[0:64, :].bitcast(BF16)
                              [:, hb + hh * 256:hb + hh * 256 + 256],
                              AF.Copy),
                          waits=[f"{ftag}_on"],
                          mark=(f"{ftag}_ofm" if hh == 1 else None))
                chain["norm"][gp % 2] = f"{ftag}_ofm"
                chain["ofmd"] = f"{ftag}_ofm"
            em.marks[f"{btag}_odone"] = em.marks[f"{btag}g7_omm"]
            for tt in range(2):
                t = 2 * b + tt
                for eh in range(2):
                    bank, fw, bidx = pj.acquire()
                    for co in range(NCH):
                        em.op("pe", lambda e, t=t, eh=eh, co=co, bank=bank:
                              e.matmul(
                                  bank[:],
                                  ofm[:, co * 512 + t * 128:
                                      co * 512 + t * 128 + 128],
                                  wout[:, co * D + eh * 512:
                                       co * D + eh * 512 + 512],
                                  start=(co == 0), stop=(co == NCH - 1)),
                              waits=([f"{btag}g6_ofm", f"{btag}g7_ofm",
                                      P + "_w1"] +
                                     ([fw] if fw else [])) if co == 0 else (),
                              mark=(f"{tag}_wmm{t}_{eh}" if co == NCH - 1
                                    else None))
                    evw = [f"{tag}_wmm{t}_{eh}"]
                    if s >= 1 and t == 0 and eh == 0:
                        evw.append(f"{P}s{s - 1}_add")
                    em.op("act", lambda e, t=t, eh=eh, bank=bank:
                          e.activation(
                              outb[:, t * D + eh * 512:t * D + eh * 512 + 512],
                              bank[:], AF.Copy),
                          waits=evw, mark=f"{tag}_wf{t}_{eh}")
                    pj.release(bidx, f"{tag}_wf{t}_{eh}")
        em.marks[f"{tag}_odone"] = em.marks[f"{tag}b1_odone"]
        em.marks[f"{tag}_pedone"] = em.marks[f"{tag}_wmm{TPS - 1}_1"]
        ln_post_add(em, lno, s, outb, sc.xs[s % len(sc.xs)], x_out, tag,
                    waits_first=[f"{tag}_wf{TPS - 1}_1"])



def build_pass_l1ffn(em, sb, ps, cd, x_in, x_out):
    ident = cd["ident_s"]
    P = "p4"
    NHC = HID // 128
    wg = sb(P + "_wg", [128, NCH * HID], BF16)
    wu = sb(P + "_wu", [128, NCH * HID], BF16)
    wd = sb(P + "_wd", [128, NHC * D], BF16)
    sc = StripeCtx(em, sb, ps, P, carry=False, nxs=1, nxnt=1)
    hact = sb(P + "_hact", [128, NHC * 512], BF16)
    wstage = [hact.bitcast(I32)[:, 0:HID]]
    silu_t = [sb(P + f"_sl{i}", [128, 512], F32) for i in range(2)]
    lno = LNBufs(sb, P + "_lo")
    pg = BankPool([ps(P + f"_pg{i}", [128, 512], F32) for i in range(4)])
    pd = [ps(P + f"_pd{i}", [128, 512], F32) for i in range(2)]

    load_weights(em, wstage, [wg, wu, wd],
                 [cd["l1f_wgT"], cd["l1f_wuT"], cd["l1f_wdT"]], P,
                 scale=1.0 / 127.0)

    for s in range(NSTRIPE):
        tag = f"{P}s{s}"
        war = [f"{P}s{s - 1}_gu{NHC - 1}"] if s >= 1 else []
        xw = f"{P}s{s - 1}_store" if s >= 1 else None
        bi = stripe_prologue(em, sc, s, x_in, ident, xnt_war=war,
                             x_extra_wait=xw)
        for hc in range(NHC):
            bkg, fwg, big = pg.acquire()
            for c in range(NCH):
                em.op("pe", lambda e, hc=hc, c=c, bkg=bkg, bi=bi: e.matmul(
                    bkg[:],
                    wg[:, c * HID + hc * 128:c * HID + hc * 128 + 128],
                    sc.chunk(bi, c),
                    start=(c == 0), stop=(c == NCH - 1)),
                    waits=([f"{tag}_xnT", P + "_w0"] +
                           ([fwg] if fwg else [])) if c == 0 else (),
                    mark=(f"{tag}_gmm{hc}" if c == NCH - 1 else None))
            bku, fwu, biu = pg.acquire()
            for c in range(NCH):
                em.op("pe", lambda e, hc=hc, c=c, bku=bku, bi=bi: e.matmul(
                    bku[:],
                    wu[:, c * HID + hc * 128:c * HID + hc * 128 + 128],
                    sc.chunk(bi, c),
                    start=(c == 0), stop=(c == NCH - 1)),
                    waits=([P + "_w1"] + ([fwu] if fwu else []))
                    if c == 0 else (),
                    mark=(f"{tag}_umm{hc}" if c == NCH - 1 else None))
            sl = silu_t[hc % 2]
            sl_war = [f"{tag}_ha{hc - 2}"] if hc >= 2 else (
                [f"{P}s{s - 1}_ha{NHC - 2 + hc}"] if s >= 1 else [])
            em.op("act", lambda e, hc=hc, bkg=bkg, sl=sl: e.activation(
                sl[:], bkg[:], AF.Silu),
                waits=[f"{tag}_gmm{hc}"] + sl_war, mark=f"{tag}_sil{hc}")
            pg.release(big, f"{tag}_sil{hc}")
            ha_war = ([f"{P}s{s - 1}_dmm{TPS - 1}_1"]
                      if (s >= 1 and hc == 0) else [])
            if s == 0 and hc == 0:
                ha_war = [P + "_w2"]
            em.op("dve", lambda e, hc=hc, bku=bku, sl=sl: e.tensor_tensor(
                hact[:, hc * 512:(hc + 1) * 512], bku[:], sl[:], ALU.mult),
                waits=[f"{tag}_umm{hc}", f"{tag}_sil{hc}"] + ha_war,
                mark=f"{tag}_ha{hc}")
            pg.release(biu, f"{tag}_ha{hc}")
        em.marks[f"{tag}_gu{NHC - 1}"] = em.marks[f"{tag}_umm{NHC - 1}"]
        xs = sc.xs[s % len(sc.xs)]
        ln = lno
        for t in range(TPS):
            for eh in range(2):
                bank = pd[eh]
                for hc in range(NHC):
                    em.op("pe", lambda e, t=t, eh=eh, hc=hc, bank=bank:
                          e.matmul(
                              bank[:],
                              hact[:, hc * 512 + t * 128:hc * 512 + t * 128 + 128],
                              wd[:, hc * D + eh * 512:hc * D + eh * 512 + 512],
                              start=(hc == 0), stop=(hc == NHC - 1)),
                          waits=([f"{tag}_ha{NHC - 1}", P + "_w2"] +
                                 ([f"{tag}t{t - 1}_ap{eh}"] if t >= 1 else
                                  ([f"{P}s{s - 1}t{TPS - 1}_ap{eh}"]
                                   if s >= 1 else []))) if hc == 0 else (),
                          mark=(f"{tag}_dmm{t}_{eh}" if hc == NHC - 1
                                else None))
            for eh in range(2):
                em.op("dve", lambda e, t=t, eh=eh: e.bn_stats(
                    ln.st[:, t * 12 + eh * 6:t * 12 + eh * 6 + 6], pd[eh][:]),
                    waits=[f"{tag}_dmm{t}_{eh}"] +
                          ([f"{P}s{s - 1}_lnfree"] if (s >= 1 and t == 0 and
                                                       eh == 0) else []))
            em.op("dve", lambda e, t=t: e.bn_aggr(
                ln.mv[:, t * 2:t * 2 + 2], ln.st[:, t * 12:t * 12 + 12]))
            em.op("dve", lambda e, t=t: e.tensor_scalar(
                ln.vs[:, t:t + 1], ln.mv[:, 2 * t + 1:2 * t + 2],
                EPS, None, ALU.add))
            newton_rsqrt(em, ln.vs[:, t:t + 1], ln.rs[:, t:t + 1],
                         ln.tmp[:, t:t + 1], ln.ci[:, t:t + 1])
            em.op("dve", lambda e, t=t: e.scalar_tensor_tensor(
                ln.nb[:, t:t + 1], ln.mv[:, 2 * t:2 * t + 1],
                -1.0, ln.rs[:, t:t + 1], ALU.mult, ALU.mult),
                mark=f"{tag}t{t}_lnp")
            for eh in range(2):
                em.op("dve", lambda e, t=t, eh=eh: e.tensor_scalar(
                    xs[:, t * D + eh * 512:t * D + eh * 512 + 512],
                    xs[:, t * D + eh * 512:t * D + eh * 512 + 512],
                    ln.nb[:, t:t + 1], None, ALU.add),
                    waits=[f"{tag}t{t}_lnp"])
                em.op("dve", lambda e, t=t, eh=eh: e.scalar_tensor_tensor(
                    xs[:, t * D + eh * 512:t * D + eh * 512 + 512],
                    pd[eh][:], ln.rs[:, t:t + 1],
                    xs[:, t * D + eh * 512:t * D + eh * 512 + 512],
                    ALU.mult, ALU.add),
                    mark=f"{tag}t{t}_ap{eh}")
        em.marks[f"{tag}_lnfree"] = em.marks[f"{tag}t{TPS - 1}_lnp"]
        em.dma("pool", lambda e, xs=xs, s=s: e.dma_start(
            x_out[s * STRIPE:(s + 1) * STRIPE, :]
            .rearrange("(t p) d -> p t d", p=128),
            xs.rearrange("p (t d) -> p t d", t=TPS)),
            inc=f"st{s % 2}", mark=f"{tag}_store",
            waits=[f"{tag}t{TPS - 1}_ap0", f"{tag}t{TPS - 1}_ap1"])



def _host_prep(inputs):
    f32 = np.float32
    bf16 = ml_dtypes.bfloat16
    shared = {}

    def T(a):
        return np.ascontiguousarray(np.asarray(a).T)

    shared["l0a_wqT"] = T(inputs["L0_attn_wq_w"]).astype(f32, copy=False)
    shared["l0a_wkT"] = T(inputs["L0_attn_wk_w"]).astype(f32, copy=False)
    shared["l0a_wvT"] = T(inputs["L0_attn_wv_w"]).astype(f32, copy=False)
    shared["l0a_woT"] = T(inputs["L0_attn_wo_w"]).astype(f32, copy=False)
    for nm in ("q", "k", "v", "o"):
        shared[f"l0a_b{nm}"] = np.ascontiguousarray(
            np.asarray(inputs[f"L0_attn_w{nm}_b"]).astype(f32, copy=False)
            .reshape(NCH, 128).T)
    shared["l0a_bvb"] = np.ascontiguousarray(np.tile(
        np.asarray(inputs["L0_attn_wv_b"]).astype(f32, copy=False)[None, :],
        (128, 1)))
    shared["l0a_bob"] = np.ascontiguousarray(np.tile(
        np.asarray(inputs["L0_attn_wo_b"]).astype(f32, copy=False)[None, :],
        (128, 1)))
    shared["l0f_wrT"] = T(inputs["L0_ffn_W_r"]).astype(f32, copy=False)
    shared["l0f_wkT"] = T(inputs["L0_ffn_W_k"]).astype(f32, copy=False)
    shared["l0f_wvT"] = T(inputs["L0_ffn_W_v"]).astype(f32, copy=False)
    shared["l0f_mur"] = np.ascontiguousarray(
        np.asarray(inputs["L0_ffn_mu_r"]).astype(f32, copy=False)
        .reshape(NCH, 128).T)
    shared["l0f_muk"] = np.ascontiguousarray(
        np.asarray(inputs["L0_ffn_mu_k"]).astype(f32, copy=False)
        .reshape(NCH, 128).T)
    perm = np.zeros(3 * D, dtype=np.int64)
    for blk in range(2):
        for h in range(H):
            base = blk * D + 64 * h
            for j in range(DH // 2):
                perm[base + j] = base + 2 * j
                perm[base + 32 + j] = base + 2 * j + 1
    perm[2 * D:] = np.arange(2 * D, 3 * D)
    wqkv = np.asarray(inputs["L1_attn_Wqkv"]).astype(f32, copy=False)[perm]
    shared["l1a_wqkvT"] = np.ascontiguousarray(wqkv.T)
    shared["l1a_woutT"] = T(inputs["L1_attn_Wout"]).astype(f32, copy=False)
    shared["l1f_wgT"] = T(np.asarray(inputs["L1_ffn_w_gate"]))
    shared["l1f_wuT"] = T(np.asarray(inputs["L1_ffn_w_up"]))
    shared["l1f_wdT"] = T(np.asarray(inputs["L1_ffn_w_down"]))
    shared["ident"] = np.eye(128, dtype=bf16)
    ck = np.arange(128)[:, None]
    cq = np.arange(128)[None, :]
    shared["l0mask4"] = np.ascontiguousarray(
        np.tile((ck <= cq).astype(bf16), (1, 4)))
    inv_freq = 1.0 / 10000.0 ** (np.arange(0, DH, 2, dtype=np.float64) / DH)
    t = np.arange(SEQLEN, dtype=np.float64)
    ang = t[None, :] * inv_freq[:, None]
    shared["l1cos"] = np.ascontiguousarray(np.tile(np.cos(ang), (4, 1))
                                           .astype(bf16))
    sin = np.sin(ang)
    ss = np.concatenate([sin, -sin, sin, -sin], axis=0)
    shared["l1sin"] = np.ascontiguousarray(ss.astype(bf16))
    ck2 = np.arange(128)[:, None]
    cq2 = np.arange(BS)[None, :]
    m0 = (ck2 <= cq2).astype(bf16)
    m1 = (ck2 + 128 <= cq2).astype(bf16)
    shared["l1mask"] = np.ascontiguousarray(
        np.concatenate([m0, m1, m0, m1], axis=1))
    return shared


W_SHAPES = {
    "l0a_wqT": ([D, D], F32), "l0a_wkT": ([D, D], F32),
    "l0a_wvT": ([D, D], F32), "l0a_woT": ([D, D], F32),
    "l0a_bq": ([128, NCH], F32), "l0a_bk": ([128, NCH], F32),
    "l0a_bv": ([128, NCH], F32), "l0a_bo": ([128, NCH], F32),
    "l0f_wrT": ([D, D], F32), "l0f_wkT": ([D, D], F32),
    "l0f_wvT": ([D, D], F32),
    "l0f_mur": ([128, NCH], F32), "l0f_muk": ([128, NCH], F32),
    "l1a_wqkvT": ([D, 3 * D], F32), "l1a_woutT": ([D, D], F32),
    "l1f_wgT": ([D, HID], I32), "l1f_wuT": ([D, HID], I32),
    "l1f_wdT": ([HID, D], I32),
    "ident": ([128, 128], BF16), "l0mask4": ([128, 512], BF16),
    "l0a_bvb": ([128, D], F32), "l0a_bob": ([128, D], F32),
    "l1cos": ([128, SEQLEN], BF16), "l1sin": ([128, SEQLEN], BF16),
    "l1mask": ([128, 4 * BS], BF16),
}

PASS_BUILDERS = {"l0ffn": build_pass_l0ffn,
                 "l0attn": build_pass_l0attn,
                 "l1ffn": build_pass_l1ffn,
                 "l1attn": build_pass_l1attn}
PASS_WEIGHTS = {
    "l0attn": ["l0a_wqT", "l0a_wkT", "l0a_wvT", "l0a_woT",
               "l0a_bq", "l0a_bk", "l0a_bvb", "l0a_bob", "l0mask4", "ident"],
    "l0ffn": ["l0f_wrT", "l0f_wkT", "l0f_wvT", "l0f_mur", "l0f_muk", "ident"],
    "l1attn": ["l1a_wqkvT", "l1a_woutT", "l1cos", "l1sin",
               "l1mask", "ident"],
    "l1ffn": ["l1f_wgT", "l1f_wuT", "l1f_wdT", "ident"],
}


def build_program(passes):
    nc = bass.Bass("TRN2", target_bir_lowering=False)
    stack = ExitStack()
    em = Em(nc, stack)

    names = sorted({w for p in passes for w in PASS_WEIGHTS[p]})
    cd = {}
    for name in names:
        shp, dt = W_SHAPES[name]
        cd[name] = nc.dram_tensor(name, shp, dt, kind="ExternalInput").ap()

    def sb(name, shape, dt):
        return stack.enter_context(nc.sbuf_tensor(name, shape, dt))[:]

    def ps(name, shape, dt):
        return stack.enter_context(nc.psum_tensor(name, shape, dt))[:]

    ident_s = sb("c_ident", [128, 128], BF16)
    em.dma("sp", lambda e: e.dma_start(ident_s, cd["ident"]), inc="cst",
           mark="c_ident")
    cd["ident_s"] = ident_s

    x_first = nc.dram_tensor("x_in", [TOK, D], F32, kind="ExternalInput").ap()
    y_last = nc.dram_tensor("y", [TOK, D], F32, kind="ExternalOutput").ap()
    mids = [nc.dram_tensor(f"xmid{k}", [TOK, D], F32, kind="Internal").ap()
            for k in range(len(passes) - 1)]
    for k, p in enumerate(passes):
        x_in = x_first if k == 0 else mids[k - 1]
        x_out = y_last if k == len(passes) - 1 else mids[k]
        with ExitStack() as pass_stack:
            saved = stack
            inner_sb = (lambda st: lambda n, shp, dt:
                        st.enter_context(nc.sbuf_tensor(n, shp, dt))[:]
                        )(pass_stack)
            inner_ps = (lambda st: lambda n, shp, dt:
                        st.enter_context(nc.psum_tensor(n, shp, dt))[:]
                        )(pass_stack)
            PASS_BUILDERS[p](em, inner_sb, inner_ps, cd, x_in, x_out)
            em.barrier()
    with nc.Block() as blk:
        em.finalize(blk)
    stack.close()
    return nc, names


def kernel(**inputs):
    shared = _host_prep(inputs)
    x = np.asarray(inputs["L0_attn_x"]).astype(np.float32, copy=False)
    x = np.ascontiguousarray(x.reshape(NCORES, TOK, D))

    passes = ["l0attn", "l0ffn", "l1attn", "l1ffn"]
    nc, names = build_program(passes)
    in_maps = []
    for c in range(NCORES):
        m = {n: shared[n] for n in names}
        m["x_in"] = x[c]
        in_maps.append(m)
    res = run_bass_kernel_spmd(nc, in_maps, core_ids=list(range(NCORES)))
    y = np.stack([np.asarray(r["y"]) for r in res.results])
    return np.ascontiguousarray(y.reshape(16, SEQLEN, D))


# version 08
# speedup vs baseline: 1.0165x; 1.0165x over previous
import numpy as np
import ml_dtypes
from contextlib import ExitStack
import concourse.bass as bass
import concourse.mybir as mybir
from concourse.bass_utils import run_bass_kernel_spmd

F32 = mybir.dt.float32
BF16 = mybir.dt.bfloat16
I32 = mybir.dt.int32
ALU = mybir.AluOpType
AF = mybir.ActivationFunctionType

NCORES = 8
D = 1024
SEQLEN = 2048
TOK = 4096
STRIPE = 512
NSTRIPE = TOK // STRIPE
TPS = 4
NCH = D // 128
H = 16
DH = 64
BS = 256
HID = 3072
EPS = 1e-5

P1STAGE = 99

ENGINES = ("pe", "act", "dve", "pool", "sp")
ENG_ATTR = {"pe": "tensor", "act": "scalar", "dve": "vector", "pool": "gpsimd",
            "sp": "sync"}


class Em:

    def __init__(self, nc, stack):
        self.nc = nc
        self.stack = stack
        self.sems = {}
        self.cnt = {}
        self.ops = {e: [] for e in ENGINES}
        self.marks = {}

    def sem(self, name):
        if name not in self.sems:
            self.sems[name] = self.stack.enter_context(self.nc.semaphore(name))
            self.cnt[name] = 0
        return self.sems[name]

    def op(self, eng, fn, waits=(), inc=None, inc_amt=1, mark=None):
        wl = []
        for w in waits:
            if w is None:
                continue
            if isinstance(w, str):
                if w not in self.marks:
                    raise KeyError(f"unknown mark {w!r}")
                wl.append(self.marks[w])
            else:
                self.sem(w[0])
                wl.append(w)
        if mark is not None and inc is None:
            inc = "s_" + eng
        rec = {"fn": fn, "waits": wl, "inc": None}
        if inc is not None:
            self.sem(inc)
            self.cnt[inc] += inc_amt
            rec["inc"] = (inc, inc_amt)
            if mark is not None:
                self.marks[mark] = (inc, self.cnt[inc])
        self.ops[eng].append(rec)

    def dma(self, eng, fn, inc, mark=None, waits=()):
        self.op(eng, fn, waits=waits, inc=inc, inc_amt=16, mark=mark)

    def barrier(self):
        for eng in ENGINES:
            if self.ops[eng]:
                rec = self.ops[eng][-1]
                if rec["inc"] is None:
                    s = "s_" + eng
                    self.sem(s)
                    self.cnt[s] += 1
                    rec["inc"] = (s, 1)
        targets = [(s, self.cnt[s]) for s in list(self.cnt) if self.cnt[s] > 0]
        for eng in ENGINES:
            if not self.ops[eng]:
                continue
            for (s, v) in targets:
                self.ops[eng].append({"fn": None, "waits": [(s, v)],
                                      "inc": None})

    def finalize(self, blk):
        for eng in ENGINES:
            if not self.ops[eng]:
                continue

            def mkbody(ops, sems, sep):
                def run(e):
                    waited = {}
                    prev_compute = False
                    for rec in ops:
                        for (s, v) in rec["waits"]:
                            if waited.get(s, -1) < v:
                                e.wait_ge(sems[s], v)
                                waited[s] = v
                        if rec["fn"] is not None:
                            if sep and prev_compute:
                                e.drain()
                            ins = rec["fn"](e)
                            prev_compute = True
                            if rec["inc"] is not None:
                                s, amt = rec["inc"]
                                ins.then_inc(sems[s], amt)
                return run

            getattr(blk, ENG_ATTR[eng])(mkbody(self.ops[eng], self.sems,
                                               eng in ('dve', 'act', 'pool')))


class BankPool:

    def __init__(self, banks):
        self.banks = banks
        self.uses = 0
        self.free_mark = [None] * len(banks)

    def acquire(self):
        i = self.uses % len(self.banks)
        self.uses += 1
        return self.banks[i], self.free_mark[i], i

    def release(self, i, mark):
        self.free_mark[i] = mark


def newton_rsqrt(em, v_ap, y_ap, tmp_ap, ci_ap, waits=(), mark=None):
    MAGIC = 0x5f3759df
    em.op("dve", lambda e: e.tensor_scalar(
        y_ap.bitcast(I32), v_ap.bitcast(I32), 1, None, ALU.arith_shift_right),
        waits=waits)
    em.op("dve", lambda e: e.memset(ci_ap, MAGIC))
    em.op("dve", lambda e: e.tensor_tensor(
        y_ap.bitcast(I32), ci_ap, y_ap.bitcast(I32), ALU.subtract))
    for it in range(3):
        em.op("dve", lambda e: e.tensor_tensor(tmp_ap, y_ap, y_ap, ALU.mult))
        em.op("dve", lambda e: e.tensor_tensor(tmp_ap, tmp_ap, v_ap, ALU.mult))
        em.op("dve", lambda e: e.tensor_scalar(
            tmp_ap, tmp_ap, -0.5, 1.5, ALU.mult, ALU.add))
        em.op("dve", lambda e: e.tensor_tensor(y_ap, y_ap, tmp_ap, ALU.mult),
              mark=(mark if it == 2 else None))


class LNBufs:
    def __init__(self, sb, tag):
        self.st = sb(f"{tag}_st", [128, TPS * 12], F32)
        self.mv = sb(f"{tag}_mv", [128, TPS * 2], F32)
        self.vs = sb(f"{tag}_vs", [128, TPS], F32)
        self.rs = sb(f"{tag}_rs", [128, TPS], F32)
        self.nb = sb(f"{tag}_nb", [128, TPS], F32)
        self.tmp = sb(f"{tag}_tmp", [128, TPS], F32)
        self.ci = sb(f"{tag}_ci", [128, TPS], I32)


def ln_stats(em, ln, src_tiles, tag, waits_first=()):
    for t in range(TPS):
        for hh in range(2):
            em.op("dve", lambda e, t=t, hh=hh: e.bn_stats(
                ln.st[:, t * 12 + hh * 6:t * 12 + hh * 6 + 6],
                src_tiles[t][:, hh * 512:hh * 512 + 512]),
                waits=(waits_first if (t == 0 and hh == 0) else ()))
    for t in range(TPS):
        em.op("dve", lambda e, t=t: e.bn_aggr(
            ln.mv[:, t * 2:t * 2 + 2], ln.st[:, t * 12:t * 12 + 12]))
    for t in range(TPS):
        em.op("dve", lambda e, t=t: e.tensor_scalar(
            ln.vs[:, t:t + 1], ln.mv[:, 2 * t + 1:2 * t + 2],
            EPS, None, ALU.add))
    newton_rsqrt(em, ln.vs[:, 0:TPS], ln.rs[:, 0:TPS], ln.tmp[:, 0:TPS], ln.ci[:, 0:TPS])
    for t in range(TPS):
        em.op("dve", lambda e, t=t: e.scalar_tensor_tensor(
            ln.nb[:, t:t + 1], ln.mv[:, 2 * t:2 * t + 1],
            -1.0, ln.rs[:, t:t + 1], ALU.mult, ALU.mult),
            mark=(f"{tag}_lnp" if t == TPS - 1 else None))


class StripeCtx:

    def __init__(self, em, sb, ps, tag, carry=False, nxs=2, nxnt=2,
                 ptr_aps=None):
        self.em = em
        self.tag = tag
        self.carry = carry
        self.nxs = nxs
        self.nxnt = nxnt
        self.ptr_aps = ptr_aps
        self.w = 520 if carry else 512
        self.off = 1 if carry else 0
        self.xs = [sb(f"{tag}_x{i}", [128, TPS * D], F32)
                   for i in range(getattr(self, "nxs", 2))]
        self.xnT = [sb(f"{tag}_xnT{i}", [128, NCH * self.w], BF16)
                    for i in range(getattr(self, "nxnt", 2))]
        self.xn = sb(f"{tag}_xn", [128, TPS * D], BF16)
        self.ln = [LNBufs(sb, f"{tag}_li{i}") for i in range(2)]
        if getattr(self, "ptr_aps", None) is not None:
            self.ptr = self.ptr_aps
        else:
            self.ptr = [ps(f"{tag}_ptr{i}", [128, NCH * 128], BF16)
                        for i in range(2)]

    def chunk(self, bi, c):
        o = c * self.w + self.off
        return self.xnT[bi][:, o:o + 512]

    def chunk_shift(self, bi, c):
        o = c * self.w + self.off
        return self.xnT[bi][:, o - 1:o + 511]

    def chunk3d(self, bi):
        return self.xnT[bi][:, 0:NCH * self.w].rearrange(
            "p (c w) -> p c w", c=NCH)


def stripe_prologue(em, sc, s, x_dram, ident, xnt_war=None, x_extra_wait=None,
                    ptr_war=None):
    bi = s % len(sc.xnT)
    xs = sc.xs[s % len(sc.xs)]
    T0 = sc.tag
    tag = f"{T0}s{s}"
    w = []
    sx = s - len(sc.xs)
    if sx >= 0 and f"{T0}s{sx}_store" in em.marks:
        w.append(f"{T0}s{sx}_store")
    if x_extra_wait:
        w.append(x_extra_wait)
    if x_dram.shape[0] == 128:
        em.dma("sp", lambda e: e.dma_start(
            xs[:], x_dram[:, s * TPS * D:(s + 1) * TPS * D]),
            inc=f"ld{s % 2}", mark=f"{tag}_xld", waits=w)
    else:
        em.dma("sp", lambda e: e.dma_start(
            xs.rearrange("p (t d) -> p t d", t=TPS),
            x_dram[s * STRIPE:(s + 1) * STRIPE, :]
            .rearrange("(t p) d -> p t d", p=128)),
            inc=f"ld{s % 2}", mark=f"{tag}_xld", waits=w)
    tiles = [xs[:, t * D:(t + 1) * D] for t in range(TPS)]
    ln_stats(em, sc.ln[bi], tiles, tag, waits_first=[f"{tag}_xld"])
    for t in range(TPS):
        ww = [f"{tag}_lnp"] if t == 0 else []
        if t == 0 and s >= 1:
            ww.append(f"{T0}s{s - 1}_tr{TPS - 1}")
        em.op("act", lambda e, t=t, bi=bi: e.activation(
            sc.xn[:, t * D:(t + 1) * D], xs[:, t * D:(t + 1) * D],
            AF.Identity,
            bias=sc.ln[bi].nb[:, t:t + 1], scale=sc.ln[bi].rs[:, t:t + 1]),
            waits=ww, mark=f"{tag}_xn{t}")
    em.marks[f"{tag}_xn"] = em.marks[f"{tag}_xn{TPS - 1}"]
    for t in range(TPS):
        pb = sc.ptr[t % 2]
        bank_war = None
        if t >= 2:
            bank_war = f"{tag}_trev{t - 2}"
        elif s >= 1:
            bank_war = f"{T0}s{s - 1}_trev{t + 2}"
        bw = [bank_war] if bank_war else []
        if t < 2 and ptr_war:
            bw += [m for m in ptr_war if m in em.marks]
        for c in range(NCH):
            em.op("pe", lambda e, t=t, c=c, pb=pb: e.transpose(
                pb[:, c * 128:(c + 1) * 128],
                sc.xn[:, t * D + c * 128:t * D + c * 128 + 128], ident),
                waits=([f"{tag}_xn{t}", "c_ident"] + bw)
                if c == 0 else (),
                mark=(f"{tag}_tr{t}" if c == NCH - 1 else None))
        ev_w = [f"{tag}_tr{t}"]
        if t == 0 and xnt_war:
            ev_w += [m for m in xnt_war if m in em.marks]
        em.op("act", lambda e, t=t, pb=pb, bi=bi: e.activation(
            sc.chunk3d(bi)[:, :, sc.off + t * 128:sc.off + t * 128 + 128],
            pb[:, 0:NCH * 128].rearrange("p (c w) -> p c w", c=NCH), AF.Copy),
            waits=ev_w, mark=f"{tag}_trev{t}")
    em.marks[f"{tag}_xnT"] = em.marks[f"{tag}_trev{TPS - 1}"]
    return bi


def ln_post_add(em, lno, s, outb, xs, x_out, tag, waits_first):
    tiles = [outb[:, t * D:(t + 1) * D] for t in range(TPS)]
    ln = lno[s % 2]
    ln_stats(em, ln, tiles, tag + "o", waits_first=waits_first)
    for t in range(TPS):
        em.op("act", lambda e, t=t: e.activation(
            outb[:, t * D:(t + 1) * D], outb[:, t * D:(t + 1) * D],
            AF.Identity,
            bias=ln.nb[:, t:t + 1], scale=ln.rs[:, t:t + 1]),
            waits=[f"{tag}o_lnp"] if t == 0 else (),
            mark=(f"{tag}_lnapp" if t == TPS - 1 else None))
    em.op("dve", lambda e: e.tensor_tensor(xs[:], outb[:], xs[:], ALU.add),
          waits=[f"{tag}_lnapp"], mark=f"{tag}_add")
    if x_out.shape[0] == 128:
        em.dma("pool", lambda e: e.dma_start(
            x_out[:, s * TPS * D:(s + 1) * TPS * D], xs[:]),
            inc=f"st{s % 2}", mark=f"{tag}_store", waits=[f"{tag}_add"])
    else:
        em.dma("pool", lambda e: e.dma_start(
            x_out[s * STRIPE:(s + 1) * STRIPE, :]
            .rearrange("(t p) d -> p t d", p=128),
            xs.rearrange("p (t d) -> p t d", t=TPS)),
            inc=f"st{s % 2}", mark=f"{tag}_store", waits=[f"{tag}_add"])


def load_weights(em, wstage, dsts, srcs, tag, scale=None):
    n = 0
    sw = wstage[0].shape[1]
    nst = len(wstage)
    for wi, (wsb, wd) in enumerate(zip(dsts, srcs)):
        ncw = wd.shape[0] // 128
        cols = wd.shape[1]
        nparts = (cols + sw - 1) // sw
        for c in range(ncw):
            for pp in range(nparts):
                w0 = pp * sw
                w1 = min(cols, w0 + sw)
                st = wstage[n % nst]
                prev = f"{tag}_cast{n - nst}" if n >= nst else None
                em.dma("sp" if n % 2 == 0 else "pool",
                       lambda e, st=st, wd=wd, c=c, w0=w0, w1=w1:
                       e.dma_start(st[:, 0:w1 - w0],
                                   wd[c * 128:(c + 1) * 128, w0:w1]),
                       inc=f"{tag}_wp{n % nst}",
                       waits=[prev] if prev else (),
                       mark=f"{tag}_wl{n}")
                kw = {} if scale is None else {"scale": scale}
                em.op("act", lambda e, wsb=wsb, st=st, c=c, cols=cols,
                      w0=w0, w1=w1, kw=kw:
                      e.activation(wsb[:, c * cols + w0:c * cols + w1],
                                   st[:, 0:w1 - w0], AF.Copy, **kw),
                      waits=[f"{tag}_wl{n}"], mark=f"{tag}_cast{n}")
                n += 1
            em.marks[f"{tag}_w{wi}c{c}"] = em.marks[f"{tag}_cast{n - 1}"]
        em.marks[f"{tag}_w{wi}"] = em.marks[f"{tag}_cast{n - 1}"]



def build_pass_l0ffn(em, sb, ps, cd, x_in, x_out):
    ident = cd["ident_s"]
    P = "p2"
    wr = sb(P + "_wr", [128, NCH * D], BF16)
    wk = sb(P + "_wk", [128, NCH * D], BF16)
    wv = sb(P + "_wv", [128, NCH * D], BF16)
    wstage = [sb(P + f"_wst{i}", [128, D], F32) for i in range(2)]
    mur = sb(P + "_mur", [128, NCH], F32)
    muk = sb(P + "_muk", [128, NCH], F32)
    sc = StripeCtx(em, sb, ps, P, carry=True)
    xrT = sb(P + "_xr", [128, NCH * 512], BF16)
    xkT = sb(P + "_xk", [128, NCH * 512], BF16)
    dT = sb(P + "_d", [128, NCH * 512], BF16)
    rsig = sb(P + "_rsig", [128, TPS * D], F32)
    kact = sb(P + "_kact", [128, NCH * 512], BF16)
    krelu = [sb(P + f"_krelu{i}", [128, 512], F32) for i in range(2)]
    outb = sb(P + "_out", [128, TPS * D], F32)
    lno = [LNBufs(sb, P + f"_lo{i}") for i in range(2)]
    pj = BankPool([ps(P + f"_pj{i}", [128, 512], F32) for i in range(4)])

    load_weights(em, wstage, [wr, wk, wv],
                 [cd["l0f_wrT"], cd["l0f_wkT"], cd["l0f_wvT"]], P)
    em.dma("sp", lambda e: e.dma_start(mur[:], cd["l0f_mur"]),
           inc=P + "_mi", mark=P + "_mul")
    em.dma("sp", lambda e: e.dma_start(muk[:], cd["l0f_muk"]),
           inc=P + "_mi", waits=[P + "_mul"], mark=P + "_mu")

    for s in range(NSTRIPE):
        tag = f"{P}s{s}"
        war = []
        if s >= 2:
            war.append(f"{P}s{s - 2}_mix")
        if s >= 1:
            war.append(f"{P}s{s - 1}_carry")
        bi = stripe_prologue(em, sc, s, x_in, ident, xnt_war=war)
        if s % (NSTRIPE // 2) == 0:
            em.op("dve", lambda e, bi=bi: e.memset(
                sc.chunk3d(bi)[:, :, 0:1], 0.0),
                waits=[f"{tag}_xnT"], mark=f"{tag}_carry")
        else:
            pb = 1 - bi
            em.op("dve", lambda e, bi=bi, pb=pb: e.tensor_copy(
                sc.chunk3d(bi)[:, :, 0:1], sc.chunk3d(pb)[:, :, 512:513]),
                waits=[f"{tag}_xnT"], mark=f"{tag}_carry")
        mix_war = []
        if s >= 1:
            mix_war = [f"{P}s{s - 1}_rmm{TPS - 1}_1", f"{P}s{s - 1}_kmm{NCH - 1}"]
        for c in range(NCH):
            em.op("dve", lambda e, c=c, bi=bi: e.tensor_tensor(
                dT[:, c * 512:(c + 1) * 512], sc.chunk(bi, c),
                sc.chunk_shift(bi, c)[:, 0:512], ALU.subtract),
                waits=([f"{tag}_carry", P + "_mu"] + mix_war) if c == 0 else ())
            em.op("dve", lambda e, c=c, bi=bi: e.scalar_tensor_tensor(
                xrT[:, c * 512:(c + 1) * 512], dT[:, c * 512:(c + 1) * 512],
                mur[:, c:c + 1], sc.chunk_shift(bi, c)[:, 0:512],
                ALU.mult, ALU.add))
            em.op("dve", lambda e, c=c, bi=bi: e.scalar_tensor_tensor(
                xkT[:, c * 512:(c + 1) * 512], dT[:, c * 512:(c + 1) * 512],
                muk[:, c:c + 1], sc.chunk_shift(bi, c)[:, 0:512],
                ALU.mult, ALU.add),
                mark=(f"{tag}_mix" if c == NCH - 1 else None))
        for t in range(TPS):
            for eh in range(2):
                bank, fw, bidx = pj.acquire()
                for c in range(NCH):
                    em.op("pe", lambda e, t=t, eh=eh, c=c, bank=bank: e.matmul(
                        bank, xrT[:, c * 512 + t * 128:c * 512 + t * 128 + 128],
                        wr[:, c * D + eh * 512:c * D + eh * 512 + 512],
                        start=(c == 0), stop=(c == NCH - 1)),
                        waits=([f"{tag}_mix", P + "_w0"] +
                               ([fw] if fw else [])) if c == 0 else (),
                        mark=(f"{tag}_rmm{t}_{eh}" if c == NCH - 1 else None))
                rs_war = ([f"{P}s{s - 1}_kv{t}_{eh}"] if s >= 1 else [])
                em.op("act", lambda e, t=t, eh=eh, bank=bank: e.activation(
                    rsig[:, t * D + eh * 512:t * D + eh * 512 + 512], bank,
                    AF.Sigmoid),
                    waits=[f"{tag}_rmm{t}_{eh}"] + rs_war,
                    mark=f"{tag}_rs{t}_{eh}")
                pj.release(bidx, f"{tag}_rs{t}_{eh}")
        for co in range(NCH):
            bank, fw, bidx = pj.acquire()
            for c in range(NCH):
                em.op("pe", lambda e, co=co, c=c, bank=bank: e.matmul(
                    bank, wk[:, c * D + co * 128:c * D + co * 128 + 128],
                    xkT[:, c * 512:(c + 1) * 512],
                    start=(c == 0), stop=(c == NCH - 1)),
                    waits=([P + "_w1"] + ([fw] if fw else [])) if c == 0 else (),
                    mark=(f"{tag}_kmm{co}" if c == NCH - 1 else None))
            kr = krelu[co % 2]
            kr_war = [f"{tag}_ksq{co - 2}"] if co >= 2 else (
                [f"{P}s{s - 1}_ksq{NCH - 2 + (co % 2) * 0 + co}"]
                if False else [])
            em.op("dve", lambda e, bank=bank, kr=kr: e.tensor_scalar(
                kr[:], bank, 0.0, None, ALU.max),
                waits=[f"{tag}_kmm{co}"] + kr_war, mark=f"{tag}_krel{co}")
            pj.release(bidx, f"{tag}_krel{co}")
            ks_war = ([f"{P}s{s - 1}_vmm{TPS - 1}_1"]
                      if (s >= 1 and co == 0) else [])
            em.op("act", lambda e, co=co, kr=kr: e.activation(
                kact[:, co * 512:(co + 1) * 512], kr[:], AF.Square),
                waits=[f"{tag}_krel{co}"] + ks_war, mark=f"{tag}_ksq{co}")
        ob = outb
        for t in range(TPS):
            for eh in range(2):
                bank, fw, bidx = pj.acquire()
                for c in range(NCH):
                    em.op("pe", lambda e, t=t, eh=eh, c=c, bank=bank: e.matmul(
                        bank, kact[:, c * 512 + t * 128:c * 512 + t * 128 + 128],
                        wv[:, c * D + eh * 512:c * D + eh * 512 + 512],
                        start=(c == 0), stop=(c == NCH - 1)),
                        waits=[f"{tag}_ksq{c}", P + "_w2"] +
                              ([fw] if (c == 0 and fw) else []),
                        mark=(f"{tag}_vmm{t}_{eh}" if c == NCH - 1 else None))
                em.op("dve", lambda e, t=t, eh=eh, bank=bank, ob=ob:
                      e.tensor_tensor(
                          ob[:, t * D + eh * 512:t * D + eh * 512 + 512],
                          bank,
                          rsig[:, t * D + eh * 512:t * D + eh * 512 + 512],
                          ALU.mult),
                      waits=[f"{tag}_vmm{t}_{eh}", f"{tag}_rs{t}_{eh}"],
                      mark=f"{tag}_kv{t}_{eh}")
                pj.release(bidx, f"{tag}_kv{t}_{eh}")
        ln_post_add(em, lno, s, outb, sc.xs[s % len(sc.xs)], x_out, tag,
                    waits_first=[f"{tag}_kv{TPS - 1}_1"])


def _p1_passthrough(em, sc, bi, s, tag, x_out, wait_mark):
    em.dma("pool", lambda e: e.dma_start(
        x_out[s * STRIPE:(s + 1) * STRIPE, :]
        .rearrange("(t p) d -> p t d", p=128),
        sc.xs[bi].rearrange("p (t d) -> p t d", t=TPS)),
        inc=f"st{s % 2}", mark=f"{tag}_store", waits=[wait_mark])



def build_pass_l0attn(em, sb, ps, cd, x_in, x_out):
    ident = cd["ident_s"]
    P = "p1"
    wq = sb(P + "_wq", [128, NCH * D], BF16)
    wk = sb(P + "_wk", [128, NCH * D], BF16)
    wv = sb(P + "_wv", [128, NCH * D], BF16)
    wo = sb(P + "_wo", [128, NCH * D], BF16)
    wstage = [sb(P + f"_wst{i}", [128, D], F32) for i in range(2)]
    bq = sb(P + "_bq", [128, NCH], F32)
    bk = sb(P + "_bk", [128, NCH], F32)
    bvb = sb(P + "_bvb", [128, D], BF16)
    bob = sb(P + "_bob", [128, D], BF16)
    bstage = wstage[1]
    mask4 = sb(P + "_mask4", [128, 512], BF16)
    sc = StripeCtx(em, sb, ps, P, carry=False)
    qf = sb(P + "_qf", [128, NCH * 512], BF16)
    kze = sb(P + "_kze", [128, NCH * 512], BF16)
    kzo = sb(P + "_kzo", [128, NCH * 512], BF16)
    vst = sb(P + "_vst", [128, TPS * D], BF16)
    ktm = sb(P + "_ktm", [128, TPS * D], BF16)
    sT = sb(P + "_sT", [128, 2 * 512], BF16)
    stf = sb(P + "_stf", [128, H * DH], F32)
    stb = sb(P + "_stb", [128, H * DH], BF16)
    ofm = sb(P + "_ofm", [128, NCH * 512], BF16)
    outb = sb(P + "_out", [128, TPS * D], F32)
    lno = [LNBufs(sb, P + f"_lo{i}") for i in range(2)]
    pj = BankPool([ps(P + f"_pj{i}", [128, 512], F32) for i in range(4)])
    po = [ps(P + f"_po{i}", [128, 512], F32) for i in range(2)]

    load_weights(em, wstage, [wq, wk, wv, wo],
                 [cd["l0a_wqT"], cd["l0a_wkT"], cd["l0a_wvT"], cd["l0a_woT"]],
                 P)
    em.dma("sp", lambda e: e.dma_start(bq[:], cd["l0a_bq"]), inc=P + "_mi",
           mark=P + "_bql")
    em.dma("sp", lambda e: e.dma_start(bk[:], cd["l0a_bk"]), inc=P + "_mi",
           waits=[P + "_bql"], mark=P + "_bqk")
    em.dma("sp", lambda e: e.dma_start(bstage[:], cd["l0a_bvb"]),
           inc=P + "_mi", waits=[P + "_bqk", P + "_w3"], mark=P + "_bvl")
    em.op("act", lambda e: e.activation(bvb[:], bstage[:], AF.Copy),
          waits=[P + "_bvl"], mark=P + "_bv")
    em.dma("sp", lambda e: e.dma_start(bstage[:], cd["l0a_bob"]),
           inc=P + "_mi", waits=[P + "_bv"], mark=P + "_bol")
    em.op("act", lambda e: e.activation(bob[:], bstage[:], AF.Copy),
          waits=[P + "_bol"], mark=P + "_bo")
    em.dma("sp", lambda e: e.dma_start(mask4[:], cd["l0mask4"]),
           inc=P + "_mi", waits=[P + "_bol"], mark=P + "_msk")
    em.op("dve", lambda e: e.memset(kze[64:128, :], 0.0))
    em.op("dve", lambda e: e.memset(kzo[0:64, :], 0.0), mark=P + "_kz0")

    for s in range(NSTRIPE):
        tag = f"{P}s{s}"
        bi = stripe_prologue(em, sc, s, x_in, ident)
        for co in range(NCH):
            bank, fw, bidx = pj.acquire()
            for c in range(NCH):
                em.op("pe", lambda e, co=co, c=c, bank=bank, bi=bi: e.matmul(
                    bank[:],
                    wq[:, c * D + co * 128:c * D + co * 128 + 128],
                    sc.chunk(bi, c),
                    start=(c == 0), stop=(c == NCH - 1)),
                    waits=([f"{tag}_xnT", P + "_w0", P + "_bqk"] +
                           ([fw] if fw else [])) if c == 0 else (),
                    mark=(f"{tag}_qmm{co}" if c == NCH - 1 else None))
            em.op("act", lambda e, co=co, bank=bank: e.activation(
                qf[:, co * 512:(co + 1) * 512], bank[:],
                AF.Identity, bias=bq[:, co:co + 1]),
                waits=[f"{tag}_qmm{co}"], mark=f"{tag}_qf{co}")
            pj.release(bidx, f"{tag}_qf{co}")
        for co in range(NCH):
            bank, fw, bidx = pj.acquire()
            for c in range(NCH):
                em.op("pe", lambda e, co=co, c=c, bank=bank, bi=bi: e.matmul(
                    bank[:],
                    wk[:, c * D + co * 128:c * D + co * 128 + 128],
                    sc.chunk(bi, c),
                    start=(c == 0), stop=(c == NCH - 1)),
                    waits=([P + "_w1", P + "_kz0"] +
                           ([fw] if fw else [])) if c == 0 else (),
                    mark=(f"{tag}_kmm{co}" if c == NCH - 1 else None))
            em.op("act", lambda e, co=co, bank=bank: e.activation(
                kze[0:64, co * 512:(co + 1) * 512], bank[0:64, :],
                AF.Identity, bias=bk[0:64, co:co + 1]),
                waits=[f"{tag}_kmm{co}"], mark=f"{tag}_kfe{co}")
            em.op("act", lambda e, co=co, bank=bank: e.activation(
                kzo[64:128, co * 512:(co + 1) * 512], bank[64:128, :],
                AF.Identity, bias=bk[64:128, co:co + 1]),
                waits=[f"{tag}_kmm{co}"], mark=f"{tag}_kfo{co}")
            pj.release(bidx, f"{tag}_kfo{co}")
        for t in range(TPS):
            for eh in range(2):
                bank, fw, bidx = pj.acquire()
                for c in range(NCH):
                    em.op("pe", lambda e, t=t, eh=eh, c=c, bank=bank,
                          bi=bi: e.matmul(
                        bank[:],
                        sc.chunk(bi, c)[:, t * 128:t * 128 + 128],
                        wv[:, c * D + eh * 512:c * D + eh * 512 + 512],
                        start=(c == 0), stop=(c == NCH - 1)),
                        waits=([f"{tag}_xnT", P + "_w2", P + "_bv"] +
                               ([fw] if fw else [])) if c == 0 else (),
                        mark=(f"{tag}_vmm{t}_{eh}" if c == NCH - 1 else None))
                em.op("dve", lambda e, t=t, eh=eh, bank=bank: e.tensor_tensor(
                    vst[:, t * D + eh * 512:t * D + eh * 512 + 512], bank[:],
                    bvb[:, eh * 512:eh * 512 + 512], ALU.add),
                    waits=[f"{tag}_vmm{t}_{eh}"], mark=f"{tag}_vf{t}_{eh}")
                pj.release(bidx, f"{tag}_vf{t}_{eh}")
        for t in range(TPS):
            pa, pb2 = sc.ptr[0], sc.ptr[1]
            war = ([f"{tag}_ktev{t - 1}", f"{tag}_kteA{t - 1}"] if t >= 1
                   else [f"{tag}_trev{TPS - 1}"])
            for p in range(NCH):
                em.op("pe", lambda e, t=t, p=p, pa=pa: e.matmul(
                    pa[:, p * 128:(p + 1) * 128],
                    kze[:, p * 512 + t * 128:p * 512 + t * 128 + 128],
                    ident, is_transpose=True, start=True, stop=True),
                    waits=([f"{tag}_kfe{NCH - 1}", f"{tag}_kfo{NCH - 1}"] + war)
                    if p == 0 else (),
                    mark=(f"{tag}_ktrA{t}" if p == NCH - 1 else None))
            for p in range(NCH):
                em.op("pe", lambda e, t=t, p=p, pb2=pb2: e.matmul(
                    pb2[:, p * 128:(p + 1) * 128],
                    kzo[:, p * 512 + t * 128:p * 512 + t * 128 + 128],
                    ident, is_transpose=True, start=True, stop=True),
                    mark=(f"{tag}_ktrB{t}" if p == NCH - 1 else None))
            em.op("act", lambda e, t=t, pa=pa: e.activation(
                ktm[:, t * D:(t + 1) * D], pa[:, 0:D], AF.Copy),
                waits=[f"{tag}_ktrA{t}"], mark=f"{tag}_kteA{t}")
            em.op("dve", lambda e, t=t, pb2=pb2: e.tensor_tensor(
                ktm[:, t * D:(t + 1) * D], ktm[:, t * D:(t + 1) * D],
                pb2[:, 0:D], ALU.add),
                waits=[f"{tag}_ktrB{t}", f"{tag}_kteA{t}"],
                mark=f"{tag}_ktev{t}")
        em.marks[f"{tag}_trev2"] = em.marks[f"{tag}_kteA{TPS - 1}"]
        em.marks[f"{tag}_trev3"] = em.marks[f"{tag}_ktev{TPS - 1}"]
        for t in range(TPS):
            j = (s % (NSTRIPE // 2)) * TPS + t
            ctag = f"{tag}c{t}"
            if j == 0:
                em.op("dve", lambda e: e.memset(stf[:], 0.0),
                      waits=([f"{P}s{s - 1}c{TPS - 1}_sadd"] if s > 0 else []),
                      mark=f"{ctag}_sadd")
            else:
                ptag = f"{tag}c{t - 1}" if t > 0 else f"{P}s{s - 1}c{TPS - 1}"
                em.op("act", lambda e: e.activation(stb[:], stf[:], AF.Copy),
                      waits=[f"{ptag}_sadd"], mark=f"{ctag}_scp")
            for g in range(4):
                sg = sT[:, (g % 2) * 512:(g % 2) * 512 + 512]
                bank, fw, bidx = pj.acquire()
                for hh in range(4):
                    h = 4 * g + hh
                    p = h // 2
                    kz = kze if h % 2 == 0 else kzo
                    em.op("pe", lambda e, p=p, hh=hh, kz=kz, bank=bank,
                          t=t: e.matmul(
                              bank[:, hh * 128:hh * 128 + 128],
                              kz[:, p * 512 + t * 128:p * 512 + t * 128 + 128],
                              qf[:, p * 512 + t * 128:p * 512 + t * 128 + 128],
                              start=True, stop=True),
                          waits=([f"{tag}_ktev{t}", P + "_msk"] +
                                 ([fw] if fw else [])) if hh == 0 else (),
                          mark=(f"{ctag}_smm{g}" if hh == 3 else None))
                em.op("dve", lambda e, sg=sg, bank=bank: e.tensor_tensor(
                    sg, bank[:], mask4[:], ALU.mult),
                    waits=[f"{ctag}_smm{g}"], mark=f"{ctag}_msk{g}")
                pj.release(bidx, f"{ctag}_msk{g}")
                for hh in range(4):
                    h = 4 * g + hh
                    p = h // 2
                    ob_bank = po[p // 4]
                    col = (p % 4) * 128
                    cb = (h % 2) * 64
                    em.op("pe", lambda e, h=h, col=col, cb=cb, sg=sg,
                          hh=hh, t=t, ob_bank=ob_bank, j=j: e.matmul(
                              ob_bank[cb:cb + 64, col:col + 128],
                              vst[:, t * D + h * DH:t * D + h * DH + 64],
                              sg[:, hh * 128:hh * 128 + 128],
                              start=True, stop=(j == 0),
                              tile_position=(0, cb)),
                          waits=([f"{ctag}_msk{g}", f"{tag}_vf{t}_0",
                                  f"{tag}_vf{t}_1"] +
                                 ([f"{tag}c{t - 1}_oev{p // 4}"]
                                  if t >= 1 else
                                  ([f"{P}s{s - 1}c{TPS - 1}_oev{p // 4}"]
                                   if s >= 1 else [])))
                          if (hh == 0 and g == 0) else
                          ([f"{ctag}_msk{g}"] if hh == 0 else ()),
                          mark=(f"{ctag}_omm{g}"
                                if (j == 0 and hh == 3) else None))
                    if j > 0:
                        em.op("pe", lambda e, h=h, col=col, cb=cb, t=t,
                              ob_bank=ob_bank: e.matmul(
                                  ob_bank[cb:cb + 64, col:col + 128],
                                  stb[:, h * DH:(h + 1) * DH],
                                  qf[:, (h // 2) * 512 + t * 128:
                                     (h // 2) * 512 + t * 128 + 128],
                                  start=False, stop=True,
                                  tile_position=(0, cb)),
                              waits=[f"{ctag}_scp"] if hh == 0 else (),
                              mark=(f"{ctag}_omm{g}" if hh == 3 else None))
            bank, fw, bidx = pj.acquire()
            for h in range(H):
                p = h // 2
                mb = (h % 2) * 64
                em.op("pe", lambda e, h=h, p=p, mb=mb, t=t, bank=bank:
                      e.matmul(
                          bank[mb:mb + 64, p * 64:(p + 1) * 64],
                          ktm[:, t * D + p * 128 + mb:t * D + p * 128 + mb + 64],
                          vst[:, t * D + h * DH:t * D + h * DH + 64],
                          start=True, stop=True, tile_position=(0, mb)),
                      waits=([f"{tag}_ktev{t}"] + ([fw] if fw else []))
                      if h == 0 else (),
                      mark=(f"{ctag}_dmm" if h == H - 1 else None))
            em.op("dve", lambda e, bank=bank: e.tensor_tensor(
                stf[0:64, :].rearrange("p (h c) -> p h c", c=DH)[:, 0::2, :],
                stf[0:64, :].rearrange("p (h c) -> p h c", c=DH)[:, 0::2, :],
                bank[0:64, :].rearrange("p (h c) -> p h c", c=DH),
                ALU.add),
                waits=[f"{ctag}_dmm"] + ([f"{ctag}_scp"] if j > 0 else []))
            em.op("dve", lambda e, bank=bank: e.tensor_tensor(
                stf[64:128, :].rearrange("p (h c) -> p h c", c=DH)[:, 1::2, :],
                stf[64:128, :].rearrange("p (h c) -> p h c", c=DH)[:, 1::2, :],
                bank[64:128, :].rearrange("p (h c) -> p h c", c=DH),
                ALU.add),
                mark=f"{ctag}_sadd")
            pj.release(bidx, f"{ctag}_sadd")
            for b in range(2):
                em.op("act", lambda e, b=b, t=t: e.activation(
                    ofm[:, 0:NCH * 512].rearrange("p (c w) -> p c w", c=NCH)
                    [:, 4 * b:4 * b + 4, t * 128:t * 128 + 128],
                    po[b][:, 0:512].rearrange("p (c w) -> p c w", c=4),
                    AF.Copy),
                    waits=[f"{ctag}_omm{2 * b}", f"{ctag}_omm{2 * b + 1}"],
                    mark=f"{ctag}_oev{b}")
            ob = outb
            for eh in range(2):
                bank, fw, bidx = pj.acquire()
                for co in range(NCH):
                    em.op("pe", lambda e, t=t, eh=eh, co=co, bank=bank:
                          e.matmul(
                              bank[:],
                              ofm[:, co * 512 + t * 128:co * 512 + t * 128 + 128],
                              wo[:, co * D + eh * 512:co * D + eh * 512 + 512],
                              start=(co == 0), stop=(co == NCH - 1)),
                          waits=([f"{ctag}_oev0", f"{ctag}_oev1",
                                  P + "_w3", P + "_bo"] +
                                 ([fw] if fw else [])) if co == 0 else (),
                          mark=(f"{ctag}_wmm{eh}" if co == NCH - 1 else None))
                em.op("dve", lambda e, t=t, eh=eh, bank=bank, ob=ob:
                      e.tensor_tensor(
                          ob[:, t * D + eh * 512:t * D + eh * 512 + 512],
                          bank[:], bob[:, eh * 512:eh * 512 + 512], ALU.add),
                      waits=[f"{ctag}_wmm{eh}"],
                      mark=f"{ctag}_wf{eh}")
                pj.release(bidx, f"{ctag}_wf{eh}")
        ln_post_add(em, lno, s, outb, sc.xs[s % len(sc.xs)], x_out, tag,
                    waits_first=[f"{tag}c{TPS - 1}_wf1"])



def build_pass_l1attn(em, sb, ps, cd, x_in, x_out):
    ident = cd["ident_s"]
    P = "p3"
    W3 = 3 * D
    wqkv = sb(P + "_wqkv", [128, NCH * W3], BF16)
    wout = sb(P + "_wout", [128, NCH * D], BF16)
    outb = sb(P + "_outb", [128, TPS * D], F32)
    wstage = [outb[:, 0:W3 // 2], outb[:, W3 // 2:W3]]
    cosT = sb(P + "_cos", [128, SEQLEN], BF16)
    ssT = sb(P + "_ss", [128, SEQLEN], BF16)
    lmask = sb(P + "_lm", [128, 4 * BS], BF16)
    pj = BankPool([ps(P + f"_pj{i}", [128, 512], F32) for i in range(2)])
    sp2 = [ps(P + "_spa", [128, 1024], F32), ps(P + "_spb", [128, 1024], F32)]
    po = ps(P + "_po", [128, 1024], F32)
    ptr_aps = [po.bitcast(BF16)[:, 0:NCH * 128],
               sp2[1].bitcast(BF16)[:, 0:NCH * 128]]
    sc = StripeCtx(em, sb, ps, P, carry=False, nxs=1, nxnt=1,
                   ptr_aps=ptr_aps)
    qraw = sb(P + "_qraw", [128, NCH * 512], BF16)
    kraw = sb(P + "_kraw", [128, NCH * 512], BF16)
    qrot = sb(P + "_qrot", [128, NCH * 512], BF16)
    kze = sb(P + "_kze", [128, NCH * 512], BF16)
    kzo = sb(P + "_kzo", [128, NCH * 512], BF16)
    tmp1 = outb[:, 3072:3328].bitcast(BF16)
    tmp2 = outb[:, 3328:3584].bitcast(BF16)
    tmp3 = outb[:, 3584:3840].bitcast(BF16)
    vst = sb(P + "_vst", [128, TPS * H * 128], BF16)
    ptile = [sb(P + f"_pt{i}", [128, 1024], BF16) for i in range(2)]
    lns = sb(P + "_lns", [128, 1024], F32)
    rcp = sb(P + "_rcp", [128, 1024], F32)
    ofm = sb(P + "_ofm", [128, NCH * 512], BF16)
    lno = [LNBufs(sb, P + f"_lo{i}") for i in range(2)]

    load_weights(em, wstage, [wqkv, wout],
                 [cd["l1a_wqkvT"], cd["l1a_woutT"]], P)
    em.dma("sp", lambda e: e.dma_start(cosT[:], cd["l1cos"]), inc=P + "_mi",
           mark=P + "_c0")
    em.dma("sp", lambda e: e.dma_start(ssT[:], cd["l1sin"]), inc=P + "_mi",
           waits=[P + "_c0"], mark=P + "_c1")
    em.dma("sp", lambda e: e.dma_start(lmask[:], cd["l1mask"]), inc=P + "_mi",
           waits=[P + "_c1"], mark=P + "_c2")
    em.op("dve", lambda e: e.memset(kze[64:128, :], 0.0))
    em.op("dve", lambda e: e.memset(kzo[0:64, :], 0.0))
    em.op("dve", lambda e: e.memset(
        vst.rearrange("p (s c) -> p s c", c=128)[:, :, 64:128], 1.0),
        mark=P + "_init")

    chain = {"exp": [None, None], "omm": [None, None],
             "norm": [None, None], "ofmd": None}

    for s in range(NSTRIPE):
        tag = f"{P}s{s}"
        war = [f"{P}s{s - 1}_pedone"] if s >= 1 else []
        xw = None
        pw = []
        if s >= 1:
            pw = [c for c in (chain["norm"][0], chain["norm"][1],
                              chain["exp"][1]) if c]
        bi = stripe_prologue(em, sc, s, x_in, ident, xnt_war=war,
                             x_extra_wait=xw, ptr_war=pw)

        def proj_fm(nm, dst, woff):
            for co in range(NCH):
                bank, fw, bidx = pj.acquire()
                for c in range(NCH):
                    em.op("pe", lambda e, co=co, c=c, bank=bank, bi=bi,
                          woff=woff: e.matmul(
                              bank[:],
                              wqkv[:, c * W3 + woff + co * 128:
                                   c * W3 + woff + co * 128 + 128],
                              sc.chunk(bi, c),
                              start=(c == 0), stop=(c == NCH - 1)),
                          waits=([f"{tag}_xnT", P + "_w0", P + "_init"] +
                                 ([fw] if fw else [])) if c == 0 else (),
                          mark=(f"{tag}_{nm}mm{co}" if c == NCH - 1
                                else None))
                evw = [f"{tag}_{nm}mm{co}"]
                if s >= 1:
                    evw.append(f"{P}s{s - 1}_{nm}rot{co}")
                em.op("act", lambda e, co=co, bank=bank, dst=dst:
                      e.activation(dst[:, co * 512:(co + 1) * 512], bank[:],
                                   AF.Copy),
                      waits=evw, mark=f"{tag}_{nm}ev{co}")
                pj.release(bidx, f"{tag}_{nm}ev{co}")
        proj_fm("q", qraw, 0)
        proj_fm("k", kraw, D)
        for t in range(TPS):
            for eh in range(2):
                bank, fw, bidx = pj.acquire()
                for c in range(NCH):
                    em.op("pe", lambda e, t=t, eh=eh, c=c, bank=bank,
                          bi=bi: e.matmul(
                              bank[:],
                              sc.chunk(bi, c)[:, t * 128:t * 128 + 128],
                              wqkv[:, c * W3 + 2 * D + eh * 512:
                                   c * W3 + 2 * D + eh * 512 + 512],
                              start=(c == 0), stop=(c == NCH - 1)),
                          waits=([fw] if (c == 0 and fw) else ()),
                          mark=(f"{tag}_vmm{t}_{eh}" if c == NCH - 1
                                else None))
                vw = [f"{tag}_vmm{t}_{eh}"]
                if s >= 1 and t == 0 and eh == 0:
                    vw.append(f"{P}s{s - 1}_odone")
                em.op("act", lambda e, t=t, eh=eh, bank=bank: e.activation(
                    vst.rearrange("p (tt h c) -> p tt h c", tt=TPS, h=H)
                    [:, t, 8 * eh:8 * eh + 8, 0:64],
                    bank[:].rearrange("p (h c) -> p h c", c=64),
                    AF.Copy),
                    waits=vw, mark=f"{tag}_vev{t}_{eh}")
                pj.release(bidx, f"{tag}_vev{t}_{eh}")
        off = (s % (NSTRIPE // 2)) * 512
        for c in range(NCH):
            em.op("dve", lambda e, c=c, off=off: e.tensor_tensor(
                qrot[:, c * 512:(c + 1) * 512],
                qraw[:, c * 512:(c + 1) * 512],
                cosT[:, off:off + 512], ALU.mult),
                waits=[f"{tag}_qev{c}", P + "_c2"] if c == 0
                else [f"{tag}_qev{c}"])
            for sq4 in range(4):
                sq = sq4 * 32
                tw = []
                if sq4 == 0:
                    if c > 0:
                        tw = [f"{tag}_qc{c - 1}"]
                    elif s >= 1:
                        tw = [f"{P}s{s - 1}_kc{NCH - 1}"]
                em.op("dve", lambda e, c=c, sq=sq, off=off:
                      e.tensor_tensor(
                          tmp1[sq:sq + 32, :],
                          qraw[sq:sq + 32, c * 512:(c + 1) * 512],
                          ssT[sq:sq + 32, off:off + 512],
                          ALU.mult),
                      waits=tw,
                      mark=(f"{tag}_qq{c}" if sq4 == 3 else None))
            for qt in range(4):
                sq = (qt ^ 1) * 32
                em.op("act", lambda e, qt=qt, sq=sq: e.activation(
                    tmp2[qt * 32:qt * 32 + 32, :], tmp1[sq:sq + 32, :],
                    AF.Copy),
                    waits=[f"{tag}_qq{c}"] if qt == 0 else (),
                    mark=(f"{tag}_qc{c}" if qt == 3 else None))
            em.op("dve", lambda e, c=c: e.tensor_tensor(
                qrot[:, c * 512:(c + 1) * 512],
                qrot[:, c * 512:(c + 1) * 512], tmp2[:], ALU.add),
                waits=[f"{tag}_qc{c}"],
                mark=f"{tag}_qrot{c}")
        for c in range(NCH):
            em.op("dve", lambda e, c=c, off=off: e.tensor_tensor(
                tmp3[:], kraw[:, c * 512:(c + 1) * 512],
                cosT[:, off:off + 512], ALU.mult),
                waits=[f"{tag}_kev{c}"])
            for sq4 in range(4):
                sq = sq4 * 32
                tw = []
                if sq4 == 0:
                    tw = [f"{tag}_kc{c - 1}" if c > 0
                          else f"{tag}_qc{NCH - 1}"]
                em.op("dve", lambda e, c=c, sq=sq, off=off:
                      e.tensor_tensor(
                          tmp1[sq:sq + 32, :],
                          kraw[sq:sq + 32, c * 512:(c + 1) * 512],
                          ssT[sq:sq + 32, off:off + 512],
                          ALU.mult),
                      waits=tw,
                      mark=(f"{tag}_kq{c}" if sq4 == 3 else None))
            for qt in range(4):
                sq = (qt ^ 1) * 32
                em.op("act", lambda e, qt=qt, sq=sq: e.activation(
                    tmp2[qt * 32:qt * 32 + 32, :], tmp1[sq:sq + 32, :],
                    AF.Copy),
                    waits=[f"{tag}_kq{c}"] if qt == 0 else (),
                    mark=(f"{tag}_kc{c}" if qt == 3 else None))
            em.op("dve", lambda e, c=c: e.tensor_tensor(
                tmp3[:], tmp3[:], tmp2[:], ALU.add),
                waits=[f"{tag}_kc{c}"])
            em.op("dve", lambda e, c=c: e.tensor_copy(
                kze[0:64, c * 512:(c + 1) * 512], tmp3[0:64, :]))
            em.op("dve", lambda e, c=c: e.tensor_copy(
                kzo[64:128, c * 512:(c + 1) * 512], tmp3[64:128, :]),
                mark=f"{tag}_krot{c}")
        em.marks[f"{tag}_rope"] = em.marks[f"{tag}_krot{NCH - 1}"]
        for b in range(2):
            btag = f"{tag}b{b}"
            for gp in range(8):
                pt = ptile[gp % 2]
                sp = sp2[gp % 2]
                poh = po[:, (gp % 2) * 512:(gp % 2) * 512 + 512]
                for hi in range(2):
                    h = 2 * gp + hi
                    co = h // 2
                    kz = kze if h % 2 == 0 else kzo
                    for kt in range(2):
                        ws = []
                        if hi == 0 and kt == 0:
                            ws = [f"{tag}_rope"]
                            if chain["exp"][gp % 2] is not None:
                                ws.append(chain["exp"][gp % 2])
                        em.op("pe", lambda e, co=co, kz=kz, kt=kt, b=b,
                              hi=hi, sp=sp: e.matmul(
                                  sp[:, hi * 512 + kt * 256:
                                     hi * 512 + kt * 256 + 256],
                                  kz[:, co * 512 + b * 256 + kt * 128:
                                     co * 512 + b * 256 + kt * 128 + 128],
                                  qrot[:, co * 512 + b * 256:
                                       co * 512 + b * 256 + 256],
                                  start=True, stop=True),
                              waits=ws,
                              mark=(f"{btag}g{gp}_smm"
                                    if (hi == 1 and kt == 1) else None))
                ptw = [f"{btag}g{gp}_smm"]
                if chain["omm"][gp % 2] is not None:
                    ptw.append(chain["omm"][gp % 2])
                em.op("act", lambda e, pt=pt, sp=sp: e.activation(
                    pt[:], sp[:], AF.Exp, scale=DH ** (-0.5)),
                    waits=ptw, mark=f"{btag}g{gp}_exp")
                chain["exp"][gp % 2] = f"{btag}g{gp}_exp"
                em.op("dve", lambda e, pt=pt: e.tensor_tensor(
                    pt[:], pt[:], lmask[:], ALU.mult),
                    waits=[f"{btag}g{gp}_exp"], mark=f"{btag}g{gp}_msk")
                for hi in range(2):
                    h = 2 * gp + hi
                    for kt in range(2):
                        ws = [f"{btag}g{gp}_msk", f"{tag}_vev{TPS - 1}_1"]
                        if hi == 0 and kt == 0:
                            if chain["norm"][gp % 2] is not None:
                                ws.append(chain["norm"][gp % 2])
                        em.op("pe", lambda e, h=h, hi=hi, kt=kt, b=b, pt=pt,
                              poh=poh: e.matmul(
                                  poh[:, hi * 256:hi * 256 + 256],
                                  vst[:, (2 * b + kt) * H * 128 + h * 128:
                                      (2 * b + kt) * H * 128 + h * 128 + 128],
                                  pt[:, hi * 512 + kt * 256:
                                     hi * 512 + kt * 256 + 256],
                                  start=(kt == 0), stop=(kt == 1)),
                              waits=ws,
                              mark=(f"{btag}g{gp}_omm"
                                    if (hi == 1 and kt == 1) else None))
                chain["omm"][gp % 2] = f"{btag}g{gp}_omm"
                ftag = f"{btag}g{gp}"
                hb = (gp % 2) * 512
                lw = [f"{btag}g{gp}_omm"]
                if chain["ofmd"] is not None:
                    lw.append(chain["ofmd"])
                em.op("act", lambda e, hb=hb: e.activation(
                    lns[64:128, hb:hb + 512], po[64:128, hb:hb + 512],
                    AF.Ln),
                    waits=lw, mark=f"{ftag}_ln")
                em.op("act", lambda e, hb=hb: e.activation(
                    rcp[0:64, hb:hb + 512], lns[64:128, hb:hb + 512],
                    AF.Exp, scale=-1.0),
                    waits=[f"{ftag}_ln"], mark=f"{ftag}_rcp")
                em.op("dve", lambda e, hb=hb: e.tensor_tensor(
                    lns[0:64, hb:hb + 512].bitcast(BF16)[:, 0:1024]
                    [:, 0:512] if False else
                    lns[0:64, :].bitcast(BF16)[:, hb:hb + 512],
                    po[0:64, hb:hb + 512], rcp[0:64, hb:hb + 512],
                    ALU.mult),
                    waits=[f"{ftag}_rcp"], mark=f"{ftag}_on")
                for hh in range(2):
                    h = 2 * gp + hh
                    cb = (h % 2) * 64
                    co = h // 2
                    em.op("act", lambda e, hh=hh, cb=cb, co=co, b=b, hb=hb:
                          e.activation(
                              ofm[cb:cb + 64,
                                  co * 512 + b * 256:co * 512 + b * 256 + 256],
                              lns[0:64, :].bitcast(BF16)
                              [:, hb + hh * 256:hb + hh * 256 + 256],
                              AF.Copy),
                          waits=[f"{ftag}_on"],
                          mark=(f"{ftag}_ofm" if hh == 1 else None))
                chain["norm"][gp % 2] = f"{ftag}_ofm"
                chain["ofmd"] = f"{ftag}_ofm"
            em.marks[f"{btag}_odone"] = em.marks[f"{btag}g7_omm"]
            for tt in range(2):
                t = 2 * b + tt
                for eh in range(2):
                    bank, fw, bidx = pj.acquire()
                    for co in range(NCH):
                        em.op("pe", lambda e, t=t, eh=eh, co=co, bank=bank:
                              e.matmul(
                                  bank[:],
                                  ofm[:, co * 512 + t * 128:
                                      co * 512 + t * 128 + 128],
                                  wout[:, co * D + eh * 512:
                                       co * D + eh * 512 + 512],
                                  start=(co == 0), stop=(co == NCH - 1)),
                              waits=([f"{btag}g6_ofm", f"{btag}g7_ofm",
                                      P + "_w1"] +
                                     ([fw] if fw else [])) if co == 0 else (),
                              mark=(f"{tag}_wmm{t}_{eh}" if co == NCH - 1
                                    else None))
                    evw = [f"{tag}_wmm{t}_{eh}"]
                    if s >= 1 and t == 0 and eh == 0:
                        evw.append(f"{P}s{s - 1}_add")
                    em.op("act", lambda e, t=t, eh=eh, bank=bank:
                          e.activation(
                              outb[:, t * D + eh * 512:t * D + eh * 512 + 512],
                              bank[:], AF.Copy),
                          waits=evw, mark=f"{tag}_wf{t}_{eh}")
                    pj.release(bidx, f"{tag}_wf{t}_{eh}")
        em.marks[f"{tag}_odone"] = em.marks[f"{tag}b1_odone"]
        em.marks[f"{tag}_pedone"] = em.marks[f"{tag}_wmm{TPS - 1}_1"]
        ln_post_add(em, lno, s, outb, sc.xs[s % len(sc.xs)], x_out, tag,
                    waits_first=[f"{tag}_wf{TPS - 1}_1"])



def build_pass_l1ffn(em, sb, ps, cd, x_in, x_out):
    ident = cd["ident_s"]
    P = "p4"
    NHC = HID // 128
    wg = sb(P + "_wg", [128, NCH * HID], BF16)
    wu = sb(P + "_wu", [128, NCH * HID], BF16)
    wd = sb(P + "_wd", [128, NHC * D], BF16)
    sc = StripeCtx(em, sb, ps, P, carry=False, nxs=1, nxnt=1)
    hact = sb(P + "_hact", [128, NHC * 512], BF16)
    wstage = [hact.bitcast(I32)[:, 0:HID // 2],
              hact.bitcast(I32)[:, HID // 2:HID]]
    silu_t = [sb(P + f"_sl{i}", [128, 512], F32) for i in range(2)]
    lno = LNBufs(sb, P + "_lo")
    pg = BankPool([ps(P + f"_pg{i}", [128, 512], F32) for i in range(4)])
    pd = [ps(P + f"_pd{i}", [128, 512], F32) for i in range(2)]

    load_weights(em, wstage, [wg, wu, wd],
                 [cd["l1f_wgT"], cd["l1f_wuT"], cd["l1f_wdT"]], P,
                 scale=1.0 / 127.0)

    for s in range(NSTRIPE):
        tag = f"{P}s{s}"
        war = [f"{P}s{s - 1}_gu{NHC - 1}"] if s >= 1 else []
        xw = f"{P}s{s - 1}_store" if s >= 1 else None
        bi = stripe_prologue(em, sc, s, x_in, ident, xnt_war=war,
                             x_extra_wait=xw)
        for hc in range(NHC):
            bkg, fwg, big = pg.acquire()
            for c in range(NCH):
                em.op("pe", lambda e, hc=hc, c=c, bkg=bkg, bi=bi: e.matmul(
                    bkg[:],
                    wg[:, c * HID + hc * 128:c * HID + hc * 128 + 128],
                    sc.chunk(bi, c),
                    start=(c == 0), stop=(c == NCH - 1)),
                    waits=([f"{tag}_xnT", P + "_w0"] +
                           ([fwg] if fwg else [])) if c == 0 else (),
                    mark=(f"{tag}_gmm{hc}" if c == NCH - 1 else None))
            bku, fwu, biu = pg.acquire()
            for c in range(NCH):
                em.op("pe", lambda e, hc=hc, c=c, bku=bku, bi=bi: e.matmul(
                    bku[:],
                    wu[:, c * HID + hc * 128:c * HID + hc * 128 + 128],
                    sc.chunk(bi, c),
                    start=(c == 0), stop=(c == NCH - 1)),
                    waits=([P + "_w1"] + ([fwu] if fwu else []))
                    if c == 0 else (),
                    mark=(f"{tag}_umm{hc}" if c == NCH - 1 else None))
            sl = silu_t[hc % 2]
            sl_war = [f"{tag}_ha{hc - 2}"] if hc >= 2 else (
                [f"{P}s{s - 1}_ha{NHC - 2 + hc}"] if s >= 1 else [])
            em.op("act", lambda e, hc=hc, bkg=bkg, sl=sl: e.activation(
                sl[:], bkg[:], AF.Silu),
                waits=[f"{tag}_gmm{hc}"] + sl_war, mark=f"{tag}_sil{hc}")
            pg.release(big, f"{tag}_sil{hc}")
            ha_war = ([f"{P}s{s - 1}_dmm{TPS - 1}_1"]
                      if (s >= 1 and hc == 0) else [])
            if s == 0 and hc == 0:
                ha_war = [P + "_w2"]
            em.op("dve", lambda e, hc=hc, bku=bku, sl=sl: e.tensor_tensor(
                hact[:, hc * 512:(hc + 1) * 512], bku[:], sl[:], ALU.mult),
                waits=[f"{tag}_umm{hc}", f"{tag}_sil{hc}"] + ha_war,
                mark=f"{tag}_ha{hc}")
            pg.release(biu, f"{tag}_ha{hc}")
        em.marks[f"{tag}_gu{NHC - 1}"] = em.marks[f"{tag}_umm{NHC - 1}"]
        xs = sc.xs[s % len(sc.xs)]
        ln = lno
        for t in range(TPS):
            for eh in range(2):
                bank = pd[eh]
                for hc in range(NHC):
                    em.op("pe", lambda e, t=t, eh=eh, hc=hc, bank=bank:
                          e.matmul(
                              bank[:],
                              hact[:, hc * 512 + t * 128:hc * 512 + t * 128 + 128],
                              wd[:, hc * D + eh * 512:hc * D + eh * 512 + 512],
                              start=(hc == 0), stop=(hc == NHC - 1)),
                          waits=([f"{tag}_ha{NHC - 1}", P + "_w2"] +
                                 ([f"{tag}t{t - 1}_ap{eh}"] if t >= 1 else
                                  ([f"{P}s{s - 1}t{TPS - 1}_ap{eh}"]
                                   if s >= 1 else []))) if hc == 0 else (),
                          mark=(f"{tag}_dmm{t}_{eh}" if hc == NHC - 1
                                else None))
            for eh in range(2):
                em.op("dve", lambda e, t=t, eh=eh: e.bn_stats(
                    ln.st[:, t * 12 + eh * 6:t * 12 + eh * 6 + 6], pd[eh][:]),
                    waits=[f"{tag}_dmm{t}_{eh}"] +
                          ([f"{P}s{s - 1}_lnfree"] if (s >= 1 and t == 0 and
                                                       eh == 0) else []))
            em.op("dve", lambda e, t=t: e.bn_aggr(
                ln.mv[:, t * 2:t * 2 + 2], ln.st[:, t * 12:t * 12 + 12]))
            em.op("dve", lambda e, t=t: e.tensor_scalar(
                ln.vs[:, t:t + 1], ln.mv[:, 2 * t + 1:2 * t + 2],
                EPS, None, ALU.add))
            newton_rsqrt(em, ln.vs[:, t:t + 1], ln.rs[:, t:t + 1],
                         ln.tmp[:, t:t + 1], ln.ci[:, t:t + 1])
            em.op("dve", lambda e, t=t: e.scalar_tensor_tensor(
                ln.nb[:, t:t + 1], ln.mv[:, 2 * t:2 * t + 1],
                -1.0, ln.rs[:, t:t + 1], ALU.mult, ALU.mult),
                mark=f"{tag}t{t}_lnp")
            for eh in range(2):
                em.op("dve", lambda e, t=t, eh=eh: e.tensor_scalar(
                    xs[:, t * D + eh * 512:t * D + eh * 512 + 512],
                    xs[:, t * D + eh * 512:t * D + eh * 512 + 512],
                    ln.nb[:, t:t + 1], None, ALU.add),
                    waits=[f"{tag}t{t}_lnp"])
                em.op("dve", lambda e, t=t, eh=eh: e.scalar_tensor_tensor(
                    xs[:, t * D + eh * 512:t * D + eh * 512 + 512],
                    pd[eh][:], ln.rs[:, t:t + 1],
                    xs[:, t * D + eh * 512:t * D + eh * 512 + 512],
                    ALU.mult, ALU.add),
                    mark=f"{tag}t{t}_ap{eh}")
        em.marks[f"{tag}_lnfree"] = em.marks[f"{tag}t{TPS - 1}_lnp"]
        if x_out.shape[0] == 128:
            em.dma("pool", lambda e, xs=xs, s=s: e.dma_start(
                x_out[:, s * TPS * D:(s + 1) * TPS * D], xs[:]),
                inc=f"st{s % 2}", mark=f"{tag}_store",
                waits=[f"{tag}t{TPS - 1}_ap0", f"{tag}t{TPS - 1}_ap1"])
        else:
            em.dma("pool", lambda e, xs=xs, s=s: e.dma_start(
                x_out[s * STRIPE:(s + 1) * STRIPE, :]
                .rearrange("(t p) d -> p t d", p=128),
                xs.rearrange("p (t d) -> p t d", t=TPS)),
                inc=f"st{s % 2}", mark=f"{tag}_store",
                waits=[f"{tag}t{TPS - 1}_ap0", f"{tag}t{TPS - 1}_ap1"])



def _host_prep(inputs):
    f32 = np.float32
    bf16 = ml_dtypes.bfloat16
    shared = {}

    def T(a):
        return np.ascontiguousarray(np.asarray(a).T)

    shared["l0a_wqT"] = T(inputs["L0_attn_wq_w"]).astype(f32, copy=False)
    shared["l0a_wkT"] = T(inputs["L0_attn_wk_w"]).astype(f32, copy=False)
    shared["l0a_wvT"] = T(inputs["L0_attn_wv_w"]).astype(f32, copy=False)
    shared["l0a_woT"] = T(inputs["L0_attn_wo_w"]).astype(f32, copy=False)
    for nm in ("q", "k", "v", "o"):
        shared[f"l0a_b{nm}"] = np.ascontiguousarray(
            np.asarray(inputs[f"L0_attn_w{nm}_b"]).astype(f32, copy=False)
            .reshape(NCH, 128).T)
    shared["l0a_bvb"] = np.ascontiguousarray(np.tile(
        np.asarray(inputs["L0_attn_wv_b"]).astype(f32, copy=False)[None, :],
        (128, 1)))
    shared["l0a_bob"] = np.ascontiguousarray(np.tile(
        np.asarray(inputs["L0_attn_wo_b"]).astype(f32, copy=False)[None, :],
        (128, 1)))
    shared["l0f_wrT"] = T(inputs["L0_ffn_W_r"]).astype(f32, copy=False)
    shared["l0f_wkT"] = T(inputs["L0_ffn_W_k"]).astype(f32, copy=False)
    shared["l0f_wvT"] = T(inputs["L0_ffn_W_v"]).astype(f32, copy=False)
    shared["l0f_mur"] = np.ascontiguousarray(
        np.asarray(inputs["L0_ffn_mu_r"]).astype(f32, copy=False)
        .reshape(NCH, 128).T)
    shared["l0f_muk"] = np.ascontiguousarray(
        np.asarray(inputs["L0_ffn_mu_k"]).astype(f32, copy=False)
        .reshape(NCH, 128).T)
    perm = np.zeros(3 * D, dtype=np.int64)
    for blk in range(2):
        for h in range(H):
            base = blk * D + 64 * h
            for j in range(DH // 2):
                perm[base + j] = base + 2 * j
                perm[base + 32 + j] = base + 2 * j + 1
    perm[2 * D:] = np.arange(2 * D, 3 * D)
    wqkv = np.asarray(inputs["L1_attn_Wqkv"]).astype(f32, copy=False)[perm]
    shared["l1a_wqkvT"] = np.ascontiguousarray(wqkv.T)
    shared["l1a_woutT"] = T(inputs["L1_attn_Wout"]).astype(f32, copy=False)
    shared["l1f_wgT"] = T(np.asarray(inputs["L1_ffn_w_gate"]))
    shared["l1f_wuT"] = T(np.asarray(inputs["L1_ffn_w_up"]))
    shared["l1f_wdT"] = T(np.asarray(inputs["L1_ffn_w_down"]))
    shared["ident"] = np.eye(128, dtype=bf16)
    ck = np.arange(128)[:, None]
    cq = np.arange(128)[None, :]
    shared["l0mask4"] = np.ascontiguousarray(
        np.tile((ck <= cq).astype(bf16), (1, 4)))
    inv_freq = 1.0 / 10000.0 ** (np.arange(0, DH, 2, dtype=np.float64) / DH)
    t = np.arange(SEQLEN, dtype=np.float64)
    ang = t[None, :] * inv_freq[:, None]
    shared["l1cos"] = np.ascontiguousarray(np.tile(np.cos(ang), (4, 1))
                                           .astype(bf16))
    sin = np.sin(ang)
    ss = np.concatenate([sin, -sin, sin, -sin], axis=0)
    shared["l1sin"] = np.ascontiguousarray(ss.astype(bf16))
    ck2 = np.arange(128)[:, None]
    cq2 = np.arange(BS)[None, :]
    m0 = (ck2 <= cq2).astype(bf16)
    m1 = (ck2 + 128 <= cq2).astype(bf16)
    shared["l1mask"] = np.ascontiguousarray(
        np.concatenate([m0, m1, m0, m1], axis=1))
    return shared


W_SHAPES = {
    "l0a_wqT": ([D, D], F32), "l0a_wkT": ([D, D], F32),
    "l0a_wvT": ([D, D], F32), "l0a_woT": ([D, D], F32),
    "l0a_bq": ([128, NCH], F32), "l0a_bk": ([128, NCH], F32),
    "l0a_bv": ([128, NCH], F32), "l0a_bo": ([128, NCH], F32),
    "l0f_wrT": ([D, D], F32), "l0f_wkT": ([D, D], F32),
    "l0f_wvT": ([D, D], F32),
    "l0f_mur": ([128, NCH], F32), "l0f_muk": ([128, NCH], F32),
    "l1a_wqkvT": ([D, 3 * D], F32), "l1a_woutT": ([D, D], F32),
    "l1f_wgT": ([D, HID], I32), "l1f_wuT": ([D, HID], I32),
    "l1f_wdT": ([HID, D], I32),
    "ident": ([128, 128], BF16), "l0mask4": ([128, 512], BF16),
    "l0a_bvb": ([128, D], F32), "l0a_bob": ([128, D], F32),
    "l1cos": ([128, SEQLEN], BF16), "l1sin": ([128, SEQLEN], BF16),
    "l1mask": ([128, 4 * BS], BF16),
}

PASS_BUILDERS = {"l0ffn": build_pass_l0ffn,
                 "l0attn": build_pass_l0attn,
                 "l1ffn": build_pass_l1ffn,
                 "l1attn": build_pass_l1attn}
PASS_WEIGHTS = {
    "l0attn": ["l0a_wqT", "l0a_wkT", "l0a_wvT", "l0a_woT",
               "l0a_bq", "l0a_bk", "l0a_bvb", "l0a_bob", "l0mask4", "ident"],
    "l0ffn": ["l0f_wrT", "l0f_wkT", "l0f_wvT", "l0f_mur", "l0f_muk", "ident"],
    "l1attn": ["l1a_wqkvT", "l1a_woutT", "l1cos", "l1sin",
               "l1mask", "ident"],
    "l1ffn": ["l1f_wgT", "l1f_wuT", "l1f_wdT", "ident"],
}


def build_program(passes):
    nc = bass.Bass("TRN2", target_bir_lowering=False)
    stack = ExitStack()
    em = Em(nc, stack)

    names = sorted({w for p in passes for w in PASS_WEIGHTS[p]})
    cd = {}
    for name in names:
        shp, dt = W_SHAPES[name]
        cd[name] = nc.dram_tensor(name, shp, dt, kind="ExternalInput").ap()

    def sb(name, shape, dt):
        return stack.enter_context(nc.sbuf_tensor(name, shape, dt))[:]

    def ps(name, shape, dt):
        return stack.enter_context(nc.psum_tensor(name, shape, dt))[:]

    ident_s = sb("c_ident", [128, 128], BF16)
    em.dma("sp", lambda e: e.dma_start(ident_s, cd["ident"]), inc="cst",
           mark="c_ident")
    cd["ident_s"] = ident_s

    x_first = nc.dram_tensor("x_in", [TOK, D], F32, kind="ExternalInput").ap()
    y_last = nc.dram_tensor("y", [TOK, D], F32, kind="ExternalOutput").ap()
    mids = [nc.dram_tensor(f"xmid{k}", [128, TOK * D // 128], F32,
                           kind="Internal").ap()
            for k in range(len(passes) - 1)]
    for k, p in enumerate(passes):
        x_in = x_first if k == 0 else mids[k - 1]
        x_out = y_last if k == len(passes) - 1 else mids[k]
        with ExitStack() as pass_stack:
            saved = stack
            inner_sb = (lambda st: lambda n, shp, dt:
                        st.enter_context(nc.sbuf_tensor(n, shp, dt))[:]
                        )(pass_stack)
            inner_ps = (lambda st: lambda n, shp, dt:
                        st.enter_context(nc.psum_tensor(n, shp, dt))[:]
                        )(pass_stack)
            PASS_BUILDERS[p](em, inner_sb, inner_ps, cd, x_in, x_out)
            em.barrier()
    with nc.Block() as blk:
        em.finalize(blk)
    stack.close()
    return nc, names


def kernel(**inputs):
    shared = _host_prep(inputs)
    x = np.asarray(inputs["L0_attn_x"]).astype(np.float32, copy=False)
    x = np.ascontiguousarray(x.reshape(NCORES, TOK, D))

    passes = ["l0attn", "l0ffn", "l1attn", "l1ffn"]
    nc, names = build_program(passes)
    in_maps = []
    for c in range(NCORES):
        m = {n: shared[n] for n in names}
        m["x_in"] = x[c]
        in_maps.append(m)
    res = run_bass_kernel_spmd(nc, in_maps, core_ids=list(range(NCORES)))
    y = np.stack([np.asarray(r["y"]) for r in res.results])
    return np.ascontiguousarray(y.reshape(16, SEQLEN, D))


# version 09
# speedup vs baseline: 1.0173x; 1.0008x over previous
import numpy as np
import ml_dtypes
from contextlib import ExitStack
import concourse.bass as bass
import concourse.mybir as mybir
from concourse.bass_utils import run_bass_kernel_spmd

F32 = mybir.dt.float32
BF16 = mybir.dt.bfloat16
I32 = mybir.dt.int32
ALU = mybir.AluOpType
AF = mybir.ActivationFunctionType

NCORES = 8
D = 1024
SEQLEN = 2048
TOK = 4096
STRIPE = 512
NSTRIPE = TOK // STRIPE
TPS = 4
NCH = D // 128
H = 16
DH = 64
BS = 256
HID = 3072
EPS = 1e-5

P1STAGE = 99

ENGINES = ("pe", "act", "dve", "pool", "sp")
ENG_ATTR = {"pe": "tensor", "act": "scalar", "dve": "vector", "pool": "gpsimd",
            "sp": "sync"}


class Em:

    def __init__(self, nc, stack):
        self.nc = nc
        self.stack = stack
        self.sems = {}
        self.cnt = {}
        self.ops = {e: [] for e in ENGINES}
        self.marks = {}

    def sem(self, name):
        if name not in self.sems:
            self.sems[name] = self.stack.enter_context(self.nc.semaphore(name))
            self.cnt[name] = 0
        return self.sems[name]

    def op(self, eng, fn, waits=(), inc=None, inc_amt=1, mark=None):
        wl = []
        for w in waits:
            if w is None:
                continue
            if isinstance(w, str):
                if w not in self.marks:
                    raise KeyError(f"unknown mark {w!r}")
                wl.append(self.marks[w])
            else:
                self.sem(w[0])
                wl.append(w)
        if mark is not None and inc is None:
            inc = "s_" + eng
        rec = {"fn": fn, "waits": wl, "inc": None}
        if inc is not None:
            self.sem(inc)
            self.cnt[inc] += inc_amt
            rec["inc"] = (inc, inc_amt)
            if mark is not None:
                self.marks[mark] = (inc, self.cnt[inc])
        self.ops[eng].append(rec)

    def dma(self, eng, fn, inc, mark=None, waits=()):
        self.op(eng, fn, waits=waits, inc=inc, inc_amt=16, mark=mark)

    def barrier(self):
        for eng in ENGINES:
            if self.ops[eng]:
                rec = self.ops[eng][-1]
                if rec["inc"] is None:
                    s = "s_" + eng
                    self.sem(s)
                    self.cnt[s] += 1
                    rec["inc"] = (s, 1)
        targets = [(s, self.cnt[s]) for s in list(self.cnt) if self.cnt[s] > 0]
        for eng in ENGINES:
            if not self.ops[eng]:
                continue
            for (s, v) in targets:
                self.ops[eng].append({"fn": None, "waits": [(s, v)],
                                      "inc": None})

    def finalize(self, blk):
        for eng in ENGINES:
            if not self.ops[eng]:
                continue

            def mkbody(ops, sems, sep):
                def run(e):
                    waited = {}
                    prev_compute = False
                    for rec in ops:
                        for (s, v) in rec["waits"]:
                            if waited.get(s, -1) < v:
                                e.wait_ge(sems[s], v)
                                waited[s] = v
                        if rec["fn"] is not None:
                            if sep and prev_compute:
                                e.drain()
                            ins = rec["fn"](e)
                            prev_compute = True
                            if rec["inc"] is not None:
                                s, amt = rec["inc"]
                                ins.then_inc(sems[s], amt)
                return run

            getattr(blk, ENG_ATTR[eng])(mkbody(self.ops[eng], self.sems,
                                               eng in ('dve', 'act', 'pool')))


class BankPool:

    def __init__(self, banks):
        self.banks = banks
        self.uses = 0
        self.free_mark = [None] * len(banks)

    def acquire(self):
        i = self.uses % len(self.banks)
        self.uses += 1
        return self.banks[i], self.free_mark[i], i

    def release(self, i, mark):
        self.free_mark[i] = mark


def newton_rsqrt(em, v_ap, y_ap, tmp_ap, ci_ap, waits=(), mark=None):
    MAGIC = 0x5f3759df
    em.op("dve", lambda e: e.tensor_scalar(
        y_ap.bitcast(I32), v_ap.bitcast(I32), 1, None, ALU.arith_shift_right),
        waits=waits)
    em.op("dve", lambda e: e.memset(ci_ap, MAGIC))
    em.op("dve", lambda e: e.tensor_tensor(
        y_ap.bitcast(I32), ci_ap, y_ap.bitcast(I32), ALU.subtract))
    for it in range(3):
        em.op("dve", lambda e: e.tensor_tensor(tmp_ap, y_ap, y_ap, ALU.mult))
        em.op("dve", lambda e: e.tensor_tensor(tmp_ap, tmp_ap, v_ap, ALU.mult))
        em.op("dve", lambda e: e.tensor_scalar(
            tmp_ap, tmp_ap, -0.5, 1.5, ALU.mult, ALU.add))
        em.op("dve", lambda e: e.tensor_tensor(y_ap, y_ap, tmp_ap, ALU.mult),
              mark=(mark if it == 2 else None))


class LNBufs:
    def __init__(self, sb, tag):
        self.st = sb(f"{tag}_st", [128, TPS * 12], F32)
        self.mv = sb(f"{tag}_mv", [128, TPS * 2], F32)
        self.vs = sb(f"{tag}_vs", [128, TPS], F32)
        self.rs = sb(f"{tag}_rs", [128, TPS], F32)
        self.nb = sb(f"{tag}_nb", [128, TPS], F32)
        self.tmp = sb(f"{tag}_tmp", [128, TPS], F32)
        self.ci = sb(f"{tag}_ci", [128, TPS], I32)


def ln_stats(em, ln, src_tiles, tag, waits_first=()):
    for t in range(TPS):
        for hh in range(2):
            em.op("dve", lambda e, t=t, hh=hh: e.bn_stats(
                ln.st[:, t * 12 + hh * 6:t * 12 + hh * 6 + 6],
                src_tiles[t][:, hh * 512:hh * 512 + 512]),
                waits=(waits_first if (t == 0 and hh == 0) else ()))
    for t in range(TPS):
        em.op("dve", lambda e, t=t: e.bn_aggr(
            ln.mv[:, t * 2:t * 2 + 2], ln.st[:, t * 12:t * 12 + 12]))
    for t in range(TPS):
        em.op("dve", lambda e, t=t: e.tensor_scalar(
            ln.vs[:, t:t + 1], ln.mv[:, 2 * t + 1:2 * t + 2],
            EPS, None, ALU.add))
    newton_rsqrt(em, ln.vs[:, 0:TPS], ln.rs[:, 0:TPS], ln.tmp[:, 0:TPS], ln.ci[:, 0:TPS])
    for t in range(TPS):
        em.op("dve", lambda e, t=t: e.scalar_tensor_tensor(
            ln.nb[:, t:t + 1], ln.mv[:, 2 * t:2 * t + 1],
            -1.0, ln.rs[:, t:t + 1], ALU.mult, ALU.mult),
            mark=(f"{tag}_lnp" if t == TPS - 1 else None))


class StripeCtx:

    def __init__(self, em, sb, ps, tag, carry=False, nxs=2, nxnt=2,
                 ptr_aps=None):
        self.em = em
        self.tag = tag
        self.carry = carry
        self.nxs = nxs
        self.nxnt = nxnt
        self.ptr_aps = ptr_aps
        self.w = 520 if carry else 512
        self.off = 1 if carry else 0
        self.xs = [sb(f"{tag}_x{i}", [128, TPS * D], F32)
                   for i in range(getattr(self, "nxs", 2))]
        self.xnT = [sb(f"{tag}_xnT{i}", [128, NCH * self.w], BF16)
                    for i in range(getattr(self, "nxnt", 2))]
        self.xn = sb(f"{tag}_xn", [128, TPS * D], BF16)
        self.ln = [LNBufs(sb, f"{tag}_li{i}") for i in range(2)]
        if getattr(self, "ptr_aps", None) is not None:
            self.ptr = self.ptr_aps
        else:
            self.ptr = [ps(f"{tag}_ptr{i}", [128, NCH * 128], BF16)
                        for i in range(2)]

    def chunk(self, bi, c):
        o = c * self.w + self.off
        return self.xnT[bi][:, o:o + 512]

    def chunk_shift(self, bi, c):
        o = c * self.w + self.off
        return self.xnT[bi][:, o - 1:o + 511]

    def chunk3d(self, bi):
        return self.xnT[bi][:, 0:NCH * self.w].rearrange(
            "p (c w) -> p c w", c=NCH)


def stripe_prologue(em, sc, s, x_dram, ident, xnt_war=None, x_extra_wait=None,
                    ptr_war=None):
    bi = s % len(sc.xnT)
    xs = sc.xs[s % len(sc.xs)]
    T0 = sc.tag
    tag = f"{T0}s{s}"
    w = []
    sx = s - len(sc.xs)
    if sx >= 0 and f"{T0}s{sx}_store" in em.marks:
        w.append(f"{T0}s{sx}_store")
    if x_extra_wait:
        w.append(x_extra_wait)
    if x_dram.shape[0] == 128:
        em.dma("sp", lambda e: e.dma_start(
            xs[:], x_dram[:, s * TPS * D:(s + 1) * TPS * D]),
            inc=f"ld{s % 2}", mark=f"{tag}_xld", waits=w)
    else:
        em.dma("sp", lambda e: e.dma_start(
            xs.rearrange("p (t d) -> p t d", t=TPS),
            x_dram[s * STRIPE:(s + 1) * STRIPE, :]
            .rearrange("(t p) d -> p t d", p=128)),
            inc=f"ld{s % 2}", mark=f"{tag}_xld", waits=w)
    tiles = [xs[:, t * D:(t + 1) * D] for t in range(TPS)]
    ln_stats(em, sc.ln[bi], tiles, tag, waits_first=[f"{tag}_xld"])
    for t in range(TPS):
        ww = [f"{tag}_lnp"] if t == 0 else []
        if t == 0 and s >= 1:
            ww.append(f"{T0}s{s - 1}_tr{TPS - 1}")
        em.op("act", lambda e, t=t, bi=bi: e.activation(
            sc.xn[:, t * D:(t + 1) * D], xs[:, t * D:(t + 1) * D],
            AF.Identity,
            bias=sc.ln[bi].nb[:, t:t + 1], scale=sc.ln[bi].rs[:, t:t + 1]),
            waits=ww, mark=f"{tag}_xn{t}")
    em.marks[f"{tag}_xn"] = em.marks[f"{tag}_xn{TPS - 1}"]
    for t in range(TPS):
        pb = sc.ptr[t % 2]
        bank_war = None
        if t >= 2:
            bank_war = f"{tag}_trev{t - 2}"
        elif s >= 1:
            bank_war = f"{T0}s{s - 1}_trev{t + 2}"
        bw = [bank_war] if bank_war else []
        if t < 2 and ptr_war:
            bw += [m for m in ptr_war if m in em.marks]
        for c in range(NCH):
            em.op("pe", lambda e, t=t, c=c, pb=pb: e.transpose(
                pb[:, c * 128:(c + 1) * 128],
                sc.xn[:, t * D + c * 128:t * D + c * 128 + 128], ident),
                waits=([f"{tag}_xn{t}", "c_ident"] + bw)
                if c == 0 else (),
                mark=(f"{tag}_tr{t}" if c == NCH - 1 else None))
        ev_w = [f"{tag}_tr{t}"]
        if t == 0 and xnt_war:
            ev_w += [m for m in xnt_war if m in em.marks]
        em.op("act", lambda e, t=t, pb=pb, bi=bi: e.activation(
            sc.chunk3d(bi)[:, :, sc.off + t * 128:sc.off + t * 128 + 128],
            pb[:, 0:NCH * 128].rearrange("p (c w) -> p c w", c=NCH), AF.Copy),
            waits=ev_w, mark=f"{tag}_trev{t}")
    em.marks[f"{tag}_xnT"] = em.marks[f"{tag}_trev{TPS - 1}"]
    return bi


def ln_post_add(em, lno, s, outb, xs, x_out, tag, waits_first):
    tiles = [outb[:, t * D:(t + 1) * D] for t in range(TPS)]
    ln = lno[s % 2]
    ln_stats(em, ln, tiles, tag + "o", waits_first=waits_first)
    for t in range(TPS):
        em.op("act", lambda e, t=t: e.activation(
            outb[:, t * D:(t + 1) * D], outb[:, t * D:(t + 1) * D],
            AF.Identity,
            bias=ln.nb[:, t:t + 1], scale=ln.rs[:, t:t + 1]),
            waits=[f"{tag}o_lnp"] if t == 0 else (),
            mark=(f"{tag}_lnapp" if t == TPS - 1 else None))
    em.op("dve", lambda e: e.tensor_tensor(xs[:], outb[:], xs[:], ALU.add),
          waits=[f"{tag}_lnapp"], mark=f"{tag}_add")
    if x_out.shape[0] == 128:
        em.dma("pool", lambda e: e.dma_start(
            x_out[:, s * TPS * D:(s + 1) * TPS * D], xs[:]),
            inc=f"st{s % 2}", mark=f"{tag}_store", waits=[f"{tag}_add"])
    else:
        em.dma("pool", lambda e: e.dma_start(
            x_out[s * STRIPE:(s + 1) * STRIPE, :]
            .rearrange("(t p) d -> p t d", p=128),
            xs.rearrange("p (t d) -> p t d", t=TPS)),
            inc=f"st{s % 2}", mark=f"{tag}_store", waits=[f"{tag}_add"])


def load_weights(em, wstage, dsts, srcs, tag, scale=None):
    n = 0
    sw = wstage[0].shape[1]
    nst = len(wstage)
    for wi, (wsb, wd) in enumerate(zip(dsts, srcs)):
        ncw = wd.shape[0] // 128
        cols = wd.shape[1]
        nparts = (cols + sw - 1) // sw
        for c in range(ncw):
            for pp in range(nparts):
                w0 = pp * sw
                w1 = min(cols, w0 + sw)
                st = wstage[n % nst]
                prev = f"{tag}_cast{n - nst}" if n >= nst else None
                em.dma("sp",
                       lambda e, st=st, wd=wd, c=c, w0=w0, w1=w1:
                       e.dma_start(st[:, 0:w1 - w0],
                                   wd[c * 128:(c + 1) * 128, w0:w1]),
                       inc=f"{tag}_wp{n % nst}",
                       waits=[prev] if prev else (),
                       mark=f"{tag}_wl{n}")
                kw = {} if scale is None else {"scale": scale}
                em.op("act", lambda e, wsb=wsb, st=st, c=c, cols=cols,
                      w0=w0, w1=w1, kw=kw:
                      e.activation(wsb[:, c * cols + w0:c * cols + w1],
                                   st[:, 0:w1 - w0], AF.Copy, **kw),
                      waits=[f"{tag}_wl{n}"], mark=f"{tag}_cast{n}")
                n += 1
            em.marks[f"{tag}_w{wi}c{c}"] = em.marks[f"{tag}_cast{n - 1}"]
        em.marks[f"{tag}_w{wi}"] = em.marks[f"{tag}_cast{n - 1}"]



def build_pass_l0ffn(em, sb, ps, cd, x_in, x_out):
    ident = cd["ident_s"]
    P = "p2"
    wr = sb(P + "_wr", [128, NCH * D], BF16)
    wk = sb(P + "_wk", [128, NCH * D], BF16)
    wv = sb(P + "_wv", [128, NCH * D], BF16)
    wstage = [sb(P + f"_wst{i}", [128, D], F32) for i in range(2)]
    mur = sb(P + "_mur", [128, NCH], F32)
    muk = sb(P + "_muk", [128, NCH], F32)
    sc = StripeCtx(em, sb, ps, P, carry=True)
    xrT = sb(P + "_xr", [128, NCH * 512], BF16)
    xkT = sb(P + "_xk", [128, NCH * 512], BF16)
    dT = sb(P + "_d", [128, NCH * 512], BF16)
    rsig = sb(P + "_rsig", [128, TPS * D], F32)
    kact = sb(P + "_kact", [128, NCH * 512], BF16)
    krelu = [sb(P + f"_krelu{i}", [128, 512], F32) for i in range(2)]
    outb = sb(P + "_out", [128, TPS * D], F32)
    lno = [LNBufs(sb, P + f"_lo{i}") for i in range(2)]
    pj = BankPool([ps(P + f"_pj{i}", [128, 512], F32) for i in range(4)])

    load_weights(em, wstage, [wr, wk, wv],
                 [cd["l0f_wrT"], cd["l0f_wkT"], cd["l0f_wvT"]], P)
    em.dma("sp", lambda e: e.dma_start(mur[:], cd["l0f_mur"]),
           inc=P + "_mi", mark=P + "_mul")
    em.dma("sp", lambda e: e.dma_start(muk[:], cd["l0f_muk"]),
           inc=P + "_mi", waits=[P + "_mul"], mark=P + "_mu")

    for s in range(NSTRIPE):
        tag = f"{P}s{s}"
        war = []
        if s >= 2:
            war.append(f"{P}s{s - 2}_mix")
        if s >= 1:
            war.append(f"{P}s{s - 1}_carry")
        bi = stripe_prologue(em, sc, s, x_in, ident, xnt_war=war)
        if s % (NSTRIPE // 2) == 0:
            em.op("dve", lambda e, bi=bi: e.memset(
                sc.chunk3d(bi)[:, :, 0:1], 0.0),
                waits=[f"{tag}_xnT"], mark=f"{tag}_carry")
        else:
            pb = 1 - bi
            em.op("dve", lambda e, bi=bi, pb=pb: e.tensor_copy(
                sc.chunk3d(bi)[:, :, 0:1], sc.chunk3d(pb)[:, :, 512:513]),
                waits=[f"{tag}_xnT"], mark=f"{tag}_carry")
        mix_war = []
        if s >= 1:
            mix_war = [f"{P}s{s - 1}_rmm{TPS - 1}_1", f"{P}s{s - 1}_kmm{NCH - 1}"]
        for c in range(NCH):
            em.op("dve", lambda e, c=c, bi=bi: e.tensor_tensor(
                dT[:, c * 512:(c + 1) * 512], sc.chunk(bi, c),
                sc.chunk_shift(bi, c)[:, 0:512], ALU.subtract),
                waits=([f"{tag}_carry", P + "_mu"] + mix_war) if c == 0 else ())
            em.op("dve", lambda e, c=c, bi=bi: e.scalar_tensor_tensor(
                xrT[:, c * 512:(c + 1) * 512], dT[:, c * 512:(c + 1) * 512],
                mur[:, c:c + 1], sc.chunk_shift(bi, c)[:, 0:512],
                ALU.mult, ALU.add))
            em.op("dve", lambda e, c=c, bi=bi: e.scalar_tensor_tensor(
                xkT[:, c * 512:(c + 1) * 512], dT[:, c * 512:(c + 1) * 512],
                muk[:, c:c + 1], sc.chunk_shift(bi, c)[:, 0:512],
                ALU.mult, ALU.add),
                mark=(f"{tag}_mix" if c == NCH - 1 else None))
        for t in range(TPS):
            for eh in range(2):
                bank, fw, bidx = pj.acquire()
                for c in range(NCH):
                    em.op("pe", lambda e, t=t, eh=eh, c=c, bank=bank: e.matmul(
                        bank, xrT[:, c * 512 + t * 128:c * 512 + t * 128 + 128],
                        wr[:, c * D + eh * 512:c * D + eh * 512 + 512],
                        start=(c == 0), stop=(c == NCH - 1)),
                        waits=([f"{tag}_mix", P + "_w0"] +
                               ([fw] if fw else [])) if c == 0 else (),
                        mark=(f"{tag}_rmm{t}_{eh}" if c == NCH - 1 else None))
                rs_war = ([f"{P}s{s - 1}_kv{t}_{eh}"] if s >= 1 else [])
                em.op("act", lambda e, t=t, eh=eh, bank=bank: e.activation(
                    rsig[:, t * D + eh * 512:t * D + eh * 512 + 512], bank,
                    AF.Sigmoid),
                    waits=[f"{tag}_rmm{t}_{eh}"] + rs_war,
                    mark=f"{tag}_rs{t}_{eh}")
                pj.release(bidx, f"{tag}_rs{t}_{eh}")
        for co in range(NCH):
            bank, fw, bidx = pj.acquire()
            for c in range(NCH):
                em.op("pe", lambda e, co=co, c=c, bank=bank: e.matmul(
                    bank, wk[:, c * D + co * 128:c * D + co * 128 + 128],
                    xkT[:, c * 512:(c + 1) * 512],
                    start=(c == 0), stop=(c == NCH - 1)),
                    waits=([P + "_w1"] + ([fw] if fw else [])) if c == 0 else (),
                    mark=(f"{tag}_kmm{co}" if c == NCH - 1 else None))
            kr = krelu[co % 2]
            kr_war = [f"{tag}_ksq{co - 2}"] if co >= 2 else (
                [f"{P}s{s - 1}_ksq{NCH - 2 + (co % 2) * 0 + co}"]
                if False else [])
            em.op("dve", lambda e, bank=bank, kr=kr: e.tensor_scalar(
                kr[:], bank, 0.0, None, ALU.max),
                waits=[f"{tag}_kmm{co}"] + kr_war, mark=f"{tag}_krel{co}")
            pj.release(bidx, f"{tag}_krel{co}")
            ks_war = ([f"{P}s{s - 1}_vmm{TPS - 1}_1"]
                      if (s >= 1 and co == 0) else [])
            em.op("act", lambda e, co=co, kr=kr: e.activation(
                kact[:, co * 512:(co + 1) * 512], kr[:], AF.Square),
                waits=[f"{tag}_krel{co}"] + ks_war, mark=f"{tag}_ksq{co}")
        ob = outb
        for t in range(TPS):
            for eh in range(2):
                bank, fw, bidx = pj.acquire()
                for c in range(NCH):
                    em.op("pe", lambda e, t=t, eh=eh, c=c, bank=bank: e.matmul(
                        bank, kact[:, c * 512 + t * 128:c * 512 + t * 128 + 128],
                        wv[:, c * D + eh * 512:c * D + eh * 512 + 512],
                        start=(c == 0), stop=(c == NCH - 1)),
                        waits=[f"{tag}_ksq{c}", P + "_w2"] +
                              ([fw] if (c == 0 and fw) else []),
                        mark=(f"{tag}_vmm{t}_{eh}" if c == NCH - 1 else None))
                em.op("dve", lambda e, t=t, eh=eh, bank=bank, ob=ob:
                      e.tensor_tensor(
                          ob[:, t * D + eh * 512:t * D + eh * 512 + 512],
                          bank,
                          rsig[:, t * D + eh * 512:t * D + eh * 512 + 512],
                          ALU.mult),
                      waits=[f"{tag}_vmm{t}_{eh}", f"{tag}_rs{t}_{eh}"],
                      mark=f"{tag}_kv{t}_{eh}")
                pj.release(bidx, f"{tag}_kv{t}_{eh}")
        ln_post_add(em, lno, s, outb, sc.xs[s % len(sc.xs)], x_out, tag,
                    waits_first=[f"{tag}_kv{TPS - 1}_1"])


def _p1_passthrough(em, sc, bi, s, tag, x_out, wait_mark):
    em.dma("pool", lambda e: e.dma_start(
        x_out[s * STRIPE:(s + 1) * STRIPE, :]
        .rearrange("(t p) d -> p t d", p=128),
        sc.xs[bi].rearrange("p (t d) -> p t d", t=TPS)),
        inc=f"st{s % 2}", mark=f"{tag}_store", waits=[wait_mark])



def build_pass_l0attn(em, sb, ps, cd, x_in, x_out):
    ident = cd["ident_s"]
    P = "p1"
    wq = sb(P + "_wq", [128, NCH * D], BF16)
    wk = sb(P + "_wk", [128, NCH * D], BF16)
    wv = sb(P + "_wv", [128, NCH * D], BF16)
    wo = sb(P + "_wo", [128, NCH * D], BF16)
    wstage = [sb(P + f"_wst{i}", [128, D], F32) for i in range(2)]
    bq = sb(P + "_bq", [128, NCH], F32)
    bk = sb(P + "_bk", [128, NCH], F32)
    bvb = sb(P + "_bvb", [128, D], BF16)
    bob = sb(P + "_bob", [128, D], BF16)
    bstage = wstage[1]
    mask4 = sb(P + "_mask4", [128, 512], BF16)
    sc = StripeCtx(em, sb, ps, P, carry=False)
    qf = sb(P + "_qf", [128, NCH * 512], BF16)
    kze = sb(P + "_kze", [128, NCH * 512], BF16)
    kzo = sb(P + "_kzo", [128, NCH * 512], BF16)
    vst = sb(P + "_vst", [128, TPS * D], BF16)
    ktm = sb(P + "_ktm", [128, TPS * D], BF16)
    sT = sb(P + "_sT", [128, 2 * 512], BF16)
    stf = sb(P + "_stf", [128, H * DH], F32)
    stb = sb(P + "_stb", [128, H * DH], BF16)
    ofm = sb(P + "_ofm", [128, NCH * 512], BF16)
    outb = sb(P + "_out", [128, TPS * D], F32)
    lno = [LNBufs(sb, P + f"_lo{i}") for i in range(2)]
    pj = BankPool([ps(P + f"_pj{i}", [128, 512], F32) for i in range(4)])
    po = [ps(P + f"_po{i}", [128, 512], F32) for i in range(2)]

    load_weights(em, wstage, [wq, wk, wv, wo],
                 [cd["l0a_wqT"], cd["l0a_wkT"], cd["l0a_wvT"], cd["l0a_woT"]],
                 P)
    em.dma("sp", lambda e: e.dma_start(bq[:], cd["l0a_bq"]), inc=P + "_mi",
           mark=P + "_bql")
    em.dma("sp", lambda e: e.dma_start(bk[:], cd["l0a_bk"]), inc=P + "_mi",
           waits=[P + "_bql"], mark=P + "_bqk")
    em.dma("sp", lambda e: e.dma_start(bstage[:], cd["l0a_bvb"]),
           inc=P + "_mi", waits=[P + "_bqk", P + "_w3"], mark=P + "_bvl")
    em.op("act", lambda e: e.activation(bvb[:], bstage[:], AF.Copy),
          waits=[P + "_bvl"], mark=P + "_bv")
    em.dma("sp", lambda e: e.dma_start(bstage[:], cd["l0a_bob"]),
           inc=P + "_mi", waits=[P + "_bv"], mark=P + "_bol")
    em.op("act", lambda e: e.activation(bob[:], bstage[:], AF.Copy),
          waits=[P + "_bol"], mark=P + "_bo")
    em.dma("sp", lambda e: e.dma_start(mask4[:], cd["l0mask4"]),
           inc=P + "_mi", waits=[P + "_bol"], mark=P + "_msk")
    em.op("dve", lambda e: e.memset(kze[64:128, :], 0.0))
    em.op("dve", lambda e: e.memset(kzo[0:64, :], 0.0), mark=P + "_kz0")

    for s in range(NSTRIPE):
        tag = f"{P}s{s}"
        bi = stripe_prologue(em, sc, s, x_in, ident)
        for co in range(NCH):
            bank, fw, bidx = pj.acquire()
            for c in range(NCH):
                em.op("pe", lambda e, co=co, c=c, bank=bank, bi=bi: e.matmul(
                    bank[:],
                    wq[:, c * D + co * 128:c * D + co * 128 + 128],
                    sc.chunk(bi, c),
                    start=(c == 0), stop=(c == NCH - 1)),
                    waits=([f"{tag}_xnT", P + "_w0", P + "_bqk"] +
                           ([fw] if fw else [])) if c == 0 else (),
                    mark=(f"{tag}_qmm{co}" if c == NCH - 1 else None))
            em.op("act", lambda e, co=co, bank=bank: e.activation(
                qf[:, co * 512:(co + 1) * 512], bank[:],
                AF.Identity, bias=bq[:, co:co + 1]),
                waits=[f"{tag}_qmm{co}"], mark=f"{tag}_qf{co}")
            pj.release(bidx, f"{tag}_qf{co}")
        for co in range(NCH):
            bank, fw, bidx = pj.acquire()
            for c in range(NCH):
                em.op("pe", lambda e, co=co, c=c, bank=bank, bi=bi: e.matmul(
                    bank[:],
                    wk[:, c * D + co * 128:c * D + co * 128 + 128],
                    sc.chunk(bi, c),
                    start=(c == 0), stop=(c == NCH - 1)),
                    waits=([P + "_w1", P + "_kz0"] +
                           ([fw] if fw else [])) if c == 0 else (),
                    mark=(f"{tag}_kmm{co}" if c == NCH - 1 else None))
            em.op("act", lambda e, co=co, bank=bank: e.activation(
                kze[0:64, co * 512:(co + 1) * 512], bank[0:64, :],
                AF.Identity, bias=bk[0:64, co:co + 1]),
                waits=[f"{tag}_kmm{co}"], mark=f"{tag}_kfe{co}")
            em.op("act", lambda e, co=co, bank=bank: e.activation(
                kzo[64:128, co * 512:(co + 1) * 512], bank[64:128, :],
                AF.Identity, bias=bk[64:128, co:co + 1]),
                waits=[f"{tag}_kmm{co}"], mark=f"{tag}_kfo{co}")
            pj.release(bidx, f"{tag}_kfo{co}")
        for t in range(TPS):
            for eh in range(2):
                bank, fw, bidx = pj.acquire()
                for c in range(NCH):
                    em.op("pe", lambda e, t=t, eh=eh, c=c, bank=bank,
                          bi=bi: e.matmul(
                        bank[:],
                        sc.chunk(bi, c)[:, t * 128:t * 128 + 128],
                        wv[:, c * D + eh * 512:c * D + eh * 512 + 512],
                        start=(c == 0), stop=(c == NCH - 1)),
                        waits=([f"{tag}_xnT", P + "_w2", P + "_bv"] +
                               ([fw] if fw else [])) if c == 0 else (),
                        mark=(f"{tag}_vmm{t}_{eh}" if c == NCH - 1 else None))
                em.op("dve", lambda e, t=t, eh=eh, bank=bank: e.tensor_tensor(
                    vst[:, t * D + eh * 512:t * D + eh * 512 + 512], bank[:],
                    bvb[:, eh * 512:eh * 512 + 512], ALU.add),
                    waits=[f"{tag}_vmm{t}_{eh}"], mark=f"{tag}_vf{t}_{eh}")
                pj.release(bidx, f"{tag}_vf{t}_{eh}")
        for t in range(TPS):
            pa, pb2 = sc.ptr[0], sc.ptr[1]
            war = ([f"{tag}_ktev{t - 1}", f"{tag}_kteA{t - 1}"] if t >= 1
                   else [f"{tag}_trev{TPS - 1}"])
            for p in range(NCH):
                em.op("pe", lambda e, t=t, p=p, pa=pa: e.matmul(
                    pa[:, p * 128:(p + 1) * 128],
                    kze[:, p * 512 + t * 128:p * 512 + t * 128 + 128],
                    ident, is_transpose=True, start=True, stop=True),
                    waits=([f"{tag}_kfe{NCH - 1}", f"{tag}_kfo{NCH - 1}"] + war)
                    if p == 0 else (),
                    mark=(f"{tag}_ktrA{t}" if p == NCH - 1 else None))
            for p in range(NCH):
                em.op("pe", lambda e, t=t, p=p, pb2=pb2: e.matmul(
                    pb2[:, p * 128:(p + 1) * 128],
                    kzo[:, p * 512 + t * 128:p * 512 + t * 128 + 128],
                    ident, is_transpose=True, start=True, stop=True),
                    mark=(f"{tag}_ktrB{t}" if p == NCH - 1 else None))
            em.op("act", lambda e, t=t, pa=pa: e.activation(
                ktm[:, t * D:(t + 1) * D], pa[:, 0:D], AF.Copy),
                waits=[f"{tag}_ktrA{t}"], mark=f"{tag}_kteA{t}")
            em.op("dve", lambda e, t=t, pb2=pb2: e.tensor_tensor(
                ktm[:, t * D:(t + 1) * D], ktm[:, t * D:(t + 1) * D],
                pb2[:, 0:D], ALU.add),
                waits=[f"{tag}_ktrB{t}", f"{tag}_kteA{t}"],
                mark=f"{tag}_ktev{t}")
        em.marks[f"{tag}_trev2"] = em.marks[f"{tag}_kteA{TPS - 1}"]
        em.marks[f"{tag}_trev3"] = em.marks[f"{tag}_ktev{TPS - 1}"]
        for t in range(TPS):
            j = (s % (NSTRIPE // 2)) * TPS + t
            ctag = f"{tag}c{t}"
            if j == 0:
                em.op("dve", lambda e: e.memset(stf[:], 0.0),
                      waits=([f"{P}s{s - 1}c{TPS - 1}_sadd"] if s > 0 else []),
                      mark=f"{ctag}_sadd")
            else:
                ptag = f"{tag}c{t - 1}" if t > 0 else f"{P}s{s - 1}c{TPS - 1}"
                em.op("act", lambda e: e.activation(stb[:], stf[:], AF.Copy),
                      waits=[f"{ptag}_sadd"], mark=f"{ctag}_scp")
            for g in range(4):
                sg = sT[:, (g % 2) * 512:(g % 2) * 512 + 512]
                bank, fw, bidx = pj.acquire()
                for hh in range(4):
                    h = 4 * g + hh
                    p = h // 2
                    kz = kze if h % 2 == 0 else kzo
                    em.op("pe", lambda e, p=p, hh=hh, kz=kz, bank=bank,
                          t=t: e.matmul(
                              bank[:, hh * 128:hh * 128 + 128],
                              kz[:, p * 512 + t * 128:p * 512 + t * 128 + 128],
                              qf[:, p * 512 + t * 128:p * 512 + t * 128 + 128],
                              start=True, stop=True),
                          waits=([f"{tag}_ktev{t}", P + "_msk"] +
                                 ([fw] if fw else [])) if hh == 0 else (),
                          mark=(f"{ctag}_smm{g}" if hh == 3 else None))
                em.op("dve", lambda e, sg=sg, bank=bank: e.tensor_tensor(
                    sg, bank[:], mask4[:], ALU.mult),
                    waits=[f"{ctag}_smm{g}"], mark=f"{ctag}_msk{g}")
                pj.release(bidx, f"{ctag}_msk{g}")
                for hh in range(4):
                    h = 4 * g + hh
                    p = h // 2
                    ob_bank = po[p // 4]
                    col = (p % 4) * 128
                    cb = (h % 2) * 64
                    em.op("pe", lambda e, h=h, col=col, cb=cb, sg=sg,
                          hh=hh, t=t, ob_bank=ob_bank, j=j: e.matmul(
                              ob_bank[cb:cb + 64, col:col + 128],
                              vst[:, t * D + h * DH:t * D + h * DH + 64],
                              sg[:, hh * 128:hh * 128 + 128],
                              start=True, stop=(j == 0),
                              tile_position=(0, cb)),
                          waits=([f"{ctag}_msk{g}", f"{tag}_vf{t}_0",
                                  f"{tag}_vf{t}_1"] +
                                 ([f"{tag}c{t - 1}_oev{p // 4}"]
                                  if t >= 1 else
                                  ([f"{P}s{s - 1}c{TPS - 1}_oev{p // 4}"]
                                   if s >= 1 else [])))
                          if (hh == 0 and g == 0) else
                          ([f"{ctag}_msk{g}"] if hh == 0 else ()),
                          mark=(f"{ctag}_omm{g}"
                                if (j == 0 and hh == 3) else None))
                    if j > 0:
                        em.op("pe", lambda e, h=h, col=col, cb=cb, t=t,
                              ob_bank=ob_bank: e.matmul(
                                  ob_bank[cb:cb + 64, col:col + 128],
                                  stb[:, h * DH:(h + 1) * DH],
                                  qf[:, (h // 2) * 512 + t * 128:
                                     (h // 2) * 512 + t * 128 + 128],
                                  start=False, stop=True,
                                  tile_position=(0, cb)),
                              waits=[f"{ctag}_scp"] if hh == 0 else (),
                              mark=(f"{ctag}_omm{g}" if hh == 3 else None))
            bank, fw, bidx = pj.acquire()
            for h in range(H):
                p = h // 2
                mb = (h % 2) * 64
                em.op("pe", lambda e, h=h, p=p, mb=mb, t=t, bank=bank:
                      e.matmul(
                          bank[mb:mb + 64, p * 64:(p + 1) * 64],
                          ktm[:, t * D + p * 128 + mb:t * D + p * 128 + mb + 64],
                          vst[:, t * D + h * DH:t * D + h * DH + 64],
                          start=True, stop=True, tile_position=(0, mb)),
                      waits=([f"{tag}_ktev{t}"] + ([fw] if fw else []))
                      if h == 0 else (),
                      mark=(f"{ctag}_dmm" if h == H - 1 else None))
            em.op("dve", lambda e, bank=bank: e.tensor_tensor(
                stf[0:64, :].rearrange("p (h c) -> p h c", c=DH)[:, 0::2, :],
                stf[0:64, :].rearrange("p (h c) -> p h c", c=DH)[:, 0::2, :],
                bank[0:64, :].rearrange("p (h c) -> p h c", c=DH),
                ALU.add),
                waits=[f"{ctag}_dmm"] + ([f"{ctag}_scp"] if j > 0 else []))
            em.op("dve", lambda e, bank=bank: e.tensor_tensor(
                stf[64:128, :].rearrange("p (h c) -> p h c", c=DH)[:, 1::2, :],
                stf[64:128, :].rearrange("p (h c) -> p h c", c=DH)[:, 1::2, :],
                bank[64:128, :].rearrange("p (h c) -> p h c", c=DH),
                ALU.add),
                mark=f"{ctag}_sadd")
            pj.release(bidx, f"{ctag}_sadd")
            for b in range(2):
                em.op("act", lambda e, b=b, t=t: e.activation(
                    ofm[:, 0:NCH * 512].rearrange("p (c w) -> p c w", c=NCH)
                    [:, 4 * b:4 * b + 4, t * 128:t * 128 + 128],
                    po[b][:, 0:512].rearrange("p (c w) -> p c w", c=4),
                    AF.Copy),
                    waits=[f"{ctag}_omm{2 * b}", f"{ctag}_omm{2 * b + 1}"],
                    mark=f"{ctag}_oev{b}")
            ob = outb
            for eh in range(2):
                bank, fw, bidx = pj.acquire()
                for co in range(NCH):
                    em.op("pe", lambda e, t=t, eh=eh, co=co, bank=bank:
                          e.matmul(
                              bank[:],
                              ofm[:, co * 512 + t * 128:co * 512 + t * 128 + 128],
                              wo[:, co * D + eh * 512:co * D + eh * 512 + 512],
                              start=(co == 0), stop=(co == NCH - 1)),
                          waits=([f"{ctag}_oev0", f"{ctag}_oev1",
                                  P + "_w3", P + "_bo"] +
                                 ([fw] if fw else [])) if co == 0 else (),
                          mark=(f"{ctag}_wmm{eh}" if co == NCH - 1 else None))
                em.op("dve", lambda e, t=t, eh=eh, bank=bank, ob=ob:
                      e.tensor_tensor(
                          ob[:, t * D + eh * 512:t * D + eh * 512 + 512],
                          bank[:], bob[:, eh * 512:eh * 512 + 512], ALU.add),
                      waits=[f"{ctag}_wmm{eh}"],
                      mark=f"{ctag}_wf{eh}")
                pj.release(bidx, f"{ctag}_wf{eh}")
        ln_post_add(em, lno, s, outb, sc.xs[s % len(sc.xs)], x_out, tag,
                    waits_first=[f"{tag}c{TPS - 1}_wf1"])



def build_pass_l1attn(em, sb, ps, cd, x_in, x_out):
    ident = cd["ident_s"]
    P = "p3"
    W3 = 3 * D
    wqkv = sb(P + "_wqkv", [128, NCH * W3], BF16)
    wout = sb(P + "_wout", [128, NCH * D], BF16)
    outb = sb(P + "_outb", [128, TPS * D], F32)
    wstage = [outb[:, 0:W3 // 2], outb[:, W3 // 2:W3]]
    cosT = sb(P + "_cos", [128, SEQLEN], BF16)
    ssT = sb(P + "_ss", [128, SEQLEN], BF16)
    lmask = sb(P + "_lm", [128, 4 * BS], BF16)
    pj = BankPool([ps(P + f"_pj{i}", [128, 512], F32) for i in range(2)])
    sp2 = [ps(P + "_spa", [128, 1024], F32), ps(P + "_spb", [128, 1024], F32)]
    po = ps(P + "_po", [128, 1024], F32)
    ptr_aps = [po.bitcast(BF16)[:, 0:NCH * 128],
               sp2[1].bitcast(BF16)[:, 0:NCH * 128]]
    sc = StripeCtx(em, sb, ps, P, carry=False, nxs=1, nxnt=1,
                   ptr_aps=ptr_aps)
    qraw = sb(P + "_qraw", [128, NCH * 512], BF16)
    kraw = sb(P + "_kraw", [128, NCH * 512], BF16)
    qrot = sb(P + "_qrot", [128, NCH * 512], BF16)
    kze = sb(P + "_kze", [128, NCH * 512], BF16)
    kzo = sb(P + "_kzo", [128, NCH * 512], BF16)
    tmp1 = outb[:, 3072:3328].bitcast(BF16)
    tmp2 = outb[:, 3328:3584].bitcast(BF16)
    tmp3 = outb[:, 3584:3840].bitcast(BF16)
    vst = sb(P + "_vst", [128, TPS * H * 128], BF16)
    ptile = [sb(P + f"_pt{i}", [128, 1024], BF16) for i in range(2)]
    lns = sb(P + "_lns", [128, 1024], F32)
    rcp = sb(P + "_rcp", [128, 1024], F32)
    ofm = sb(P + "_ofm", [128, NCH * 512], BF16)
    lno = [LNBufs(sb, P + f"_lo{i}") for i in range(2)]

    load_weights(em, wstage, [wqkv, wout],
                 [cd["l1a_wqkvT"], cd["l1a_woutT"]], P)
    em.dma("sp", lambda e: e.dma_start(cosT[:], cd["l1cos"]), inc=P + "_mi",
           mark=P + "_c0")
    em.dma("sp", lambda e: e.dma_start(ssT[:], cd["l1sin"]), inc=P + "_mi",
           waits=[P + "_c0"], mark=P + "_c1")
    em.dma("sp", lambda e: e.dma_start(lmask[:], cd["l1mask"]), inc=P + "_mi",
           waits=[P + "_c1"], mark=P + "_c2")
    em.op("dve", lambda e: e.memset(kze[64:128, :], 0.0))
    em.op("dve", lambda e: e.memset(kzo[0:64, :], 0.0))
    em.op("dve", lambda e: e.memset(
        vst.rearrange("p (s c) -> p s c", c=128)[:, :, 64:128], 1.0),
        mark=P + "_init")

    chain = {"exp": [None, None], "omm": [None, None],
             "norm": [None, None], "ofmd": None}

    for s in range(NSTRIPE):
        tag = f"{P}s{s}"
        war = [f"{P}s{s - 1}_pedone"] if s >= 1 else []
        xw = None
        pw = []
        if s >= 1:
            pw = [c for c in (chain["norm"][0], chain["norm"][1],
                              chain["exp"][1]) if c]
        bi = stripe_prologue(em, sc, s, x_in, ident, xnt_war=war,
                             x_extra_wait=xw, ptr_war=pw)

        def proj_fm(nm, dst, woff):
            for co in range(NCH):
                bank, fw, bidx = pj.acquire()
                for c in range(NCH):
                    em.op("pe", lambda e, co=co, c=c, bank=bank, bi=bi,
                          woff=woff: e.matmul(
                              bank[:],
                              wqkv[:, c * W3 + woff + co * 128:
                                   c * W3 + woff + co * 128 + 128],
                              sc.chunk(bi, c),
                              start=(c == 0), stop=(c == NCH - 1)),
                          waits=([f"{tag}_xnT", P + "_w0", P + "_init"] +
                                 ([fw] if fw else [])) if c == 0 else (),
                          mark=(f"{tag}_{nm}mm{co}" if c == NCH - 1
                                else None))
                evw = [f"{tag}_{nm}mm{co}"]
                if s >= 1:
                    evw.append(f"{P}s{s - 1}_{nm}rot{co}")
                em.op("act", lambda e, co=co, bank=bank, dst=dst:
                      e.activation(dst[:, co * 512:(co + 1) * 512], bank[:],
                                   AF.Copy),
                      waits=evw, mark=f"{tag}_{nm}ev{co}")
                pj.release(bidx, f"{tag}_{nm}ev{co}")
        proj_fm("q", qraw, 0)
        proj_fm("k", kraw, D)
        for t in range(TPS):
            for eh in range(2):
                bank, fw, bidx = pj.acquire()
                for c in range(NCH):
                    em.op("pe", lambda e, t=t, eh=eh, c=c, bank=bank,
                          bi=bi: e.matmul(
                              bank[:],
                              sc.chunk(bi, c)[:, t * 128:t * 128 + 128],
                              wqkv[:, c * W3 + 2 * D + eh * 512:
                                   c * W3 + 2 * D + eh * 512 + 512],
                              start=(c == 0), stop=(c == NCH - 1)),
                          waits=([fw] if (c == 0 and fw) else ()),
                          mark=(f"{tag}_vmm{t}_{eh}" if c == NCH - 1
                                else None))
                vw = [f"{tag}_vmm{t}_{eh}"]
                if s >= 1 and t == 0 and eh == 0:
                    vw.append(f"{P}s{s - 1}_odone")
                em.op("act", lambda e, t=t, eh=eh, bank=bank: e.activation(
                    vst.rearrange("p (tt h c) -> p tt h c", tt=TPS, h=H)
                    [:, t, 8 * eh:8 * eh + 8, 0:64],
                    bank[:].rearrange("p (h c) -> p h c", c=64),
                    AF.Copy),
                    waits=vw, mark=f"{tag}_vev{t}_{eh}")
                pj.release(bidx, f"{tag}_vev{t}_{eh}")
        off = (s % (NSTRIPE // 2)) * 512
        for c in range(NCH):
            em.op("dve", lambda e, c=c, off=off: e.tensor_tensor(
                qrot[:, c * 512:(c + 1) * 512],
                qraw[:, c * 512:(c + 1) * 512],
                cosT[:, off:off + 512], ALU.mult),
                waits=[f"{tag}_qev{c}", P + "_c2"] if c == 0
                else [f"{tag}_qev{c}"])
            for sq4 in range(4):
                sq = sq4 * 32
                tw = []
                if sq4 == 0:
                    if c > 0:
                        tw = [f"{tag}_qc{c - 1}"]
                    elif s >= 1:
                        tw = [f"{P}s{s - 1}_kc{NCH - 1}"]
                em.op("dve", lambda e, c=c, sq=sq, off=off:
                      e.tensor_tensor(
                          tmp1[sq:sq + 32, :],
                          qraw[sq:sq + 32, c * 512:(c + 1) * 512],
                          ssT[sq:sq + 32, off:off + 512],
                          ALU.mult),
                      waits=tw,
                      mark=(f"{tag}_qq{c}" if sq4 == 3 else None))
            for qt in range(4):
                sq = (qt ^ 1) * 32
                em.op("act", lambda e, qt=qt, sq=sq: e.activation(
                    tmp2[qt * 32:qt * 32 + 32, :], tmp1[sq:sq + 32, :],
                    AF.Copy),
                    waits=[f"{tag}_qq{c}"] if qt == 0 else (),
                    mark=(f"{tag}_qc{c}" if qt == 3 else None))
            em.op("dve", lambda e, c=c: e.tensor_tensor(
                qrot[:, c * 512:(c + 1) * 512],
                qrot[:, c * 512:(c + 1) * 512], tmp2[:], ALU.add),
                waits=[f"{tag}_qc{c}"],
                mark=f"{tag}_qrot{c}")
        for c in range(NCH):
            em.op("dve", lambda e, c=c, off=off: e.tensor_tensor(
                tmp3[:], kraw[:, c * 512:(c + 1) * 512],
                cosT[:, off:off + 512], ALU.mult),
                waits=[f"{tag}_kev{c}"])
            for sq4 in range(4):
                sq = sq4 * 32
                tw = []
                if sq4 == 0:
                    tw = [f"{tag}_kc{c - 1}" if c > 0
                          else f"{tag}_qc{NCH - 1}"]
                em.op("dve", lambda e, c=c, sq=sq, off=off:
                      e.tensor_tensor(
                          tmp1[sq:sq + 32, :],
                          kraw[sq:sq + 32, c * 512:(c + 1) * 512],
                          ssT[sq:sq + 32, off:off + 512],
                          ALU.mult),
                      waits=tw,
                      mark=(f"{tag}_kq{c}" if sq4 == 3 else None))
            for qt in range(4):
                sq = (qt ^ 1) * 32
                em.op("act", lambda e, qt=qt, sq=sq: e.activation(
                    tmp2[qt * 32:qt * 32 + 32, :], tmp1[sq:sq + 32, :],
                    AF.Copy),
                    waits=[f"{tag}_kq{c}"] if qt == 0 else (),
                    mark=(f"{tag}_kc{c}" if qt == 3 else None))
            em.op("dve", lambda e, c=c: e.tensor_tensor(
                tmp3[:], tmp3[:], tmp2[:], ALU.add),
                waits=[f"{tag}_kc{c}"])
            em.op("dve", lambda e, c=c: e.tensor_copy(
                kze[0:64, c * 512:(c + 1) * 512], tmp3[0:64, :]))
            em.op("dve", lambda e, c=c: e.tensor_copy(
                kzo[64:128, c * 512:(c + 1) * 512], tmp3[64:128, :]),
                mark=f"{tag}_krot{c}")
        em.marks[f"{tag}_rope"] = em.marks[f"{tag}_krot{NCH - 1}"]
        for b in range(2):
            btag = f"{tag}b{b}"
            for gp in range(8):
                pt = ptile[gp % 2]
                sp = sp2[gp % 2]
                poh = po[:, (gp % 2) * 512:(gp % 2) * 512 + 512]
                for hi in range(2):
                    h = 2 * gp + hi
                    co = h // 2
                    kz = kze if h % 2 == 0 else kzo
                    for kt in range(2):
                        ws = []
                        if hi == 0 and kt == 0:
                            ws = [f"{tag}_rope"]
                            if chain["exp"][gp % 2] is not None:
                                ws.append(chain["exp"][gp % 2])
                        em.op("pe", lambda e, co=co, kz=kz, kt=kt, b=b,
                              hi=hi, sp=sp: e.matmul(
                                  sp[:, hi * 512 + kt * 256:
                                     hi * 512 + kt * 256 + 256],
                                  kz[:, co * 512 + b * 256 + kt * 128:
                                     co * 512 + b * 256 + kt * 128 + 128],
                                  qrot[:, co * 512 + b * 256:
                                       co * 512 + b * 256 + 256],
                                  start=True, stop=True),
                              waits=ws,
                              mark=(f"{btag}g{gp}_smm"
                                    if (hi == 1 and kt == 1) else None))
                ptw = [f"{btag}g{gp}_smm"]
                if chain["omm"][gp % 2] is not None:
                    ptw.append(chain["omm"][gp % 2])
                em.op("act", lambda e, pt=pt, sp=sp: e.activation(
                    pt[:], sp[:], AF.Exp, scale=DH ** (-0.5)),
                    waits=ptw, mark=f"{btag}g{gp}_exp")
                chain["exp"][gp % 2] = f"{btag}g{gp}_exp"
                em.op("dve", lambda e, pt=pt: e.tensor_tensor(
                    pt[:], pt[:], lmask[:], ALU.mult),
                    waits=[f"{btag}g{gp}_exp"], mark=f"{btag}g{gp}_msk")
                for hi in range(2):
                    h = 2 * gp + hi
                    for kt in range(2):
                        ws = [f"{btag}g{gp}_msk", f"{tag}_vev{TPS - 1}_1"]
                        if hi == 0 and kt == 0:
                            if chain["norm"][gp % 2] is not None:
                                ws.append(chain["norm"][gp % 2])
                        em.op("pe", lambda e, h=h, hi=hi, kt=kt, b=b, pt=pt,
                              poh=poh: e.matmul(
                                  poh[:, hi * 256:hi * 256 + 256],
                                  vst[:, (2 * b + kt) * H * 128 + h * 128:
                                      (2 * b + kt) * H * 128 + h * 128 + 128],
                                  pt[:, hi * 512 + kt * 256:
                                     hi * 512 + kt * 256 + 256],
                                  start=(kt == 0), stop=(kt == 1)),
                              waits=ws,
                              mark=(f"{btag}g{gp}_omm"
                                    if (hi == 1 and kt == 1) else None))
                chain["omm"][gp % 2] = f"{btag}g{gp}_omm"
                ftag = f"{btag}g{gp}"
                hb = (gp % 2) * 512
                lw = [f"{btag}g{gp}_omm"]
                if chain["ofmd"] is not None:
                    lw.append(chain["ofmd"])
                em.op("act", lambda e, hb=hb: e.activation(
                    lns[64:128, hb:hb + 512], po[64:128, hb:hb + 512],
                    AF.Ln),
                    waits=lw, mark=f"{ftag}_ln")
                em.op("act", lambda e, hb=hb: e.activation(
                    rcp[0:64, hb:hb + 512], lns[64:128, hb:hb + 512],
                    AF.Exp, scale=-1.0),
                    waits=[f"{ftag}_ln"], mark=f"{ftag}_rcp")
                em.op("dve", lambda e, hb=hb: e.tensor_tensor(
                    lns[0:64, hb:hb + 512].bitcast(BF16)[:, 0:1024]
                    [:, 0:512] if False else
                    lns[0:64, :].bitcast(BF16)[:, hb:hb + 512],
                    po[0:64, hb:hb + 512], rcp[0:64, hb:hb + 512],
                    ALU.mult),
                    waits=[f"{ftag}_rcp"], mark=f"{ftag}_on")
                for hh in range(2):
                    h = 2 * gp + hh
                    cb = (h % 2) * 64
                    co = h // 2
                    em.op("act", lambda e, hh=hh, cb=cb, co=co, b=b, hb=hb:
                          e.activation(
                              ofm[cb:cb + 64,
                                  co * 512 + b * 256:co * 512 + b * 256 + 256],
                              lns[0:64, :].bitcast(BF16)
                              [:, hb + hh * 256:hb + hh * 256 + 256],
                              AF.Copy),
                          waits=[f"{ftag}_on"],
                          mark=(f"{ftag}_ofm" if hh == 1 else None))
                chain["norm"][gp % 2] = f"{ftag}_ofm"
                chain["ofmd"] = f"{ftag}_ofm"
            em.marks[f"{btag}_odone"] = em.marks[f"{btag}g7_omm"]
            for tt in range(2):
                t = 2 * b + tt
                for eh in range(2):
                    bank, fw, bidx = pj.acquire()
                    for co in range(NCH):
                        em.op("pe", lambda e, t=t, eh=eh, co=co, bank=bank:
                              e.matmul(
                                  bank[:],
                                  ofm[:, co * 512 + t * 128:
                                      co * 512 + t * 128 + 128],
                                  wout[:, co * D + eh * 512:
                                       co * D + eh * 512 + 512],
                                  start=(co == 0), stop=(co == NCH - 1)),
                              waits=([f"{btag}g6_ofm", f"{btag}g7_ofm",
                                      P + "_w1"] +
                                     ([fw] if fw else [])) if co == 0 else (),
                              mark=(f"{tag}_wmm{t}_{eh}" if co == NCH - 1
                                    else None))
                    evw = [f"{tag}_wmm{t}_{eh}"]
                    if s >= 1 and t == 0 and eh == 0:
                        evw.append(f"{P}s{s - 1}_add")
                    em.op("act", lambda e, t=t, eh=eh, bank=bank:
                          e.activation(
                              outb[:, t * D + eh * 512:t * D + eh * 512 + 512],
                              bank[:], AF.Copy),
                          waits=evw, mark=f"{tag}_wf{t}_{eh}")
                    pj.release(bidx, f"{tag}_wf{t}_{eh}")
        em.marks[f"{tag}_odone"] = em.marks[f"{tag}b1_odone"]
        em.marks[f"{tag}_pedone"] = em.marks[f"{tag}_wmm{TPS - 1}_1"]
        ln_post_add(em, lno, s, outb, sc.xs[s % len(sc.xs)], x_out, tag,
                    waits_first=[f"{tag}_wf{TPS - 1}_1"])



def build_pass_l1ffn(em, sb, ps, cd, x_in, x_out):
    ident = cd["ident_s"]
    P = "p4"
    NHC = HID // 128
    wg = sb(P + "_wg", [128, NCH * HID], BF16)
    wu = sb(P + "_wu", [128, NCH * HID], BF16)
    wd = sb(P + "_wd", [128, NHC * D], BF16)
    sc = StripeCtx(em, sb, ps, P, carry=False, nxs=1, nxnt=1)
    hact = sb(P + "_hact", [128, NHC * 512], BF16)
    wstage = [hact.bitcast(I32)[:, 0:HID // 2],
              hact.bitcast(I32)[:, HID // 2:HID]]
    silu_t = [sb(P + f"_sl{i}", [128, 512], F32) for i in range(2)]
    lno = LNBufs(sb, P + "_lo")
    pg = BankPool([ps(P + f"_pg{i}", [128, 512], F32) for i in range(4)])
    pd = [ps(P + f"_pd{i}", [128, 512], F32) for i in range(2)]

    load_weights(em, wstage, [wg, wu, wd],
                 [cd["l1f_wgT"], cd["l1f_wuT"], cd["l1f_wdT"]], P,
                 scale=1.0 / 127.0)

    for s in range(NSTRIPE):
        tag = f"{P}s{s}"
        war = [f"{P}s{s - 1}_gu{NHC - 1}"] if s >= 1 else []
        xw = f"{P}s{s - 1}_store" if s >= 1 else None
        bi = stripe_prologue(em, sc, s, x_in, ident, xnt_war=war,
                             x_extra_wait=xw)
        for hc in range(NHC):
            bkg, fwg, big = pg.acquire()
            for c in range(NCH):
                em.op("pe", lambda e, hc=hc, c=c, bkg=bkg, bi=bi: e.matmul(
                    bkg[:],
                    wg[:, c * HID + hc * 128:c * HID + hc * 128 + 128],
                    sc.chunk(bi, c),
                    start=(c == 0), stop=(c == NCH - 1)),
                    waits=([f"{tag}_xnT", P + "_w0"] +
                           ([fwg] if fwg else [])) if c == 0 else (),
                    mark=(f"{tag}_gmm{hc}" if c == NCH - 1 else None))
            bku, fwu, biu = pg.acquire()
            for c in range(NCH):
                em.op("pe", lambda e, hc=hc, c=c, bku=bku, bi=bi: e.matmul(
                    bku[:],
                    wu[:, c * HID + hc * 128:c * HID + hc * 128 + 128],
                    sc.chunk(bi, c),
                    start=(c == 0), stop=(c == NCH - 1)),
                    waits=([P + "_w1"] + ([fwu] if fwu else []))
                    if c == 0 else (),
                    mark=(f"{tag}_umm{hc}" if c == NCH - 1 else None))
            sl = silu_t[hc % 2]
            sl_war = [f"{tag}_ha{hc - 2}"] if hc >= 2 else (
                [f"{P}s{s - 1}_ha{NHC - 2 + hc}"] if s >= 1 else [])
            em.op("act", lambda e, hc=hc, bkg=bkg, sl=sl: e.activation(
                sl[:], bkg[:], AF.Silu),
                waits=[f"{tag}_gmm{hc}"] + sl_war, mark=f"{tag}_sil{hc}")
            pg.release(big, f"{tag}_sil{hc}")
            ha_war = ([f"{P}s{s - 1}_dmm{TPS - 1}_1"]
                      if (s >= 1 and hc == 0) else [])
            if s == 0 and hc == 0:
                ha_war = [P + "_w2"]
            em.op("dve", lambda e, hc=hc, bku=bku, sl=sl: e.tensor_tensor(
                hact[:, hc * 512:(hc + 1) * 512], bku[:], sl[:], ALU.mult),
                waits=[f"{tag}_umm{hc}", f"{tag}_sil{hc}"] + ha_war,
                mark=f"{tag}_ha{hc}")
            pg.release(biu, f"{tag}_ha{hc}")
        em.marks[f"{tag}_gu{NHC - 1}"] = em.marks[f"{tag}_umm{NHC - 1}"]
        xs = sc.xs[s % len(sc.xs)]
        ln = lno
        for t in range(TPS):
            for eh in range(2):
                bank = pd[eh]
                for hc in range(NHC):
                    em.op("pe", lambda e, t=t, eh=eh, hc=hc, bank=bank:
                          e.matmul(
                              bank[:],
                              hact[:, hc * 512 + t * 128:hc * 512 + t * 128 + 128],
                              wd[:, hc * D + eh * 512:hc * D + eh * 512 + 512],
                              start=(hc == 0), stop=(hc == NHC - 1)),
                          waits=([f"{tag}_ha{NHC - 1}", P + "_w2"] +
                                 ([f"{tag}t{t - 1}_ap{eh}"] if t >= 1 else
                                  ([f"{P}s{s - 1}t{TPS - 1}_ap{eh}"]
                                   if s >= 1 else []))) if hc == 0 else (),
                          mark=(f"{tag}_dmm{t}_{eh}" if hc == NHC - 1
                                else None))
            for eh in range(2):
                em.op("dve", lambda e, t=t, eh=eh: e.bn_stats(
                    ln.st[:, t * 12 + eh * 6:t * 12 + eh * 6 + 6], pd[eh][:]),
                    waits=[f"{tag}_dmm{t}_{eh}"] +
                          ([f"{P}s{s - 1}_lnfree"] if (s >= 1 and t == 0 and
                                                       eh == 0) else []))
            em.op("dve", lambda e, t=t: e.bn_aggr(
                ln.mv[:, t * 2:t * 2 + 2], ln.st[:, t * 12:t * 12 + 12]))
            em.op("dve", lambda e, t=t: e.tensor_scalar(
                ln.vs[:, t:t + 1], ln.mv[:, 2 * t + 1:2 * t + 2],
                EPS, None, ALU.add))
            newton_rsqrt(em, ln.vs[:, t:t + 1], ln.rs[:, t:t + 1],
                         ln.tmp[:, t:t + 1], ln.ci[:, t:t + 1])
            em.op("dve", lambda e, t=t: e.scalar_tensor_tensor(
                ln.nb[:, t:t + 1], ln.mv[:, 2 * t:2 * t + 1],
                -1.0, ln.rs[:, t:t + 1], ALU.mult, ALU.mult),
                mark=f"{tag}t{t}_lnp")
            for eh in range(2):
                em.op("dve", lambda e, t=t, eh=eh: e.tensor_scalar(
                    xs[:, t * D + eh * 512:t * D + eh * 512 + 512],
                    xs[:, t * D + eh * 512:t * D + eh * 512 + 512],
                    ln.nb[:, t:t + 1], None, ALU.add),
                    waits=[f"{tag}t{t}_lnp"])
                em.op("dve", lambda e, t=t, eh=eh: e.scalar_tensor_tensor(
                    xs[:, t * D + eh * 512:t * D + eh * 512 + 512],
                    pd[eh][:], ln.rs[:, t:t + 1],
                    xs[:, t * D + eh * 512:t * D + eh * 512 + 512],
                    ALU.mult, ALU.add),
                    mark=f"{tag}t{t}_ap{eh}")
        em.marks[f"{tag}_lnfree"] = em.marks[f"{tag}t{TPS - 1}_lnp"]
        if x_out.shape[0] == 128:
            em.dma("pool", lambda e, xs=xs, s=s: e.dma_start(
                x_out[:, s * TPS * D:(s + 1) * TPS * D], xs[:]),
                inc=f"st{s % 2}", mark=f"{tag}_store",
                waits=[f"{tag}t{TPS - 1}_ap0", f"{tag}t{TPS - 1}_ap1"])
        else:
            em.dma("pool", lambda e, xs=xs, s=s: e.dma_start(
                x_out[s * STRIPE:(s + 1) * STRIPE, :]
                .rearrange("(t p) d -> p t d", p=128),
                xs.rearrange("p (t d) -> p t d", t=TPS)),
                inc=f"st{s % 2}", mark=f"{tag}_store",
                waits=[f"{tag}t{TPS - 1}_ap0", f"{tag}t{TPS - 1}_ap1"])



def _host_prep(inputs):
    f32 = np.float32
    bf16 = ml_dtypes.bfloat16
    shared = {}

    def T(a):
        return np.ascontiguousarray(np.asarray(a).T)

    shared["l0a_wqT"] = T(inputs["L0_attn_wq_w"]).astype(f32, copy=False)
    shared["l0a_wkT"] = T(inputs["L0_attn_wk_w"]).astype(f32, copy=False)
    shared["l0a_wvT"] = T(inputs["L0_attn_wv_w"]).astype(f32, copy=False)
    shared["l0a_woT"] = T(inputs["L0_attn_wo_w"]).astype(f32, copy=False)
    for nm in ("q", "k", "v", "o"):
        shared[f"l0a_b{nm}"] = np.ascontiguousarray(
            np.asarray(inputs[f"L0_attn_w{nm}_b"]).astype(f32, copy=False)
            .reshape(NCH, 128).T)
    shared["l0a_bvb"] = np.ascontiguousarray(np.tile(
        np.asarray(inputs["L0_attn_wv_b"]).astype(f32, copy=False)[None, :],
        (128, 1)))
    shared["l0a_bob"] = np.ascontiguousarray(np.tile(
        np.asarray(inputs["L0_attn_wo_b"]).astype(f32, copy=False)[None, :],
        (128, 1)))
    shared["l0f_wrT"] = T(inputs["L0_ffn_W_r"]).astype(f32, copy=False)
    shared["l0f_wkT"] = T(inputs["L0_ffn_W_k"]).astype(f32, copy=False)
    shared["l0f_wvT"] = T(inputs["L0_ffn_W_v"]).astype(f32, copy=False)
    shared["l0f_mur"] = np.ascontiguousarray(
        np.asarray(inputs["L0_ffn_mu_r"]).astype(f32, copy=False)
        .reshape(NCH, 128).T)
    shared["l0f_muk"] = np.ascontiguousarray(
        np.asarray(inputs["L0_ffn_mu_k"]).astype(f32, copy=False)
        .reshape(NCH, 128).T)
    perm = np.zeros(3 * D, dtype=np.int64)
    for blk in range(2):
        for h in range(H):
            base = blk * D + 64 * h
            for j in range(DH // 2):
                perm[base + j] = base + 2 * j
                perm[base + 32 + j] = base + 2 * j + 1
    perm[2 * D:] = np.arange(2 * D, 3 * D)
    wqkv = np.asarray(inputs["L1_attn_Wqkv"]).astype(f32, copy=False)[perm]
    shared["l1a_wqkvT"] = np.ascontiguousarray(wqkv.T)
    shared["l1a_woutT"] = T(inputs["L1_attn_Wout"]).astype(f32, copy=False)
    shared["l1f_wgT"] = T(np.asarray(inputs["L1_ffn_w_gate"]))
    shared["l1f_wuT"] = T(np.asarray(inputs["L1_ffn_w_up"]))
    shared["l1f_wdT"] = T(np.asarray(inputs["L1_ffn_w_down"]))
    shared["ident"] = np.eye(128, dtype=bf16)
    ck = np.arange(128)[:, None]
    cq = np.arange(128)[None, :]
    shared["l0mask4"] = np.ascontiguousarray(
        np.tile((ck <= cq).astype(bf16), (1, 4)))
    inv_freq = 1.0 / 10000.0 ** (np.arange(0, DH, 2, dtype=np.float64) / DH)
    t = np.arange(SEQLEN, dtype=np.float64)
    ang = t[None, :] * inv_freq[:, None]
    shared["l1cos"] = np.ascontiguousarray(np.tile(np.cos(ang), (4, 1))
                                           .astype(bf16))
    sin = np.sin(ang)
    ss = np.concatenate([sin, -sin, sin, -sin], axis=0)
    shared["l1sin"] = np.ascontiguousarray(ss.astype(bf16))
    ck2 = np.arange(128)[:, None]
    cq2 = np.arange(BS)[None, :]
    m0 = (ck2 <= cq2).astype(bf16)
    m1 = (ck2 + 128 <= cq2).astype(bf16)
    shared["l1mask"] = np.ascontiguousarray(
        np.concatenate([m0, m1, m0, m1], axis=1))
    return shared


W_SHAPES = {
    "l0a_wqT": ([D, D], F32), "l0a_wkT": ([D, D], F32),
    "l0a_wvT": ([D, D], F32), "l0a_woT": ([D, D], F32),
    "l0a_bq": ([128, NCH], F32), "l0a_bk": ([128, NCH], F32),
    "l0a_bv": ([128, NCH], F32), "l0a_bo": ([128, NCH], F32),
    "l0f_wrT": ([D, D], F32), "l0f_wkT": ([D, D], F32),
    "l0f_wvT": ([D, D], F32),
    "l0f_mur": ([128, NCH], F32), "l0f_muk": ([128, NCH], F32),
    "l1a_wqkvT": ([D, 3 * D], F32), "l1a_woutT": ([D, D], F32),
    "l1f_wgT": ([D, HID], I32), "l1f_wuT": ([D, HID], I32),
    "l1f_wdT": ([HID, D], I32),
    "ident": ([128, 128], BF16), "l0mask4": ([128, 512], BF16),
    "l0a_bvb": ([128, D], F32), "l0a_bob": ([128, D], F32),
    "l1cos": ([128, SEQLEN], BF16), "l1sin": ([128, SEQLEN], BF16),
    "l1mask": ([128, 4 * BS], BF16),
}

PASS_BUILDERS = {"l0ffn": build_pass_l0ffn,
                 "l0attn": build_pass_l0attn,
                 "l1ffn": build_pass_l1ffn,
                 "l1attn": build_pass_l1attn}
PASS_WEIGHTS = {
    "l0attn": ["l0a_wqT", "l0a_wkT", "l0a_wvT", "l0a_woT",
               "l0a_bq", "l0a_bk", "l0a_bvb", "l0a_bob", "l0mask4", "ident"],
    "l0ffn": ["l0f_wrT", "l0f_wkT", "l0f_wvT", "l0f_mur", "l0f_muk", "ident"],
    "l1attn": ["l1a_wqkvT", "l1a_woutT", "l1cos", "l1sin",
               "l1mask", "ident"],
    "l1ffn": ["l1f_wgT", "l1f_wuT", "l1f_wdT", "ident"],
}


def build_program(passes):
    nc = bass.Bass("TRN2", target_bir_lowering=False)
    stack = ExitStack()
    em = Em(nc, stack)

    names = sorted({w for p in passes for w in PASS_WEIGHTS[p]})
    cd = {}
    for name in names:
        shp, dt = W_SHAPES[name]
        cd[name] = nc.dram_tensor(name, shp, dt, kind="ExternalInput").ap()

    def sb(name, shape, dt):
        return stack.enter_context(nc.sbuf_tensor(name, shape, dt))[:]

    def ps(name, shape, dt):
        return stack.enter_context(nc.psum_tensor(name, shape, dt))[:]

    ident_s = sb("c_ident", [128, 128], BF16)
    em.dma("sp", lambda e: e.dma_start(ident_s, cd["ident"]), inc="cst",
           mark="c_ident")
    cd["ident_s"] = ident_s

    x_first = nc.dram_tensor("x_in", [TOK, D], F32, kind="ExternalInput").ap()
    y_last = nc.dram_tensor("y", [TOK, D], F32, kind="ExternalOutput").ap()
    mids = [nc.dram_tensor(f"xmid{k}", [128, TOK * D // 128], F32,
                           kind="Internal").ap()
            for k in range(len(passes) - 1)]
    for k, p in enumerate(passes):
        x_in = x_first if k == 0 else mids[k - 1]
        x_out = y_last if k == len(passes) - 1 else mids[k]
        with ExitStack() as pass_stack:
            saved = stack
            inner_sb = (lambda st: lambda n, shp, dt:
                        st.enter_context(nc.sbuf_tensor(n, shp, dt))[:]
                        )(pass_stack)
            inner_ps = (lambda st: lambda n, shp, dt:
                        st.enter_context(nc.psum_tensor(n, shp, dt))[:]
                        )(pass_stack)
            PASS_BUILDERS[p](em, inner_sb, inner_ps, cd, x_in, x_out)
            em.barrier()
    with nc.Block() as blk:
        em.finalize(blk)
    stack.close()
    return nc, names


def kernel(**inputs):
    shared = _host_prep(inputs)
    x = np.asarray(inputs["L0_attn_x"]).astype(np.float32, copy=False)
    x = np.ascontiguousarray(x.reshape(NCORES, TOK, D))

    passes = ["l0attn", "l0ffn", "l1attn", "l1ffn"]
    nc, names = build_program(passes)
    in_maps = []
    for c in range(NCORES):
        m = {n: shared[n] for n in names}
        m["x_in"] = x[c]
        in_maps.append(m)
    res = run_bass_kernel_spmd(nc, in_maps, core_ids=list(range(NCORES)))
    y = np.stack([np.asarray(r["y"]) for r in res.results])
    return np.ascontiguousarray(y.reshape(16, SEQLEN, D))


# version 10
# speedup vs baseline: 1.0276x; 1.0101x over previous
import numpy as np
import ml_dtypes
from contextlib import ExitStack
import concourse.bass as bass
import concourse.mybir as mybir
from concourse.bass_utils import run_bass_kernel_spmd

F32 = mybir.dt.float32
BF16 = mybir.dt.bfloat16
I32 = mybir.dt.int32
ALU = mybir.AluOpType
AF = mybir.ActivationFunctionType

NCORES = 8
D = 1024
SEQLEN = 2048
TOK = 4096
STRIPE = 512
NSTRIPE = TOK // STRIPE
TPS = 4
NCH = D // 128
H = 16
DH = 64
BS = 256
HID = 3072
EPS = 1e-5

P1STAGE = 99

ENGINES = ("pe", "act", "dve", "pool", "sp")
ENG_ATTR = {"pe": "tensor", "act": "scalar", "dve": "vector", "pool": "gpsimd",
            "sp": "sync"}


class Em:

    def __init__(self, nc, stack):
        self.nc = nc
        self.stack = stack
        self.sems = {}
        self.cnt = {}
        self.ops = {e: [] for e in ENGINES}
        self.marks = {}

    def sem(self, name):
        if name not in self.sems:
            self.sems[name] = self.stack.enter_context(self.nc.semaphore(name))
            self.cnt[name] = 0
        return self.sems[name]

    def op(self, eng, fn, waits=(), inc=None, inc_amt=1, mark=None):
        wl = []
        for w in waits:
            if w is None:
                continue
            if isinstance(w, str):
                if w not in self.marks:
                    raise KeyError(f"unknown mark {w!r}")
                wl.append(self.marks[w])
            else:
                self.sem(w[0])
                wl.append(w)
        if mark is not None and inc is None:
            inc = "s_" + eng
        rec = {"fn": fn, "waits": wl, "inc": None}
        if inc is not None:
            self.sem(inc)
            self.cnt[inc] += inc_amt
            rec["inc"] = (inc, inc_amt)
            if mark is not None:
                self.marks[mark] = (inc, self.cnt[inc])
        self.ops[eng].append(rec)

    def dma(self, eng, fn, inc, mark=None, waits=()):
        self.op(eng, fn, waits=waits, inc=inc, inc_amt=16, mark=mark)

    def barrier(self):
        for eng in ENGINES:
            if self.ops[eng]:
                rec = self.ops[eng][-1]
                if rec["inc"] is None:
                    s = "s_" + eng
                    self.sem(s)
                    self.cnt[s] += 1
                    rec["inc"] = (s, 1)
        targets = [(s, self.cnt[s]) for s in list(self.cnt) if self.cnt[s] > 0]
        for eng in ENGINES:
            if not self.ops[eng]:
                continue
            for (s, v) in targets:
                self.ops[eng].append({"fn": None, "waits": [(s, v)],
                                      "inc": None})

    def finalize(self, blk):
        for eng in ENGINES:
            if not self.ops[eng]:
                continue

            def mkbody(ops, sems, sep):
                def run(e):
                    waited = {}
                    prev_compute = False
                    for rec in ops:
                        for (s, v) in rec["waits"]:
                            if waited.get(s, -1) < v:
                                e.wait_ge(sems[s], v)
                                waited[s] = v
                        if rec["fn"] is not None:
                            if sep and prev_compute:
                                e.drain()
                            ins = rec["fn"](e)
                            prev_compute = True
                            if rec["inc"] is not None:
                                s, amt = rec["inc"]
                                ins.then_inc(sems[s], amt)
                return run

            getattr(blk, ENG_ATTR[eng])(mkbody(self.ops[eng], self.sems,
                                               eng in ('dve', 'act', 'pool')))


class BankPool:

    def __init__(self, banks):
        self.banks = banks
        self.uses = 0
        self.free_mark = [None] * len(banks)

    def acquire(self):
        i = self.uses % len(self.banks)
        self.uses += 1
        return self.banks[i], self.free_mark[i], i

    def release(self, i, mark):
        self.free_mark[i] = mark


def newton_rsqrt(em, v_ap, y_ap, tmp_ap, ci_ap, waits=(), mark=None):
    MAGIC = 0x5f3759df
    em.op("dve", lambda e: e.tensor_scalar(
        y_ap.bitcast(I32), v_ap.bitcast(I32), 1, None, ALU.arith_shift_right),
        waits=waits)
    em.op("dve", lambda e: e.memset(ci_ap, MAGIC))
    em.op("dve", lambda e: e.tensor_tensor(
        y_ap.bitcast(I32), ci_ap, y_ap.bitcast(I32), ALU.subtract))
    for it in range(3):
        em.op("dve", lambda e: e.tensor_tensor(tmp_ap, y_ap, y_ap, ALU.mult))
        em.op("dve", lambda e: e.tensor_tensor(tmp_ap, tmp_ap, v_ap, ALU.mult))
        em.op("dve", lambda e: e.tensor_scalar(
            tmp_ap, tmp_ap, -0.5, 1.5, ALU.mult, ALU.add))
        em.op("dve", lambda e: e.tensor_tensor(y_ap, y_ap, tmp_ap, ALU.mult),
              mark=(mark if it == 2 else None))


class LNBufs:
    def __init__(self, sb, tag):
        self.st = sb(f"{tag}_st", [128, TPS * 12], F32)
        self.mv = sb(f"{tag}_mv", [128, TPS * 2], F32)
        self.vs = sb(f"{tag}_vs", [128, TPS], F32)
        self.rs = sb(f"{tag}_rs", [128, TPS], F32)
        self.nb = sb(f"{tag}_nb", [128, TPS], F32)
        self.tmp = sb(f"{tag}_tmp", [128, TPS], F32)
        self.ci = sb(f"{tag}_ci", [128, TPS], I32)


def ln_stats(em, ln, src_tiles, tag, waits_first=()):
    for t in range(TPS):
        for hh in range(2):
            em.op("dve", lambda e, t=t, hh=hh: e.bn_stats(
                ln.st[:, t * 12 + hh * 6:t * 12 + hh * 6 + 6],
                src_tiles[t][:, hh * 512:hh * 512 + 512]),
                waits=(waits_first if (t == 0 and hh == 0) else ()))
    for t in range(TPS):
        em.op("dve", lambda e, t=t: e.bn_aggr(
            ln.mv[:, t * 2:t * 2 + 2], ln.st[:, t * 12:t * 12 + 12]))
    for t in range(TPS):
        em.op("dve", lambda e, t=t: e.tensor_scalar(
            ln.vs[:, t:t + 1], ln.mv[:, 2 * t + 1:2 * t + 2],
            EPS, None, ALU.add))
    newton_rsqrt(em, ln.vs[:, 0:TPS], ln.rs[:, 0:TPS], ln.tmp[:, 0:TPS], ln.ci[:, 0:TPS])
    for t in range(TPS):
        em.op("dve", lambda e, t=t: e.scalar_tensor_tensor(
            ln.nb[:, t:t + 1], ln.mv[:, 2 * t:2 * t + 1],
            -1.0, ln.rs[:, t:t + 1], ALU.mult, ALU.mult),
            mark=(f"{tag}_lnp" if t == TPS - 1 else None))


class StripeCtx:

    def __init__(self, em, sb, ps, tag, carry=False, nxs=2, nxnt=2,
                 ptr_aps=None):
        self.em = em
        self.tag = tag
        self.carry = carry
        self.nxs = nxs
        self.nxnt = nxnt
        self.ptr_aps = ptr_aps
        self.w = 520 if carry else 512
        self.off = 1 if carry else 0
        self.xs = [sb(f"{tag}_x{i}", [128, TPS * D], F32)
                   for i in range(getattr(self, "nxs", 2))]
        self.xnT = [sb(f"{tag}_xnT{i}", [128, NCH * self.w], BF16)
                    for i in range(getattr(self, "nxnt", 2))]
        self.xn = sb(f"{tag}_xn", [128, TPS * D], BF16)
        self.ln = [LNBufs(sb, f"{tag}_li{i}") for i in range(2)]
        if getattr(self, "ptr_aps", None) is not None:
            self.ptr = self.ptr_aps
        else:
            self.ptr = [ps(f"{tag}_ptr{i}", [128, NCH * 128], BF16)
                        for i in range(2)]

    def chunk(self, bi, c):
        o = c * self.w + self.off
        return self.xnT[bi][:, o:o + 512]

    def chunk_shift(self, bi, c):
        o = c * self.w + self.off
        return self.xnT[bi][:, o - 1:o + 511]

    def chunk3d(self, bi):
        return self.xnT[bi][:, 0:NCH * self.w].rearrange(
            "p (c w) -> p c w", c=NCH)


def stripe_prologue(em, sc, s, x_dram, ident, xnt_war=None, x_extra_wait=None,
                    ptr_war=None):
    bi = s % len(sc.xnT)
    xs = sc.xs[s % len(sc.xs)]
    T0 = sc.tag
    tag = f"{T0}s{s}"
    w = []
    sx = s - len(sc.xs)
    if sx >= 0 and f"{T0}s{sx}_add" in em.marks:
        w.append(f"{T0}s{sx}_add")
    elif sx >= 0 and f"{T0}s{sx}_store" in em.marks:
        w.append(f"{T0}s{sx}_store")
    if x_extra_wait:
        w.append(x_extra_wait)
    if x_dram.shape[0] == 128:
        em.dma("sp", lambda e: e.dma_start(
            xs[:], x_dram[:, s * TPS * D:(s + 1) * TPS * D]),
            inc=f"ld{s % 2}", mark=f"{tag}_xld", waits=w)
    else:
        em.dma("sp", lambda e: e.dma_start(
            xs.rearrange("p (t d) -> p t d", t=TPS),
            x_dram[s * STRIPE:(s + 1) * STRIPE, :]
            .rearrange("(t p) d -> p t d", p=128)),
            inc=f"ld{s % 2}", mark=f"{tag}_xld", waits=w)
    tiles = [xs[:, t * D:(t + 1) * D] for t in range(TPS)]
    ln_stats(em, sc.ln[bi], tiles, tag, waits_first=[f"{tag}_xld"])
    for t in range(TPS):
        ww = [f"{tag}_lnp"] if t == 0 else []
        if t == 0 and s >= 1:
            ww.append(f"{T0}s{s - 1}_tr{TPS - 1}")
        em.op("act", lambda e, t=t, bi=bi: e.activation(
            sc.xn[:, t * D:(t + 1) * D], xs[:, t * D:(t + 1) * D],
            AF.Identity,
            bias=sc.ln[bi].nb[:, t:t + 1], scale=sc.ln[bi].rs[:, t:t + 1]),
            waits=ww, mark=f"{tag}_xn{t}")
    em.marks[f"{tag}_xn"] = em.marks[f"{tag}_xn{TPS - 1}"]
    for t in range(TPS):
        pb = sc.ptr[t % 2]
        bank_war = None
        if t >= 2:
            bank_war = f"{tag}_trev{t - 2}"
        elif s >= 1:
            bank_war = f"{T0}s{s - 1}_trev{t + 2}"
        bw = [bank_war] if bank_war else []
        if t < 2 and ptr_war:
            bw += [m for m in ptr_war if m in em.marks]
        for c in range(NCH):
            em.op("pe", lambda e, t=t, c=c, pb=pb: e.transpose(
                pb[:, c * 128:(c + 1) * 128],
                sc.xn[:, t * D + c * 128:t * D + c * 128 + 128], ident),
                waits=([f"{tag}_xn{t}", "c_ident"] + bw)
                if c == 0 else (),
                mark=(f"{tag}_tr{t}" if c == NCH - 1 else None))
        ev_w = [f"{tag}_tr{t}"]
        if t == 0 and xnt_war:
            ev_w += [m for m in xnt_war if m in em.marks]
        em.op("act", lambda e, t=t, pb=pb, bi=bi: e.activation(
            sc.chunk3d(bi)[:, :, sc.off + t * 128:sc.off + t * 128 + 128],
            pb[:, 0:NCH * 128].rearrange("p (c w) -> p c w", c=NCH), AF.Copy),
            waits=ev_w, mark=f"{tag}_trev{t}")
    em.marks[f"{tag}_xnT"] = em.marks[f"{tag}_trev{TPS - 1}"]
    return bi


def ln_post_add(em, lno, s, outb, xs, x_out, tag, waits_first):
    tiles = [outb[:, t * D:(t + 1) * D] for t in range(TPS)]
    ln = lno[s % 2]
    ln_stats(em, ln, tiles, tag + "o", waits_first=waits_first)
    for t in range(TPS):
        em.op("act", lambda e, t=t: e.activation(
            outb[:, t * D:(t + 1) * D], outb[:, t * D:(t + 1) * D],
            AF.Identity,
            bias=ln.nb[:, t:t + 1], scale=ln.rs[:, t:t + 1]),
            waits=[f"{tag}o_lnp"] if t == 0 else (),
            mark=(f"{tag}_lnapp" if t == TPS - 1 else None))
    em.op("dve", lambda e: e.tensor_tensor(outb[:], outb[:], xs[:], ALU.add),
          waits=[f"{tag}_lnapp"], mark=f"{tag}_add")
    if x_out.shape[0] == 128:
        em.dma("pool", lambda e: e.dma_start(
            x_out[:, s * TPS * D:(s + 1) * TPS * D], outb[:]),
            inc=f"st{s % 2}", mark=f"{tag}_store", waits=[f"{tag}_add"])
    else:
        em.dma("pool", lambda e: e.dma_start(
            x_out[s * STRIPE:(s + 1) * STRIPE, :]
            .rearrange("(t p) d -> p t d", p=128),
            outb.rearrange("p (t d) -> p t d", t=TPS)),
            inc=f"st{s % 2}", mark=f"{tag}_store", waits=[f"{tag}_add"])


def load_weights(em, wstage, dsts, srcs, tag, scale=None):
    n = 0
    sw = wstage[0].shape[1]
    nst = len(wstage)
    for wi, (wsb, wd) in enumerate(zip(dsts, srcs)):
        ncw = wd.shape[0] // 128
        cols = wd.shape[1]
        nparts = (cols + sw - 1) // sw
        for c in range(ncw):
            for pp in range(nparts):
                w0 = pp * sw
                w1 = min(cols, w0 + sw)
                st = wstage[n % nst]
                prev = f"{tag}_cast{n - nst}" if n >= nst else None
                em.dma("sp",
                       lambda e, st=st, wd=wd, c=c, w0=w0, w1=w1:
                       e.dma_start(st[:, 0:w1 - w0],
                                   wd[c * 128:(c + 1) * 128, w0:w1]),
                       inc=f"{tag}_wp{n % nst}",
                       waits=[prev] if prev else (),
                       mark=f"{tag}_wl{n}")
                kw = {} if scale is None else {"scale": scale}
                em.op("act", lambda e, wsb=wsb, st=st, c=c, cols=cols,
                      w0=w0, w1=w1, kw=kw:
                      e.activation(wsb[:, c * cols + w0:c * cols + w1],
                                   st[:, 0:w1 - w0], AF.Copy, **kw),
                      waits=[f"{tag}_wl{n}"], mark=f"{tag}_cast{n}")
                n += 1
            em.marks[f"{tag}_w{wi}c{c}"] = em.marks[f"{tag}_cast{n - 1}"]
        em.marks[f"{tag}_w{wi}"] = em.marks[f"{tag}_cast{n - 1}"]



def build_pass_l0ffn(em, sb, ps, cd, x_in, x_out):
    ident = cd["ident_s"]
    P = "p2"
    wr = sb(P + "_wr", [128, NCH * D], BF16)
    wk = sb(P + "_wk", [128, NCH * D], BF16)
    wv = sb(P + "_wv", [128, NCH * D], BF16)
    wstage = [sb(P + f"_wst{i}", [128, D], F32) for i in range(2)]
    mur = sb(P + "_mur", [128, NCH], F32)
    muk = sb(P + "_muk", [128, NCH], F32)
    sc = StripeCtx(em, sb, ps, P, carry=True)
    xrT = sb(P + "_xr", [128, NCH * 512], BF16)
    xkT = sb(P + "_xk", [128, NCH * 512], BF16)
    dT = sb(P + "_d", [128, NCH * 512], BF16)
    rsig = sb(P + "_rsig", [128, TPS * D], F32)
    kact = sb(P + "_kact", [128, NCH * 512], BF16)
    krelu = [sb(P + f"_krelu{i}", [128, 512], F32) for i in range(2)]
    outb = sb(P + "_out", [128, TPS * D], F32)
    lno = [LNBufs(sb, P + f"_lo{i}") for i in range(2)]
    pj = BankPool([ps(P + f"_pj{i}", [128, 512], F32) for i in range(4)])

    load_weights(em, wstage, [wr, wk, wv],
                 [cd["l0f_wrT"], cd["l0f_wkT"], cd["l0f_wvT"]], P)
    em.dma("sp", lambda e: e.dma_start(mur[:], cd["l0f_mur"]),
           inc=P + "_mi", mark=P + "_mul")
    em.dma("sp", lambda e: e.dma_start(muk[:], cd["l0f_muk"]),
           inc=P + "_mi", waits=[P + "_mul"], mark=P + "_mu")

    for s in range(NSTRIPE):
        tag = f"{P}s{s}"
        war = []
        if s >= 2:
            war.append(f"{P}s{s - 2}_mix")
        if s >= 1:
            war.append(f"{P}s{s - 1}_carry")
        bi = stripe_prologue(em, sc, s, x_in, ident, xnt_war=war)
        if s % (NSTRIPE // 2) == 0:
            em.op("dve", lambda e, bi=bi: e.memset(
                sc.chunk3d(bi)[:, :, 0:1], 0.0),
                waits=[f"{tag}_xnT"], mark=f"{tag}_carry")
        else:
            pb = 1 - bi
            em.op("dve", lambda e, bi=bi, pb=pb: e.tensor_copy(
                sc.chunk3d(bi)[:, :, 0:1], sc.chunk3d(pb)[:, :, 512:513]),
                waits=[f"{tag}_xnT"], mark=f"{tag}_carry")
        mix_war = []
        if s >= 1:
            mix_war = [f"{P}s{s - 1}_rmm{TPS - 1}_1", f"{P}s{s - 1}_kmm{NCH - 1}"]
        for c in range(NCH):
            em.op("dve", lambda e, c=c, bi=bi: e.tensor_tensor(
                dT[:, c * 512:(c + 1) * 512], sc.chunk(bi, c),
                sc.chunk_shift(bi, c)[:, 0:512], ALU.subtract),
                waits=([f"{tag}_carry", P + "_mu"] + mix_war) if c == 0 else ())
            em.op("dve", lambda e, c=c, bi=bi: e.scalar_tensor_tensor(
                xrT[:, c * 512:(c + 1) * 512], dT[:, c * 512:(c + 1) * 512],
                mur[:, c:c + 1], sc.chunk_shift(bi, c)[:, 0:512],
                ALU.mult, ALU.add))
            em.op("dve", lambda e, c=c, bi=bi: e.scalar_tensor_tensor(
                xkT[:, c * 512:(c + 1) * 512], dT[:, c * 512:(c + 1) * 512],
                muk[:, c:c + 1], sc.chunk_shift(bi, c)[:, 0:512],
                ALU.mult, ALU.add),
                mark=(f"{tag}_mix" if c == NCH - 1 else None))
        for t in range(TPS):
            for eh in range(2):
                bank, fw, bidx = pj.acquire()
                for c in range(NCH):
                    em.op("pe", lambda e, t=t, eh=eh, c=c, bank=bank: e.matmul(
                        bank, xrT[:, c * 512 + t * 128:c * 512 + t * 128 + 128],
                        wr[:, c * D + eh * 512:c * D + eh * 512 + 512],
                        start=(c == 0), stop=(c == NCH - 1)),
                        waits=([f"{tag}_mix", P + "_w0"] +
                               ([fw] if fw else [])) if c == 0 else (),
                        mark=(f"{tag}_rmm{t}_{eh}" if c == NCH - 1 else None))
                rs_war = ([f"{P}s{s - 1}_kv{t}_{eh}"] if s >= 1 else [])
                em.op("act", lambda e, t=t, eh=eh, bank=bank: e.activation(
                    rsig[:, t * D + eh * 512:t * D + eh * 512 + 512], bank,
                    AF.Sigmoid),
                    waits=[f"{tag}_rmm{t}_{eh}"] + rs_war,
                    mark=f"{tag}_rs{t}_{eh}")
                pj.release(bidx, f"{tag}_rs{t}_{eh}")
        for co in range(NCH):
            bank, fw, bidx = pj.acquire()
            for c in range(NCH):
                em.op("pe", lambda e, co=co, c=c, bank=bank: e.matmul(
                    bank, wk[:, c * D + co * 128:c * D + co * 128 + 128],
                    xkT[:, c * 512:(c + 1) * 512],
                    start=(c == 0), stop=(c == NCH - 1)),
                    waits=([P + "_w1"] + ([fw] if fw else [])) if c == 0 else (),
                    mark=(f"{tag}_kmm{co}" if c == NCH - 1 else None))
            kr = krelu[co % 2]
            kr_war = [f"{tag}_ksq{co - 2}"] if co >= 2 else (
                [f"{P}s{s - 1}_ksq{NCH - 2 + (co % 2) * 0 + co}"]
                if False else [])
            em.op("dve", lambda e, bank=bank, kr=kr: e.tensor_scalar(
                kr[:], bank, 0.0, None, ALU.max),
                waits=[f"{tag}_kmm{co}"] + kr_war, mark=f"{tag}_krel{co}")
            pj.release(bidx, f"{tag}_krel{co}")
            ks_war = ([f"{P}s{s - 1}_vmm{TPS - 1}_1"]
                      if (s >= 1 and co == 0) else [])
            em.op("act", lambda e, co=co, kr=kr: e.activation(
                kact[:, co * 512:(co + 1) * 512], kr[:], AF.Square),
                waits=[f"{tag}_krel{co}"] + ks_war, mark=f"{tag}_ksq{co}")
        ob = outb
        for t in range(TPS):
            for eh in range(2):
                bank, fw, bidx = pj.acquire()
                for c in range(NCH):
                    em.op("pe", lambda e, t=t, eh=eh, c=c, bank=bank: e.matmul(
                        bank, kact[:, c * 512 + t * 128:c * 512 + t * 128 + 128],
                        wv[:, c * D + eh * 512:c * D + eh * 512 + 512],
                        start=(c == 0), stop=(c == NCH - 1)),
                        waits=[f"{tag}_ksq{c}", P + "_w2"] +
                              ([fw] if (c == 0 and fw) else []),
                        mark=(f"{tag}_vmm{t}_{eh}" if c == NCH - 1 else None))
                kvw = [f"{tag}_vmm{t}_{eh}", f"{tag}_rs{t}_{eh}"]
                if s >= 1 and t == 0 and eh == 0:
                    kvw.append(f"{P}s{s - 1}_store")
                em.op("dve", lambda e, t=t, eh=eh, bank=bank, ob=ob:
                      e.tensor_tensor(
                          ob[:, t * D + eh * 512:t * D + eh * 512 + 512],
                          bank,
                          rsig[:, t * D + eh * 512:t * D + eh * 512 + 512],
                          ALU.mult),
                      waits=kvw,
                      mark=f"{tag}_kv{t}_{eh}")
                pj.release(bidx, f"{tag}_kv{t}_{eh}")
        ln_post_add(em, lno, s, outb, sc.xs[s % len(sc.xs)], x_out, tag,
                    waits_first=[f"{tag}_kv{TPS - 1}_1"])


def _p1_passthrough(em, sc, bi, s, tag, x_out, wait_mark):
    em.dma("pool", lambda e: e.dma_start(
        x_out[s * STRIPE:(s + 1) * STRIPE, :]
        .rearrange("(t p) d -> p t d", p=128),
        sc.xs[bi].rearrange("p (t d) -> p t d", t=TPS)),
        inc=f"st{s % 2}", mark=f"{tag}_store", waits=[wait_mark])



def build_pass_l0attn(em, sb, ps, cd, x_in, x_out):
    ident = cd["ident_s"]
    P = "p1"
    wq = sb(P + "_wq", [128, NCH * D], BF16)
    wk = sb(P + "_wk", [128, NCH * D], BF16)
    wv = sb(P + "_wv", [128, NCH * D], BF16)
    wo = sb(P + "_wo", [128, NCH * D], BF16)
    wstage = [sb(P + f"_wst{i}", [128, D], F32) for i in range(2)]
    bq = sb(P + "_bq", [128, NCH], F32)
    bk = sb(P + "_bk", [128, NCH], F32)
    bvb = sb(P + "_bvb", [128, D], BF16)
    bob = sb(P + "_bob", [128, D], BF16)
    bstage = wstage[1]
    mask4 = sb(P + "_mask4", [128, 512], BF16)
    sc = StripeCtx(em, sb, ps, P, carry=False)
    qf = sb(P + "_qf", [128, NCH * 512], BF16)
    kze = sb(P + "_kze", [128, NCH * 512], BF16)
    kzo = sb(P + "_kzo", [128, NCH * 512], BF16)
    vst = sb(P + "_vst", [128, TPS * D], BF16)
    ktm = sb(P + "_ktm", [128, TPS * D], BF16)
    sT = sb(P + "_sT", [128, 2 * 512], BF16)
    stf = sb(P + "_stf", [128, H * DH], F32)
    stb = sb(P + "_stb", [128, H * DH], BF16)
    ofm = sb(P + "_ofm", [128, NCH * 512], BF16)
    outb = sb(P + "_out", [128, TPS * D], F32)
    lno = [LNBufs(sb, P + f"_lo{i}") for i in range(2)]
    pj = BankPool([ps(P + f"_pj{i}", [128, 512], F32) for i in range(4)])
    po = [ps(P + f"_po{i}", [128, 512], F32) for i in range(2)]

    load_weights(em, wstage, [wq, wk, wv, wo],
                 [cd["l0a_wqT"], cd["l0a_wkT"], cd["l0a_wvT"], cd["l0a_woT"]],
                 P)
    em.dma("sp", lambda e: e.dma_start(bq[:], cd["l0a_bq"]), inc=P + "_mi",
           mark=P + "_bql")
    em.dma("sp", lambda e: e.dma_start(bk[:], cd["l0a_bk"]), inc=P + "_mi",
           waits=[P + "_bql"], mark=P + "_bqk")
    em.dma("sp", lambda e: e.dma_start(bstage[:], cd["l0a_bvb"]),
           inc=P + "_mi", waits=[P + "_bqk", P + "_w3"], mark=P + "_bvl")
    em.op("act", lambda e: e.activation(bvb[:], bstage[:], AF.Copy),
          waits=[P + "_bvl"], mark=P + "_bv")
    em.dma("sp", lambda e: e.dma_start(bstage[:], cd["l0a_bob"]),
           inc=P + "_mi", waits=[P + "_bv"], mark=P + "_bol")
    em.op("act", lambda e: e.activation(bob[:], bstage[:], AF.Copy),
          waits=[P + "_bol"], mark=P + "_bo")
    em.dma("sp", lambda e: e.dma_start(mask4[:], cd["l0mask4"]),
           inc=P + "_mi", waits=[P + "_bol"], mark=P + "_msk")
    em.op("dve", lambda e: e.memset(kze[64:128, :], 0.0))
    em.op("dve", lambda e: e.memset(kzo[0:64, :], 0.0), mark=P + "_kz0")

    for s in range(NSTRIPE):
        tag = f"{P}s{s}"
        bi = stripe_prologue(em, sc, s, x_in, ident)
        for co in range(NCH):
            bank, fw, bidx = pj.acquire()
            for c in range(NCH):
                em.op("pe", lambda e, co=co, c=c, bank=bank, bi=bi: e.matmul(
                    bank[:],
                    wq[:, c * D + co * 128:c * D + co * 128 + 128],
                    sc.chunk(bi, c),
                    start=(c == 0), stop=(c == NCH - 1)),
                    waits=([f"{tag}_xnT", P + "_w0", P + "_bqk"] +
                           ([fw] if fw else [])) if c == 0 else (),
                    mark=(f"{tag}_qmm{co}" if c == NCH - 1 else None))
            em.op("act", lambda e, co=co, bank=bank: e.activation(
                qf[:, co * 512:(co + 1) * 512], bank[:],
                AF.Identity, bias=bq[:, co:co + 1]),
                waits=[f"{tag}_qmm{co}"], mark=f"{tag}_qf{co}")
            pj.release(bidx, f"{tag}_qf{co}")
        for co in range(NCH):
            bank, fw, bidx = pj.acquire()
            for c in range(NCH):
                em.op("pe", lambda e, co=co, c=c, bank=bank, bi=bi: e.matmul(
                    bank[:],
                    wk[:, c * D + co * 128:c * D + co * 128 + 128],
                    sc.chunk(bi, c),
                    start=(c == 0), stop=(c == NCH - 1)),
                    waits=([P + "_w1", P + "_kz0"] +
                           ([fw] if fw else [])) if c == 0 else (),
                    mark=(f"{tag}_kmm{co}" if c == NCH - 1 else None))
            em.op("act", lambda e, co=co, bank=bank: e.activation(
                kze[0:64, co * 512:(co + 1) * 512], bank[0:64, :],
                AF.Identity, bias=bk[0:64, co:co + 1]),
                waits=[f"{tag}_kmm{co}"], mark=f"{tag}_kfe{co}")
            em.op("act", lambda e, co=co, bank=bank: e.activation(
                kzo[64:128, co * 512:(co + 1) * 512], bank[64:128, :],
                AF.Identity, bias=bk[64:128, co:co + 1]),
                waits=[f"{tag}_kmm{co}"], mark=f"{tag}_kfo{co}")
            pj.release(bidx, f"{tag}_kfo{co}")
        for t in range(TPS):
            for eh in range(2):
                bank, fw, bidx = pj.acquire()
                for c in range(NCH):
                    em.op("pe", lambda e, t=t, eh=eh, c=c, bank=bank,
                          bi=bi: e.matmul(
                        bank[:],
                        sc.chunk(bi, c)[:, t * 128:t * 128 + 128],
                        wv[:, c * D + eh * 512:c * D + eh * 512 + 512],
                        start=(c == 0), stop=(c == NCH - 1)),
                        waits=([f"{tag}_xnT", P + "_w2", P + "_bv"] +
                               ([fw] if fw else [])) if c == 0 else (),
                        mark=(f"{tag}_vmm{t}_{eh}" if c == NCH - 1 else None))
                em.op("dve", lambda e, t=t, eh=eh, bank=bank: e.tensor_tensor(
                    vst[:, t * D + eh * 512:t * D + eh * 512 + 512], bank[:],
                    bvb[:, eh * 512:eh * 512 + 512], ALU.add),
                    waits=[f"{tag}_vmm{t}_{eh}"], mark=f"{tag}_vf{t}_{eh}")
                pj.release(bidx, f"{tag}_vf{t}_{eh}")
        for t in range(TPS):
            pa, pb2 = sc.ptr[0], sc.ptr[1]
            war = ([f"{tag}_ktev{t - 1}", f"{tag}_kteA{t - 1}"] if t >= 1
                   else [f"{tag}_trev{TPS - 1}"])
            for p in range(NCH):
                em.op("pe", lambda e, t=t, p=p, pa=pa: e.matmul(
                    pa[:, p * 128:(p + 1) * 128],
                    kze[:, p * 512 + t * 128:p * 512 + t * 128 + 128],
                    ident, is_transpose=True, start=True, stop=True),
                    waits=([f"{tag}_kfe{NCH - 1}", f"{tag}_kfo{NCH - 1}"] + war)
                    if p == 0 else (),
                    mark=(f"{tag}_ktrA{t}" if p == NCH - 1 else None))
            for p in range(NCH):
                em.op("pe", lambda e, t=t, p=p, pb2=pb2: e.matmul(
                    pb2[:, p * 128:(p + 1) * 128],
                    kzo[:, p * 512 + t * 128:p * 512 + t * 128 + 128],
                    ident, is_transpose=True, start=True, stop=True),
                    mark=(f"{tag}_ktrB{t}" if p == NCH - 1 else None))
            em.op("act", lambda e, t=t, pa=pa: e.activation(
                ktm[:, t * D:(t + 1) * D], pa[:, 0:D], AF.Copy),
                waits=[f"{tag}_ktrA{t}"], mark=f"{tag}_kteA{t}")
            em.op("dve", lambda e, t=t, pb2=pb2: e.tensor_tensor(
                ktm[:, t * D:(t + 1) * D], ktm[:, t * D:(t + 1) * D],
                pb2[:, 0:D], ALU.add),
                waits=[f"{tag}_ktrB{t}", f"{tag}_kteA{t}"],
                mark=f"{tag}_ktev{t}")
        em.marks[f"{tag}_trev2"] = em.marks[f"{tag}_kteA{TPS - 1}"]
        em.marks[f"{tag}_trev3"] = em.marks[f"{tag}_ktev{TPS - 1}"]
        for t in range(TPS):
            j = (s % (NSTRIPE // 2)) * TPS + t
            ctag = f"{tag}c{t}"
            if j == 0:
                em.op("dve", lambda e: e.memset(stf[:], 0.0),
                      waits=([f"{P}s{s - 1}c{TPS - 1}_sadd"] if s > 0 else []),
                      mark=f"{ctag}_sadd")
            else:
                ptag = f"{tag}c{t - 1}" if t > 0 else f"{P}s{s - 1}c{TPS - 1}"
                em.op("act", lambda e: e.activation(stb[:], stf[:], AF.Copy),
                      waits=[f"{ptag}_sadd"], mark=f"{ctag}_scp")
            for g in range(4):
                sg = sT[:, (g % 2) * 512:(g % 2) * 512 + 512]
                bank, fw, bidx = pj.acquire()
                for hh in range(4):
                    h = 4 * g + hh
                    p = h // 2
                    kz = kze if h % 2 == 0 else kzo
                    em.op("pe", lambda e, p=p, hh=hh, kz=kz, bank=bank,
                          t=t: e.matmul(
                              bank[:, hh * 128:hh * 128 + 128],
                              kz[:, p * 512 + t * 128:p * 512 + t * 128 + 128],
                              qf[:, p * 512 + t * 128:p * 512 + t * 128 + 128],
                              start=True, stop=True),
                          waits=([f"{tag}_ktev{t}", P + "_msk"] +
                                 ([fw] if fw else [])) if hh == 0 else (),
                          mark=(f"{ctag}_smm{g}" if hh == 3 else None))
                em.op("dve", lambda e, sg=sg, bank=bank: e.tensor_tensor(
                    sg, bank[:], mask4[:], ALU.mult),
                    waits=[f"{ctag}_smm{g}"], mark=f"{ctag}_msk{g}")
                pj.release(bidx, f"{ctag}_msk{g}")
                for hh in range(4):
                    h = 4 * g + hh
                    p = h // 2
                    ob_bank = po[p // 4]
                    col = (p % 4) * 128
                    cb = (h % 2) * 64
                    em.op("pe", lambda e, h=h, col=col, cb=cb, sg=sg,
                          hh=hh, t=t, ob_bank=ob_bank, j=j: e.matmul(
                              ob_bank[cb:cb + 64, col:col + 128],
                              vst[:, t * D + h * DH:t * D + h * DH + 64],
                              sg[:, hh * 128:hh * 128 + 128],
                              start=True, stop=(j == 0),
                              tile_position=(0, cb)),
                          waits=([f"{ctag}_msk{g}", f"{tag}_vf{t}_0",
                                  f"{tag}_vf{t}_1"] +
                                 ([f"{tag}c{t - 1}_oev{p // 4}"]
                                  if t >= 1 else
                                  ([f"{P}s{s - 1}c{TPS - 1}_oev{p // 4}"]
                                   if s >= 1 else [])))
                          if (hh == 0 and g == 0) else
                          ([f"{ctag}_msk{g}"] if hh == 0 else ()),
                          mark=(f"{ctag}_omm{g}"
                                if (j == 0 and hh == 3) else None))
                    if j > 0:
                        em.op("pe", lambda e, h=h, col=col, cb=cb, t=t,
                              ob_bank=ob_bank: e.matmul(
                                  ob_bank[cb:cb + 64, col:col + 128],
                                  stb[:, h * DH:(h + 1) * DH],
                                  qf[:, (h // 2) * 512 + t * 128:
                                     (h // 2) * 512 + t * 128 + 128],
                                  start=False, stop=True,
                                  tile_position=(0, cb)),
                              waits=[f"{ctag}_scp"] if hh == 0 else (),
                              mark=(f"{ctag}_omm{g}" if hh == 3 else None))
            bank, fw, bidx = pj.acquire()
            for h in range(H):
                p = h // 2
                mb = (h % 2) * 64
                em.op("pe", lambda e, h=h, p=p, mb=mb, t=t, bank=bank:
                      e.matmul(
                          bank[mb:mb + 64, p * 64:(p + 1) * 64],
                          ktm[:, t * D + p * 128 + mb:t * D + p * 128 + mb + 64],
                          vst[:, t * D + h * DH:t * D + h * DH + 64],
                          start=True, stop=True, tile_position=(0, mb)),
                      waits=([f"{tag}_ktev{t}"] + ([fw] if fw else []))
                      if h == 0 else (),
                      mark=(f"{ctag}_dmm" if h == H - 1 else None))
            em.op("dve", lambda e, bank=bank: e.tensor_tensor(
                stf[0:64, :].rearrange("p (h c) -> p h c", c=DH)[:, 0::2, :],
                stf[0:64, :].rearrange("p (h c) -> p h c", c=DH)[:, 0::2, :],
                bank[0:64, :].rearrange("p (h c) -> p h c", c=DH),
                ALU.add),
                waits=[f"{ctag}_dmm"] + ([f"{ctag}_scp"] if j > 0 else []))
            em.op("dve", lambda e, bank=bank: e.tensor_tensor(
                stf[64:128, :].rearrange("p (h c) -> p h c", c=DH)[:, 1::2, :],
                stf[64:128, :].rearrange("p (h c) -> p h c", c=DH)[:, 1::2, :],
                bank[64:128, :].rearrange("p (h c) -> p h c", c=DH),
                ALU.add),
                mark=f"{ctag}_sadd")
            pj.release(bidx, f"{ctag}_sadd")
            for b in range(2):
                em.op("act", lambda e, b=b, t=t: e.activation(
                    ofm[:, 0:NCH * 512].rearrange("p (c w) -> p c w", c=NCH)
                    [:, 4 * b:4 * b + 4, t * 128:t * 128 + 128],
                    po[b][:, 0:512].rearrange("p (c w) -> p c w", c=4),
                    AF.Copy),
                    waits=[f"{ctag}_omm{2 * b}", f"{ctag}_omm{2 * b + 1}"],
                    mark=f"{ctag}_oev{b}")
            ob = outb
            for eh in range(2):
                bank, fw, bidx = pj.acquire()
                for co in range(NCH):
                    em.op("pe", lambda e, t=t, eh=eh, co=co, bank=bank:
                          e.matmul(
                              bank[:],
                              ofm[:, co * 512 + t * 128:co * 512 + t * 128 + 128],
                              wo[:, co * D + eh * 512:co * D + eh * 512 + 512],
                              start=(co == 0), stop=(co == NCH - 1)),
                          waits=([f"{ctag}_oev0", f"{ctag}_oev1",
                                  P + "_w3", P + "_bo"] +
                                 ([fw] if fw else [])) if co == 0 else (),
                          mark=(f"{ctag}_wmm{eh}" if co == NCH - 1 else None))
                wfw = [f"{ctag}_wmm{eh}"]
                if s >= 1 and t == 0 and eh == 0:
                    wfw.append(f"{tag[:-1]}{s - 1}_store"
                               if False else f"{P}s{s - 1}_store")
                em.op("dve", lambda e, t=t, eh=eh, bank=bank, ob=ob:
                      e.tensor_tensor(
                          ob[:, t * D + eh * 512:t * D + eh * 512 + 512],
                          bank[:], bob[:, eh * 512:eh * 512 + 512], ALU.add),
                      waits=wfw,
                      mark=f"{ctag}_wf{eh}")
                pj.release(bidx, f"{ctag}_wf{eh}")
        ln_post_add(em, lno, s, outb, sc.xs[s % len(sc.xs)], x_out, tag,
                    waits_first=[f"{tag}c{TPS - 1}_wf1"])



def build_pass_l1attn(em, sb, ps, cd, x_in, x_out):
    ident = cd["ident_s"]
    P = "p3"
    W3 = 3 * D
    wqkv = sb(P + "_wqkv", [128, NCH * W3], BF16)
    wout = sb(P + "_wout", [128, NCH * D], BF16)
    outb = sb(P + "_outb", [128, TPS * D], F32)
    wstage = [outb[:, 0:W3 // 2], outb[:, W3 // 2:W3]]
    cosT = sb(P + "_cos", [128, SEQLEN], BF16)
    ssT = sb(P + "_ss", [128, SEQLEN], BF16)
    lmask = sb(P + "_lm", [128, 4 * BS], BF16)
    pj = BankPool([ps(P + f"_pj{i}", [128, 512], F32) for i in range(2)])
    sp2 = [ps(P + "_spa", [128, 1024], F32), ps(P + "_spb", [128, 1024], F32)]
    po = ps(P + "_po", [128, 1024], F32)
    ptr_aps = [po.bitcast(BF16)[:, 0:NCH * 128],
               sp2[1].bitcast(BF16)[:, 0:NCH * 128]]
    sc = StripeCtx(em, sb, ps, P, carry=False, nxs=1, nxnt=1,
                   ptr_aps=ptr_aps)
    qraw = sb(P + "_qraw", [128, NCH * 512], BF16)
    kraw = sb(P + "_kraw", [128, NCH * 512], BF16)
    qrot = sb(P + "_qrot", [128, NCH * 512], BF16)
    kze = sb(P + "_kze", [128, NCH * 512], BF16)
    kzo = sb(P + "_kzo", [128, NCH * 512], BF16)
    tmp1 = outb[:, 3072:3328].bitcast(BF16)
    tmp2 = outb[:, 3328:3584].bitcast(BF16)
    tmp3 = outb[:, 3584:3840].bitcast(BF16)
    vst = sb(P + "_vst", [128, TPS * H * 128], BF16)
    ptile = [sb(P + f"_pt{i}", [128, 1024], BF16) for i in range(2)]
    lns = sb(P + "_lns", [128, 1024], F32)
    rcp = sb(P + "_rcp", [128, 1024], F32)
    ofm = sb(P + "_ofm", [128, NCH * 512], BF16)
    lno = [LNBufs(sb, P + f"_lo{i}") for i in range(2)]

    load_weights(em, wstage, [wqkv, wout],
                 [cd["l1a_wqkvT"], cd["l1a_woutT"]], P)
    em.dma("sp", lambda e: e.dma_start(cosT[:], cd["l1cos"]), inc=P + "_mi",
           mark=P + "_c0")
    em.dma("sp", lambda e: e.dma_start(ssT[:], cd["l1sin"]), inc=P + "_mi",
           waits=[P + "_c0"], mark=P + "_c1")
    em.dma("sp", lambda e: e.dma_start(lmask[:], cd["l1mask"]), inc=P + "_mi",
           waits=[P + "_c1"], mark=P + "_c2")
    em.op("dve", lambda e: e.memset(kze[64:128, :], 0.0))
    em.op("dve", lambda e: e.memset(kzo[0:64, :], 0.0))
    em.op("dve", lambda e: e.memset(
        vst.rearrange("p (s c) -> p s c", c=128)[:, :, 64:128], 1.0),
        mark=P + "_init")

    chain = {"exp": [None, None], "omm": [None, None],
             "norm": [None, None], "ofmd": None}

    for s in range(NSTRIPE):
        tag = f"{P}s{s}"
        war = [f"{P}s{s - 1}_pedone"] if s >= 1 else []
        xw = None
        pw = []
        if s >= 1:
            pw = [c for c in (chain["norm"][0], chain["norm"][1],
                              chain["exp"][1]) if c]
        bi = stripe_prologue(em, sc, s, x_in, ident, xnt_war=war,
                             x_extra_wait=xw, ptr_war=pw)

        def proj_fm(nm, dst, woff):
            for co in range(NCH):
                bank, fw, bidx = pj.acquire()
                for c in range(NCH):
                    em.op("pe", lambda e, co=co, c=c, bank=bank, bi=bi,
                          woff=woff: e.matmul(
                              bank[:],
                              wqkv[:, c * W3 + woff + co * 128:
                                   c * W3 + woff + co * 128 + 128],
                              sc.chunk(bi, c),
                              start=(c == 0), stop=(c == NCH - 1)),
                          waits=([f"{tag}_xnT", P + "_w0", P + "_init"] +
                                 ([fw] if fw else [])) if c == 0 else (),
                          mark=(f"{tag}_{nm}mm{co}" if c == NCH - 1
                                else None))
                evw = [f"{tag}_{nm}mm{co}"]
                if s >= 1:
                    evw.append(f"{P}s{s - 1}_{nm}rot{co}")
                em.op("act", lambda e, co=co, bank=bank, dst=dst:
                      e.activation(dst[:, co * 512:(co + 1) * 512], bank[:],
                                   AF.Copy),
                      waits=evw, mark=f"{tag}_{nm}ev{co}")
                pj.release(bidx, f"{tag}_{nm}ev{co}")
        proj_fm("q", qraw, 0)
        proj_fm("k", kraw, D)
        for t in range(TPS):
            for eh in range(2):
                bank, fw, bidx = pj.acquire()
                for c in range(NCH):
                    em.op("pe", lambda e, t=t, eh=eh, c=c, bank=bank,
                          bi=bi: e.matmul(
                              bank[:],
                              sc.chunk(bi, c)[:, t * 128:t * 128 + 128],
                              wqkv[:, c * W3 + 2 * D + eh * 512:
                                   c * W3 + 2 * D + eh * 512 + 512],
                              start=(c == 0), stop=(c == NCH - 1)),
                          waits=([fw] if (c == 0 and fw) else ()),
                          mark=(f"{tag}_vmm{t}_{eh}" if c == NCH - 1
                                else None))
                vw = [f"{tag}_vmm{t}_{eh}"]
                if s >= 1 and t == 0 and eh == 0:
                    vw.append(f"{P}s{s - 1}_odone")
                em.op("act", lambda e, t=t, eh=eh, bank=bank: e.activation(
                    vst.rearrange("p (tt h c) -> p tt h c", tt=TPS, h=H)
                    [:, t, 8 * eh:8 * eh + 8, 0:64],
                    bank[:].rearrange("p (h c) -> p h c", c=64),
                    AF.Copy),
                    waits=vw, mark=f"{tag}_vev{t}_{eh}")
                pj.release(bidx, f"{tag}_vev{t}_{eh}")
        off = (s % (NSTRIPE // 2)) * 512
        for c in range(NCH):
            em.op("dve", lambda e, c=c, off=off: e.tensor_tensor(
                qrot[:, c * 512:(c + 1) * 512],
                qraw[:, c * 512:(c + 1) * 512],
                cosT[:, off:off + 512], ALU.mult),
                waits=[f"{tag}_qev{c}", P + "_c2"] if c == 0
                else [f"{tag}_qev{c}"])
            for sq4 in range(4):
                sq = sq4 * 32
                tw = []
                if sq4 == 0:
                    if c > 0:
                        tw = [f"{tag}_qc{c - 1}"]
                    elif s >= 1:
                        tw = [f"{P}s{s - 1}_kc{NCH - 1}"]
                em.op("dve", lambda e, c=c, sq=sq, off=off:
                      e.tensor_tensor(
                          tmp1[sq:sq + 32, :],
                          qraw[sq:sq + 32, c * 512:(c + 1) * 512],
                          ssT[sq:sq + 32, off:off + 512],
                          ALU.mult),
                      waits=tw,
                      mark=(f"{tag}_qq{c}" if sq4 == 3 else None))
            for qt in range(4):
                sq = (qt ^ 1) * 32
                em.op("act", lambda e, qt=qt, sq=sq: e.activation(
                    tmp2[qt * 32:qt * 32 + 32, :], tmp1[sq:sq + 32, :],
                    AF.Copy),
                    waits=[f"{tag}_qq{c}"] if qt == 0 else (),
                    mark=(f"{tag}_qc{c}" if qt == 3 else None))
            em.op("dve", lambda e, c=c: e.tensor_tensor(
                qrot[:, c * 512:(c + 1) * 512],
                qrot[:, c * 512:(c + 1) * 512], tmp2[:], ALU.add),
                waits=[f"{tag}_qc{c}"],
                mark=f"{tag}_qrot{c}")
        for c in range(NCH):
            em.op("dve", lambda e, c=c, off=off: e.tensor_tensor(
                tmp3[:], kraw[:, c * 512:(c + 1) * 512],
                cosT[:, off:off + 512], ALU.mult),
                waits=[f"{tag}_kev{c}"])
            for sq4 in range(4):
                sq = sq4 * 32
                tw = []
                if sq4 == 0:
                    tw = [f"{tag}_kc{c - 1}" if c > 0
                          else f"{tag}_qc{NCH - 1}"]
                em.op("dve", lambda e, c=c, sq=sq, off=off:
                      e.tensor_tensor(
                          tmp1[sq:sq + 32, :],
                          kraw[sq:sq + 32, c * 512:(c + 1) * 512],
                          ssT[sq:sq + 32, off:off + 512],
                          ALU.mult),
                      waits=tw,
                      mark=(f"{tag}_kq{c}" if sq4 == 3 else None))
            for qt in range(4):
                sq = (qt ^ 1) * 32
                em.op("act", lambda e, qt=qt, sq=sq: e.activation(
                    tmp2[qt * 32:qt * 32 + 32, :], tmp1[sq:sq + 32, :],
                    AF.Copy),
                    waits=[f"{tag}_kq{c}"] if qt == 0 else (),
                    mark=(f"{tag}_kc{c}" if qt == 3 else None))
            em.op("dve", lambda e, c=c: e.tensor_tensor(
                tmp3[:], tmp3[:], tmp2[:], ALU.add),
                waits=[f"{tag}_kc{c}"])
            em.op("dve", lambda e, c=c: e.tensor_copy(
                kze[0:64, c * 512:(c + 1) * 512], tmp3[0:64, :]))
            em.op("dve", lambda e, c=c: e.tensor_copy(
                kzo[64:128, c * 512:(c + 1) * 512], tmp3[64:128, :]),
                mark=f"{tag}_krot{c}")
        em.marks[f"{tag}_rope"] = em.marks[f"{tag}_krot{NCH - 1}"]
        for b in range(2):
            btag = f"{tag}b{b}"
            for gp in range(8):
                pt = ptile[gp % 2]
                sp = sp2[gp % 2]
                poh = po[:, (gp % 2) * 512:(gp % 2) * 512 + 512]
                for hi in range(2):
                    h = 2 * gp + hi
                    co = h // 2
                    kz = kze if h % 2 == 0 else kzo
                    for kt in range(2):
                        ws = []
                        if hi == 0 and kt == 0:
                            ws = [f"{tag}_rope"]
                            if chain["exp"][gp % 2] is not None:
                                ws.append(chain["exp"][gp % 2])
                        em.op("pe", lambda e, co=co, kz=kz, kt=kt, b=b,
                              hi=hi, sp=sp: e.matmul(
                                  sp[:, hi * 512 + kt * 256:
                                     hi * 512 + kt * 256 + 256],
                                  kz[:, co * 512 + b * 256 + kt * 128:
                                     co * 512 + b * 256 + kt * 128 + 128],
                                  qrot[:, co * 512 + b * 256:
                                       co * 512 + b * 256 + 256],
                                  start=True, stop=True),
                              waits=ws,
                              mark=(f"{btag}g{gp}_smm"
                                    if (hi == 1 and kt == 1) else None))
                ptw = [f"{btag}g{gp}_smm"]
                if chain["omm"][gp % 2] is not None:
                    ptw.append(chain["omm"][gp % 2])
                em.op("act", lambda e, pt=pt, sp=sp: e.activation(
                    pt[:], sp[:], AF.Exp, scale=DH ** (-0.5)),
                    waits=ptw, mark=f"{btag}g{gp}_exp")
                chain["exp"][gp % 2] = f"{btag}g{gp}_exp"
                em.op("dve", lambda e, pt=pt: e.tensor_tensor(
                    pt[:], pt[:], lmask[:], ALU.mult),
                    waits=[f"{btag}g{gp}_exp"], mark=f"{btag}g{gp}_msk")
                for hi in range(2):
                    h = 2 * gp + hi
                    for kt in range(2):
                        ws = [f"{btag}g{gp}_msk", f"{tag}_vev{TPS - 1}_1"]
                        if hi == 0 and kt == 0:
                            if chain["norm"][gp % 2] is not None:
                                ws.append(chain["norm"][gp % 2])
                        em.op("pe", lambda e, h=h, hi=hi, kt=kt, b=b, pt=pt,
                              poh=poh: e.matmul(
                                  poh[:, hi * 256:hi * 256 + 256],
                                  vst[:, (2 * b + kt) * H * 128 + h * 128:
                                      (2 * b + kt) * H * 128 + h * 128 + 128],
                                  pt[:, hi * 512 + kt * 256:
                                     hi * 512 + kt * 256 + 256],
                                  start=(kt == 0), stop=(kt == 1)),
                              waits=ws,
                              mark=(f"{btag}g{gp}_omm"
                                    if (hi == 1 and kt == 1) else None))
                chain["omm"][gp % 2] = f"{btag}g{gp}_omm"
                ftag = f"{btag}g{gp}"
                hb = (gp % 2) * 512
                lw = [f"{btag}g{gp}_omm"]
                if chain["ofmd"] is not None:
                    lw.append(chain["ofmd"])
                em.op("act", lambda e, hb=hb: e.activation(
                    lns[64:128, hb:hb + 512], po[64:128, hb:hb + 512],
                    AF.Ln),
                    waits=lw, mark=f"{ftag}_ln")
                em.op("act", lambda e, hb=hb: e.activation(
                    rcp[0:64, hb:hb + 512], lns[64:128, hb:hb + 512],
                    AF.Exp, scale=-1.0),
                    waits=[f"{ftag}_ln"], mark=f"{ftag}_rcp")
                em.op("dve", lambda e, hb=hb: e.tensor_tensor(
                    lns[0:64, hb:hb + 512].bitcast(BF16)[:, 0:1024]
                    [:, 0:512] if False else
                    lns[0:64, :].bitcast(BF16)[:, hb:hb + 512],
                    po[0:64, hb:hb + 512], rcp[0:64, hb:hb + 512],
                    ALU.mult),
                    waits=[f"{ftag}_rcp"], mark=f"{ftag}_on")
                for hh in range(2):
                    h = 2 * gp + hh
                    cb = (h % 2) * 64
                    co = h // 2
                    em.op("act", lambda e, hh=hh, cb=cb, co=co, b=b, hb=hb:
                          e.activation(
                              ofm[cb:cb + 64,
                                  co * 512 + b * 256:co * 512 + b * 256 + 256],
                              lns[0:64, :].bitcast(BF16)
                              [:, hb + hh * 256:hb + hh * 256 + 256],
                              AF.Copy),
                          waits=[f"{ftag}_on"],
                          mark=(f"{ftag}_ofm" if hh == 1 else None))
                chain["norm"][gp % 2] = f"{ftag}_ofm"
                chain["ofmd"] = f"{ftag}_ofm"
            em.marks[f"{btag}_odone"] = em.marks[f"{btag}g7_omm"]
            for tt in range(2):
                t = 2 * b + tt
                for eh in range(2):
                    bank, fw, bidx = pj.acquire()
                    for co in range(NCH):
                        em.op("pe", lambda e, t=t, eh=eh, co=co, bank=bank:
                              e.matmul(
                                  bank[:],
                                  ofm[:, co * 512 + t * 128:
                                      co * 512 + t * 128 + 128],
                                  wout[:, co * D + eh * 512:
                                       co * D + eh * 512 + 512],
                                  start=(co == 0), stop=(co == NCH - 1)),
                              waits=([f"{btag}g6_ofm", f"{btag}g7_ofm",
                                      P + "_w1"] +
                                     ([fw] if fw else [])) if co == 0 else (),
                              mark=(f"{tag}_wmm{t}_{eh}" if co == NCH - 1
                                    else None))
                    evw = [f"{tag}_wmm{t}_{eh}"]
                    if s >= 1 and t == 0 and eh == 0:
                        evw.append(f"{P}s{s - 1}_store")
                    em.op("act", lambda e, t=t, eh=eh, bank=bank:
                          e.activation(
                              outb[:, t * D + eh * 512:t * D + eh * 512 + 512],
                              bank[:], AF.Copy),
                          waits=evw, mark=f"{tag}_wf{t}_{eh}")
                    pj.release(bidx, f"{tag}_wf{t}_{eh}")
        em.marks[f"{tag}_odone"] = em.marks[f"{tag}b1_odone"]
        em.marks[f"{tag}_pedone"] = em.marks[f"{tag}_wmm{TPS - 1}_1"]
        ln_post_add(em, lno, s, outb, sc.xs[s % len(sc.xs)], x_out, tag,
                    waits_first=[f"{tag}_wf{TPS - 1}_1"])



def build_pass_l1ffn(em, sb, ps, cd, x_in, x_out):
    ident = cd["ident_s"]
    P = "p4"
    NHC = HID // 128
    wg = sb(P + "_wg", [128, NCH * HID], BF16)
    wu = sb(P + "_wu", [128, NCH * HID], BF16)
    wd = sb(P + "_wd", [128, NHC * D], BF16)
    sc = StripeCtx(em, sb, ps, P, carry=False, nxs=1, nxnt=1)
    hact = sb(P + "_hact", [128, NHC * 512], BF16)
    wstage = [hact.bitcast(I32)[:, 0:HID // 2],
              hact.bitcast(I32)[:, HID // 2:HID]]
    silu_t = [sb(P + f"_sl{i}", [128, 512], F32) for i in range(2)]
    lno = LNBufs(sb, P + "_lo")
    pg = BankPool([ps(P + f"_pg{i}", [128, 512], F32) for i in range(4)])
    pd = [ps(P + f"_pd{i}", [128, 512], F32) for i in range(2)]

    load_weights(em, wstage, [wg, wu, wd],
                 [cd["l1f_wgT"], cd["l1f_wuT"], cd["l1f_wdT"]], P,
                 scale=1.0 / 127.0)

    for s in range(NSTRIPE):
        tag = f"{P}s{s}"
        war = [f"{P}s{s - 1}_gu{NHC - 1}"] if s >= 1 else []
        xw = f"{P}s{s - 1}_store" if s >= 1 else None
        bi = stripe_prologue(em, sc, s, x_in, ident, xnt_war=war,
                             x_extra_wait=xw)
        for hc in range(NHC):
            bkg, fwg, big = pg.acquire()
            for c in range(NCH):
                em.op("pe", lambda e, hc=hc, c=c, bkg=bkg, bi=bi: e.matmul(
                    bkg[:],
                    wg[:, c * HID + hc * 128:c * HID + hc * 128 + 128],
                    sc.chunk(bi, c),
                    start=(c == 0), stop=(c == NCH - 1)),
                    waits=([f"{tag}_xnT", P + "_w0"] +
                           ([fwg] if fwg else [])) if c == 0 else (),
                    mark=(f"{tag}_gmm{hc}" if c == NCH - 1 else None))
            bku, fwu, biu = pg.acquire()
            for c in range(NCH):
                em.op("pe", lambda e, hc=hc, c=c, bku=bku, bi=bi: e.matmul(
                    bku[:],
                    wu[:, c * HID + hc * 128:c * HID + hc * 128 + 128],
                    sc.chunk(bi, c),
                    start=(c == 0), stop=(c == NCH - 1)),
                    waits=([P + "_w1"] + ([fwu] if fwu else []))
                    if c == 0 else (),
                    mark=(f"{tag}_umm{hc}" if c == NCH - 1 else None))
            sl = silu_t[hc % 2]
            sl_war = [f"{tag}_ha{hc - 2}"] if hc >= 2 else (
                [f"{P}s{s - 1}_ha{NHC - 2 + hc}"] if s >= 1 else [])
            em.op("act", lambda e, hc=hc, bkg=bkg, sl=sl: e.activation(
                sl[:], bkg[:], AF.Silu),
                waits=[f"{tag}_gmm{hc}"] + sl_war, mark=f"{tag}_sil{hc}")
            pg.release(big, f"{tag}_sil{hc}")
            ha_war = ([f"{P}s{s - 1}_dmm{TPS - 1}_1"]
                      if (s >= 1 and hc == 0) else [])
            if s == 0 and hc == 0:
                ha_war = [P + "_w2"]
            em.op("dve", lambda e, hc=hc, bku=bku, sl=sl: e.tensor_tensor(
                hact[:, hc * 512:(hc + 1) * 512], bku[:], sl[:], ALU.mult),
                waits=[f"{tag}_umm{hc}", f"{tag}_sil{hc}"] + ha_war,
                mark=f"{tag}_ha{hc}")
            pg.release(biu, f"{tag}_ha{hc}")
        em.marks[f"{tag}_gu{NHC - 1}"] = em.marks[f"{tag}_umm{NHC - 1}"]
        xs = sc.xs[s % len(sc.xs)]
        ln = lno
        for t in range(TPS):
            for eh in range(2):
                bank = pd[eh]
                for hc in range(NHC):
                    em.op("pe", lambda e, t=t, eh=eh, hc=hc, bank=bank:
                          e.matmul(
                              bank[:],
                              hact[:, hc * 512 + t * 128:hc * 512 + t * 128 + 128],
                              wd[:, hc * D + eh * 512:hc * D + eh * 512 + 512],
                              start=(hc == 0), stop=(hc == NHC - 1)),
                          waits=([f"{tag}_ha{NHC - 1}", P + "_w2"] +
                                 ([f"{tag}t{t - 1}_ap{eh}"] if t >= 1 else
                                  ([f"{P}s{s - 1}t{TPS - 1}_ap{eh}"]
                                   if s >= 1 else []))) if hc == 0 else (),
                          mark=(f"{tag}_dmm{t}_{eh}" if hc == NHC - 1
                                else None))
            for eh in range(2):
                em.op("dve", lambda e, t=t, eh=eh: e.bn_stats(
                    ln.st[:, t * 12 + eh * 6:t * 12 + eh * 6 + 6], pd[eh][:]),
                    waits=[f"{tag}_dmm{t}_{eh}"] +
                          ([f"{P}s{s - 1}_lnfree"] if (s >= 1 and t == 0 and
                                                       eh == 0) else []))
            em.op("dve", lambda e, t=t: e.bn_aggr(
                ln.mv[:, t * 2:t * 2 + 2], ln.st[:, t * 12:t * 12 + 12]))
            em.op("dve", lambda e, t=t: e.tensor_scalar(
                ln.vs[:, t:t + 1], ln.mv[:, 2 * t + 1:2 * t + 2],
                EPS, None, ALU.add))
            newton_rsqrt(em, ln.vs[:, t:t + 1], ln.rs[:, t:t + 1],
                         ln.tmp[:, t:t + 1], ln.ci[:, t:t + 1])
            em.op("dve", lambda e, t=t: e.scalar_tensor_tensor(
                ln.nb[:, t:t + 1], ln.mv[:, 2 * t:2 * t + 1],
                -1.0, ln.rs[:, t:t + 1], ALU.mult, ALU.mult),
                mark=f"{tag}t{t}_lnp")
            for eh in range(2):
                em.op("dve", lambda e, t=t, eh=eh: e.tensor_scalar(
                    xs[:, t * D + eh * 512:t * D + eh * 512 + 512],
                    xs[:, t * D + eh * 512:t * D + eh * 512 + 512],
                    ln.nb[:, t:t + 1], None, ALU.add),
                    waits=[f"{tag}t{t}_lnp"])
                em.op("dve", lambda e, t=t, eh=eh: e.scalar_tensor_tensor(
                    xs[:, t * D + eh * 512:t * D + eh * 512 + 512],
                    pd[eh][:], ln.rs[:, t:t + 1],
                    xs[:, t * D + eh * 512:t * D + eh * 512 + 512],
                    ALU.mult, ALU.add),
                    mark=f"{tag}t{t}_ap{eh}")
        em.marks[f"{tag}_lnfree"] = em.marks[f"{tag}t{TPS - 1}_lnp"]
        if x_out.shape[0] == 128:
            em.dma("pool", lambda e, xs=xs, s=s: e.dma_start(
                x_out[:, s * TPS * D:(s + 1) * TPS * D], xs[:]),
                inc=f"st{s % 2}", mark=f"{tag}_store",
                waits=[f"{tag}t{TPS - 1}_ap0", f"{tag}t{TPS - 1}_ap1"])
        else:
            em.dma("pool", lambda e, xs=xs, s=s: e.dma_start(
                x_out[s * STRIPE:(s + 1) * STRIPE, :]
                .rearrange("(t p) d -> p t d", p=128),
                xs.rearrange("p (t d) -> p t d", t=TPS)),
                inc=f"st{s % 2}", mark=f"{tag}_store",
                waits=[f"{tag}t{TPS - 1}_ap0", f"{tag}t{TPS - 1}_ap1"])



def _host_prep(inputs):
    f32 = np.float32
    bf16 = ml_dtypes.bfloat16
    shared = {}

    def T(a):
        return np.ascontiguousarray(np.asarray(a).T)

    shared["l0a_wqT"] = T(inputs["L0_attn_wq_w"]).astype(f32, copy=False)
    shared["l0a_wkT"] = T(inputs["L0_attn_wk_w"]).astype(f32, copy=False)
    shared["l0a_wvT"] = T(inputs["L0_attn_wv_w"]).astype(f32, copy=False)
    shared["l0a_woT"] = T(inputs["L0_attn_wo_w"]).astype(f32, copy=False)
    for nm in ("q", "k", "v", "o"):
        shared[f"l0a_b{nm}"] = np.ascontiguousarray(
            np.asarray(inputs[f"L0_attn_w{nm}_b"]).astype(f32, copy=False)
            .reshape(NCH, 128).T)
    shared["l0a_bvb"] = np.ascontiguousarray(np.tile(
        np.asarray(inputs["L0_attn_wv_b"]).astype(f32, copy=False)[None, :],
        (128, 1)))
    shared["l0a_bob"] = np.ascontiguousarray(np.tile(
        np.asarray(inputs["L0_attn_wo_b"]).astype(f32, copy=False)[None, :],
        (128, 1)))
    shared["l0f_wrT"] = T(inputs["L0_ffn_W_r"]).astype(f32, copy=False)
    shared["l0f_wkT"] = T(inputs["L0_ffn_W_k"]).astype(f32, copy=False)
    shared["l0f_wvT"] = T(inputs["L0_ffn_W_v"]).astype(f32, copy=False)
    shared["l0f_mur"] = np.ascontiguousarray(
        np.asarray(inputs["L0_ffn_mu_r"]).astype(f32, copy=False)
        .reshape(NCH, 128).T)
    shared["l0f_muk"] = np.ascontiguousarray(
        np.asarray(inputs["L0_ffn_mu_k"]).astype(f32, copy=False)
        .reshape(NCH, 128).T)
    perm = np.zeros(3 * D, dtype=np.int64)
    for blk in range(2):
        for h in range(H):
            base = blk * D + 64 * h
            for j in range(DH // 2):
                perm[base + j] = base + 2 * j
                perm[base + 32 + j] = base + 2 * j + 1
    perm[2 * D:] = np.arange(2 * D, 3 * D)
    wqkv = np.asarray(inputs["L1_attn_Wqkv"]).astype(f32, copy=False)[perm]
    shared["l1a_wqkvT"] = np.ascontiguousarray(wqkv.T)
    shared["l1a_woutT"] = T(inputs["L1_attn_Wout"]).astype(f32, copy=False)
    shared["l1f_wgT"] = T(np.asarray(inputs["L1_ffn_w_gate"]))
    shared["l1f_wuT"] = T(np.asarray(inputs["L1_ffn_w_up"]))
    shared["l1f_wdT"] = T(np.asarray(inputs["L1_ffn_w_down"]))
    shared["ident"] = np.eye(128, dtype=bf16)
    ck = np.arange(128)[:, None]
    cq = np.arange(128)[None, :]
    shared["l0mask4"] = np.ascontiguousarray(
        np.tile((ck <= cq).astype(bf16), (1, 4)))
    inv_freq = 1.0 / 10000.0 ** (np.arange(0, DH, 2, dtype=np.float64) / DH)
    t = np.arange(SEQLEN, dtype=np.float64)
    ang = t[None, :] * inv_freq[:, None]
    shared["l1cos"] = np.ascontiguousarray(np.tile(np.cos(ang), (4, 1))
                                           .astype(bf16))
    sin = np.sin(ang)
    ss = np.concatenate([sin, -sin, sin, -sin], axis=0)
    shared["l1sin"] = np.ascontiguousarray(ss.astype(bf16))
    ck2 = np.arange(128)[:, None]
    cq2 = np.arange(BS)[None, :]
    m0 = (ck2 <= cq2).astype(bf16)
    m1 = (ck2 + 128 <= cq2).astype(bf16)
    shared["l1mask"] = np.ascontiguousarray(
        np.concatenate([m0, m1, m0, m1], axis=1))
    return shared


W_SHAPES = {
    "l0a_wqT": ([D, D], F32), "l0a_wkT": ([D, D], F32),
    "l0a_wvT": ([D, D], F32), "l0a_woT": ([D, D], F32),
    "l0a_bq": ([128, NCH], F32), "l0a_bk": ([128, NCH], F32),
    "l0a_bv": ([128, NCH], F32), "l0a_bo": ([128, NCH], F32),
    "l0f_wrT": ([D, D], F32), "l0f_wkT": ([D, D], F32),
    "l0f_wvT": ([D, D], F32),
    "l0f_mur": ([128, NCH], F32), "l0f_muk": ([128, NCH], F32),
    "l1a_wqkvT": ([D, 3 * D], F32), "l1a_woutT": ([D, D], F32),
    "l1f_wgT": ([D, HID], I32), "l1f_wuT": ([D, HID], I32),
    "l1f_wdT": ([HID, D], I32),
    "ident": ([128, 128], BF16), "l0mask4": ([128, 512], BF16),
    "l0a_bvb": ([128, D], F32), "l0a_bob": ([128, D], F32),
    "l1cos": ([128, SEQLEN], BF16), "l1sin": ([128, SEQLEN], BF16),
    "l1mask": ([128, 4 * BS], BF16),
}

PASS_BUILDERS = {"l0ffn": build_pass_l0ffn,
                 "l0attn": build_pass_l0attn,
                 "l1ffn": build_pass_l1ffn,
                 "l1attn": build_pass_l1attn}
PASS_WEIGHTS = {
    "l0attn": ["l0a_wqT", "l0a_wkT", "l0a_wvT", "l0a_woT",
               "l0a_bq", "l0a_bk", "l0a_bvb", "l0a_bob", "l0mask4", "ident"],
    "l0ffn": ["l0f_wrT", "l0f_wkT", "l0f_wvT", "l0f_mur", "l0f_muk", "ident"],
    "l1attn": ["l1a_wqkvT", "l1a_woutT", "l1cos", "l1sin",
               "l1mask", "ident"],
    "l1ffn": ["l1f_wgT", "l1f_wuT", "l1f_wdT", "ident"],
}


def build_program(passes):
    nc = bass.Bass("TRN2", target_bir_lowering=False)
    stack = ExitStack()
    em = Em(nc, stack)

    names = sorted({w for p in passes for w in PASS_WEIGHTS[p]})
    cd = {}
    for name in names:
        shp, dt = W_SHAPES[name]
        cd[name] = nc.dram_tensor(name, shp, dt, kind="ExternalInput").ap()

    def sb(name, shape, dt):
        return stack.enter_context(nc.sbuf_tensor(name, shape, dt))[:]

    def ps(name, shape, dt):
        return stack.enter_context(nc.psum_tensor(name, shape, dt))[:]

    ident_s = sb("c_ident", [128, 128], BF16)
    em.dma("sp", lambda e: e.dma_start(ident_s, cd["ident"]), inc="cst",
           mark="c_ident")
    cd["ident_s"] = ident_s

    x_first = nc.dram_tensor("x_in", [TOK, D], F32, kind="ExternalInput").ap()
    y_last = nc.dram_tensor("y", [TOK, D], F32, kind="ExternalOutput").ap()
    mids = [nc.dram_tensor(f"xmid{k}", [128, TOK * D // 128], F32,
                           kind="Internal").ap()
            for k in range(len(passes) - 1)]
    for k, p in enumerate(passes):
        x_in = x_first if k == 0 else mids[k - 1]
        x_out = y_last if k == len(passes) - 1 else mids[k]
        with ExitStack() as pass_stack:
            saved = stack
            inner_sb = (lambda st: lambda n, shp, dt:
                        st.enter_context(nc.sbuf_tensor(n, shp, dt))[:]
                        )(pass_stack)
            inner_ps = (lambda st: lambda n, shp, dt:
                        st.enter_context(nc.psum_tensor(n, shp, dt))[:]
                        )(pass_stack)
            PASS_BUILDERS[p](em, inner_sb, inner_ps, cd, x_in, x_out)
            em.barrier()
    with nc.Block() as blk:
        em.finalize(blk)
    stack.close()
    return nc, names


def kernel(**inputs):
    shared = _host_prep(inputs)
    x = np.asarray(inputs["L0_attn_x"]).astype(np.float32, copy=False)
    x = np.ascontiguousarray(x.reshape(NCORES, TOK, D))

    passes = ["l0attn", "l0ffn", "l1attn", "l1ffn"]
    nc, names = build_program(passes)
    in_maps = []
    for c in range(NCORES):
        m = {n: shared[n] for n in names}
        m["x_in"] = x[c]
        in_maps.append(m)
    res = run_bass_kernel_spmd(nc, in_maps, core_ids=list(range(NCORES)))
    y = np.stack([np.asarray(r["y"]) for r in res.results])
    return np.ascontiguousarray(y.reshape(16, SEQLEN, D))


# version 11
# speedup vs baseline: 1.0320x; 1.0043x over previous
import numpy as np
import ml_dtypes
from contextlib import ExitStack
import concourse.bass as bass
import concourse.mybir as mybir
from concourse.bass_utils import run_bass_kernel_spmd

F32 = mybir.dt.float32
BF16 = mybir.dt.bfloat16
I32 = mybir.dt.int32
ALU = mybir.AluOpType
AF = mybir.ActivationFunctionType

NCORES = 8
D = 1024
SEQLEN = 2048
TOK = 4096
STRIPE = 512
NSTRIPE = TOK // STRIPE
TPS = 4
NCH = D // 128
H = 16
DH = 64
BS = 256
HID = 3072
EPS = 1e-5

P1STAGE = 99

ENGINES = ("pe", "act", "dve", "pool", "sp")
ENG_ATTR = {"pe": "tensor", "act": "scalar", "dve": "vector", "pool": "gpsimd",
            "sp": "sync"}


class Em:

    def __init__(self, nc, stack):
        self.nc = nc
        self.stack = stack
        self.sems = {}
        self.cnt = {}
        self.ops = {e: [] for e in ENGINES}
        self.marks = {}

    def sem(self, name):
        if name not in self.sems:
            self.sems[name] = self.stack.enter_context(self.nc.semaphore(name))
            self.cnt[name] = 0
        return self.sems[name]

    def op(self, eng, fn, waits=(), inc=None, inc_amt=1, mark=None):
        wl = []
        for w in waits:
            if w is None:
                continue
            if isinstance(w, str):
                if w not in self.marks:
                    raise KeyError(f"unknown mark {w!r}")
                wl.append(self.marks[w])
            else:
                self.sem(w[0])
                wl.append(w)
        if mark is not None and inc is None:
            inc = "s_" + eng
        rec = {"fn": fn, "waits": wl, "inc": None}
        if inc is not None:
            self.sem(inc)
            self.cnt[inc] += inc_amt
            rec["inc"] = (inc, inc_amt)
            if mark is not None:
                self.marks[mark] = (inc, self.cnt[inc])
        self.ops[eng].append(rec)

    def dma(self, eng, fn, inc, mark=None, waits=()):
        self.op(eng, fn, waits=waits, inc=inc, inc_amt=16, mark=mark)

    def barrier(self):
        for eng in ENGINES:
            if self.ops[eng]:
                rec = self.ops[eng][-1]
                if rec["inc"] is None:
                    s = "s_" + eng
                    self.sem(s)
                    self.cnt[s] += 1
                    rec["inc"] = (s, 1)
        targets = [(s, self.cnt[s]) for s in list(self.cnt) if self.cnt[s] > 0]
        for eng in ENGINES:
            if not self.ops[eng]:
                continue
            for (s, v) in targets:
                self.ops[eng].append({"fn": None, "waits": [(s, v)],
                                      "inc": None})

    def finalize(self, blk):
        for eng in ENGINES:
            if not self.ops[eng]:
                continue

            def mkbody(ops, sems, sep):
                def run(e):
                    waited = {}
                    prev_compute = False
                    for rec in ops:
                        for (s, v) in rec["waits"]:
                            if waited.get(s, -1) < v:
                                e.wait_ge(sems[s], v)
                                waited[s] = v
                        if rec["fn"] is not None:
                            if sep and prev_compute:
                                e.drain()
                            ins = rec["fn"](e)
                            prev_compute = True
                            if rec["inc"] is not None:
                                s, amt = rec["inc"]
                                ins.then_inc(sems[s], amt)
                return run

            getattr(blk, ENG_ATTR[eng])(mkbody(self.ops[eng], self.sems,
                                               eng in ('dve', 'act', 'pool')))


class BankPool:

    def __init__(self, banks):
        self.banks = banks
        self.uses = 0
        self.free_mark = [None] * len(banks)

    def acquire(self):
        i = self.uses % len(self.banks)
        self.uses += 1
        return self.banks[i], self.free_mark[i], i

    def release(self, i, mark):
        self.free_mark[i] = mark


def newton_rsqrt(em, v_ap, y_ap, tmp_ap, ci_ap, waits=(), mark=None):
    MAGIC = 0x5f3759df
    em.op("dve", lambda e: e.tensor_scalar(
        y_ap.bitcast(I32), v_ap.bitcast(I32), 1, None, ALU.arith_shift_right),
        waits=waits)
    em.op("dve", lambda e: e.memset(ci_ap, MAGIC))
    em.op("dve", lambda e: e.tensor_tensor(
        y_ap.bitcast(I32), ci_ap, y_ap.bitcast(I32), ALU.subtract))
    for it in range(3):
        em.op("dve", lambda e: e.tensor_tensor(tmp_ap, y_ap, y_ap, ALU.mult))
        em.op("dve", lambda e: e.tensor_tensor(tmp_ap, tmp_ap, v_ap, ALU.mult))
        em.op("dve", lambda e: e.tensor_scalar(
            tmp_ap, tmp_ap, -0.5, 1.5, ALU.mult, ALU.add))
        em.op("dve", lambda e: e.tensor_tensor(y_ap, y_ap, tmp_ap, ALU.mult),
              mark=(mark if it == 2 else None))


class LNBufs:
    def __init__(self, sb, tag):
        self.st = sb(f"{tag}_st", [128, TPS * 12], F32)
        self.mv = sb(f"{tag}_mv", [128, TPS * 2], F32)
        self.vs = sb(f"{tag}_vs", [128, TPS], F32)
        self.rs = sb(f"{tag}_rs", [128, TPS], F32)
        self.nb = sb(f"{tag}_nb", [128, TPS], F32)
        self.tmp = sb(f"{tag}_tmp", [128, TPS], F32)
        self.ci = sb(f"{tag}_ci", [128, TPS], I32)


def ln_stats(em, ln, src_tiles, tag, waits_first=()):
    for t in range(TPS):
        for hh in range(2):
            em.op("dve", lambda e, t=t, hh=hh: e.bn_stats(
                ln.st[:, t * 12 + hh * 6:t * 12 + hh * 6 + 6],
                src_tiles[t][:, hh * 512:hh * 512 + 512]),
                waits=(waits_first if (t == 0 and hh == 0) else ()))
    for t in range(TPS):
        em.op("dve", lambda e, t=t: e.bn_aggr(
            ln.mv[:, t * 2:t * 2 + 2], ln.st[:, t * 12:t * 12 + 12]))
    for t in range(TPS):
        em.op("dve", lambda e, t=t: e.tensor_scalar(
            ln.vs[:, t:t + 1], ln.mv[:, 2 * t + 1:2 * t + 2],
            EPS, None, ALU.add))
    newton_rsqrt(em, ln.vs[:, 0:TPS], ln.rs[:, 0:TPS], ln.tmp[:, 0:TPS], ln.ci[:, 0:TPS])
    for t in range(TPS):
        em.op("dve", lambda e, t=t: e.scalar_tensor_tensor(
            ln.nb[:, t:t + 1], ln.mv[:, 2 * t:2 * t + 1],
            -1.0, ln.rs[:, t:t + 1], ALU.mult, ALU.mult),
            mark=(f"{tag}_lnp" if t == TPS - 1 else None))


class StripeCtx:

    def __init__(self, em, sb, ps, tag, carry=False, nxs=2, nxnt=2,
                 ptr_aps=None):
        self.em = em
        self.tag = tag
        self.carry = carry
        self.nxs = nxs
        self.nxnt = nxnt
        self.ptr_aps = ptr_aps
        self.w = 520 if carry else 512
        self.off = 1 if carry else 0
        self.xs = [sb(f"{tag}_x{i}", [128, TPS * D], F32)
                   for i in range(getattr(self, "nxs", 2))]
        self.xnT = [sb(f"{tag}_xnT{i}", [128, NCH * self.w], BF16)
                    for i in range(getattr(self, "nxnt", 2))]
        self.xn = sb(f"{tag}_xn", [128, TPS * D], BF16)
        self.ln = [LNBufs(sb, f"{tag}_li{i}") for i in range(2)]
        if getattr(self, "ptr_aps", None) is not None:
            self.ptr = self.ptr_aps
        else:
            self.ptr = [ps(f"{tag}_ptr{i}", [128, NCH * 128], BF16)
                        for i in range(2)]

    def chunk(self, bi, c):
        o = c * self.w + self.off
        return self.xnT[bi][:, o:o + 512]

    def chunk_shift(self, bi, c):
        o = c * self.w + self.off
        return self.xnT[bi][:, o - 1:o + 511]

    def chunk3d(self, bi):
        return self.xnT[bi][:, 0:NCH * self.w].rearrange(
            "p (c w) -> p c w", c=NCH)


def stripe_prologue(em, sc, s, x_dram, ident, xnt_war=None, x_extra_wait=None,
                    ptr_war=None):
    bi = s % len(sc.xnT)
    xs = sc.xs[s % len(sc.xs)]
    T0 = sc.tag
    tag = f"{T0}s{s}"
    w = []
    sx = s - len(sc.xs)
    if sx >= 0 and f"{T0}s{sx}_add" in em.marks:
        w.append(f"{T0}s{sx}_add")
    elif sx >= 0 and f"{T0}s{sx}_store" in em.marks:
        w.append(f"{T0}s{sx}_store")
    if x_extra_wait:
        w.append(x_extra_wait)
    if x_dram.shape[0] == 128:
        em.dma("sp", lambda e: e.dma_start(
            xs[:], x_dram[:, s * TPS * D:(s + 1) * TPS * D]),
            inc=f"ld{s % 2}", mark=f"{tag}_xld", waits=w)
    else:
        em.dma("sp", lambda e: e.dma_start(
            xs.rearrange("p (t d) -> p t d", t=TPS),
            x_dram[s * STRIPE:(s + 1) * STRIPE, :]
            .rearrange("(t p) d -> p t d", p=128)),
            inc=f"ld{s % 2}", mark=f"{tag}_xld", waits=w)
    tiles = [xs[:, t * D:(t + 1) * D] for t in range(TPS)]
    ln_stats(em, sc.ln[bi], tiles, tag, waits_first=[f"{tag}_xld"])
    for t in range(TPS):
        ww = [f"{tag}_lnp"] if t == 0 else []
        if t == 0 and s >= 1:
            ww.append(f"{T0}s{s - 1}_tr{TPS - 1}")
        em.op("act", lambda e, t=t, bi=bi: e.activation(
            sc.xn[:, t * D:(t + 1) * D], xs[:, t * D:(t + 1) * D],
            AF.Identity,
            bias=sc.ln[bi].nb[:, t:t + 1], scale=sc.ln[bi].rs[:, t:t + 1]),
            waits=ww, mark=f"{tag}_xn{t}")
    em.marks[f"{tag}_xn"] = em.marks[f"{tag}_xn{TPS - 1}"]
    for t in range(TPS):
        pb = sc.ptr[t % 2]
        bank_war = None
        if t >= 2:
            bank_war = f"{tag}_trev{t - 2}"
        elif s >= 1:
            bank_war = f"{T0}s{s - 1}_trev{t + 2}"
        bw = [bank_war] if bank_war else []
        if t < 2 and ptr_war:
            bw += [m for m in ptr_war if m in em.marks]
        for c in range(NCH):
            em.op("pe", lambda e, t=t, c=c, pb=pb: e.transpose(
                pb[:, c * 128:(c + 1) * 128],
                sc.xn[:, t * D + c * 128:t * D + c * 128 + 128], ident),
                waits=([f"{tag}_xn{t}", "c_ident"] + bw)
                if c == 0 else (),
                mark=(f"{tag}_tr{t}" if c == NCH - 1 else None))
        ev_w = [f"{tag}_tr{t}"]
        if t == 0 and xnt_war:
            ev_w += [m for m in xnt_war if m in em.marks]
        em.op("act", lambda e, t=t, pb=pb, bi=bi: e.activation(
            sc.chunk3d(bi)[:, :, sc.off + t * 128:sc.off + t * 128 + 128],
            pb[:, 0:NCH * 128].rearrange("p (c w) -> p c w", c=NCH), AF.Copy),
            waits=ev_w, mark=f"{tag}_trev{t}")
    em.marks[f"{tag}_xnT"] = em.marks[f"{tag}_trev{TPS - 1}"]
    return bi


def ln_post_add(em, lno, s, outb, xs, x_out, tag, waits_first):
    tiles = [outb[:, t * D:(t + 1) * D] for t in range(TPS)]
    ln = lno[s % 2]
    ln_stats(em, ln, tiles, tag + "o", waits_first=waits_first)
    for t in range(TPS):
        em.op("act", lambda e, t=t: e.activation(
            outb[:, t * D:(t + 1) * D], outb[:, t * D:(t + 1) * D],
            AF.Identity,
            bias=ln.nb[:, t:t + 1], scale=ln.rs[:, t:t + 1]),
            waits=[f"{tag}o_lnp"] if t == 0 else (),
            mark=(f"{tag}_lnapp" if t == TPS - 1 else None))
    em.op("dve", lambda e: e.tensor_tensor(outb[:], outb[:], xs[:], ALU.add),
          waits=[f"{tag}_lnapp"], mark=f"{tag}_add")
    if x_out.shape[0] == 128:
        em.dma("pool", lambda e: e.dma_start(
            x_out[:, s * TPS * D:(s + 1) * TPS * D], outb[:]),
            inc=f"st{s % 2}", mark=f"{tag}_store", waits=[f"{tag}_add"])
    else:
        em.dma("pool", lambda e: e.dma_start(
            x_out[s * STRIPE:(s + 1) * STRIPE, :]
            .rearrange("(t p) d -> p t d", p=128),
            outb.rearrange("p (t d) -> p t d", t=TPS)),
            inc=f"st{s % 2}", mark=f"{tag}_store", waits=[f"{tag}_add"])


def load_weights(em, wstage, dsts, srcs, tag, scale=None):
    n = 0
    sw = wstage[0].shape[1]
    nst = len(wstage)
    for wi, (wsb, wd) in enumerate(zip(dsts, srcs)):
        ncw = wd.shape[0] // 128
        cols = wd.shape[1]
        nparts = (cols + sw - 1) // sw
        for c in range(ncw):
            for pp in range(nparts):
                w0 = pp * sw
                w1 = min(cols, w0 + sw)
                st = wstage[n % nst]
                prev = f"{tag}_cast{n - nst}" if n >= nst else None
                em.dma("sp",
                       lambda e, st=st, wd=wd, c=c, w0=w0, w1=w1:
                       e.dma_start(st[:, 0:w1 - w0],
                                   wd[c * 128:(c + 1) * 128, w0:w1]),
                       inc=f"{tag}_wp{n % nst}",
                       waits=[prev] if prev else (),
                       mark=f"{tag}_wl{n}")
                kw = {} if scale is None else {"scale": scale}
                em.op("act", lambda e, wsb=wsb, st=st, c=c, cols=cols,
                      w0=w0, w1=w1, kw=kw:
                      e.activation(wsb[:, c * cols + w0:c * cols + w1],
                                   st[:, 0:w1 - w0], AF.Copy, **kw),
                      waits=[f"{tag}_wl{n}"], mark=f"{tag}_cast{n}")
                n += 1
            em.marks[f"{tag}_w{wi}c{c}"] = em.marks[f"{tag}_cast{n - 1}"]
        em.marks[f"{tag}_w{wi}"] = em.marks[f"{tag}_cast{n - 1}"]



def build_pass_l0ffn(em, sb, ps, cd, x_in, x_out):
    ident = cd["ident_s"]
    P = "p2"
    wr = sb(P + "_wr", [128, NCH * D], BF16)
    wk = sb(P + "_wk", [128, NCH * D], BF16)
    wv = sb(P + "_wv", [128, NCH * D], BF16)
    wstage = [sb(P + f"_wst{i}", [128, D], F32) for i in range(2)]
    mur = sb(P + "_mur", [128, NCH], F32)
    muk = sb(P + "_muk", [128, NCH], F32)
    sc = StripeCtx(em, sb, ps, P, carry=True)
    xrT = sb(P + "_xr", [128, NCH * 512], BF16)
    xkT = sb(P + "_xk", [128, NCH * 512], BF16)
    dT = sb(P + "_d", [128, NCH * 512], BF16)
    rsig = sb(P + "_rsig", [128, TPS * D], F32)
    kact = sb(P + "_kact", [128, NCH * 512], BF16)
    krelu = [sb(P + f"_krelu{i}", [128, 512], F32) for i in range(2)]
    outb = sb(P + "_out", [128, TPS * D], F32)
    lno = [LNBufs(sb, P + f"_lo{i}") for i in range(2)]
    pj = BankPool([ps(P + f"_pj{i}", [128, 512], F32) for i in range(4)])

    load_weights(em, wstage, [wr, wk, wv],
                 [cd["l0f_wrT"], cd["l0f_wkT"], cd["l0f_wvT"]], P)
    em.dma("sp", lambda e: e.dma_start(mur[:], cd["l0f_mur"]),
           inc=P + "_mi", mark=P + "_mul")
    em.dma("sp", lambda e: e.dma_start(muk[:], cd["l0f_muk"]),
           inc=P + "_mi", waits=[P + "_mul"], mark=P + "_mu")

    for s in range(NSTRIPE):
        tag = f"{P}s{s}"
        war = []
        if s >= 2:
            war.append(f"{P}s{s - 2}_mix")
        if s >= 1:
            war.append(f"{P}s{s - 1}_carry")
        bi = stripe_prologue(em, sc, s, x_in, ident, xnt_war=war)
        if s % (NSTRIPE // 2) == 0:
            em.op("dve", lambda e, bi=bi: e.memset(
                sc.chunk3d(bi)[:, :, 0:1], 0.0),
                waits=[f"{tag}_xnT"], mark=f"{tag}_carry")
        else:
            pb = 1 - bi
            em.op("dve", lambda e, bi=bi, pb=pb: e.tensor_copy(
                sc.chunk3d(bi)[:, :, 0:1], sc.chunk3d(pb)[:, :, 512:513]),
                waits=[f"{tag}_xnT"], mark=f"{tag}_carry")
        mix_war = []
        if s >= 1:
            mix_war = [f"{P}s{s - 1}_rmm{TPS - 1}_1", f"{P}s{s - 1}_kmm{NCH - 1}"]
        for c in range(NCH):
            em.op("dve", lambda e, c=c, bi=bi: e.tensor_tensor(
                dT[:, c * 512:(c + 1) * 512], sc.chunk(bi, c),
                sc.chunk_shift(bi, c)[:, 0:512], ALU.subtract),
                waits=([f"{tag}_carry", P + "_mu"] + mix_war) if c == 0 else ())
            em.op("dve", lambda e, c=c, bi=bi: e.scalar_tensor_tensor(
                xrT[:, c * 512:(c + 1) * 512], dT[:, c * 512:(c + 1) * 512],
                mur[:, c:c + 1], sc.chunk_shift(bi, c)[:, 0:512],
                ALU.mult, ALU.add))
            em.op("dve", lambda e, c=c, bi=bi: e.scalar_tensor_tensor(
                xkT[:, c * 512:(c + 1) * 512], dT[:, c * 512:(c + 1) * 512],
                muk[:, c:c + 1], sc.chunk_shift(bi, c)[:, 0:512],
                ALU.mult, ALU.add),
                mark=(f"{tag}_mix" if c == NCH - 1 else None))
        for t in range(TPS):
            for eh in range(2):
                bank, fw, bidx = pj.acquire()
                for c in range(NCH):
                    em.op("pe", lambda e, t=t, eh=eh, c=c, bank=bank: e.matmul(
                        bank, xrT[:, c * 512 + t * 128:c * 512 + t * 128 + 128],
                        wr[:, c * D + eh * 512:c * D + eh * 512 + 512],
                        start=(c == 0), stop=(c == NCH - 1)),
                        waits=([f"{tag}_mix", P + "_w0"] +
                               ([fw] if fw else [])) if c == 0 else (),
                        mark=(f"{tag}_rmm{t}_{eh}" if c == NCH - 1 else None))
                rs_war = ([f"{P}s{s - 1}_kv{t}_{eh}"] if s >= 1 else [])
                em.op("act", lambda e, t=t, eh=eh, bank=bank: e.activation(
                    rsig[:, t * D + eh * 512:t * D + eh * 512 + 512], bank,
                    AF.Sigmoid),
                    waits=[f"{tag}_rmm{t}_{eh}"] + rs_war,
                    mark=f"{tag}_rs{t}_{eh}")
                pj.release(bidx, f"{tag}_rs{t}_{eh}")
        for co in range(NCH):
            bank, fw, bidx = pj.acquire()
            for c in range(NCH):
                em.op("pe", lambda e, co=co, c=c, bank=bank: e.matmul(
                    bank, wk[:, c * D + co * 128:c * D + co * 128 + 128],
                    xkT[:, c * 512:(c + 1) * 512],
                    start=(c == 0), stop=(c == NCH - 1)),
                    waits=([P + "_w1"] + ([fw] if fw else [])) if c == 0 else (),
                    mark=(f"{tag}_kmm{co}" if c == NCH - 1 else None))
            kr = krelu[co % 2]
            kr_war = [f"{tag}_ksq{co - 2}"] if co >= 2 else (
                [f"{P}s{s - 1}_ksq{NCH - 2 + (co % 2) * 0 + co}"]
                if False else [])
            em.op("dve", lambda e, bank=bank, kr=kr: e.tensor_scalar(
                kr[:], bank, 0.0, None, ALU.max),
                waits=[f"{tag}_kmm{co}"] + kr_war, mark=f"{tag}_krel{co}")
            pj.release(bidx, f"{tag}_krel{co}")
            ks_war = ([f"{P}s{s - 1}_vmm{TPS - 1}_1"]
                      if (s >= 1 and co == 0) else [])
            em.op("act", lambda e, co=co, kr=kr: e.activation(
                kact[:, co * 512:(co + 1) * 512], kr[:], AF.Square),
                waits=[f"{tag}_krel{co}"] + ks_war, mark=f"{tag}_ksq{co}")
        ob = outb
        for t in range(TPS):
            for eh in range(2):
                bank, fw, bidx = pj.acquire()
                for c in range(NCH):
                    em.op("pe", lambda e, t=t, eh=eh, c=c, bank=bank: e.matmul(
                        bank, kact[:, c * 512 + t * 128:c * 512 + t * 128 + 128],
                        wv[:, c * D + eh * 512:c * D + eh * 512 + 512],
                        start=(c == 0), stop=(c == NCH - 1)),
                        waits=[f"{tag}_ksq{c}", P + "_w2"] +
                              ([fw] if (c == 0 and fw) else []),
                        mark=(f"{tag}_vmm{t}_{eh}" if c == NCH - 1 else None))
                kvw = [f"{tag}_vmm{t}_{eh}", f"{tag}_rs{t}_{eh}"]
                if s >= 1 and t == 0 and eh == 0:
                    kvw.append(f"{P}s{s - 1}_store")
                em.op("dve", lambda e, t=t, eh=eh, bank=bank, ob=ob:
                      e.tensor_tensor(
                          ob[:, t * D + eh * 512:t * D + eh * 512 + 512],
                          bank,
                          rsig[:, t * D + eh * 512:t * D + eh * 512 + 512],
                          ALU.mult),
                      waits=kvw,
                      mark=f"{tag}_kv{t}_{eh}")
                pj.release(bidx, f"{tag}_kv{t}_{eh}")
        ln_post_add(em, lno, s, outb, sc.xs[s % len(sc.xs)], x_out, tag,
                    waits_first=[f"{tag}_kv{TPS - 1}_1"])


def _p1_passthrough(em, sc, bi, s, tag, x_out, wait_mark):
    em.dma("pool", lambda e: e.dma_start(
        x_out[s * STRIPE:(s + 1) * STRIPE, :]
        .rearrange("(t p) d -> p t d", p=128),
        sc.xs[bi].rearrange("p (t d) -> p t d", t=TPS)),
        inc=f"st{s % 2}", mark=f"{tag}_store", waits=[wait_mark])



def build_pass_l0attn(em, sb, ps, cd, x_in, x_out):
    ident = cd["ident_s"]
    P = "p1"
    wq = sb(P + "_wq", [128, NCH * D], BF16)
    wk = sb(P + "_wk", [128, NCH * D], BF16)
    wv = sb(P + "_wv", [128, NCH * D], BF16)
    wo = sb(P + "_wo", [128, NCH * D], BF16)
    wstage = [sb(P + f"_wst{i}", [128, D], F32) for i in range(2)]
    bq = sb(P + "_bq", [128, NCH], F32)
    bk = sb(P + "_bk", [128, NCH], F32)
    bvb = sb(P + "_bvb", [128, D], BF16)
    bob = sb(P + "_bob", [128, D], BF16)
    bstage = wstage[1]
    mask4 = sb(P + "_mask4", [128, 512], BF16)
    sc = StripeCtx(em, sb, ps, P, carry=False)
    qf = sb(P + "_qf", [128, NCH * 512], BF16)
    kze = sb(P + "_kze", [128, NCH * 512], BF16)
    kzo = sb(P + "_kzo", [128, NCH * 512], BF16)
    vst = sb(P + "_vst", [128, TPS * D], BF16)
    ktm = sb(P + "_ktm", [128, TPS * D], BF16)
    sT = sb(P + "_sT", [128, 2 * 512], BF16)
    stf = sb(P + "_stf", [128, H * DH], F32)
    stb = sb(P + "_stb", [128, H * DH], BF16)
    ofm = sb(P + "_ofm", [128, NCH * 512], BF16)
    outb = sb(P + "_out", [128, TPS * D], F32)
    lno = [LNBufs(sb, P + f"_lo{i}") for i in range(2)]
    pj = BankPool([ps(P + f"_pj{i}", [128, 512], F32) for i in range(4)])
    po = [ps(P + f"_po{i}", [128, 512], F32) for i in range(2)]

    load_weights(em, wstage, [wq, wk, wv, wo],
                 [cd["l0a_wqT"], cd["l0a_wkT"], cd["l0a_wvT"], cd["l0a_woT"]],
                 P)
    em.dma("sp", lambda e: e.dma_start(bq[:], cd["l0a_bq"]), inc=P + "_mi",
           mark=P + "_bql")
    em.dma("sp", lambda e: e.dma_start(bk[:], cd["l0a_bk"]), inc=P + "_mi",
           waits=[P + "_bql"], mark=P + "_bqk")
    em.dma("sp", lambda e: e.dma_start(bstage[:], cd["l0a_bvb"]),
           inc=P + "_mi", waits=[P + "_bqk", P + "_w3"], mark=P + "_bvl")
    em.op("act", lambda e: e.activation(bvb[:], bstage[:], AF.Copy),
          waits=[P + "_bvl"], mark=P + "_bv")
    em.dma("sp", lambda e: e.dma_start(bstage[:], cd["l0a_bob"]),
           inc=P + "_mi", waits=[P + "_bv"], mark=P + "_bol")
    em.op("act", lambda e: e.activation(bob[:], bstage[:], AF.Copy),
          waits=[P + "_bol"], mark=P + "_bo")
    em.dma("sp", lambda e: e.dma_start(mask4[:], cd["l0mask4"]),
           inc=P + "_mi", waits=[P + "_bol"], mark=P + "_msk")
    em.op("dve", lambda e: e.memset(kze[64:128, :], 0.0))
    em.op("dve", lambda e: e.memset(kzo[0:64, :], 0.0), mark=P + "_kz0")

    for s in range(NSTRIPE):
        tag = f"{P}s{s}"
        bi = stripe_prologue(em, sc, s, x_in, ident)
        for co in range(NCH):
            bank, fw, bidx = pj.acquire()
            for c in range(NCH):
                em.op("pe", lambda e, co=co, c=c, bank=bank, bi=bi: e.matmul(
                    bank[:],
                    wq[:, c * D + co * 128:c * D + co * 128 + 128],
                    sc.chunk(bi, c),
                    start=(c == 0), stop=(c == NCH - 1)),
                    waits=([f"{tag}_xnT", P + "_w0", P + "_bqk"] +
                           ([fw] if fw else [])) if c == 0 else (),
                    mark=(f"{tag}_qmm{co}" if c == NCH - 1 else None))
            em.op("act", lambda e, co=co, bank=bank: e.activation(
                qf[:, co * 512:(co + 1) * 512], bank[:],
                AF.Identity, bias=bq[:, co:co + 1]),
                waits=[f"{tag}_qmm{co}"], mark=f"{tag}_qf{co}")
            pj.release(bidx, f"{tag}_qf{co}")
        for co in range(NCH):
            bank, fw, bidx = pj.acquire()
            for c in range(NCH):
                em.op("pe", lambda e, co=co, c=c, bank=bank, bi=bi: e.matmul(
                    bank[:],
                    wk[:, c * D + co * 128:c * D + co * 128 + 128],
                    sc.chunk(bi, c),
                    start=(c == 0), stop=(c == NCH - 1)),
                    waits=([P + "_w1", P + "_kz0"] +
                           ([fw] if fw else [])) if c == 0 else (),
                    mark=(f"{tag}_kmm{co}" if c == NCH - 1 else None))
            em.op("act", lambda e, co=co, bank=bank: e.activation(
                kze[0:64, co * 512:(co + 1) * 512], bank[0:64, :],
                AF.Identity, bias=bk[0:64, co:co + 1]),
                waits=[f"{tag}_kmm{co}"], mark=f"{tag}_kfe{co}")
            em.op("act", lambda e, co=co, bank=bank: e.activation(
                kzo[64:128, co * 512:(co + 1) * 512], bank[64:128, :],
                AF.Identity, bias=bk[64:128, co:co + 1]),
                waits=[f"{tag}_kmm{co}"], mark=f"{tag}_kfo{co}")
            pj.release(bidx, f"{tag}_kfo{co}")
        for t in range(TPS):
            for eh in range(2):
                bank, fw, bidx = pj.acquire()
                for c in range(NCH):
                    em.op("pe", lambda e, t=t, eh=eh, c=c, bank=bank,
                          bi=bi: e.matmul(
                        bank[:],
                        sc.chunk(bi, c)[:, t * 128:t * 128 + 128],
                        wv[:, c * D + eh * 512:c * D + eh * 512 + 512],
                        start=(c == 0), stop=(c == NCH - 1)),
                        waits=([f"{tag}_xnT", P + "_w2", P + "_bv"] +
                               ([fw] if fw else [])) if c == 0 else (),
                        mark=(f"{tag}_vmm{t}_{eh}" if c == NCH - 1 else None))
                em.op("dve", lambda e, t=t, eh=eh, bank=bank: e.tensor_tensor(
                    vst[:, t * D + eh * 512:t * D + eh * 512 + 512], bank[:],
                    bvb[:, eh * 512:eh * 512 + 512], ALU.add),
                    waits=[f"{tag}_vmm{t}_{eh}"], mark=f"{tag}_vf{t}_{eh}")
                pj.release(bidx, f"{tag}_vf{t}_{eh}")
        for t in range(TPS):
            pa, pb2 = sc.ptr[0], sc.ptr[1]
            war = ([f"{tag}_ktev{t - 1}", f"{tag}_kteA{t - 1}"] if t >= 1
                   else [f"{tag}_trev{TPS - 1}"])
            for p in range(NCH):
                em.op("pe", lambda e, t=t, p=p, pa=pa: e.matmul(
                    pa[:, p * 128:(p + 1) * 128],
                    kze[:, p * 512 + t * 128:p * 512 + t * 128 + 128],
                    ident, is_transpose=True, start=True, stop=True),
                    waits=([f"{tag}_kfe{NCH - 1}", f"{tag}_kfo{NCH - 1}"] + war)
                    if p == 0 else (),
                    mark=(f"{tag}_ktrA{t}" if p == NCH - 1 else None))
            for p in range(NCH):
                em.op("pe", lambda e, t=t, p=p, pb2=pb2: e.matmul(
                    pb2[:, p * 128:(p + 1) * 128],
                    kzo[:, p * 512 + t * 128:p * 512 + t * 128 + 128],
                    ident, is_transpose=True, start=True, stop=True),
                    mark=(f"{tag}_ktrB{t}" if p == NCH - 1 else None))
            em.op("act", lambda e, t=t, pa=pa: e.activation(
                ktm[:, t * D:(t + 1) * D], pa[:, 0:D], AF.Copy),
                waits=[f"{tag}_ktrA{t}"], mark=f"{tag}_kteA{t}")
            em.op("dve", lambda e, t=t, pb2=pb2: e.tensor_tensor(
                ktm[:, t * D:(t + 1) * D], ktm[:, t * D:(t + 1) * D],
                pb2[:, 0:D], ALU.add),
                waits=[f"{tag}_ktrB{t}", f"{tag}_kteA{t}"],
                mark=f"{tag}_ktev{t}")
        em.marks[f"{tag}_trev2"] = em.marks[f"{tag}_kteA{TPS - 1}"]
        em.marks[f"{tag}_trev3"] = em.marks[f"{tag}_ktev{TPS - 1}"]
        for t in range(TPS):
            j = (s % (NSTRIPE // 2)) * TPS + t
            ctag = f"{tag}c{t}"
            if j == 0:
                em.op("dve", lambda e: e.memset(stf[:], 0.0),
                      waits=([f"{P}s{s - 1}c{TPS - 1}_sadd"] if s > 0 else []),
                      mark=f"{ctag}_sadd")
            else:
                ptag = f"{tag}c{t - 1}" if t > 0 else f"{P}s{s - 1}c{TPS - 1}"
                em.op("act", lambda e: e.activation(stb[:], stf[:], AF.Copy),
                      waits=[f"{ptag}_sadd"], mark=f"{ctag}_scp")
            for g in range(4):
                sg = sT[:, (g % 2) * 512:(g % 2) * 512 + 512]
                bank, fw, bidx = pj.acquire()
                for hh in range(4):
                    h = 4 * g + hh
                    p = h // 2
                    kz = kze if h % 2 == 0 else kzo
                    em.op("pe", lambda e, p=p, hh=hh, kz=kz, bank=bank,
                          t=t: e.matmul(
                              bank[:, hh * 128:hh * 128 + 128],
                              kz[:, p * 512 + t * 128:p * 512 + t * 128 + 128],
                              qf[:, p * 512 + t * 128:p * 512 + t * 128 + 128],
                              start=True, stop=True),
                          waits=([f"{tag}_ktev{t}", P + "_msk"] +
                                 ([fw] if fw else [])) if hh == 0 else (),
                          mark=(f"{ctag}_smm{g}" if hh == 3 else None))
                em.op("dve", lambda e, sg=sg, bank=bank: e.tensor_tensor(
                    sg, bank[:], mask4[:], ALU.mult),
                    waits=[f"{ctag}_smm{g}"], mark=f"{ctag}_msk{g}")
                pj.release(bidx, f"{ctag}_msk{g}")
                for hh in range(4):
                    h = 4 * g + hh
                    p = h // 2
                    ob_bank = po[p // 4]
                    col = (p % 4) * 128
                    cb = (h % 2) * 64
                    em.op("pe", lambda e, h=h, col=col, cb=cb, sg=sg,
                          hh=hh, t=t, ob_bank=ob_bank, j=j: e.matmul(
                              ob_bank[cb:cb + 64, col:col + 128],
                              vst[:, t * D + h * DH:t * D + h * DH + 64],
                              sg[:, hh * 128:hh * 128 + 128],
                              start=True, stop=(j == 0),
                              tile_position=(0, cb)),
                          waits=([f"{ctag}_msk{g}", f"{tag}_vf{t}_0",
                                  f"{tag}_vf{t}_1"] +
                                 ([f"{tag}c{t - 1}_oev{p // 4}"]
                                  if t >= 1 else
                                  ([f"{P}s{s - 1}c{TPS - 1}_oev{p // 4}"]
                                   if s >= 1 else [])))
                          if (hh == 0 and g == 0) else
                          ([f"{ctag}_msk{g}"] if hh == 0 else ()),
                          mark=(f"{ctag}_omm{g}"
                                if (j == 0 and hh == 3) else None))
                    if j > 0:
                        em.op("pe", lambda e, h=h, col=col, cb=cb, t=t,
                              ob_bank=ob_bank: e.matmul(
                                  ob_bank[cb:cb + 64, col:col + 128],
                                  stb[:, h * DH:(h + 1) * DH],
                                  qf[:, (h // 2) * 512 + t * 128:
                                     (h // 2) * 512 + t * 128 + 128],
                                  start=False, stop=True,
                                  tile_position=(0, cb)),
                              waits=[f"{ctag}_scp"] if hh == 0 else (),
                              mark=(f"{ctag}_omm{g}" if hh == 3 else None))
            bank, fw, bidx = pj.acquire()
            for h in range(H):
                p = h // 2
                mb = (h % 2) * 64
                em.op("pe", lambda e, h=h, p=p, mb=mb, t=t, bank=bank:
                      e.matmul(
                          bank[mb:mb + 64, p * 64:(p + 1) * 64],
                          ktm[:, t * D + p * 128 + mb:t * D + p * 128 + mb + 64],
                          vst[:, t * D + h * DH:t * D + h * DH + 64],
                          start=True, stop=True, tile_position=(0, mb)),
                      waits=([f"{tag}_ktev{t}"] + ([fw] if fw else []))
                      if h == 0 else (),
                      mark=(f"{ctag}_dmm" if h == H - 1 else None))
            em.op("dve", lambda e, bank=bank: e.tensor_tensor(
                stf[0:64, :].rearrange("p (h c) -> p h c", c=DH)[:, 0::2, :],
                stf[0:64, :].rearrange("p (h c) -> p h c", c=DH)[:, 0::2, :],
                bank[0:64, :].rearrange("p (h c) -> p h c", c=DH),
                ALU.add),
                waits=[f"{ctag}_dmm"] + ([f"{ctag}_scp"] if j > 0 else []))
            em.op("dve", lambda e, bank=bank: e.tensor_tensor(
                stf[64:128, :].rearrange("p (h c) -> p h c", c=DH)[:, 1::2, :],
                stf[64:128, :].rearrange("p (h c) -> p h c", c=DH)[:, 1::2, :],
                bank[64:128, :].rearrange("p (h c) -> p h c", c=DH),
                ALU.add),
                mark=f"{ctag}_sadd")
            pj.release(bidx, f"{ctag}_sadd")
            for b in range(2):
                em.op("act", lambda e, b=b, t=t: e.activation(
                    ofm[:, 0:NCH * 512].rearrange("p (c w) -> p c w", c=NCH)
                    [:, 4 * b:4 * b + 4, t * 128:t * 128 + 128],
                    po[b][:, 0:512].rearrange("p (c w) -> p c w", c=4),
                    AF.Copy),
                    waits=[f"{ctag}_omm{2 * b}", f"{ctag}_omm{2 * b + 1}"],
                    mark=f"{ctag}_oev{b}")
            ob = outb
            for eh in range(2):
                bank, fw, bidx = pj.acquire()
                for co in range(NCH):
                    em.op("pe", lambda e, t=t, eh=eh, co=co, bank=bank:
                          e.matmul(
                              bank[:],
                              ofm[:, co * 512 + t * 128:co * 512 + t * 128 + 128],
                              wo[:, co * D + eh * 512:co * D + eh * 512 + 512],
                              start=(co == 0), stop=(co == NCH - 1)),
                          waits=([f"{ctag}_oev0", f"{ctag}_oev1",
                                  P + "_w3", P + "_bo"] +
                                 ([fw] if fw else [])) if co == 0 else (),
                          mark=(f"{ctag}_wmm{eh}" if co == NCH - 1 else None))
                wfw = [f"{ctag}_wmm{eh}"]
                if s >= 1 and t == 0 and eh == 0:
                    wfw.append(f"{tag[:-1]}{s - 1}_store"
                               if False else f"{P}s{s - 1}_store")
                em.op("dve", lambda e, t=t, eh=eh, bank=bank, ob=ob:
                      e.tensor_tensor(
                          ob[:, t * D + eh * 512:t * D + eh * 512 + 512],
                          bank[:], bob[:, eh * 512:eh * 512 + 512], ALU.add),
                      waits=wfw,
                      mark=f"{ctag}_wf{eh}")
                pj.release(bidx, f"{ctag}_wf{eh}")
        ln_post_add(em, lno, s, outb, sc.xs[s % len(sc.xs)], x_out, tag,
                    waits_first=[f"{tag}c{TPS - 1}_wf1"])



def build_pass_l1attn(em, sb, ps, cd, x_in, x_out):
    ident = cd["ident_s"]
    P = "p3"
    W3 = 3 * D
    wqkv = sb(P + "_wqkv", [128, NCH * W3], BF16)
    wout = sb(P + "_wout", [128, NCH * D], BF16)
    outb = sb(P + "_outb", [128, TPS * D], F32)
    wstage = [outb[:, 0:W3 // 2], outb[:, W3 // 2:W3]]
    cosT = sb(P + "_cos", [128, SEQLEN], BF16)
    ssT = sb(P + "_ss", [128, SEQLEN], BF16)
    lmask = sb(P + "_lm", [128, 4 * BS], BF16)
    pj = BankPool([ps(P + f"_pj{i}", [128, 512], F32) for i in range(2)])
    sp2 = [ps(P + "_spa", [128, 1024], F32), ps(P + "_spb", [128, 1024], F32)]
    po = ps(P + "_po", [128, 1024], F32)
    ptr_aps = [po.bitcast(BF16)[:, 0:NCH * 128],
               sp2[1].bitcast(BF16)[:, 0:NCH * 128]]
    sc = StripeCtx(em, sb, ps, P, carry=False, nxs=1, nxnt=1,
                   ptr_aps=ptr_aps)
    qraw = sb(P + "_qraw", [128, NCH * 512], BF16)
    kraw = sb(P + "_kraw", [128, NCH * 512], BF16)
    qrot = sb(P + "_qrot", [128, NCH * 512], BF16)
    kze = sb(P + "_kze", [128, NCH * 512], BF16)
    kzo = sb(P + "_kzo", [128, NCH * 512], BF16)
    tmp1 = outb[:, 3072:3328].bitcast(BF16)
    tmp2 = outb[:, 3328:3584].bitcast(BF16)
    tmp3 = outb[:, 3584:3840].bitcast(BF16)
    vst = sb(P + "_vst", [128, TPS * H * 128], BF16)
    ptile = [sb(P + f"_pt{i}", [128, 1024], BF16) for i in range(2)]
    lns = sb(P + "_lns", [128, 1024], F32)
    rcp = sb(P + "_rcp", [128, 1024], F32)
    ofm = sb(P + "_ofm", [128, NCH * 512], BF16)
    lno = [LNBufs(sb, P + f"_lo{i}") for i in range(2)]

    load_weights(em, wstage, [wqkv, wout],
                 [cd["l1a_wqkvT"], cd["l1a_woutT"]], P)
    em.dma("sp", lambda e: e.dma_start(cosT[:], cd["l1cos"]), inc=P + "_mi",
           mark=P + "_c0")
    em.dma("sp", lambda e: e.dma_start(ssT[:], cd["l1sin"]), inc=P + "_mi",
           waits=[P + "_c0"], mark=P + "_c1")
    em.dma("sp", lambda e: e.dma_start(lmask[:], cd["l1mask"]), inc=P + "_mi",
           waits=[P + "_c1"], mark=P + "_c2")
    em.op("dve", lambda e: e.memset(kze[64:128, :], 0.0))
    em.op("dve", lambda e: e.memset(kzo[0:64, :], 0.0))
    em.op("dve", lambda e: e.memset(
        vst.rearrange("p (s c) -> p s c", c=128)[:, :, 64:128], 1.0),
        mark=P + "_init")

    chain = {"exp": [None, None], "omm": [None, None],
             "norm": [None, None], "ofmd": None}

    for s in range(NSTRIPE):
        tag = f"{P}s{s}"
        war = [f"{P}s{s - 1}_pedone"] if s >= 1 else []
        xw = None
        pw = []
        if s >= 1:
            pw = [c for c in (chain["norm"][0], chain["norm"][1],
                              chain["exp"][1]) if c]
        bi = stripe_prologue(em, sc, s, x_in, ident, xnt_war=war,
                             x_extra_wait=xw, ptr_war=pw)

        def proj_fm(nm, dst, woff):
            for co in range(NCH):
                bank, fw, bidx = pj.acquire()
                for c in range(NCH):
                    em.op("pe", lambda e, co=co, c=c, bank=bank, bi=bi,
                          woff=woff: e.matmul(
                              bank[:],
                              wqkv[:, c * W3 + woff + co * 128:
                                   c * W3 + woff + co * 128 + 128],
                              sc.chunk(bi, c),
                              start=(c == 0), stop=(c == NCH - 1)),
                          waits=([f"{tag}_xnT", P + "_w0", P + "_init"] +
                                 ([fw] if fw else [])) if c == 0 else (),
                          mark=(f"{tag}_{nm}mm{co}" if c == NCH - 1
                                else None))
                evw = [f"{tag}_{nm}mm{co}"]
                if s >= 1:
                    evw.append(f"{P}s{s - 1}_{nm}rot{co}")
                em.op("act", lambda e, co=co, bank=bank, dst=dst:
                      e.activation(dst[:, co * 512:(co + 1) * 512], bank[:],
                                   AF.Copy),
                      waits=evw, mark=f"{tag}_{nm}ev{co}")
                pj.release(bidx, f"{tag}_{nm}ev{co}")
        proj_fm("q", qraw, 0)
        proj_fm("k", kraw, D)
        for t in range(TPS):
            for eh in range(2):
                bank, fw, bidx = pj.acquire()
                for c in range(NCH):
                    em.op("pe", lambda e, t=t, eh=eh, c=c, bank=bank,
                          bi=bi: e.matmul(
                              bank[:],
                              sc.chunk(bi, c)[:, t * 128:t * 128 + 128],
                              wqkv[:, c * W3 + 2 * D + eh * 512:
                                   c * W3 + 2 * D + eh * 512 + 512],
                              start=(c == 0), stop=(c == NCH - 1)),
                          waits=([fw] if (c == 0 and fw) else ()),
                          mark=(f"{tag}_vmm{t}_{eh}" if c == NCH - 1
                                else None))
                vw = [f"{tag}_vmm{t}_{eh}"]
                if s >= 1 and t == 0 and eh == 0:
                    vw.append(f"{P}s{s - 1}_odone")
                em.op("act", lambda e, t=t, eh=eh, bank=bank: e.activation(
                    vst.rearrange("p (tt h c) -> p tt h c", tt=TPS, h=H)
                    [:, t, 8 * eh:8 * eh + 8, 0:64],
                    bank[:].rearrange("p (h c) -> p h c", c=64),
                    AF.Copy),
                    waits=vw, mark=f"{tag}_vev{t}_{eh}")
                pj.release(bidx, f"{tag}_vev{t}_{eh}")
        off = (s % (NSTRIPE // 2)) * 512
        for c in range(NCH):
            em.op("dve", lambda e, c=c, off=off: e.tensor_tensor(
                qrot[:, c * 512:(c + 1) * 512],
                qraw[:, c * 512:(c + 1) * 512],
                cosT[:, off:off + 512], ALU.mult),
                waits=[f"{tag}_qev{c}", P + "_c2"] if c == 0
                else [f"{tag}_qev{c}"])
            for sq4 in range(4):
                sq = sq4 * 32
                tw = []
                if sq4 == 0:
                    if c > 0:
                        tw = [f"{tag}_qc{c - 1}"]
                    elif s >= 1:
                        tw = [f"{P}s{s - 1}_kc{NCH - 1}",
                              f"{P}s{s - 1}_store"]
                em.op("dve", lambda e, c=c, sq=sq, off=off:
                      e.tensor_tensor(
                          tmp1[sq:sq + 32, :],
                          qraw[sq:sq + 32, c * 512:(c + 1) * 512],
                          ssT[sq:sq + 32, off:off + 512],
                          ALU.mult),
                      waits=tw,
                      mark=(f"{tag}_qq{c}" if sq4 == 3 else None))
            for qt in range(4):
                sq = (qt ^ 1) * 32
                em.op("act", lambda e, qt=qt, sq=sq: e.activation(
                    tmp2[qt * 32:qt * 32 + 32, :], tmp1[sq:sq + 32, :],
                    AF.Copy),
                    waits=[f"{tag}_qq{c}"] if qt == 0 else (),
                    mark=(f"{tag}_qc{c}" if qt == 3 else None))
            em.op("dve", lambda e, c=c: e.tensor_tensor(
                qrot[:, c * 512:(c + 1) * 512],
                qrot[:, c * 512:(c + 1) * 512], tmp2[:], ALU.add),
                waits=[f"{tag}_qc{c}"],
                mark=f"{tag}_qrot{c}")
        for c in range(NCH):
            em.op("dve", lambda e, c=c, off=off: e.tensor_tensor(
                tmp3[:], kraw[:, c * 512:(c + 1) * 512],
                cosT[:, off:off + 512], ALU.mult),
                waits=[f"{tag}_kev{c}"])
            for sq4 in range(4):
                sq = sq4 * 32
                tw = []
                if sq4 == 0:
                    tw = [f"{tag}_kc{c - 1}" if c > 0
                          else f"{tag}_qc{NCH - 1}"]
                em.op("dve", lambda e, c=c, sq=sq, off=off:
                      e.tensor_tensor(
                          tmp1[sq:sq + 32, :],
                          kraw[sq:sq + 32, c * 512:(c + 1) * 512],
                          ssT[sq:sq + 32, off:off + 512],
                          ALU.mult),
                      waits=tw,
                      mark=(f"{tag}_kq{c}" if sq4 == 3 else None))
            for qt in range(4):
                sq = (qt ^ 1) * 32
                em.op("act", lambda e, qt=qt, sq=sq: e.activation(
                    tmp2[qt * 32:qt * 32 + 32, :], tmp1[sq:sq + 32, :],
                    AF.Copy),
                    waits=[f"{tag}_kq{c}"] if qt == 0 else (),
                    mark=(f"{tag}_kc{c}" if qt == 3 else None))
            em.op("dve", lambda e, c=c: e.tensor_tensor(
                tmp3[:], tmp3[:], tmp2[:], ALU.add),
                waits=[f"{tag}_kc{c}"])
            em.op("dve", lambda e, c=c: e.tensor_copy(
                kze[0:64, c * 512:(c + 1) * 512], tmp3[0:64, :]))
            em.op("dve", lambda e, c=c: e.tensor_copy(
                kzo[64:128, c * 512:(c + 1) * 512], tmp3[64:128, :]),
                mark=f"{tag}_krot{c}")
        em.marks[f"{tag}_rope"] = em.marks[f"{tag}_krot{NCH - 1}"]
        for b in range(2):
            btag = f"{tag}b{b}"
            for gp in range(8):
                pt = ptile[gp % 2]
                sp = sp2[gp % 2]
                poh = po[:, (gp % 2) * 512:(gp % 2) * 512 + 512]
                for hi in range(2):
                    h = 2 * gp + hi
                    co = h // 2
                    kz = kze if h % 2 == 0 else kzo
                    for kt in range(2):
                        ws = []
                        if hi == 0 and kt == 0:
                            ws = [f"{tag}_rope"]
                            if chain["exp"][gp % 2] is not None:
                                ws.append(chain["exp"][gp % 2])
                        em.op("pe", lambda e, co=co, kz=kz, kt=kt, b=b,
                              hi=hi, sp=sp: e.matmul(
                                  sp[:, hi * 512 + kt * 256:
                                     hi * 512 + kt * 256 + 256],
                                  kz[:, co * 512 + b * 256 + kt * 128:
                                     co * 512 + b * 256 + kt * 128 + 128],
                                  qrot[:, co * 512 + b * 256:
                                       co * 512 + b * 256 + 256],
                                  start=True, stop=True),
                              waits=ws,
                              mark=(f"{btag}g{gp}_smm"
                                    if (hi == 1 and kt == 1) else None))
                ptw = [f"{btag}g{gp}_smm"]
                if chain["omm"][gp % 2] is not None:
                    ptw.append(chain["omm"][gp % 2])
                em.op("act", lambda e, pt=pt, sp=sp: e.activation(
                    pt[:], sp[:], AF.Exp, scale=DH ** (-0.5)),
                    waits=ptw, mark=f"{btag}g{gp}_exp")
                chain["exp"][gp % 2] = f"{btag}g{gp}_exp"
                em.op("dve", lambda e, pt=pt: e.tensor_tensor(
                    pt[:], pt[:], lmask[:], ALU.mult),
                    waits=[f"{btag}g{gp}_exp"], mark=f"{btag}g{gp}_msk")
                for hi in range(2):
                    h = 2 * gp + hi
                    for kt in range(2):
                        ws = [f"{btag}g{gp}_msk", f"{tag}_vev{TPS - 1}_1"]
                        if hi == 0 and kt == 0:
                            if chain["norm"][gp % 2] is not None:
                                ws.append(chain["norm"][gp % 2])
                        em.op("pe", lambda e, h=h, hi=hi, kt=kt, b=b, pt=pt,
                              poh=poh: e.matmul(
                                  poh[:, hi * 256:hi * 256 + 256],
                                  vst[:, (2 * b + kt) * H * 128 + h * 128:
                                      (2 * b + kt) * H * 128 + h * 128 + 128],
                                  pt[:, hi * 512 + kt * 256:
                                     hi * 512 + kt * 256 + 256],
                                  start=(kt == 0), stop=(kt == 1)),
                              waits=ws,
                              mark=(f"{btag}g{gp}_omm"
                                    if (hi == 1 and kt == 1) else None))
                chain["omm"][gp % 2] = f"{btag}g{gp}_omm"
                ftag = f"{btag}g{gp}"
                hb = (gp % 2) * 512
                lw = [f"{btag}g{gp}_omm"]
                if chain["ofmd"] is not None:
                    lw.append(chain["ofmd"])
                em.op("act", lambda e, hb=hb: e.activation(
                    lns[64:128, hb:hb + 512], po[64:128, hb:hb + 512],
                    AF.Ln),
                    waits=lw, mark=f"{ftag}_ln")
                em.op("act", lambda e, hb=hb: e.activation(
                    rcp[0:64, hb:hb + 512], lns[64:128, hb:hb + 512],
                    AF.Exp, scale=-1.0),
                    waits=[f"{ftag}_ln"], mark=f"{ftag}_rcp")
                em.op("dve", lambda e, hb=hb: e.tensor_tensor(
                    lns[0:64, hb:hb + 512].bitcast(BF16)[:, 0:1024]
                    [:, 0:512] if False else
                    lns[0:64, :].bitcast(BF16)[:, hb:hb + 512],
                    po[0:64, hb:hb + 512], rcp[0:64, hb:hb + 512],
                    ALU.mult),
                    waits=[f"{ftag}_rcp"], mark=f"{ftag}_on")
                for hh in range(2):
                    h = 2 * gp + hh
                    cb = (h % 2) * 64
                    co = h // 2
                    em.op("act", lambda e, hh=hh, cb=cb, co=co, b=b, hb=hb:
                          e.activation(
                              ofm[cb:cb + 64,
                                  co * 512 + b * 256:co * 512 + b * 256 + 256],
                              lns[0:64, :].bitcast(BF16)
                              [:, hb + hh * 256:hb + hh * 256 + 256],
                              AF.Copy),
                          waits=[f"{ftag}_on"],
                          mark=(f"{ftag}_ofm" if hh == 1 else None))
                chain["norm"][gp % 2] = f"{ftag}_ofm"
                chain["ofmd"] = f"{ftag}_ofm"
            em.marks[f"{btag}_odone"] = em.marks[f"{btag}g7_omm"]
            for tt in range(2):
                t = 2 * b + tt
                for eh in range(2):
                    bank, fw, bidx = pj.acquire()
                    for co in range(NCH):
                        em.op("pe", lambda e, t=t, eh=eh, co=co, bank=bank:
                              e.matmul(
                                  bank[:],
                                  ofm[:, co * 512 + t * 128:
                                      co * 512 + t * 128 + 128],
                                  wout[:, co * D + eh * 512:
                                       co * D + eh * 512 + 512],
                                  start=(co == 0), stop=(co == NCH - 1)),
                              waits=([f"{btag}g6_ofm", f"{btag}g7_ofm",
                                      P + "_w1"] +
                                     ([fw] if fw else [])) if co == 0 else (),
                              mark=(f"{tag}_wmm{t}_{eh}" if co == NCH - 1
                                    else None))
                    evw = [f"{tag}_wmm{t}_{eh}"]
                    if s >= 1 and t == 0 and eh == 0:
                        evw.append(f"{P}s{s - 1}_store")
                    em.op("act", lambda e, t=t, eh=eh, bank=bank:
                          e.activation(
                              outb[:, t * D + eh * 512:t * D + eh * 512 + 512],
                              bank[:], AF.Copy),
                          waits=evw, mark=f"{tag}_wf{t}_{eh}")
                    pj.release(bidx, f"{tag}_wf{t}_{eh}")
        em.marks[f"{tag}_odone"] = em.marks[f"{tag}b1_odone"]
        em.marks[f"{tag}_pedone"] = em.marks[f"{tag}_wmm{TPS - 1}_1"]
        ln_post_add(em, lno, s, outb, sc.xs[s % len(sc.xs)], x_out, tag,
                    waits_first=[f"{tag}_wf{TPS - 1}_1"])



def build_pass_l1ffn(em, sb, ps, cd, x_in, x_out):
    ident = cd["ident_s"]
    P = "p4"
    NHC = HID // 128
    wg = sb(P + "_wg", [128, NCH * HID], BF16)
    wu = sb(P + "_wu", [128, NCH * HID], BF16)
    wd = sb(P + "_wd", [128, NHC * D], BF16)
    sc = StripeCtx(em, sb, ps, P, carry=False, nxs=1, nxnt=1)
    hact = sb(P + "_hact", [128, NHC * 512], BF16)
    wstage = [hact.bitcast(I32)[:, 0:HID // 2],
              hact.bitcast(I32)[:, HID // 2:HID]]
    silu_t = [sb(P + f"_sl{i}", [128, 512], F32) for i in range(2)]
    lno = LNBufs(sb, P + "_lo")
    pg = BankPool([ps(P + f"_pg{i}", [128, 512], F32) for i in range(4)])
    pd = [ps(P + f"_pd{i}", [128, 512], F32) for i in range(2)]

    load_weights(em, wstage, [wg, wu, wd],
                 [cd["l1f_wgT"], cd["l1f_wuT"], cd["l1f_wdT"]], P,
                 scale=1.0 / 127.0)

    for s in range(NSTRIPE):
        tag = f"{P}s{s}"
        war = [f"{P}s{s - 1}_gu{NHC - 1}"] if s >= 1 else []
        xw = f"{P}s{s - 1}_store" if s >= 1 else None
        bi = stripe_prologue(em, sc, s, x_in, ident, xnt_war=war,
                             x_extra_wait=xw)
        for hc in range(NHC):
            bkg, fwg, big = pg.acquire()
            for c in range(NCH):
                em.op("pe", lambda e, hc=hc, c=c, bkg=bkg, bi=bi: e.matmul(
                    bkg[:],
                    wg[:, c * HID + hc * 128:c * HID + hc * 128 + 128],
                    sc.chunk(bi, c),
                    start=(c == 0), stop=(c == NCH - 1)),
                    waits=([f"{tag}_xnT", P + "_w0"] +
                           ([fwg] if fwg else [])) if c == 0 else (),
                    mark=(f"{tag}_gmm{hc}" if c == NCH - 1 else None))
            bku, fwu, biu = pg.acquire()
            for c in range(NCH):
                em.op("pe", lambda e, hc=hc, c=c, bku=bku, bi=bi: e.matmul(
                    bku[:],
                    wu[:, c * HID + hc * 128:c * HID + hc * 128 + 128],
                    sc.chunk(bi, c),
                    start=(c == 0), stop=(c == NCH - 1)),
                    waits=([P + "_w1"] + ([fwu] if fwu else []))
                    if c == 0 else (),
                    mark=(f"{tag}_umm{hc}" if c == NCH - 1 else None))
            sl = silu_t[hc % 2]
            sl_war = [f"{tag}_ha{hc - 2}"] if hc >= 2 else (
                [f"{P}s{s - 1}_ha{NHC - 2 + hc}"] if s >= 1 else [])
            em.op("act", lambda e, hc=hc, bkg=bkg, sl=sl: e.activation(
                sl[:], bkg[:], AF.Silu),
                waits=[f"{tag}_gmm{hc}"] + sl_war, mark=f"{tag}_sil{hc}")
            pg.release(big, f"{tag}_sil{hc}")
            ha_war = ([f"{P}s{s - 1}_dmm{TPS - 1}_1"]
                      if (s >= 1 and hc == 0) else [])
            if s == 0 and hc == 0:
                ha_war = [P + "_w2"]
            em.op("dve", lambda e, hc=hc, bku=bku, sl=sl: e.tensor_tensor(
                hact[:, hc * 512:(hc + 1) * 512], bku[:], sl[:], ALU.mult),
                waits=[f"{tag}_umm{hc}", f"{tag}_sil{hc}"] + ha_war,
                mark=f"{tag}_ha{hc}")
            pg.release(biu, f"{tag}_ha{hc}")
        em.marks[f"{tag}_gu{NHC - 1}"] = em.marks[f"{tag}_umm{NHC - 1}"]
        xs = sc.xs[s % len(sc.xs)]
        ln = lno
        for t in range(TPS):
            for eh in range(2):
                bank = pd[eh]
                for hc in range(NHC):
                    em.op("pe", lambda e, t=t, eh=eh, hc=hc, bank=bank:
                          e.matmul(
                              bank[:],
                              hact[:, hc * 512 + t * 128:hc * 512 + t * 128 + 128],
                              wd[:, hc * D + eh * 512:hc * D + eh * 512 + 512],
                              start=(hc == 0), stop=(hc == NHC - 1)),
                          waits=([f"{tag}_ha{NHC - 1}", P + "_w2"] +
                                 ([f"{tag}t{t - 1}_ap{eh}"] if t >= 1 else
                                  ([f"{P}s{s - 1}t{TPS - 1}_ap{eh}"]
                                   if s >= 1 else []))) if hc == 0 else (),
                          mark=(f"{tag}_dmm{t}_{eh}" if hc == NHC - 1
                                else None))
            for eh in range(2):
                em.op("dve", lambda e, t=t, eh=eh: e.bn_stats(
                    ln.st[:, t * 12 + eh * 6:t * 12 + eh * 6 + 6], pd[eh][:]),
                    waits=[f"{tag}_dmm{t}_{eh}"] +
                          ([f"{P}s{s - 1}_lnfree"] if (s >= 1 and t == 0 and
                                                       eh == 0) else []))
            em.op("dve", lambda e, t=t: e.bn_aggr(
                ln.mv[:, t * 2:t * 2 + 2], ln.st[:, t * 12:t * 12 + 12]))
            em.op("dve", lambda e, t=t: e.tensor_scalar(
                ln.vs[:, t:t + 1], ln.mv[:, 2 * t + 1:2 * t + 2],
                EPS, None, ALU.add))
            newton_rsqrt(em, ln.vs[:, t:t + 1], ln.rs[:, t:t + 1],
                         ln.tmp[:, t:t + 1], ln.ci[:, t:t + 1])
            em.op("dve", lambda e, t=t: e.scalar_tensor_tensor(
                ln.nb[:, t:t + 1], ln.mv[:, 2 * t:2 * t + 1],
                -1.0, ln.rs[:, t:t + 1], ALU.mult, ALU.mult),
                mark=f"{tag}t{t}_lnp")
            for eh in range(2):
                em.op("dve", lambda e, t=t, eh=eh: e.tensor_scalar(
                    xs[:, t * D + eh * 512:t * D + eh * 512 + 512],
                    xs[:, t * D + eh * 512:t * D + eh * 512 + 512],
                    ln.nb[:, t:t + 1], None, ALU.add),
                    waits=[f"{tag}t{t}_lnp"])
                em.op("dve", lambda e, t=t, eh=eh: e.scalar_tensor_tensor(
                    xs[:, t * D + eh * 512:t * D + eh * 512 + 512],
                    pd[eh][:], ln.rs[:, t:t + 1],
                    xs[:, t * D + eh * 512:t * D + eh * 512 + 512],
                    ALU.mult, ALU.add),
                    mark=f"{tag}t{t}_ap{eh}")
        em.marks[f"{tag}_lnfree"] = em.marks[f"{tag}t{TPS - 1}_lnp"]
        if x_out.shape[0] == 128:
            em.dma("pool", lambda e, xs=xs, s=s: e.dma_start(
                x_out[:, s * TPS * D:(s + 1) * TPS * D], xs[:]),
                inc=f"st{s % 2}", mark=f"{tag}_store",
                waits=[f"{tag}t{TPS - 1}_ap0", f"{tag}t{TPS - 1}_ap1"])
        else:
            em.dma("pool", lambda e, xs=xs, s=s: e.dma_start(
                x_out[s * STRIPE:(s + 1) * STRIPE, :]
                .rearrange("(t p) d -> p t d", p=128),
                xs.rearrange("p (t d) -> p t d", t=TPS)),
                inc=f"st{s % 2}", mark=f"{tag}_store",
                waits=[f"{tag}t{TPS - 1}_ap0", f"{tag}t{TPS - 1}_ap1"])



def _host_prep(inputs):
    f32 = np.float32
    bf16 = ml_dtypes.bfloat16
    shared = {}

    def T(a):
        return np.ascontiguousarray(np.asarray(a).T)

    shared["l0a_wqT"] = T(inputs["L0_attn_wq_w"]).astype(f32, copy=False)
    shared["l0a_wkT"] = T(inputs["L0_attn_wk_w"]).astype(f32, copy=False)
    shared["l0a_wvT"] = T(inputs["L0_attn_wv_w"]).astype(f32, copy=False)
    shared["l0a_woT"] = T(inputs["L0_attn_wo_w"]).astype(f32, copy=False)
    for nm in ("q", "k", "v", "o"):
        shared[f"l0a_b{nm}"] = np.ascontiguousarray(
            np.asarray(inputs[f"L0_attn_w{nm}_b"]).astype(f32, copy=False)
            .reshape(NCH, 128).T)
    shared["l0a_bvb"] = np.ascontiguousarray(np.tile(
        np.asarray(inputs["L0_attn_wv_b"]).astype(f32, copy=False)[None, :],
        (128, 1)))
    shared["l0a_bob"] = np.ascontiguousarray(np.tile(
        np.asarray(inputs["L0_attn_wo_b"]).astype(f32, copy=False)[None, :],
        (128, 1)))
    shared["l0f_wrT"] = T(inputs["L0_ffn_W_r"]).astype(f32, copy=False)
    shared["l0f_wkT"] = T(inputs["L0_ffn_W_k"]).astype(f32, copy=False)
    shared["l0f_wvT"] = T(inputs["L0_ffn_W_v"]).astype(f32, copy=False)
    shared["l0f_mur"] = np.ascontiguousarray(
        np.asarray(inputs["L0_ffn_mu_r"]).astype(f32, copy=False)
        .reshape(NCH, 128).T)
    shared["l0f_muk"] = np.ascontiguousarray(
        np.asarray(inputs["L0_ffn_mu_k"]).astype(f32, copy=False)
        .reshape(NCH, 128).T)
    perm = np.zeros(3 * D, dtype=np.int64)
    for blk in range(2):
        for h in range(H):
            base = blk * D + 64 * h
            for j in range(DH // 2):
                perm[base + j] = base + 2 * j
                perm[base + 32 + j] = base + 2 * j + 1
    perm[2 * D:] = np.arange(2 * D, 3 * D)
    wqkv = np.asarray(inputs["L1_attn_Wqkv"]).astype(f32, copy=False)[perm]
    shared["l1a_wqkvT"] = np.ascontiguousarray(wqkv.T)
    shared["l1a_woutT"] = T(inputs["L1_attn_Wout"]).astype(f32, copy=False)
    shared["l1f_wgT"] = T(np.asarray(inputs["L1_ffn_w_gate"]))
    shared["l1f_wuT"] = T(np.asarray(inputs["L1_ffn_w_up"]))
    shared["l1f_wdT"] = T(np.asarray(inputs["L1_ffn_w_down"]))
    shared["ident"] = np.eye(128, dtype=bf16)
    ck = np.arange(128)[:, None]
    cq = np.arange(128)[None, :]
    shared["l0mask4"] = np.ascontiguousarray(
        np.tile((ck <= cq).astype(bf16), (1, 4)))
    inv_freq = 1.0 / 10000.0 ** (np.arange(0, DH, 2, dtype=np.float64) / DH)
    t = np.arange(SEQLEN, dtype=np.float64)
    ang = t[None, :] * inv_freq[:, None]
    shared["l1cos"] = np.ascontiguousarray(np.tile(np.cos(ang), (4, 1))
                                           .astype(bf16))
    sin = np.sin(ang)
    ss = np.concatenate([sin, -sin, sin, -sin], axis=0)
    shared["l1sin"] = np.ascontiguousarray(ss.astype(bf16))
    ck2 = np.arange(128)[:, None]
    cq2 = np.arange(BS)[None, :]
    m0 = (ck2 <= cq2).astype(bf16)
    m1 = (ck2 + 128 <= cq2).astype(bf16)
    shared["l1mask"] = np.ascontiguousarray(
        np.concatenate([m0, m1, m0, m1], axis=1))
    return shared


W_SHAPES = {
    "l0a_wqT": ([D, D], F32), "l0a_wkT": ([D, D], F32),
    "l0a_wvT": ([D, D], F32), "l0a_woT": ([D, D], F32),
    "l0a_bq": ([128, NCH], F32), "l0a_bk": ([128, NCH], F32),
    "l0a_bv": ([128, NCH], F32), "l0a_bo": ([128, NCH], F32),
    "l0f_wrT": ([D, D], F32), "l0f_wkT": ([D, D], F32),
    "l0f_wvT": ([D, D], F32),
    "l0f_mur": ([128, NCH], F32), "l0f_muk": ([128, NCH], F32),
    "l1a_wqkvT": ([D, 3 * D], F32), "l1a_woutT": ([D, D], F32),
    "l1f_wgT": ([D, HID], I32), "l1f_wuT": ([D, HID], I32),
    "l1f_wdT": ([HID, D], I32),
    "ident": ([128, 128], BF16), "l0mask4": ([128, 512], BF16),
    "l0a_bvb": ([128, D], F32), "l0a_bob": ([128, D], F32),
    "l1cos": ([128, SEQLEN], BF16), "l1sin": ([128, SEQLEN], BF16),
    "l1mask": ([128, 4 * BS], BF16),
}

PASS_BUILDERS = {"l0ffn": build_pass_l0ffn,
                 "l0attn": build_pass_l0attn,
                 "l1ffn": build_pass_l1ffn,
                 "l1attn": build_pass_l1attn}
PASS_WEIGHTS = {
    "l0attn": ["l0a_wqT", "l0a_wkT", "l0a_wvT", "l0a_woT",
               "l0a_bq", "l0a_bk", "l0a_bvb", "l0a_bob", "l0mask4", "ident"],
    "l0ffn": ["l0f_wrT", "l0f_wkT", "l0f_wvT", "l0f_mur", "l0f_muk", "ident"],
    "l1attn": ["l1a_wqkvT", "l1a_woutT", "l1cos", "l1sin",
               "l1mask", "ident"],
    "l1ffn": ["l1f_wgT", "l1f_wuT", "l1f_wdT", "ident"],
}


def build_program(passes):
    nc = bass.Bass("TRN2", target_bir_lowering=False)
    stack = ExitStack()
    em = Em(nc, stack)

    names = sorted({w for p in passes for w in PASS_WEIGHTS[p]})
    cd = {}
    for name in names:
        shp, dt = W_SHAPES[name]
        cd[name] = nc.dram_tensor(name, shp, dt, kind="ExternalInput").ap()

    def sb(name, shape, dt):
        return stack.enter_context(nc.sbuf_tensor(name, shape, dt))[:]

    def ps(name, shape, dt):
        return stack.enter_context(nc.psum_tensor(name, shape, dt))[:]

    ident_s = sb("c_ident", [128, 128], BF16)
    em.dma("sp", lambda e: e.dma_start(ident_s, cd["ident"]), inc="cst",
           mark="c_ident")
    cd["ident_s"] = ident_s

    x_first = nc.dram_tensor("x_in", [TOK, D], F32, kind="ExternalInput").ap()
    y_last = nc.dram_tensor("y", [TOK, D], F32, kind="ExternalOutput").ap()
    mids = [nc.dram_tensor(f"xmid{k}", [128, TOK * D // 128], F32,
                           kind="Internal").ap()
            for k in range(len(passes) - 1)]
    for k, p in enumerate(passes):
        x_in = x_first if k == 0 else mids[k - 1]
        x_out = y_last if k == len(passes) - 1 else mids[k]
        with ExitStack() as pass_stack:
            saved = stack
            inner_sb = (lambda st: lambda n, shp, dt:
                        st.enter_context(nc.sbuf_tensor(n, shp, dt))[:]
                        )(pass_stack)
            inner_ps = (lambda st: lambda n, shp, dt:
                        st.enter_context(nc.psum_tensor(n, shp, dt))[:]
                        )(pass_stack)
            PASS_BUILDERS[p](em, inner_sb, inner_ps, cd, x_in, x_out)
            em.barrier()
    with nc.Block() as blk:
        em.finalize(blk)
    stack.close()
    return nc, names


def kernel(**inputs):
    shared = _host_prep(inputs)
    x = np.asarray(inputs["L0_attn_x"]).astype(np.float32, copy=False)
    x = np.ascontiguousarray(x.reshape(NCORES, TOK, D))

    passes = ["l0attn", "l0ffn", "l1attn", "l1ffn"]
    nc, names = build_program(passes)
    in_maps = []
    for c in range(NCORES):
        m = {n: shared[n] for n in names}
        m["x_in"] = x[c]
        in_maps.append(m)
    res = run_bass_kernel_spmd(nc, in_maps, core_ids=list(range(NCORES)))
    y = np.stack([np.asarray(r["y"]) for r in res.results])
    return np.ascontiguousarray(y.reshape(16, SEQLEN, D))


# version 12 (final)
# speedup vs baseline: 1.0357x; 1.0036x over previous
import numpy as np
import ml_dtypes
from contextlib import ExitStack
import concourse.bass as bass
import concourse.mybir as mybir
from concourse.bass_utils import run_bass_kernel_spmd

F32 = mybir.dt.float32
BF16 = mybir.dt.bfloat16
I32 = mybir.dt.int32
ALU = mybir.AluOpType
AF = mybir.ActivationFunctionType

NCORES = 8
D = 1024
SEQLEN = 2048
TOK = 4096
STRIPE = 512
NSTRIPE = TOK // STRIPE
TPS = 4
NCH = D // 128
H = 16
DH = 64
BS = 256
HID = 3072
EPS = 1e-5

P1STAGE = 99

ENGINES = ("pe", "act", "dve", "pool", "sp")
ENG_ATTR = {"pe": "tensor", "act": "scalar", "dve": "vector", "pool": "gpsimd",
            "sp": "sync"}


class Em:

    def __init__(self, nc, stack):
        self.nc = nc
        self.stack = stack
        self.sems = {}
        self.cnt = {}
        self.ops = {e: [] for e in ENGINES}
        self.marks = {}

    def sem(self, name):
        if name not in self.sems:
            self.sems[name] = self.stack.enter_context(self.nc.semaphore(name))
            self.cnt[name] = 0
        return self.sems[name]

    def op(self, eng, fn, waits=(), inc=None, inc_amt=1, mark=None):
        wl = []
        for w in waits:
            if w is None:
                continue
            if isinstance(w, str):
                if w not in self.marks:
                    raise KeyError(f"unknown mark {w!r}")
                wl.append(self.marks[w])
            else:
                self.sem(w[0])
                wl.append(w)
        if mark is not None and inc is None:
            inc = "s_" + eng
        rec = {"fn": fn, "waits": wl, "inc": None}
        if inc is not None:
            self.sem(inc)
            self.cnt[inc] += inc_amt
            rec["inc"] = (inc, inc_amt)
            if mark is not None:
                self.marks[mark] = (inc, self.cnt[inc])
        self.ops[eng].append(rec)

    def dma(self, eng, fn, inc, mark=None, waits=()):
        self.op(eng, fn, waits=waits, inc=inc, inc_amt=16, mark=mark)

    def barrier(self):
        for eng in ENGINES:
            if self.ops[eng]:
                rec = self.ops[eng][-1]
                if rec["inc"] is None:
                    s = "s_" + eng
                    self.sem(s)
                    self.cnt[s] += 1
                    rec["inc"] = (s, 1)
        targets = [(s, self.cnt[s]) for s in list(self.cnt) if self.cnt[s] > 0]
        for eng in ENGINES:
            if not self.ops[eng]:
                continue
            for (s, v) in targets:
                self.ops[eng].append({"fn": None, "waits": [(s, v)],
                                      "inc": None})

    def finalize(self, blk):
        for eng in ENGINES:
            if not self.ops[eng]:
                continue

            def mkbody(ops, sems, sep):
                def run(e):
                    waited = {}
                    prev_compute = False
                    for rec in ops:
                        for (s, v) in rec["waits"]:
                            if waited.get(s, -1) < v:
                                e.wait_ge(sems[s], v)
                                waited[s] = v
                        if rec["fn"] is not None:
                            if sep and prev_compute:
                                e.drain()
                            ins = rec["fn"](e)
                            prev_compute = True
                            if rec["inc"] is not None:
                                s, amt = rec["inc"]
                                ins.then_inc(sems[s], amt)
                return run

            getattr(blk, ENG_ATTR[eng])(mkbody(self.ops[eng], self.sems,
                                               eng in ('dve', 'act', 'pool')))


class BankPool:

    def __init__(self, banks):
        self.banks = banks
        self.uses = 0
        self.free_mark = [None] * len(banks)

    def acquire(self):
        i = self.uses % len(self.banks)
        self.uses += 1
        return self.banks[i], self.free_mark[i], i

    def release(self, i, mark):
        self.free_mark[i] = mark


def newton_rsqrt(em, v_ap, y_ap, tmp_ap, ci_ap, waits=(), mark=None):
    MAGIC = 0x5f3759df
    em.op("dve", lambda e: e.tensor_scalar(
        y_ap.bitcast(I32), v_ap.bitcast(I32), 1, None, ALU.arith_shift_right),
        waits=waits)
    em.op("dve", lambda e: e.memset(ci_ap, MAGIC))
    em.op("dve", lambda e: e.tensor_tensor(
        y_ap.bitcast(I32), ci_ap, y_ap.bitcast(I32), ALU.subtract))
    for it in range(3):
        em.op("dve", lambda e: e.tensor_tensor(tmp_ap, y_ap, y_ap, ALU.mult))
        em.op("dve", lambda e: e.tensor_tensor(tmp_ap, tmp_ap, v_ap, ALU.mult))
        em.op("dve", lambda e: e.tensor_scalar(
            tmp_ap, tmp_ap, -0.5, 1.5, ALU.mult, ALU.add))
        em.op("dve", lambda e: e.tensor_tensor(y_ap, y_ap, tmp_ap, ALU.mult),
              mark=(mark if it == 2 else None))


class LNBufs:
    def __init__(self, sb, tag):
        self.st = sb(f"{tag}_st", [128, TPS * 12], F32)
        self.mv = sb(f"{tag}_mv", [128, TPS * 2], F32)
        self.vs = sb(f"{tag}_vs", [128, TPS], F32)
        self.rs = sb(f"{tag}_rs", [128, TPS], F32)
        self.nb = sb(f"{tag}_nb", [128, TPS], F32)
        self.tmp = sb(f"{tag}_tmp", [128, TPS], F32)
        self.ci = sb(f"{tag}_ci", [128, TPS], I32)


def ln_stats(em, ln, src_tiles, tag, waits_first=()):
    for t in range(TPS):
        for hh in range(2):
            em.op("dve", lambda e, t=t, hh=hh: e.bn_stats(
                ln.st[:, t * 12 + hh * 6:t * 12 + hh * 6 + 6],
                src_tiles[t][:, hh * 512:hh * 512 + 512]),
                waits=(waits_first if (t == 0 and hh == 0) else ()))
    for t in range(TPS):
        em.op("dve", lambda e, t=t: e.bn_aggr(
            ln.mv[:, t * 2:t * 2 + 2], ln.st[:, t * 12:t * 12 + 12]))
    for t in range(TPS):
        em.op("dve", lambda e, t=t: e.tensor_scalar(
            ln.vs[:, t:t + 1], ln.mv[:, 2 * t + 1:2 * t + 2],
            EPS, None, ALU.add))
    newton_rsqrt(em, ln.vs[:, 0:TPS], ln.rs[:, 0:TPS], ln.tmp[:, 0:TPS], ln.ci[:, 0:TPS])
    for t in range(TPS):
        em.op("dve", lambda e, t=t: e.scalar_tensor_tensor(
            ln.nb[:, t:t + 1], ln.mv[:, 2 * t:2 * t + 1],
            -1.0, ln.rs[:, t:t + 1], ALU.mult, ALU.mult),
            mark=(f"{tag}_lnp" if t == TPS - 1 else None))


class StripeCtx:

    def __init__(self, em, sb, ps, tag, carry=False, nxs=2, nxnt=2,
                 ptr_aps=None):
        self.em = em
        self.tag = tag
        self.carry = carry
        self.nxs = nxs
        self.nxnt = nxnt
        self.ptr_aps = ptr_aps
        self.w = 520 if carry else 512
        self.off = 1 if carry else 0
        self.xs = [sb(f"{tag}_x{i}", [128, TPS * D], F32)
                   for i in range(getattr(self, "nxs", 2))]
        self.xnT = [sb(f"{tag}_xnT{i}", [128, NCH * self.w], BF16)
                    for i in range(getattr(self, "nxnt", 2))]
        self.xn = sb(f"{tag}_xn", [128, TPS * D], BF16)
        self.ln = [LNBufs(sb, f"{tag}_li{i}") for i in range(2)]
        if getattr(self, "ptr_aps", None) is not None:
            self.ptr = self.ptr_aps
        else:
            self.ptr = [ps(f"{tag}_ptr{i}", [128, NCH * 128], BF16)
                        for i in range(2)]

    def chunk(self, bi, c):
        o = c * self.w + self.off
        return self.xnT[bi][:, o:o + 512]

    def chunk_shift(self, bi, c):
        o = c * self.w + self.off
        return self.xnT[bi][:, o - 1:o + 511]

    def chunk3d(self, bi):
        return self.xnT[bi][:, 0:NCH * self.w].rearrange(
            "p (c w) -> p c w", c=NCH)


def stripe_prologue(em, sc, s, x_dram, ident, xnt_war=None, x_extra_wait=None,
                    ptr_war=None):
    bi = s % len(sc.xnT)
    xs = sc.xs[s % len(sc.xs)]
    T0 = sc.tag
    tag = f"{T0}s{s}"
    w = []
    sx = s - len(sc.xs)
    if sx >= 0 and f"{T0}s{sx}_add" in em.marks:
        w.append(f"{T0}s{sx}_add")
    elif sx >= 0 and f"{T0}s{sx}_store" in em.marks:
        w.append(f"{T0}s{sx}_store")
    if x_extra_wait:
        w.append(x_extra_wait)
    if x_dram.shape[0] == 128:
        em.dma("sp", lambda e: e.dma_start(
            xs[:], x_dram[:, s * TPS * D:(s + 1) * TPS * D]),
            inc=f"ld{s % 2}", mark=f"{tag}_xld", waits=w)
    else:
        em.dma("sp", lambda e: e.dma_start(
            xs.rearrange("p (t d) -> p t d", t=TPS),
            x_dram[s * STRIPE:(s + 1) * STRIPE, :]
            .rearrange("(t p) d -> p t d", p=128)),
            inc=f"ld{s % 2}", mark=f"{tag}_xld", waits=w)
    tiles = [xs[:, t * D:(t + 1) * D] for t in range(TPS)]
    ln_stats(em, sc.ln[bi], tiles, tag, waits_first=[f"{tag}_xld"])
    for t in range(TPS):
        ww = [f"{tag}_lnp"] if t == 0 else []
        if t == 0 and s >= 1:
            ww.append(f"{T0}s{s - 1}_tr{TPS - 1}")
        em.op("act", lambda e, t=t, bi=bi: e.activation(
            sc.xn[:, t * D:(t + 1) * D], xs[:, t * D:(t + 1) * D],
            AF.Identity,
            bias=sc.ln[bi].nb[:, t:t + 1], scale=sc.ln[bi].rs[:, t:t + 1]),
            waits=ww, mark=f"{tag}_xn{t}")
    em.marks[f"{tag}_xn"] = em.marks[f"{tag}_xn{TPS - 1}"]
    for t in range(TPS):
        pb = sc.ptr[t % 2]
        bank_war = None
        if t >= 2:
            bank_war = f"{tag}_trev{t - 2}"
        elif s >= 1:
            bank_war = f"{T0}s{s - 1}_trev{t + 2}"
        bw = [bank_war] if bank_war else []
        if t < 2 and ptr_war:
            bw += [m for m in ptr_war if m in em.marks]
        for c in range(NCH):
            em.op("pe", lambda e, t=t, c=c, pb=pb: e.transpose(
                pb[:, c * 128:(c + 1) * 128],
                sc.xn[:, t * D + c * 128:t * D + c * 128 + 128], ident),
                waits=([f"{tag}_xn{t}", "c_ident"] + bw)
                if c == 0 else (),
                mark=(f"{tag}_tr{t}" if c == NCH - 1 else None))
        ev_w = [f"{tag}_tr{t}"]
        if t == 0 and xnt_war:
            ev_w += [m for m in xnt_war if m in em.marks]
        em.op("act", lambda e, t=t, pb=pb, bi=bi: e.activation(
            sc.chunk3d(bi)[:, :, sc.off + t * 128:sc.off + t * 128 + 128],
            pb[:, 0:NCH * 128].rearrange("p (c w) -> p c w", c=NCH), AF.Copy),
            waits=ev_w, mark=f"{tag}_trev{t}")
    em.marks[f"{tag}_xnT"] = em.marks[f"{tag}_trev{TPS - 1}"]
    return bi


def ln_post_add(em, lno, s, outb, xs, x_out, tag, waits_first):
    tiles = [outb[:, t * D:(t + 1) * D] for t in range(TPS)]
    ln = lno[s % 2]
    ln_stats(em, ln, tiles, tag + "o", waits_first=waits_first)
    for t in range(TPS):
        em.op("act", lambda e, t=t: e.activation(
            outb[:, t * D:(t + 1) * D], outb[:, t * D:(t + 1) * D],
            AF.Identity,
            bias=ln.nb[:, t:t + 1], scale=ln.rs[:, t:t + 1]),
            waits=[f"{tag}o_lnp"] if t == 0 else (),
            mark=(f"{tag}_lnapp" if t == TPS - 1 else None))
    em.op("dve", lambda e: e.tensor_tensor(outb[:], outb[:], xs[:], ALU.add),
          waits=[f"{tag}_lnapp"], mark=f"{tag}_add")
    if x_out.shape[0] == 128:
        em.dma("pool", lambda e: e.dma_start(
            x_out[:, s * TPS * D:(s + 1) * TPS * D], outb[:]),
            inc=f"st{s % 2}", mark=f"{tag}_store", waits=[f"{tag}_add"])
    else:
        em.dma("pool", lambda e: e.dma_start(
            x_out[s * STRIPE:(s + 1) * STRIPE, :]
            .rearrange("(t p) d -> p t d", p=128),
            outb.rearrange("p (t d) -> p t d", t=TPS)),
            inc=f"st{s % 2}", mark=f"{tag}_store", waits=[f"{tag}_add"])


def load_weights(em, wstage, dsts, srcs, tag, scale=None):
    n = 0
    sw = wstage[0].shape[1]
    nst = len(wstage)
    for wi, (wsb, wd) in enumerate(zip(dsts, srcs)):
        ncw = wd.shape[0] // 128
        cols = wd.shape[1]
        nparts = (cols + sw - 1) // sw
        for c in range(ncw):
            for pp in range(nparts):
                w0 = pp * sw
                w1 = min(cols, w0 + sw)
                st = wstage[n % nst]
                prev = f"{tag}_cast{n - nst}" if n >= nst else None
                em.dma("sp",
                       lambda e, st=st, wd=wd, c=c, w0=w0, w1=w1:
                       e.dma_start(st[:, 0:w1 - w0],
                                   wd[c * 128:(c + 1) * 128, w0:w1]),
                       inc=f"{tag}_wp{n % nst}",
                       waits=[prev] if prev else (),
                       mark=f"{tag}_wl{n}")
                kw = {} if scale is None else {"scale": scale}
                em.op("act", lambda e, wsb=wsb, st=st, c=c, cols=cols,
                      w0=w0, w1=w1, kw=kw:
                      e.activation(wsb[:, c * cols + w0:c * cols + w1],
                                   st[:, 0:w1 - w0], AF.Copy, **kw),
                      waits=[f"{tag}_wl{n}"], mark=f"{tag}_cast{n}")
                n += 1
            em.marks[f"{tag}_w{wi}c{c}"] = em.marks[f"{tag}_cast{n - 1}"]
        em.marks[f"{tag}_w{wi}"] = em.marks[f"{tag}_cast{n - 1}"]



def build_pass_l0ffn(em, sb, ps, cd, x_in, x_out):
    ident = cd["ident_s"]
    P = "p2"
    wr = sb(P + "_wr", [128, NCH * D], BF16)
    wk = sb(P + "_wk", [128, NCH * D], BF16)
    wv = sb(P + "_wv", [128, NCH * D], BF16)
    wstage = [sb(P + f"_wst{i}", [128, D], F32) for i in range(2)]
    mur = sb(P + "_mur", [128, NCH], F32)
    muk = sb(P + "_muk", [128, NCH], F32)
    sc = StripeCtx(em, sb, ps, P, carry=True)
    xrT = sb(P + "_xr", [128, NCH * 512], BF16)
    xkT = sb(P + "_xk", [128, NCH * 512], BF16)
    dT = sb(P + "_d", [128, NCH * 512], BF16)
    rsig = sb(P + "_rsig", [128, TPS * D], F32)
    kact = sb(P + "_kact", [128, NCH * 512], BF16)
    krelu = [sb(P + f"_krelu{i}", [128, 512], F32) for i in range(2)]
    outb = sb(P + "_out", [128, TPS * D], F32)
    lno = [LNBufs(sb, P + f"_lo{i}") for i in range(2)]
    pj = BankPool([ps(P + f"_pj{i}", [128, 512], F32) for i in range(4)])

    load_weights(em, wstage, [wr, wk, wv],
                 [cd["l0f_wrT"], cd["l0f_wkT"], cd["l0f_wvT"]], P)
    em.dma("sp", lambda e: e.dma_start(mur[:], cd["l0f_mur"]),
           inc=P + "_mi", mark=P + "_mul")
    em.dma("sp", lambda e: e.dma_start(muk[:], cd["l0f_muk"]),
           inc=P + "_mi", waits=[P + "_mul"], mark=P + "_mu")

    for s in range(NSTRIPE):
        tag = f"{P}s{s}"
        war = []
        if s >= 2:
            war.append(f"{P}s{s - 2}_mix")
        if s >= 1:
            war.append(f"{P}s{s - 1}_carry")
        bi = stripe_prologue(em, sc, s, x_in, ident, xnt_war=war)
        if s % (NSTRIPE // 2) == 0:
            em.op("dve", lambda e, bi=bi: e.memset(
                sc.chunk3d(bi)[:, :, 0:1], 0.0),
                waits=[f"{tag}_xnT"], mark=f"{tag}_carry")
        else:
            pb = 1 - bi
            em.op("dve", lambda e, bi=bi, pb=pb: e.tensor_copy(
                sc.chunk3d(bi)[:, :, 0:1], sc.chunk3d(pb)[:, :, 512:513]),
                waits=[f"{tag}_xnT"], mark=f"{tag}_carry")
        mix_war = []
        if s >= 1:
            mix_war = [f"{P}s{s - 1}_rmm{TPS - 1}_1", f"{P}s{s - 1}_kmm{NCH - 1}"]
        for c in range(NCH):
            em.op("dve", lambda e, c=c, bi=bi: e.tensor_tensor(
                dT[:, c * 512:(c + 1) * 512], sc.chunk(bi, c),
                sc.chunk_shift(bi, c)[:, 0:512], ALU.subtract),
                waits=([f"{tag}_carry", P + "_mu"] + mix_war) if c == 0 else ())
            em.op("dve", lambda e, c=c, bi=bi: e.scalar_tensor_tensor(
                xrT[:, c * 512:(c + 1) * 512], dT[:, c * 512:(c + 1) * 512],
                mur[:, c:c + 1], sc.chunk_shift(bi, c)[:, 0:512],
                ALU.mult, ALU.add))
            em.op("dve", lambda e, c=c, bi=bi: e.scalar_tensor_tensor(
                xkT[:, c * 512:(c + 1) * 512], dT[:, c * 512:(c + 1) * 512],
                muk[:, c:c + 1], sc.chunk_shift(bi, c)[:, 0:512],
                ALU.mult, ALU.add),
                mark=(f"{tag}_mix" if c == NCH - 1 else None))
        for t in range(TPS):
            for eh in range(2):
                bank, fw, bidx = pj.acquire()
                for c in range(NCH):
                    em.op("pe", lambda e, t=t, eh=eh, c=c, bank=bank: e.matmul(
                        bank, xrT[:, c * 512 + t * 128:c * 512 + t * 128 + 128],
                        wr[:, c * D + eh * 512:c * D + eh * 512 + 512],
                        start=(c == 0), stop=(c == NCH - 1)),
                        waits=([f"{tag}_mix", P + "_w0"] +
                               ([fw] if fw else [])) if c == 0 else (),
                        mark=(f"{tag}_rmm{t}_{eh}" if c == NCH - 1 else None))
                rs_war = ([f"{P}s{s - 1}_kv{t}_{eh}"] if s >= 1 else [])
                em.op("act", lambda e, t=t, eh=eh, bank=bank: e.activation(
                    rsig[:, t * D + eh * 512:t * D + eh * 512 + 512], bank,
                    AF.Sigmoid),
                    waits=[f"{tag}_rmm{t}_{eh}"] + rs_war,
                    mark=f"{tag}_rs{t}_{eh}")
                pj.release(bidx, f"{tag}_rs{t}_{eh}")
        for co in range(NCH):
            bank, fw, bidx = pj.acquire()
            for c in range(NCH):
                em.op("pe", lambda e, co=co, c=c, bank=bank: e.matmul(
                    bank, wk[:, c * D + co * 128:c * D + co * 128 + 128],
                    xkT[:, c * 512:(c + 1) * 512],
                    start=(c == 0), stop=(c == NCH - 1)),
                    waits=([P + "_w1"] + ([fw] if fw else [])) if c == 0 else (),
                    mark=(f"{tag}_kmm{co}" if c == NCH - 1 else None))
            kr = krelu[co % 2]
            kr_war = [f"{tag}_ksq{co - 2}"] if co >= 2 else (
                [f"{P}s{s - 1}_ksq{NCH - 2 + (co % 2) * 0 + co}"]
                if False else [])
            em.op("dve", lambda e, bank=bank, kr=kr: e.tensor_scalar(
                kr[:], bank, 0.0, None, ALU.max),
                waits=[f"{tag}_kmm{co}"] + kr_war, mark=f"{tag}_krel{co}")
            pj.release(bidx, f"{tag}_krel{co}")
            ks_war = ([f"{P}s{s - 1}_vmm{TPS - 1}_1"]
                      if (s >= 1 and co == 0) else [])
            em.op("act", lambda e, co=co, kr=kr: e.activation(
                kact[:, co * 512:(co + 1) * 512], kr[:], AF.Square),
                waits=[f"{tag}_krel{co}"] + ks_war, mark=f"{tag}_ksq{co}")
        ob = outb
        for t in range(TPS):
            for eh in range(2):
                bank, fw, bidx = pj.acquire()
                for c in range(NCH):
                    em.op("pe", lambda e, t=t, eh=eh, c=c, bank=bank: e.matmul(
                        bank, kact[:, c * 512 + t * 128:c * 512 + t * 128 + 128],
                        wv[:, c * D + eh * 512:c * D + eh * 512 + 512],
                        start=(c == 0), stop=(c == NCH - 1)),
                        waits=[f"{tag}_ksq{c}", P + "_w2"] +
                              ([fw] if (c == 0 and fw) else []),
                        mark=(f"{tag}_vmm{t}_{eh}" if c == NCH - 1 else None))
                kvw = [f"{tag}_vmm{t}_{eh}", f"{tag}_rs{t}_{eh}"]
                if s >= 1 and t == 0 and eh == 0:
                    kvw.append(f"{P}s{s - 1}_store")
                em.op("dve", lambda e, t=t, eh=eh, bank=bank, ob=ob:
                      e.tensor_tensor(
                          ob[:, t * D + eh * 512:t * D + eh * 512 + 512],
                          bank,
                          rsig[:, t * D + eh * 512:t * D + eh * 512 + 512],
                          ALU.mult),
                      waits=kvw,
                      mark=f"{tag}_kv{t}_{eh}")
                pj.release(bidx, f"{tag}_kv{t}_{eh}")
        ln_post_add(em, lno, s, outb, sc.xs[s % len(sc.xs)], x_out, tag,
                    waits_first=[f"{tag}_kv{TPS - 1}_1"])


def _p1_passthrough(em, sc, bi, s, tag, x_out, wait_mark):
    em.dma("pool", lambda e: e.dma_start(
        x_out[s * STRIPE:(s + 1) * STRIPE, :]
        .rearrange("(t p) d -> p t d", p=128),
        sc.xs[bi].rearrange("p (t d) -> p t d", t=TPS)),
        inc=f"st{s % 2}", mark=f"{tag}_store", waits=[wait_mark])



def build_pass_l0attn(em, sb, ps, cd, x_in, x_out):
    ident = cd["ident_s"]
    P = "p1"
    wq = sb(P + "_wq", [128, NCH * D], BF16)
    wk = sb(P + "_wk", [128, NCH * D], BF16)
    wv = sb(P + "_wv", [128, NCH * D], BF16)
    wo = sb(P + "_wo", [128, NCH * D], BF16)
    wstage = [sb(P + f"_wst{i}", [128, D], F32) for i in range(2)]
    bq = sb(P + "_bq", [128, NCH], F32)
    bk = sb(P + "_bk", [128, NCH], F32)
    bvb = sb(P + "_bvb", [128, D], BF16)
    bob = sb(P + "_bob", [128, D], BF16)
    bstage = wstage[1]
    mask4 = sb(P + "_mask4", [128, 512], BF16)
    sc = StripeCtx(em, sb, ps, P, carry=False)
    qf = sb(P + "_qf", [128, NCH * 512], BF16)
    kze = sb(P + "_kze", [128, NCH * 512], BF16)
    kzo = sb(P + "_kzo", [128, NCH * 512], BF16)
    vst = sb(P + "_vst", [128, TPS * D], BF16)
    ktm = sb(P + "_ktm", [128, TPS * D], BF16)
    sT = sb(P + "_sT", [128, 2 * 512], BF16)
    stf = sb(P + "_stf", [128, H * DH], F32)
    stb = sb(P + "_stb", [128, H * DH], BF16)
    ofm = sb(P + "_ofm", [128, NCH * 512], BF16)
    outb = sb(P + "_out", [128, TPS * D], F32)
    lno = [LNBufs(sb, P + f"_lo{i}") for i in range(2)]
    pj = BankPool([ps(P + f"_pj{i}", [128, 512], F32) for i in range(4)])
    po = [ps(P + f"_po{i}", [128, 512], F32) for i in range(2)]

    load_weights(em, wstage, [wq, wk, wv, wo],
                 [cd["l0a_wqT"], cd["l0a_wkT"], cd["l0a_wvT"], cd["l0a_woT"]],
                 P)
    em.dma("sp", lambda e: e.dma_start(bq[:], cd["l0a_bq"]), inc=P + "_mi",
           mark=P + "_bql")
    em.dma("sp", lambda e: e.dma_start(bk[:], cd["l0a_bk"]), inc=P + "_mi",
           waits=[P + "_bql"], mark=P + "_bqk")
    em.dma("sp", lambda e: e.dma_start(bstage[:], cd["l0a_bvb"]),
           inc=P + "_mi", waits=[P + "_bqk", P + "_w3"], mark=P + "_bvl")
    em.op("act", lambda e: e.activation(bvb[:], bstage[:], AF.Copy),
          waits=[P + "_bvl"], mark=P + "_bv")
    em.dma("sp", lambda e: e.dma_start(bstage[:], cd["l0a_bob"]),
           inc=P + "_mi", waits=[P + "_bv"], mark=P + "_bol")
    em.op("act", lambda e: e.activation(bob[:], bstage[:], AF.Copy),
          waits=[P + "_bol"], mark=P + "_bo")
    em.dma("sp", lambda e: e.dma_start(mask4[:], cd["l0mask4"]),
           inc=P + "_mi", waits=[P + "_bol"], mark=P + "_msk")
    em.op("dve", lambda e: e.memset(kze[64:128, :], 0.0))
    em.op("dve", lambda e: e.memset(kzo[0:64, :], 0.0), mark=P + "_kz0")

    for s in range(NSTRIPE):
        tag = f"{P}s{s}"
        bi = stripe_prologue(em, sc, s, x_in, ident)
        for co in range(NCH):
            bank, fw, bidx = pj.acquire()
            for c in range(NCH):
                em.op("pe", lambda e, co=co, c=c, bank=bank, bi=bi: e.matmul(
                    bank[:],
                    wq[:, c * D + co * 128:c * D + co * 128 + 128],
                    sc.chunk(bi, c),
                    start=(c == 0), stop=(c == NCH - 1)),
                    waits=([f"{tag}_xnT", P + "_w0", P + "_bqk"] +
                           ([fw] if fw else [])) if c == 0 else (),
                    mark=(f"{tag}_qmm{co}" if c == NCH - 1 else None))
            em.op("act", lambda e, co=co, bank=bank: e.activation(
                qf[:, co * 512:(co + 1) * 512], bank[:],
                AF.Identity, bias=bq[:, co:co + 1]),
                waits=[f"{tag}_qmm{co}"], mark=f"{tag}_qf{co}")
            pj.release(bidx, f"{tag}_qf{co}")
        for co in range(NCH):
            bank, fw, bidx = pj.acquire()
            for c in range(NCH):
                em.op("pe", lambda e, co=co, c=c, bank=bank, bi=bi: e.matmul(
                    bank[:],
                    wk[:, c * D + co * 128:c * D + co * 128 + 128],
                    sc.chunk(bi, c),
                    start=(c == 0), stop=(c == NCH - 1)),
                    waits=([P + "_w1", P + "_kz0"] +
                           ([fw] if fw else [])) if c == 0 else (),
                    mark=(f"{tag}_kmm{co}" if c == NCH - 1 else None))
            em.op("act", lambda e, co=co, bank=bank: e.activation(
                kze[0:64, co * 512:(co + 1) * 512], bank[0:64, :],
                AF.Identity, bias=bk[0:64, co:co + 1]),
                waits=[f"{tag}_kmm{co}"], mark=f"{tag}_kfe{co}")
            em.op("act", lambda e, co=co, bank=bank: e.activation(
                kzo[64:128, co * 512:(co + 1) * 512], bank[64:128, :],
                AF.Identity, bias=bk[64:128, co:co + 1]),
                waits=[f"{tag}_kmm{co}"], mark=f"{tag}_kfo{co}")
            pj.release(bidx, f"{tag}_kfo{co}")
        for t in range(TPS):
            for eh in range(2):
                bank, fw, bidx = pj.acquire()
                for c in range(NCH):
                    em.op("pe", lambda e, t=t, eh=eh, c=c, bank=bank,
                          bi=bi: e.matmul(
                        bank[:],
                        sc.chunk(bi, c)[:, t * 128:t * 128 + 128],
                        wv[:, c * D + eh * 512:c * D + eh * 512 + 512],
                        start=(c == 0), stop=(c == NCH - 1)),
                        waits=([f"{tag}_xnT", P + "_w2", P + "_bv"] +
                               ([fw] if fw else [])) if c == 0 else (),
                        mark=(f"{tag}_vmm{t}_{eh}" if c == NCH - 1 else None))
                em.op("dve", lambda e, t=t, eh=eh, bank=bank: e.tensor_tensor(
                    vst[:, t * D + eh * 512:t * D + eh * 512 + 512], bank[:],
                    bvb[:, eh * 512:eh * 512 + 512], ALU.add),
                    waits=[f"{tag}_vmm{t}_{eh}"], mark=f"{tag}_vf{t}_{eh}")
                pj.release(bidx, f"{tag}_vf{t}_{eh}")
        for t in range(TPS):
            pa, pb2 = sc.ptr[0], sc.ptr[1]
            war = ([f"{tag}_ktev{t - 1}", f"{tag}_kteA{t - 1}"] if t >= 1
                   else [f"{tag}_trev{TPS - 1}"])
            for p in range(NCH):
                em.op("pe", lambda e, t=t, p=p, pa=pa: e.matmul(
                    pa[:, p * 128:(p + 1) * 128],
                    kze[:, p * 512 + t * 128:p * 512 + t * 128 + 128],
                    ident, is_transpose=True, start=True, stop=True),
                    waits=([f"{tag}_kfe{NCH - 1}", f"{tag}_kfo{NCH - 1}"] + war)
                    if p == 0 else (),
                    mark=(f"{tag}_ktrA{t}" if p == NCH - 1 else None))
            for p in range(NCH):
                em.op("pe", lambda e, t=t, p=p, pb2=pb2: e.matmul(
                    pb2[:, p * 128:(p + 1) * 128],
                    kzo[:, p * 512 + t * 128:p * 512 + t * 128 + 128],
                    ident, is_transpose=True, start=True, stop=True),
                    mark=(f"{tag}_ktrB{t}" if p == NCH - 1 else None))
            em.op("act", lambda e, t=t, pa=pa: e.activation(
                ktm[:, t * D:(t + 1) * D], pa[:, 0:D], AF.Copy),
                waits=[f"{tag}_ktrA{t}"], mark=f"{tag}_kteA{t}")
            em.op("dve", lambda e, t=t, pb2=pb2: e.tensor_tensor(
                ktm[:, t * D:(t + 1) * D], ktm[:, t * D:(t + 1) * D],
                pb2[:, 0:D], ALU.add),
                waits=[f"{tag}_ktrB{t}", f"{tag}_kteA{t}"],
                mark=f"{tag}_ktev{t}")
        em.marks[f"{tag}_trev2"] = em.marks[f"{tag}_kteA{TPS - 1}"]
        em.marks[f"{tag}_trev3"] = em.marks[f"{tag}_ktev{TPS - 1}"]
        for t in range(TPS):
            j = (s % (NSTRIPE // 2)) * TPS + t
            ctag = f"{tag}c{t}"
            if j == 0:
                em.op("dve", lambda e: e.memset(stf[:], 0.0),
                      waits=([f"{P}s{s - 1}c{TPS - 1}_sadd"] if s > 0 else []),
                      mark=f"{ctag}_sadd")
            else:
                ptag = f"{tag}c{t - 1}" if t > 0 else f"{P}s{s - 1}c{TPS - 1}"
                em.op("act", lambda e: e.activation(stb[:], stf[:], AF.Copy),
                      waits=[f"{ptag}_sadd"], mark=f"{ctag}_scp")
            for g in range(4):
                sg = sT[:, (g % 2) * 512:(g % 2) * 512 + 512]
                bank, fw, bidx = pj.acquire()
                for hh in range(4):
                    h = 4 * g + hh
                    p = h // 2
                    kz = kze if h % 2 == 0 else kzo
                    em.op("pe", lambda e, p=p, hh=hh, kz=kz, bank=bank,
                          t=t: e.matmul(
                              bank[:, hh * 128:hh * 128 + 128],
                              kz[:, p * 512 + t * 128:p * 512 + t * 128 + 128],
                              qf[:, p * 512 + t * 128:p * 512 + t * 128 + 128],
                              start=True, stop=True),
                          waits=([f"{tag}_ktev{t}", P + "_msk"] +
                                 ([fw] if fw else [])) if hh == 0 else (),
                          mark=(f"{ctag}_smm{g}" if hh == 3 else None))
                em.op("dve", lambda e, sg=sg, bank=bank: e.tensor_tensor(
                    sg, bank[:], mask4[:], ALU.mult),
                    waits=[f"{ctag}_smm{g}"], mark=f"{ctag}_msk{g}")
                pj.release(bidx, f"{ctag}_msk{g}")
                for hh in range(4):
                    h = 4 * g + hh
                    p = h // 2
                    ob_bank = po[p // 4]
                    col = (p % 4) * 128
                    cb = (h % 2) * 64
                    em.op("pe", lambda e, h=h, col=col, cb=cb, sg=sg,
                          hh=hh, t=t, ob_bank=ob_bank, j=j: e.matmul(
                              ob_bank[cb:cb + 64, col:col + 128],
                              vst[:, t * D + h * DH:t * D + h * DH + 64],
                              sg[:, hh * 128:hh * 128 + 128],
                              start=True, stop=(j == 0),
                              tile_position=(0, cb)),
                          waits=([f"{ctag}_msk{g}", f"{tag}_vf{t}_0",
                                  f"{tag}_vf{t}_1"] +
                                 ([f"{tag}c{t - 1}_oev{p // 4}"]
                                  if t >= 1 else
                                  ([f"{P}s{s - 1}c{TPS - 1}_oev{p // 4}"]
                                   if s >= 1 else [])))
                          if (hh == 0 and g == 0) else
                          ([f"{ctag}_msk{g}"] if hh == 0 else ()),
                          mark=(f"{ctag}_omm{g}"
                                if (j == 0 and hh == 3) else None))
                    if j > 0:
                        em.op("pe", lambda e, h=h, col=col, cb=cb, t=t,
                              ob_bank=ob_bank: e.matmul(
                                  ob_bank[cb:cb + 64, col:col + 128],
                                  stb[:, h * DH:(h + 1) * DH],
                                  qf[:, (h // 2) * 512 + t * 128:
                                     (h // 2) * 512 + t * 128 + 128],
                                  start=False, stop=True,
                                  tile_position=(0, cb)),
                              waits=[f"{ctag}_scp"] if hh == 0 else (),
                              mark=(f"{ctag}_omm{g}" if hh == 3 else None))
            bank, fw, bidx = pj.acquire()
            for h in range(H):
                p = h // 2
                mb = (h % 2) * 64
                em.op("pe", lambda e, h=h, p=p, mb=mb, t=t, bank=bank:
                      e.matmul(
                          bank[mb:mb + 64, p * 64:(p + 1) * 64],
                          ktm[:, t * D + p * 128 + mb:t * D + p * 128 + mb + 64],
                          vst[:, t * D + h * DH:t * D + h * DH + 64],
                          start=True, stop=True, tile_position=(0, mb)),
                      waits=([f"{tag}_ktev{t}"] + ([fw] if fw else []))
                      if h == 0 else (),
                      mark=(f"{ctag}_dmm" if h == H - 1 else None))
            em.op("dve", lambda e, bank=bank: e.tensor_tensor(
                stf[0:64, :].rearrange("p (h c) -> p h c", c=DH)[:, 0::2, :],
                stf[0:64, :].rearrange("p (h c) -> p h c", c=DH)[:, 0::2, :],
                bank[0:64, :].rearrange("p (h c) -> p h c", c=DH),
                ALU.add),
                waits=[f"{ctag}_dmm"] + ([f"{ctag}_scp"] if j > 0 else []))
            em.op("dve", lambda e, bank=bank: e.tensor_tensor(
                stf[64:128, :].rearrange("p (h c) -> p h c", c=DH)[:, 1::2, :],
                stf[64:128, :].rearrange("p (h c) -> p h c", c=DH)[:, 1::2, :],
                bank[64:128, :].rearrange("p (h c) -> p h c", c=DH),
                ALU.add),
                mark=f"{ctag}_sadd")
            pj.release(bidx, f"{ctag}_sadd")
            for b in range(2):
                em.op("act", lambda e, b=b, t=t: e.activation(
                    ofm[:, 0:NCH * 512].rearrange("p (c w) -> p c w", c=NCH)
                    [:, 4 * b:4 * b + 4, t * 128:t * 128 + 128],
                    po[b][:, 0:512].rearrange("p (c w) -> p c w", c=4),
                    AF.Copy),
                    waits=[f"{ctag}_omm{2 * b}", f"{ctag}_omm{2 * b + 1}"],
                    mark=f"{ctag}_oev{b}")
            ob = outb
            for eh in range(2):
                bank, fw, bidx = pj.acquire()
                for co in range(NCH):
                    em.op("pe", lambda e, t=t, eh=eh, co=co, bank=bank:
                          e.matmul(
                              bank[:],
                              ofm[:, co * 512 + t * 128:co * 512 + t * 128 + 128],
                              wo[:, co * D + eh * 512:co * D + eh * 512 + 512],
                              start=(co == 0), stop=(co == NCH - 1)),
                          waits=([f"{ctag}_oev0", f"{ctag}_oev1",
                                  P + "_w3", P + "_bo"] +
                                 ([fw] if fw else [])) if co == 0 else (),
                          mark=(f"{ctag}_wmm{eh}" if co == NCH - 1 else None))
                wfw = [f"{ctag}_wmm{eh}"]
                if s >= 1 and t == 0 and eh == 0:
                    wfw.append(f"{tag[:-1]}{s - 1}_store"
                               if False else f"{P}s{s - 1}_store")
                em.op("dve", lambda e, t=t, eh=eh, bank=bank, ob=ob:
                      e.tensor_tensor(
                          ob[:, t * D + eh * 512:t * D + eh * 512 + 512],
                          bank[:], bob[:, eh * 512:eh * 512 + 512], ALU.add),
                      waits=wfw,
                      mark=f"{ctag}_wf{eh}")
                pj.release(bidx, f"{ctag}_wf{eh}")
        ln_post_add(em, lno, s, outb, sc.xs[s % len(sc.xs)], x_out, tag,
                    waits_first=[f"{tag}c{TPS - 1}_wf1"])



def build_pass_l1attn(em, sb, ps, cd, x_in, x_out):
    ident = cd["ident_s"]
    P = "p3"
    W3 = 3 * D
    wqkv = sb(P + "_wqkv", [128, NCH * W3], BF16)
    wout = sb(P + "_wout", [128, NCH * D], BF16)
    outb = sb(P + "_outb", [128, TPS * D], F32)
    wstage = [outb[:, 0:W3 // 2], outb[:, W3 // 2:W3]]
    cosT = sb(P + "_cos", [128, SEQLEN], BF16)
    ssT = sb(P + "_ss", [128, SEQLEN], BF16)
    lmask = sb(P + "_lm", [128, 4 * BS], BF16)
    pj = BankPool([ps(P + f"_pj{i}", [128, 512], F32) for i in range(2)])
    sp2 = [ps(P + "_spa", [128, 1024], F32), ps(P + "_spb", [128, 1024], F32)]
    po = ps(P + "_po", [128, 1024], F32)
    ptr_aps = [po.bitcast(BF16)[:, 0:NCH * 128],
               sp2[1].bitcast(BF16)[:, 0:NCH * 128]]
    sc = StripeCtx(em, sb, ps, P, carry=False, nxs=1, nxnt=1,
                   ptr_aps=ptr_aps)
    qraw = sb(P + "_qraw", [128, NCH * 512], BF16)
    kraw = sb(P + "_kraw", [128, NCH * 512], BF16)
    qrot = sb(P + "_qrot", [128, NCH * 512], BF16)
    kze = sb(P + "_kze", [128, NCH * 512], BF16)
    kzo = sb(P + "_kzo", [128, NCH * 512], BF16)
    tmp1 = outb[:, 3072:3328].bitcast(BF16)
    tmp2 = outb[:, 3328:3584].bitcast(BF16)
    tmp3 = outb[:, 3584:3840].bitcast(BF16)
    vst = sb(P + "_vst", [128, TPS * H * 128], BF16)
    ptile = [sb(P + f"_pt{i}", [128, 1024], BF16) for i in range(2)]
    lns = sb(P + "_lns", [128, 1024], F32)
    rcp = sb(P + "_rcp", [128, 1024], F32)
    ofm = sb(P + "_ofm", [128, NCH * 512], BF16)
    lno = [LNBufs(sb, P + f"_lo{i}") for i in range(2)]

    load_weights(em, wstage, [wqkv, wout],
                 [cd["l1a_wqkvT"], cd["l1a_woutT"]], P)
    em.dma("sp", lambda e: e.dma_start(cosT[:], cd["l1cos"]), inc=P + "_mi",
           mark=P + "_c0")
    em.dma("sp", lambda e: e.dma_start(ssT[:], cd["l1sin"]), inc=P + "_mi",
           waits=[P + "_c0"], mark=P + "_c1")
    em.dma("sp", lambda e: e.dma_start(lmask[:], cd["l1mask"]), inc=P + "_mi",
           waits=[P + "_c1"], mark=P + "_c2")
    em.op("dve", lambda e: e.memset(kze[64:128, :], 0.0))
    em.op("dve", lambda e: e.memset(kzo[0:64, :], 0.0))
    em.op("dve", lambda e: e.memset(
        vst.rearrange("p (s c) -> p s c", c=128)[:, :, 64:128], 1.0),
        mark=P + "_init")

    chain = {"exp": [None, None], "omm": [None, None],
             "norm": [None, None], "ofmd": None}

    for s in range(NSTRIPE):
        tag = f"{P}s{s}"
        war = [f"{P}s{s - 1}_pedone"] if s >= 1 else []
        xw = None
        pw = []
        if s >= 1:
            pw = [c for c in (chain["norm"][0], chain["norm"][1],
                              chain["exp"][1]) if c]
        bi = stripe_prologue(em, sc, s, x_in, ident, xnt_war=war,
                             x_extra_wait=xw, ptr_war=pw)

        def proj_fm(nm, dst, woff):
            for co in range(NCH):
                bank, fw, bidx = pj.acquire()
                for c in range(NCH):
                    em.op("pe", lambda e, co=co, c=c, bank=bank, bi=bi,
                          woff=woff: e.matmul(
                              bank[:],
                              wqkv[:, c * W3 + woff + co * 128:
                                   c * W3 + woff + co * 128 + 128],
                              sc.chunk(bi, c),
                              start=(c == 0), stop=(c == NCH - 1)),
                          waits=([f"{tag}_xnT", P + "_w0", P + "_init"] +
                                 ([fw] if fw else [])) if c == 0 else (),
                          mark=(f"{tag}_{nm}mm{co}" if c == NCH - 1
                                else None))
                evw = [f"{tag}_{nm}mm{co}"]
                if s >= 1:
                    evw.append(f"{P}s{s - 1}_{nm}rot{co}")
                em.op("act", lambda e, co=co, bank=bank, dst=dst:
                      e.activation(dst[:, co * 512:(co + 1) * 512], bank[:],
                                   AF.Copy),
                      waits=evw, mark=f"{tag}_{nm}ev{co}")
                pj.release(bidx, f"{tag}_{nm}ev{co}")
        proj_fm("q", qraw, 0)
        proj_fm("k", kraw, D)
        for t in range(TPS):
            for eh in range(2):
                bank, fw, bidx = pj.acquire()
                for c in range(NCH):
                    em.op("pe", lambda e, t=t, eh=eh, c=c, bank=bank,
                          bi=bi: e.matmul(
                              bank[:],
                              sc.chunk(bi, c)[:, t * 128:t * 128 + 128],
                              wqkv[:, c * W3 + 2 * D + eh * 512:
                                   c * W3 + 2 * D + eh * 512 + 512],
                              start=(c == 0), stop=(c == NCH - 1)),
                          waits=([fw] if (c == 0 and fw) else ()),
                          mark=(f"{tag}_vmm{t}_{eh}" if c == NCH - 1
                                else None))
                vw = [f"{tag}_vmm{t}_{eh}"]
                if s >= 1 and t == 0 and eh == 0:
                    vw.append(f"{P}s{s - 1}_odone")
                em.op("act", lambda e, t=t, eh=eh, bank=bank: e.activation(
                    vst.rearrange("p (tt h c) -> p tt h c", tt=TPS, h=H)
                    [:, t, 8 * eh:8 * eh + 8, 0:64],
                    bank[:].rearrange("p (h c) -> p h c", c=64),
                    AF.Copy),
                    waits=vw, mark=f"{tag}_vev{t}_{eh}")
                pj.release(bidx, f"{tag}_vev{t}_{eh}")
        off = (s % (NSTRIPE // 2)) * 512
        for c in range(NCH):
            em.op("dve", lambda e, c=c, off=off: e.tensor_tensor(
                qrot[:, c * 512:(c + 1) * 512],
                qraw[:, c * 512:(c + 1) * 512],
                cosT[:, off:off + 512], ALU.mult),
                waits=[f"{tag}_qev{c}", P + "_c2"] if c == 0
                else [f"{tag}_qev{c}"])
            for sq4 in range(4):
                sq = sq4 * 32
                tw = []
                if sq4 == 0:
                    if c > 0:
                        tw = [f"{tag}_qc{c - 1}"]
                    elif s >= 1:
                        tw = [f"{P}s{s - 1}_kc{NCH - 1}",
                              f"{P}s{s - 1}_store"]
                em.op("dve", lambda e, c=c, sq=sq, off=off:
                      e.tensor_tensor(
                          tmp1[sq:sq + 32, :],
                          qraw[sq:sq + 32, c * 512:(c + 1) * 512],
                          ssT[sq:sq + 32, off:off + 512],
                          ALU.mult),
                      waits=tw,
                      mark=(f"{tag}_qq{c}" if sq4 == 3 else None))
            for qt in range(4):
                sq = (qt ^ 1) * 32
                em.op("act", lambda e, qt=qt, sq=sq: e.activation(
                    tmp2[qt * 32:qt * 32 + 32, :], tmp1[sq:sq + 32, :],
                    AF.Copy),
                    waits=[f"{tag}_qq{c}"] if qt == 0 else (),
                    mark=(f"{tag}_qc{c}" if qt == 3 else None))
            em.op("dve", lambda e, c=c: e.tensor_tensor(
                qrot[:, c * 512:(c + 1) * 512],
                qrot[:, c * 512:(c + 1) * 512], tmp2[:], ALU.add),
                waits=[f"{tag}_qc{c}"],
                mark=f"{tag}_qrot{c}")
        for c in range(NCH):
            em.op("dve", lambda e, c=c, off=off: e.tensor_tensor(
                tmp3[:], kraw[:, c * 512:(c + 1) * 512],
                cosT[:, off:off + 512], ALU.mult),
                waits=[f"{tag}_kev{c}"])
            for sq4 in range(4):
                sq = sq4 * 32
                tw = []
                if sq4 == 0:
                    tw = [f"{tag}_kc{c - 1}" if c > 0
                          else f"{tag}_qc{NCH - 1}"]
                em.op("dve", lambda e, c=c, sq=sq, off=off:
                      e.tensor_tensor(
                          tmp1[sq:sq + 32, :],
                          kraw[sq:sq + 32, c * 512:(c + 1) * 512],
                          ssT[sq:sq + 32, off:off + 512],
                          ALU.mult),
                      waits=tw,
                      mark=(f"{tag}_kq{c}" if sq4 == 3 else None))
            for qt in range(4):
                sq = (qt ^ 1) * 32
                em.op("act", lambda e, qt=qt, sq=sq: e.activation(
                    tmp2[qt * 32:qt * 32 + 32, :], tmp1[sq:sq + 32, :],
                    AF.Copy),
                    waits=[f"{tag}_kq{c}"] if qt == 0 else (),
                    mark=(f"{tag}_kc{c}" if qt == 3 else None))
            em.op("dve", lambda e, c=c: e.tensor_tensor(
                tmp3[:], tmp3[:], tmp2[:], ALU.add),
                waits=[f"{tag}_kc{c}"])
            em.op("dve", lambda e, c=c: e.tensor_copy(
                kze[0:64, c * 512:(c + 1) * 512], tmp3[0:64, :]))
            em.op("dve", lambda e, c=c: e.tensor_copy(
                kzo[64:128, c * 512:(c + 1) * 512], tmp3[64:128, :]),
                mark=f"{tag}_krot{c}")
        em.marks[f"{tag}_rope"] = em.marks[f"{tag}_krot{NCH - 1}"]
        for b in range(2):
            btag = f"{tag}b{b}"
            for gp in range(8):
                pt = ptile[gp % 2]
                sp = sp2[gp % 2]
                poh = po[:, (gp % 2) * 512:(gp % 2) * 512 + 512]
                for hi in range(2):
                    h = 2 * gp + hi
                    co = h // 2
                    kz = kze if h % 2 == 0 else kzo
                    for kt in range(2):
                        ws = []
                        if hi == 0 and kt == 0:
                            ws = [f"{tag}_qrot{co}", f"{tag}_krot{co}"]
                            if chain["exp"][gp % 2] is not None:
                                ws.append(chain["exp"][gp % 2])
                        em.op("pe", lambda e, co=co, kz=kz, kt=kt, b=b,
                              hi=hi, sp=sp: e.matmul(
                                  sp[:, hi * 512 + kt * 256:
                                     hi * 512 + kt * 256 + 256],
                                  kz[:, co * 512 + b * 256 + kt * 128:
                                     co * 512 + b * 256 + kt * 128 + 128],
                                  qrot[:, co * 512 + b * 256:
                                       co * 512 + b * 256 + 256],
                                  start=True, stop=True),
                              waits=ws,
                              mark=(f"{btag}g{gp}_smm"
                                    if (hi == 1 and kt == 1) else None))
                ptw = [f"{btag}g{gp}_smm"]
                if chain["omm"][gp % 2] is not None:
                    ptw.append(chain["omm"][gp % 2])
                em.op("act", lambda e, pt=pt, sp=sp: e.activation(
                    pt[:], sp[:], AF.Exp, scale=DH ** (-0.5)),
                    waits=ptw, mark=f"{btag}g{gp}_exp")
                chain["exp"][gp % 2] = f"{btag}g{gp}_exp"
                em.op("dve", lambda e, pt=pt: e.tensor_tensor(
                    pt[:], pt[:], lmask[:], ALU.mult),
                    waits=[f"{btag}g{gp}_exp"], mark=f"{btag}g{gp}_msk")
                for hi in range(2):
                    h = 2 * gp + hi
                    for kt in range(2):
                        ws = [f"{btag}g{gp}_msk", f"{tag}_vev{TPS - 1}_1"]
                        if hi == 0 and kt == 0:
                            if chain["norm"][gp % 2] is not None:
                                ws.append(chain["norm"][gp % 2])
                        em.op("pe", lambda e, h=h, hi=hi, kt=kt, b=b, pt=pt,
                              poh=poh: e.matmul(
                                  poh[:, hi * 256:hi * 256 + 256],
                                  vst[:, (2 * b + kt) * H * 128 + h * 128:
                                      (2 * b + kt) * H * 128 + h * 128 + 128],
                                  pt[:, hi * 512 + kt * 256:
                                     hi * 512 + kt * 256 + 256],
                                  start=(kt == 0), stop=(kt == 1)),
                              waits=ws,
                              mark=(f"{btag}g{gp}_omm"
                                    if (hi == 1 and kt == 1) else None))
                chain["omm"][gp % 2] = f"{btag}g{gp}_omm"
                ftag = f"{btag}g{gp}"
                hb = (gp % 2) * 512
                lw = [f"{btag}g{gp}_omm"]
                if chain["ofmd"] is not None:
                    lw.append(chain["ofmd"])
                em.op("act", lambda e, hb=hb: e.activation(
                    lns[64:128, hb:hb + 512], po[64:128, hb:hb + 512],
                    AF.Ln),
                    waits=lw, mark=f"{ftag}_ln")
                em.op("act", lambda e, hb=hb: e.activation(
                    rcp[0:64, hb:hb + 512], lns[64:128, hb:hb + 512],
                    AF.Exp, scale=-1.0),
                    waits=[f"{ftag}_ln"], mark=f"{ftag}_rcp")
                em.op("dve", lambda e, hb=hb: e.tensor_tensor(
                    lns[0:64, hb:hb + 512].bitcast(BF16)[:, 0:1024]
                    [:, 0:512] if False else
                    lns[0:64, :].bitcast(BF16)[:, hb:hb + 512],
                    po[0:64, hb:hb + 512], rcp[0:64, hb:hb + 512],
                    ALU.mult),
                    waits=[f"{ftag}_rcp"], mark=f"{ftag}_on")
                for hh in range(2):
                    h = 2 * gp + hh
                    cb = (h % 2) * 64
                    co = h // 2
                    em.op("act", lambda e, hh=hh, cb=cb, co=co, b=b, hb=hb:
                          e.activation(
                              ofm[cb:cb + 64,
                                  co * 512 + b * 256:co * 512 + b * 256 + 256],
                              lns[0:64, :].bitcast(BF16)
                              [:, hb + hh * 256:hb + hh * 256 + 256],
                              AF.Copy),
                          waits=[f"{ftag}_on"],
                          mark=(f"{ftag}_ofm" if hh == 1 else None))
                chain["norm"][gp % 2] = f"{ftag}_ofm"
                chain["ofmd"] = f"{ftag}_ofm"
            em.marks[f"{btag}_odone"] = em.marks[f"{btag}g7_omm"]
            for tt in range(2):
                t = 2 * b + tt
                for eh in range(2):
                    bank, fw, bidx = pj.acquire()
                    for co in range(NCH):
                        em.op("pe", lambda e, t=t, eh=eh, co=co, bank=bank:
                              e.matmul(
                                  bank[:],
                                  ofm[:, co * 512 + t * 128:
                                      co * 512 + t * 128 + 128],
                                  wout[:, co * D + eh * 512:
                                       co * D + eh * 512 + 512],
                                  start=(co == 0), stop=(co == NCH - 1)),
                              waits=([f"{btag}g6_ofm", f"{btag}g7_ofm",
                                      P + "_w1"] +
                                     ([fw] if fw else [])) if co == 0 else (),
                              mark=(f"{tag}_wmm{t}_{eh}" if co == NCH - 1
                                    else None))
                    evw = [f"{tag}_wmm{t}_{eh}"]
                    if s >= 1 and t == 0 and eh == 0:
                        evw.append(f"{P}s{s - 1}_store")
                    em.op("act", lambda e, t=t, eh=eh, bank=bank:
                          e.activation(
                              outb[:, t * D + eh * 512:t * D + eh * 512 + 512],
                              bank[:], AF.Copy),
                          waits=evw, mark=f"{tag}_wf{t}_{eh}")
                    pj.release(bidx, f"{tag}_wf{t}_{eh}")
        em.marks[f"{tag}_odone"] = em.marks[f"{tag}b1_odone"]
        em.marks[f"{tag}_pedone"] = em.marks[f"{tag}_wmm{TPS - 1}_1"]
        ln_post_add(em, lno, s, outb, sc.xs[s % len(sc.xs)], x_out, tag,
                    waits_first=[f"{tag}_wf{TPS - 1}_1"])



def build_pass_l1ffn(em, sb, ps, cd, x_in, x_out):
    ident = cd["ident_s"]
    P = "p4"
    NHC = HID // 128
    wg = sb(P + "_wg", [128, NCH * HID], BF16)
    wu = sb(P + "_wu", [128, NCH * HID], BF16)
    wd = sb(P + "_wd", [128, NHC * D], BF16)
    sc = StripeCtx(em, sb, ps, P, carry=False, nxs=1, nxnt=1)
    hact = sb(P + "_hact", [128, NHC * 512], BF16)
    wstage = [hact.bitcast(I32)[:, 0:HID // 2],
              hact.bitcast(I32)[:, HID // 2:HID]]
    silu_t = [sb(P + f"_sl{i}", [128, 512], F32) for i in range(2)]
    lno = LNBufs(sb, P + "_lo")
    pg = BankPool([ps(P + f"_pg{i}", [128, 512], F32) for i in range(4)])
    pd = [ps(P + f"_pd{i}", [128, 512], F32) for i in range(2)]

    load_weights(em, wstage, [wg, wu, wd],
                 [cd["l1f_wgT"], cd["l1f_wuT"], cd["l1f_wdT"]], P,
                 scale=1.0 / 127.0)

    for s in range(NSTRIPE):
        tag = f"{P}s{s}"
        war = [f"{P}s{s - 1}_gu{NHC - 1}"] if s >= 1 else []
        xw = f"{P}s{s - 1}_store" if s >= 1 else None
        bi = stripe_prologue(em, sc, s, x_in, ident, xnt_war=war,
                             x_extra_wait=xw)
        for hc in range(NHC):
            bkg, fwg, big = pg.acquire()
            for c in range(NCH):
                em.op("pe", lambda e, hc=hc, c=c, bkg=bkg, bi=bi: e.matmul(
                    bkg[:],
                    wg[:, c * HID + hc * 128:c * HID + hc * 128 + 128],
                    sc.chunk(bi, c),
                    start=(c == 0), stop=(c == NCH - 1)),
                    waits=([f"{tag}_xnT", P + "_w0"] +
                           ([fwg] if fwg else [])) if c == 0 else (),
                    mark=(f"{tag}_gmm{hc}" if c == NCH - 1 else None))
            bku, fwu, biu = pg.acquire()
            for c in range(NCH):
                em.op("pe", lambda e, hc=hc, c=c, bku=bku, bi=bi: e.matmul(
                    bku[:],
                    wu[:, c * HID + hc * 128:c * HID + hc * 128 + 128],
                    sc.chunk(bi, c),
                    start=(c == 0), stop=(c == NCH - 1)),
                    waits=([P + "_w1"] + ([fwu] if fwu else []))
                    if c == 0 else (),
                    mark=(f"{tag}_umm{hc}" if c == NCH - 1 else None))
            sl = silu_t[hc % 2]
            sl_war = [f"{tag}_ha{hc - 2}"] if hc >= 2 else (
                [f"{P}s{s - 1}_ha{NHC - 2 + hc}"] if s >= 1 else [])
            em.op("act", lambda e, hc=hc, bkg=bkg, sl=sl: e.activation(
                sl[:], bkg[:], AF.Silu),
                waits=[f"{tag}_gmm{hc}"] + sl_war, mark=f"{tag}_sil{hc}")
            pg.release(big, f"{tag}_sil{hc}")
            ha_war = ([f"{P}s{s - 1}_dmm{TPS - 1}_1"]
                      if (s >= 1 and hc == 0) else [])
            if s == 0 and hc == 0:
                ha_war = [P + "_w2"]
            em.op("dve", lambda e, hc=hc, bku=bku, sl=sl: e.tensor_tensor(
                hact[:, hc * 512:(hc + 1) * 512], bku[:], sl[:], ALU.mult),
                waits=[f"{tag}_umm{hc}", f"{tag}_sil{hc}"] + ha_war,
                mark=f"{tag}_ha{hc}")
            pg.release(biu, f"{tag}_ha{hc}")
        em.marks[f"{tag}_gu{NHC - 1}"] = em.marks[f"{tag}_umm{NHC - 1}"]
        xs = sc.xs[s % len(sc.xs)]
        ln = lno
        for t in range(TPS):
            for eh in range(2):
                bank = pd[eh]
                for hc in range(NHC):
                    em.op("pe", lambda e, t=t, eh=eh, hc=hc, bank=bank:
                          e.matmul(
                              bank[:],
                              hact[:, hc * 512 + t * 128:hc * 512 + t * 128 + 128],
                              wd[:, hc * D + eh * 512:hc * D + eh * 512 + 512],
                              start=(hc == 0), stop=(hc == NHC - 1)),
                          waits=([f"{tag}_ha{NHC - 1}", P + "_w2"] +
                                 ([f"{tag}t{t - 1}_ap{eh}"] if t >= 1 else
                                  ([f"{P}s{s - 1}t{TPS - 1}_ap{eh}"]
                                   if s >= 1 else []))) if hc == 0 else (),
                          mark=(f"{tag}_dmm{t}_{eh}" if hc == NHC - 1
                                else None))
            for eh in range(2):
                em.op("dve", lambda e, t=t, eh=eh: e.bn_stats(
                    ln.st[:, t * 12 + eh * 6:t * 12 + eh * 6 + 6], pd[eh][:]),
                    waits=[f"{tag}_dmm{t}_{eh}"] +
                          ([f"{P}s{s - 1}_lnfree"] if (s >= 1 and t == 0 and
                                                       eh == 0) else []))
            em.op("dve", lambda e, t=t: e.bn_aggr(
                ln.mv[:, t * 2:t * 2 + 2], ln.st[:, t * 12:t * 12 + 12]))
            em.op("dve", lambda e, t=t: e.tensor_scalar(
                ln.vs[:, t:t + 1], ln.mv[:, 2 * t + 1:2 * t + 2],
                EPS, None, ALU.add))
            newton_rsqrt(em, ln.vs[:, t:t + 1], ln.rs[:, t:t + 1],
                         ln.tmp[:, t:t + 1], ln.ci[:, t:t + 1])
            em.op("dve", lambda e, t=t: e.scalar_tensor_tensor(
                ln.nb[:, t:t + 1], ln.mv[:, 2 * t:2 * t + 1],
                -1.0, ln.rs[:, t:t + 1], ALU.mult, ALU.mult),
                mark=f"{tag}t{t}_lnp")
            for eh in range(2):
                em.op("dve", lambda e, t=t, eh=eh: e.tensor_scalar(
                    xs[:, t * D + eh * 512:t * D + eh * 512 + 512],
                    xs[:, t * D + eh * 512:t * D + eh * 512 + 512],
                    ln.nb[:, t:t + 1], None, ALU.add),
                    waits=[f"{tag}t{t}_lnp"])
                em.op("dve", lambda e, t=t, eh=eh: e.scalar_tensor_tensor(
                    xs[:, t * D + eh * 512:t * D + eh * 512 + 512],
                    pd[eh][:], ln.rs[:, t:t + 1],
                    xs[:, t * D + eh * 512:t * D + eh * 512 + 512],
                    ALU.mult, ALU.add),
                    mark=f"{tag}t{t}_ap{eh}")
        em.marks[f"{tag}_lnfree"] = em.marks[f"{tag}t{TPS - 1}_lnp"]
        if x_out.shape[0] == 128:
            em.dma("pool", lambda e, xs=xs, s=s: e.dma_start(
                x_out[:, s * TPS * D:(s + 1) * TPS * D], xs[:]),
                inc=f"st{s % 2}", mark=f"{tag}_store",
                waits=[f"{tag}t{TPS - 1}_ap0", f"{tag}t{TPS - 1}_ap1"])
        else:
            em.dma("pool", lambda e, xs=xs, s=s: e.dma_start(
                x_out[s * STRIPE:(s + 1) * STRIPE, :]
                .rearrange("(t p) d -> p t d", p=128),
                xs.rearrange("p (t d) -> p t d", t=TPS)),
                inc=f"st{s % 2}", mark=f"{tag}_store",
                waits=[f"{tag}t{TPS - 1}_ap0", f"{tag}t{TPS - 1}_ap1"])



def _host_prep(inputs):
    f32 = np.float32
    bf16 = ml_dtypes.bfloat16
    shared = {}

    def T(a):
        return np.ascontiguousarray(np.asarray(a).T)

    shared["l0a_wqT"] = T(inputs["L0_attn_wq_w"]).astype(f32, copy=False)
    shared["l0a_wkT"] = T(inputs["L0_attn_wk_w"]).astype(f32, copy=False)
    shared["l0a_wvT"] = T(inputs["L0_attn_wv_w"]).astype(f32, copy=False)
    shared["l0a_woT"] = T(inputs["L0_attn_wo_w"]).astype(f32, copy=False)
    for nm in ("q", "k", "v", "o"):
        shared[f"l0a_b{nm}"] = np.ascontiguousarray(
            np.asarray(inputs[f"L0_attn_w{nm}_b"]).astype(f32, copy=False)
            .reshape(NCH, 128).T)
    shared["l0a_bvb"] = np.ascontiguousarray(np.tile(
        np.asarray(inputs["L0_attn_wv_b"]).astype(f32, copy=False)[None, :],
        (128, 1)))
    shared["l0a_bob"] = np.ascontiguousarray(np.tile(
        np.asarray(inputs["L0_attn_wo_b"]).astype(f32, copy=False)[None, :],
        (128, 1)))
    shared["l0f_wrT"] = T(inputs["L0_ffn_W_r"]).astype(f32, copy=False)
    shared["l0f_wkT"] = T(inputs["L0_ffn_W_k"]).astype(f32, copy=False)
    shared["l0f_wvT"] = T(inputs["L0_ffn_W_v"]).astype(f32, copy=False)
    shared["l0f_mur"] = np.ascontiguousarray(
        np.asarray(inputs["L0_ffn_mu_r"]).astype(f32, copy=False)
        .reshape(NCH, 128).T)
    shared["l0f_muk"] = np.ascontiguousarray(
        np.asarray(inputs["L0_ffn_mu_k"]).astype(f32, copy=False)
        .reshape(NCH, 128).T)
    perm = np.zeros(3 * D, dtype=np.int64)
    for blk in range(2):
        for h in range(H):
            base = blk * D + 64 * h
            for j in range(DH // 2):
                perm[base + j] = base + 2 * j
                perm[base + 32 + j] = base + 2 * j + 1
    perm[2 * D:] = np.arange(2 * D, 3 * D)
    wqkv = np.asarray(inputs["L1_attn_Wqkv"]).astype(f32, copy=False)[perm]
    shared["l1a_wqkvT"] = np.ascontiguousarray(wqkv.T)
    shared["l1a_woutT"] = T(inputs["L1_attn_Wout"]).astype(f32, copy=False)
    shared["l1f_wgT"] = T(np.asarray(inputs["L1_ffn_w_gate"]))
    shared["l1f_wuT"] = T(np.asarray(inputs["L1_ffn_w_up"]))
    shared["l1f_wdT"] = T(np.asarray(inputs["L1_ffn_w_down"]))
    shared["ident"] = np.eye(128, dtype=bf16)
    ck = np.arange(128)[:, None]
    cq = np.arange(128)[None, :]
    shared["l0mask4"] = np.ascontiguousarray(
        np.tile((ck <= cq).astype(bf16), (1, 4)))
    inv_freq = 1.0 / 10000.0 ** (np.arange(0, DH, 2, dtype=np.float64) / DH)
    t = np.arange(SEQLEN, dtype=np.float64)
    ang = t[None, :] * inv_freq[:, None]
    shared["l1cos"] = np.ascontiguousarray(np.tile(np.cos(ang), (4, 1))
                                           .astype(bf16))
    sin = np.sin(ang)
    ss = np.concatenate([sin, -sin, sin, -sin], axis=0)
    shared["l1sin"] = np.ascontiguousarray(ss.astype(bf16))
    ck2 = np.arange(128)[:, None]
    cq2 = np.arange(BS)[None, :]
    m0 = (ck2 <= cq2).astype(bf16)
    m1 = (ck2 + 128 <= cq2).astype(bf16)
    shared["l1mask"] = np.ascontiguousarray(
        np.concatenate([m0, m1, m0, m1], axis=1))
    return shared


W_SHAPES = {
    "l0a_wqT": ([D, D], F32), "l0a_wkT": ([D, D], F32),
    "l0a_wvT": ([D, D], F32), "l0a_woT": ([D, D], F32),
    "l0a_bq": ([128, NCH], F32), "l0a_bk": ([128, NCH], F32),
    "l0a_bv": ([128, NCH], F32), "l0a_bo": ([128, NCH], F32),
    "l0f_wrT": ([D, D], F32), "l0f_wkT": ([D, D], F32),
    "l0f_wvT": ([D, D], F32),
    "l0f_mur": ([128, NCH], F32), "l0f_muk": ([128, NCH], F32),
    "l1a_wqkvT": ([D, 3 * D], F32), "l1a_woutT": ([D, D], F32),
    "l1f_wgT": ([D, HID], I32), "l1f_wuT": ([D, HID], I32),
    "l1f_wdT": ([HID, D], I32),
    "ident": ([128, 128], BF16), "l0mask4": ([128, 512], BF16),
    "l0a_bvb": ([128, D], F32), "l0a_bob": ([128, D], F32),
    "l1cos": ([128, SEQLEN], BF16), "l1sin": ([128, SEQLEN], BF16),
    "l1mask": ([128, 4 * BS], BF16),
}

PASS_BUILDERS = {"l0ffn": build_pass_l0ffn,
                 "l0attn": build_pass_l0attn,
                 "l1ffn": build_pass_l1ffn,
                 "l1attn": build_pass_l1attn}
PASS_WEIGHTS = {
    "l0attn": ["l0a_wqT", "l0a_wkT", "l0a_wvT", "l0a_woT",
               "l0a_bq", "l0a_bk", "l0a_bvb", "l0a_bob", "l0mask4", "ident"],
    "l0ffn": ["l0f_wrT", "l0f_wkT", "l0f_wvT", "l0f_mur", "l0f_muk", "ident"],
    "l1attn": ["l1a_wqkvT", "l1a_woutT", "l1cos", "l1sin",
               "l1mask", "ident"],
    "l1ffn": ["l1f_wgT", "l1f_wuT", "l1f_wdT", "ident"],
}


def build_program(passes):
    nc = bass.Bass("TRN2", target_bir_lowering=False)
    stack = ExitStack()
    em = Em(nc, stack)

    names = sorted({w for p in passes for w in PASS_WEIGHTS[p]})
    cd = {}
    for name in names:
        shp, dt = W_SHAPES[name]
        cd[name] = nc.dram_tensor(name, shp, dt, kind="ExternalInput").ap()

    def sb(name, shape, dt):
        return stack.enter_context(nc.sbuf_tensor(name, shape, dt))[:]

    def ps(name, shape, dt):
        return stack.enter_context(nc.psum_tensor(name, shape, dt))[:]

    ident_s = sb("c_ident", [128, 128], BF16)
    em.dma("sp", lambda e: e.dma_start(ident_s, cd["ident"]), inc="cst",
           mark="c_ident")
    cd["ident_s"] = ident_s

    x_first = nc.dram_tensor("x_in", [TOK, D], F32, kind="ExternalInput").ap()
    y_last = nc.dram_tensor("y", [TOK, D], F32, kind="ExternalOutput").ap()
    mids = [nc.dram_tensor(f"xmid{k}", [128, TOK * D // 128], F32,
                           kind="Internal").ap()
            for k in range(len(passes) - 1)]
    for k, p in enumerate(passes):
        x_in = x_first if k == 0 else mids[k - 1]
        x_out = y_last if k == len(passes) - 1 else mids[k]
        with ExitStack() as pass_stack:
            saved = stack
            inner_sb = (lambda st: lambda n, shp, dt:
                        st.enter_context(nc.sbuf_tensor(n, shp, dt))[:]
                        )(pass_stack)
            inner_ps = (lambda st: lambda n, shp, dt:
                        st.enter_context(nc.psum_tensor(n, shp, dt))[:]
                        )(pass_stack)
            PASS_BUILDERS[p](em, inner_sb, inner_ps, cd, x_in, x_out)
            em.barrier()
    with nc.Block() as blk:
        em.finalize(blk)
    stack.close()
    return nc, names


def kernel(**inputs):
    shared = _host_prep(inputs)
    x = np.asarray(inputs["L0_attn_x"]).astype(np.float32, copy=False)
    x = np.ascontiguousarray(x.reshape(NCORES, TOK, D))

    passes = ["l0attn", "l0ffn", "l1attn", "l1ffn"]
    nc, names = build_program(passes)
    in_maps = []
    for c in range(NCORES):
        m = {n: shared[n] for n in names}
        m["x_in"] = x[c]
        in_maps.append(m)
    res = run_bass_kernel_spmd(nc, in_maps, core_ids=list(range(NCORES)))
    y = np.stack([np.asarray(r["y"]) for r in res.results])
    return np.ascontiguousarray(y.reshape(16, SEQLEN, D))
